# version 24
# baseline (speedup 1.0000x reference)
import numpy as np
from contextlib import ExitStack

import concourse.bass as bass
import concourse.mybir as mybir
from concourse.bass_utils import run_bass_kernel_spmd

F32 = mybir.dt.float32
BF16 = mybir.dt.bfloat16
AF = mybir.ActivationFunctionType
ALU = mybir.AluOpType
AX = mybir.AxisListType

T = 4096
D = 1024
NCH = D // 128



class Sem:
    def __init__(self, nc, es, name):
        self.h = es.enter_context(nc.semaphore(name))
        self.v = 0
        self.name = name

    def at(self):
        return (self, self.v)


class Prog:

    ENGINES = ("sync", "tensor", "vector", "scalar", "gpsimd")

    def __init__(self, nc, es):
        self.nc = nc
        self.es = es
        self.streams = {e: [] for e in self.ENGINES}
        self.named = {}
        self.pending_barrier = {}
        self.esem = {e: self.sem("e_" + e) for e in ("vector", "scalar", "gpsimd")}

    def sem(self, name):
        s = Sem(self.nc, self.es, name)
        self.named[name] = s
        return s

    def nsem(self, name):
        if name not in self.named:
            self.named[name] = Sem(self.nc, self.es, name)
        return self.named[name]

    def stage_barrier(self):
        vals = [(s, s.v) for s in self.named.values() if s.v > 0]
        self.pending_barrier = {e: list(vals) for e in self.ENGINES}

    def add(self, eng, fn, waits=(), inc=None, inc_n=1, is_dma=False):
        wl = []
        pb = self.pending_barrier.pop(eng, None)
        if pb:
            wl.extend(pb)
        for w in waits:
            s, val = (w, w.v) if isinstance(w, Sem) else (w[0], w[1])
            if val > 0:
                wl.append((s, val))
        if not is_dma and eng in self.esem:
            es_ = self.esem[eng]
            if es_.v > 0:
                wl.append((es_, es_.v))
            assert inc is None or inc is es_
            inc = es_
        if inc is not None:
            inc.v += inc_n
        ret = (inc, inc.v) if inc is not None else None

        def emit(engine, fn=fn, wl=wl, inc=inc, inc_n=inc_n):
            for s, val in wl:
                engine.wait_ge(s.h, val)
            instr = fn(engine)
            if inc is not None:
                assert instr is not None, "inc requested but fn returned no instr"
                instr.then_inc(inc.h, inc_n)

        self.streams[eng].append(emit)
        return ret

    def finalize(self):
        nc = self.nc
        with nc.Block() as block:
            for name in self.ENGINES:
                fns = self.streams[name]
                if not fns:
                    continue

                def runner(engine, fns=fns):
                    for f in fns:
                        f(engine)

                getattr(block, name)(runner)



def emit_stage2(nc, P, x_d, wfc_d, wproj_d, xo_d, pc2=None, gc=None):
    es = ExitStack()
    sb = lambda name, shp, dt=F32: es.enter_context(nc.sbuf_tensor('z2_' + name, shp, dt))
    psum = lambda name, shp, dt=F32: es.enter_context(nc.psum_tensor('z2_' + name, shp, dt))

    TB = 512
    NTB = T // TB
    HF = 128

    wfc = sb("wfc_s", [128, NCH * 4096], BF16)
    wproj = sb("wproj_s", [128, 32 * 1024], BF16)
    xbt = [sb(f"xbt{i}", [128, NCH * TB], BF16) for i in range(2)]
    h2all = sb("h2all", [128, 32 * TB], BF16)
    ysb = sb("ysb", [128, NCH * TB])
    spk = sb("spk", [128, NCH * HF], BF16)
    rsb = [sb(f"rsb{i}", [128, NCH * HF], BF16) for i in range(4)]
    r2bf = sb("r2bf", [128, NCH * HF], BF16)
    rows = sb("rows", [1, 3 * HF])
    ones_c = sb("ones_c", [128, 1], BF16)
    xo_s2 = r2bf
    ones_r, epsb, neghalf = gc

    pool = [psum(f"pool{i}", [128, 512]) for i in range(6)]
    mem = psum("memp", [128, NCH * HF])

    xl = [P.nsem("xl0"), P.nsem("xl1")]
    st = P.nsem("stp")
    pe = P.nsem("pe")
    sv = P.esem["vector"]
    ss = P.esem["scalar"]
    sg = P.esem["gpsimd"]

    pool_free = [None] * 6
    _np = [0]

    def next_pool():
        i = _np[0] % 6
        _np[0] += 1
        return i

    P.add("gpsimd", lambda g: g.memset(ones_c[:], 1.0))
    consts_ready = sg.at()
    ls2 = P.nsem("ls")
    first = [True]
    for c in range(NCH):
        P.add("sync", lambda s, c=c: s.dma_start(
            out=wfc[:, c * 4096:(c + 1) * 4096], in_=wfc_d[c]),
            waits=(([ls2] + ([pc2] if pc2 is not None else [])) if first[0] else []),
            inc=ls2, inc_n=16, is_dma=True)
        first[0] = False
    for f in range(32):
        P.add("sync", lambda s, f=f: s.dma_start(
            out=wproj[:, f * 1024:(f + 1) * 1024], in_=wproj_d[f]), inc=ls2, inc_n=16, is_dma=True)
    w_ready = ls2.at()

    def dep(x):
        return [x] if x is not None else []

    S = {
        "xbt_free": [None, None], "h2all_free": None, "ysb_free": None,
        "rsb_free": [None] * 4, "r2bf_free": None, "statA_free": None,
        "statB_free": None, "rows_free": None,
    }

    x_ready = []
    P.add("sync", lambda s: s.dma_start(
        out=xbt[0][:].rearrange("p (c t) -> p c t", c=NCH),
        in_=x_d[:, :, 0:TB].rearrange("c p t -> p c t")),
        waits=[xl[0]], inc=xl[0], inc_n=16, is_dma=True)
    x_ready.append(xl[0].at())

    def emit_tail(gh, r_ready):
        par2 = gh % 4
        t0g = gh * HF
        rbuf = rsb[par2]
        P.add("gpsimd", lambda g, rbuf=rbuf: g.tensor_tensor(
            out=r2bf[:], in0=rbuf[:], in1=rbuf[:], op=ALU.mult),
            waits=[r_ready, (st, st.v)] + dep(S["r2bf_free"]))
        r2_ready = sg.at()
        piA = next_pool()
        for c in range(NCH):
            P.add("tensor", lambda t, c=c, piA=piA, rbuf=rbuf: t.matmul(
                out=pool[piA][0:1, 0:HF], lhsT=ones_c[:], rhs=rbuf[:, c * HF:(c + 1) * HF],
                start=(c == 0), stop=(c == NCH - 1)),
                waits=[r_ready, consts_ready] + dep(pool_free[piA]) + dep(S["statA_free"]),
                inc=pe if c == NCH - 1 else None)
        peA = pe.at()
        piB = next_pool()
        for c in range(NCH):
            P.add("tensor", lambda t, c=c, piB=piB: t.matmul(
                out=pool[piB][0:1, 0:HF], lhsT=ones_c[:], rhs=r2bf[:, c * HF:(c + 1) * HF],
                start=(c == 0), stop=(c == NCH - 1)),
                waits=[r2_ready] + dep(pool_free[piB]) + dep(S["statB_free"]),
                inc=pe if c == NCH - 1 else None)
        peB = pe.at()
        S["r2bf_free"] = peB
        MU, EX2, W = 0, 1, 2
        rw = lambda i: rows[0:1, i * HF:(i + 1) * HF]
        P.add("scalar", lambda s, piA=piA: s.mul(
            out=rw(MU), in_=pool[piA][0:1, 0:HF], mul=-1.0 / D),
            waits=[peA] + dep(S["rows_free"]))
        S["statA_free"] = ss.at()
        P.add("scalar", lambda s, piB=piB: s.mul(
            out=rw(EX2), in_=pool[piB][0:1, 0:HF], mul=1.0 / D),
            waits=[peB])
        S["statB_free"] = ss.at()
        pool_free[piA] = S["statA_free"]
        pool_free[piB] = S["statB_free"]
        P.add("gpsimd", lambda g: g.tensor_tensor(out=rw(W), in0=rw(MU), in1=rw(MU), op=ALU.mult),
              waits=[ss])
        P.add("gpsimd", lambda g: g.tensor_tensor(out=rw(W), in0=rw(EX2), in1=rw(W), op=ALU.subtract))
        P.add("scalar", lambda s: s.activation(out=rw(W), in_=rw(W), func=AF.Ln, bias=epsb),
              waits=[sg])
        P.add("scalar", lambda s: s.activation(out=rw(W), in_=rw(W), func=AF.Exp, scale=neghalf))
        rows_ready = ss.at()
        pi1 = next_pool()
        P.add("tensor", lambda t, pi1=pi1: t.matmul(
            out=pool[pi1][:, 0:HF], lhsT=ones_r, rhs=rw(W), start=True, stop=True),
            waits=[rows_ready] + dep(pool_free[pi1]), inc=pe)
        pi2 = next_pool()
        P.add("tensor", lambda t, pi2=pi2: t.matmul(
            out=pool[pi2][:, 0:HF], lhsT=ones_r, rhs=rw(MU), start=True, stop=True),
            waits=dep(pool_free[pi2]), inc=pe)
        pe_bc = pe.at()
        S["rows_free"] = pe_bc
        P.add("vector", lambda v, pi2=pi2, rbuf=rbuf: v.tensor_tensor(
            out=xo_s2[:].rearrange("p (c t) -> p c t", c=NCH),
            in0=rbuf[:].rearrange("p (c t) -> p c t", c=NCH),
            in1=pool[pi2][:, 0:HF].unsqueeze(1).broadcast_to((128, NCH, HF)),
            op=ALU.add),
            waits=[pe_bc, (st, st.v)])
        pool_free[pi2] = sv.at()
        P.add("vector", lambda v, pi1=pi1: v.tensor_tensor(
            out=xo_s2[:].rearrange("p (c t) -> p c t", c=NCH),
            in0=xo_s2[:].rearrange("p (c t) -> p c t", c=NCH),
            in1=pool[pi1][:, 0:HF].unsqueeze(1).broadcast_to((128, NCH, HF)),
            op=ALU.mult))
        pool_free[pi1] = sv.at()
        out_ready = sv.at()
        S["rsb_free"][par2] = out_ready
        P.add("sync", lambda s, t0g=t0g: s.dma_start(
            out=xo_d[:, :, t0g:t0g + HF].rearrange("c p t -> p c t"),
            in_=xo_s2[:].rearrange("p (c t) -> p c t", c=NCH)),
            waits=[out_ready, st], inc=st, inc_n=16, is_dma=True)

    pend = []
    for tb in range(NTB):
        xpar = tb % 2
        if tb + 1 < NTB:
            pp = (tb + 1) % 2
            P.add("sync", lambda s, tb=tb, pp=pp: s.dma_start(
                out=xbt[pp][:].rearrange("p (c t) -> p c t", c=NCH),
                in_=x_d[:, :, (tb + 1) * TB:(tb + 2) * TB].rearrange("c p t -> p c t")),
                waits=dep(S["xbt_free"][pp]) + [xl[pp]], inc=xl[pp], inc_n=16, is_dma=True)
            x_ready.append(xl[pp].at())

        sq_ready = [None] * 32
        for fc in range(32):
            pi = next_pool()
            for c in range(NCH):
                P.add("tensor", lambda t, c=c, fc=fc, pi=pi, xpar=xpar: t.matmul(
                    out=pool[pi][:],
                    lhsT=wfc[:, c * 4096 + fc * 128: c * 4096 + (fc + 1) * 128],
                    rhs=xbt[xpar][:, c * TB:(c + 1) * TB],
                    start=(c == 0), stop=(c == NCH - 1)),
                    waits=[w_ready, x_ready[tb]] + dep(pool_free[pi]),
                    inc=pe if c == NCH - 1 else None)
            pe_h1 = pe.at()
            P.add("scalar", lambda s, pi=pi, fc=fc: s.activation(
                out=h2all[:, fc * TB:(fc + 1) * TB], in_=pool[pi][:], func=AF.Relu),
                waits=[pe_h1] + dep(S["h2all_free"] if fc == 0 else None))
            P.add("vector", lambda v, pi=pi, fc=fc: v.tensor_tensor(
                out=h2all[:, fc * TB:(fc + 1) * TB],
                in0=h2all[:, fc * TB:(fc + 1) * TB],
                in1=pool[pi][:], op=ALU.mult),
                waits=[ss])
            pool_free[pi] = sv.at()
            sq_ready[fc] = sv.at()

        for d2 in range(NCH):
            pi = next_pool()
            for fc in range(32):
                P.add("tensor", lambda t, d2=d2, fc=fc, pi=pi: t.matmul(
                    out=pool[pi][:],
                    lhsT=wproj[:, fc * 1024 + d2 * 128: fc * 1024 + (d2 + 1) * 128],
                    rhs=h2all[:, fc * TB:(fc + 1) * TB],
                    start=(fc == 0), stop=(fc == 31)),
                    waits=[sq_ready[fc]] + dep(pool_free[pi]),
                    inc=pe if fc == 31 else None)
            pe_y = pe.at()
            P.add("scalar", lambda s, d2=d2, pi=pi: s.copy(
                out=ysb[:, d2 * TB:(d2 + 1) * TB], in_=pool[pi][:]),
                waits=[pe_y] + (dep(S["ysb_free"]) if d2 == 0 else []))
            pool_free[pi] = ss.at()
        S["h2all_free"] = pe.at()
        y_ready = ss.at()

        for h in range(4):
            gh = tb * 4 + h
            rpar = gh % 4
            yv = ysb[:].rearrange("p (c t) -> p c t", c=NCH)[:, :, h * HF:(h + 1) * HF]
            xv = xbt[xpar][:].rearrange("p (c t) -> p c t", c=NCH)[:, :, h * HF:(h + 1) * HF]
            mv = mem[:].rearrange("p (c t) -> p c t", c=NCH)
            sp = spk[:].rearrange("p (c t) -> p c t", c=NCH)
            rv = rsb[rpar][:].rearrange("p (c t) -> p c t", c=NCH)
            P.add("scalar", lambda s, yv=yv, mv=mv: s.copy(out=mv, in_=yv),
                  waits=[y_ready, sv])
            P.add("vector", lambda v, yv=yv, sp=sp: v.tensor_single_scalar(
                out=sp, in_=yv, scalar=1.0, op=ALU.is_gt),
                waits=[y_ready])
            P.add("scalar", lambda s, sp=sp, rv=rv: s.copy(out=rv, in_=sp),
                  waits=[sv] + dep(S["rsb_free"][rpar]))
            for step in range(4):
                P.add("vector", lambda v, yv=yv, mv=mv: v.scalar_tensor_tensor(
                    out=mv, in0=mv, scalar=0.9, op0=ALU.mult, in1=yv, op1=ALU.add),
                    waits=[ss] if step == 0 else [])
                P.add("vector", lambda v, mv=mv, sp=sp: v.tensor_tensor(
                    out=mv, in0=mv, in1=sp, op=ALU.subtract))
                P.add("vector", lambda v, mv=mv, sp=sp: v.tensor_single_scalar(
                    out=sp, in_=mv, scalar=1.0, op=ALU.is_gt))
                P.add("vector", lambda v, sp=sp, rv=rv: v.tensor_tensor(
                    out=rv, in0=rv, in1=sp, op=ALU.add))
            P.add("vector", lambda v, rv=rv, xv=xv: v.scalar_tensor_tensor(
                out=rv, in0=rv, scalar=0.2, op0=ALU.mult, in1=xv, op1=ALU.add))
            r_ready = sv.at()
            if h == 3:
                S["xbt_free"][xpar] = r_ready
                S["ysb_free"] = r_ready

            if len(pend) == 3:
                pend.pop(0)()
            pend.append(lambda gh=gh, r=r_ready: emit_tail(gh, r))

    for f in pend:
        f()
    es.close()




class LNPack:

    def __init__(self, nc, es, TB, tag=""):
        sb = lambda name, shp, dt=F32: es.enter_context(nc.sbuf_tensor(name, shp, dt))
        self.TB = TB
        self.rbf = sb(f"ln_rbf{tag}", [128, NCH * TB], BF16)
        self.r2bf = sb(f"ln_r2bf{tag}", [128, NCH * TB], BF16)
        self.rows = sb(f"ln_rows{tag}", [1, 6 * TB])
        self.statp = es.enter_context(nc.psum_tensor(f"ln_statp{tag}", [1, 2 * TB], F32))
        self.bcp = es.enter_context(nc.psum_tensor(f"ln_bcp{tag}", [128, 2 * TB], F32))
        self.statp_free = None
        self.bcp_free = None
        self.rbuf_free = None


def emit_ln(P, pe, consts, lnp, rsb, xo_s, r_ready, out_free=None):
    sv, ss = P.esem["vector"], P.esem["scalar"]
    TB = lnp.TB
    dep = lambda x: [x] if x is not None else []
    ones_c, ones_r, epsb, consts_ready = consts

    P.add("vector", lambda v: v.tensor_copy(out=lnp.rbf[:], in_=rsb),
          waits=[r_ready])
    rbf_ready = sv.at()
    P.add("scalar", lambda s: s.activation(out=lnp.r2bf[:], in_=rsb, func=AF.Square),
          waits=[r_ready])
    r2_ready = ss.at()
    for c in range(NCH):
        P.add("tensor", lambda t, c=c: t.matmul(
            out=lnp.statp[0:1, 0:TB],
            lhsT=ones_c[:], rhs=lnp.rbf[:, c * TB:(c + 1) * TB],
            start=(c == 0), stop=(c == NCH - 1)),
            waits=[rbf_ready, consts_ready] + dep(lnp.statp_free),
            inc=pe if c == NCH - 1 else None)
    for c in range(NCH):
        P.add("tensor", lambda t, c=c: t.matmul(
            out=lnp.statp[0:1, TB:2 * TB],
            lhsT=ones_c[:], rhs=lnp.r2bf[:, c * TB:(c + 1) * TB],
            start=(c == 0), stop=(c == NCH - 1)),
            waits=[r2_ready], inc=pe if c == NCH - 1 else None)
    pe_stat = pe.at()
    lnp.rbuf_free = pe_stat
    MU, EX2, VAR, RSTD, NMR, SCR = 0, 1, 2, 3, 4, 5
    rw = lambda i: lnp.rows[0:1, i * TB:(i + 1) * TB]
    P.add("scalar", lambda s: s.mul(out=rw(MU), in_=lnp.statp[0:1, 0:TB], mul=1.0 / D),
          waits=[pe_stat])
    P.add("scalar", lambda s: s.mul(out=rw(EX2), in_=lnp.statp[0:1, TB:2 * TB], mul=1.0 / D),
          waits=[pe_stat])
    lnp.statp_free = ss.at()
    P.add("vector", lambda v: v.tensor_tensor(
        out=rw(VAR), in0=rw(MU), in1=rw(MU), op=ALU.mult), waits=[ss])
    P.add("vector", lambda v: v.tensor_tensor(
        out=rw(VAR), in0=rw(EX2), in1=rw(VAR), op=ALU.subtract))
    P.add("scalar", lambda s: s.activation(
        out=rw(SCR), in_=rw(VAR), func=AF.Sqrt, bias=epsb[:]), waits=[sv])
    P.add("vector", lambda v: v.reciprocal(out=rw(RSTD), in_=rw(SCR)), waits=[ss])
    P.add("vector", lambda v: v.tensor_tensor(
        out=rw(NMR), in0=rw(MU), in1=rw(RSTD), op=ALU.mult))
    P.add("scalar", lambda s: s.mul(out=rw(NMR), in_=rw(NMR), mul=-1.0), waits=[sv])
    rows_ready = ss.at()
    P.add("tensor", lambda t: t.matmul(
        out=lnp.bcp[:, 0:TB], lhsT=ones_r[:], rhs=rw(RSTD), start=True, stop=True),
        waits=[rows_ready] + dep(lnp.bcp_free), inc=pe)
    P.add("tensor", lambda t: t.matmul(
        out=lnp.bcp[:, TB:2 * TB], lhsT=ones_r[:], rhs=rw(NMR), start=True, stop=True),
        inc=pe)
    pe_bc = pe.at()
    P.add("vector", lambda v: v.tensor_tensor(
        out=xo_s.rearrange("p (c t) -> p c t", c=NCH),
        in0=rsb.rearrange("p (c t) -> p c t", c=NCH),
        in1=lnp.bcp[:, 0:TB].unsqueeze(1).broadcast_to((128, NCH, TB)),
        op=ALU.mult),
        waits=[pe_bc, sv] + dep(out_free))
    P.add("vector", lambda v: v.tensor_tensor(
        out=xo_s.rearrange("p (c t) -> p c t", c=NCH),
        in0=xo_s.rearrange("p (c t) -> p c t", c=NCH),
        in1=lnp.bcp[:, TB:2 * TB].unsqueeze(1).broadcast_to((128, NCH, TB)),
        op=ALU.add))
    out_ready = sv.at()
    lnp.bcp_free = out_ready
    return out_ready



def emit_stage3(nc, P, x_d, wq_d, wk_d, wv_d, wo_d, bqk_d, bo_d, bv_d, mask_d, id_d, xo_d, pc3=None):
    W = 256
    NW = T // W
    es = ExitStack()
    sb = lambda name, shp, dt=F32: es.enter_context(nc.sbuf_tensor('z3_' + name, shp, dt))
    psum = lambda name, shp, dt=F32: es.enter_context(nc.psum_tensor('z3_' + name, shp, dt))

    wq = sb("wq_s", [128, NCH * 1024], BF16)
    wk = sb("wk_s", [128, NCH * 1024], BF16)
    wv = sb("wv_s", [128, NCH * 1024], BF16)
    wo = sb("wo_s", [128, NCH * 1024], BF16)
    bqk = sb("bqk_s", [128, 16])
    bo_c = sb("bo_s", [128, 8])
    bv_r = sb("bv_s", [1, 1024], BF16)
    maskc = sb("mask_s", [128, 512])
    identb = sb("id_s", [128, 128], BF16)
    ones_c = sb("ones_c", [128, 1], BF16)
    ones_r = sb("ones_r", [1, 128])
    onesr_b = sb("onesr_b", [1, 128], BF16)
    epsb = sb("epsb", [1, 1])
    c0125 = sb("c0125", [128, 1])

    WP = 2 * W
    xw = [sb(f"xw{i}", [128, NCH * WP], BF16) for i in range(2)]
    q_bf = sb("q_bf", [128, NCH * WP], BF16)
    k_bf = sb("k_bf", [128, NCH * WP], BF16)
    vtok = sb("vtok", [128, 4 * 1024], BF16)
    obf = sb("obf", [128, NCH * WP], BF16)
    sm = sb("sm", [128, W])
    p_bf = sb("p_bf", [128, W], BF16)
    pn_bf = sb("pn_bf", [128, W], BF16)
    pt_bf = sb("pt_bf", [128, 2 * 128], BF16)
    nmv = sb("nmv", [128, 2])
    sums = sb("sums", [128, 2])
    tsb = [sb(f"tsb{i}", [128, WP]) for i in range(2)]
    rsb = sb("rsb", [128, NCH * WP])
    xo_s = sb("xo_s", [128, NCH * WP])

    pool = [psum(f"pool{i}", [128, 512]) for i in range(2)]
    ptp = [psum(f"ptp{i}", [128, 128], BF16) for i in range(2)]
    lnp = LNPack(nc, es, WP)

    lg = P.nsem("lg")
    xl = [P.nsem("xl0"), P.nsem("xl1")]
    st = P.nsem("st")
    pe = P.nsem("pe")
    sv = P.esem["vector"]
    ss = P.esem["scalar"]
    sg = P.esem["gpsimd"]

    P.add("gpsimd", lambda g: g.memset(ones_c[:], 1.0))
    P.add("gpsimd", lambda g: g.memset(ones_r[:], 1.0))
    P.add("gpsimd", lambda g: g.memset(onesr_b[:], 1.0))
    P.add("gpsimd", lambda g: g.memset(epsb[:], 1e-5))
    P.add("gpsimd", lambda g: g.memset(c0125[:], 0.125))
    consts_ready = sg.at()
    consts = (ones_c, ones_r, epsb, consts_ready)
    ls3 = P.nsem("ls")
    first3 = [True]
    for c in range(NCH):
        for dst, srcw in ((wq, wq_d), (wk, wk_d), (wv, wv_d), (wo, wo_d)):
            P.add("sync", lambda s, c=c, dst=dst, srcw=srcw: s.dma_start(
                out=dst[:, c * 1024:(c + 1) * 1024], in_=srcw[c]),
                waits=([ls3, pc3] if first3[0] else []), inc=ls3, inc_n=16, is_dma=True)
            first3[0] = False
    P.add("gpsimd", lambda g: g.dma_start(out=bqk[:], in_=bqk_d), inc=lg, inc_n=16, is_dma=True)
    P.add("gpsimd", lambda g: g.dma_start(out=bo_c[:], in_=bo_d), inc=lg, inc_n=16, is_dma=True)
    P.add("gpsimd", lambda g: g.dma_start(out=bv_r[:], in_=bv_d), inc=lg, inc_n=16, is_dma=True)
    P.add("gpsimd", lambda g: g.dma_start(out=maskc[:], in_=mask_d), inc=lg, inc_n=16, is_dma=True)
    P.add("gpsimd", lambda g: g.dma_start(out=identb[:], in_=id_d), inc=lg, inc_n=16, is_dma=True)
    w_ready = lg.at()
    w_big = ls3.at()

    dep = lambda x: [x] if x is not None else []
    pool_free = [None, None]
    ptp_free = [None, None]
    qk_free = None
    vtok_free = None
    obf_free = None
    xw_free = [None, None]
    sm_free = None
    p_free = None
    pn_free = [None, None]
    pt_free = None
    tsb_free = [None, None]
    rsb_free = None
    out_free = None

    x_ready = []
    P.add("sync", lambda s: s.dma_start(
        out=xw[0][:].rearrange("p (c t) -> p c t", c=NCH),
        in_=x_d[:, :, 0:WP].rearrange("c p t -> p c t")),
        waits=[xl[0]], inc=xl[0], inc_n=16, is_dma=True)
    x_ready.append(xl[0].at())

    npool = 0

    def next_pool():
        nonlocal npool
        i = npool % 2
        npool += 1
        return i

    NWP = NW // 2
    for wp in range(NWP):
        t0 = wp * WP
        wpar = wp % 2
        if wp + 1 < NWP:
            pp = (wp + 1) % 2
            P.add("sync", lambda s, wp=wp, pp=pp: s.dma_start(
                out=xw[pp][:].rearrange("p (c t) -> p c t", c=NCH),
                in_=x_d[:, :, (wp + 1) * WP:(wp + 2) * WP].rearrange("c p t -> p c t")),
                waits=dep(xw_free[pp]) + [xl[pp]], inc=xl[pp], inc_n=16, is_dma=True)
            x_ready.append(xl[pp].at())

        for which, wt, dst, boff in (("q", wq, q_bf, 0), ("k", wk, k_bf, 8)):
            for m in range(NCH):
                pi = next_pool()
                for c in range(NCH):
                    P.add("tensor", lambda t, c=c, m=m, pi=pi, wt=wt, wpar=wpar: t.matmul(
                        out=pool[pi][:],
                        lhsT=wt[:, c * 1024 + m * 128: c * 1024 + (m + 1) * 128],
                        rhs=xw[wpar][:, c * WP:(c + 1) * WP],
                        start=(c == 0), stop=(c == NCH - 1)),
                        waits=[w_ready, w_big, x_ready[wp]] + dep(pool_free[pi]) + dep(qk_free if (which == "q" and m == 0) else None),
                        inc=pe if c == NCH - 1 else None)
                pe_m = pe.at()
                P.add("scalar", lambda s, m=m, pi=pi, dst=dst, boff=boff: s.activation(
                    out=dst[:, m * WP:(m + 1) * WP], in_=pool[pi][:],
                    func=AF.Identity, bias=bqk[:, boff + m:boff + m + 1]),
                    waits=[pe_m])
                pool_free[pi] = ss.at()
        qk_ready = ss.at()

        for tt in range(4):
            for half in range(2):
                pi = next_pool()
                for c in range(NCH):
                    P.add("tensor", lambda t, c=c, tt=tt, half=half, pi=pi, wpar=wpar: t.matmul(
                        out=pool[pi][:],
                        lhsT=xw[wpar][:, c * WP + tt * 128: c * WP + tt * 128 + 128],
                        rhs=wv[:, c * 1024 + half * 512: c * 1024 + (half + 1) * 512],
                        start=(c == 0), stop=False),
                        waits=[w_ready, w_big, x_ready[wp]] + dep(pool_free[pi]) + dep(vtok_free if (tt == 0 and half == 0) else None),
                        inc=None)
                P.add("tensor", lambda t, tt=tt, half=half, pi=pi: t.matmul(
                    out=pool[pi][:],
                    lhsT=onesr_b[:],
                    rhs=bv_r[0:1, half * 512:(half + 1) * 512],
                    start=False, stop=True),
                    waits=[consts_ready], inc=pe)
                pe_v = pe.at()
                P.add("vector", lambda v, tt=tt, half=half, pi=pi: v.tensor_copy(
                    out=vtok[:, tt * 1024 + half * 512: tt * 1024 + (half + 1) * 512],
                    in_=pool[pi][:]),
                    waits=[pe_v])
                pool_free[pi] = sv.at()
        vtok_ready = sv.at()

        for sub in range(2):
            so = sub * W
            for qt in range(2):
                pi = next_pool()
                for c in range(NCH):
                    P.add("tensor", lambda t, c=c, qt=qt, pi=pi, so=so: t.matmul(
                        out=pool[pi][:, 0:W],
                        lhsT=q_bf[:, c * WP + so + qt * 128: c * WP + so + qt * 128 + 128],
                        rhs=k_bf[:, c * WP + so: c * WP + so + W],
                        start=(c == 0), stop=(c == NCH - 1)),
                        waits=[qk_ready] + dep(pool_free[pi]),
                        inc=pe if c == NCH - 1 else None)
                pe_sim = pe.at()
                P.add("vector", lambda v, qt=qt, pi=pi: v.tensor_tensor(
                    out=sm[:], in0=pool[pi][:, 0:W], in1=maskc[:, qt * W:(qt + 1) * W],
                    op=ALU.add),
                    waits=[pe_sim, w_ready] + dep(sm_free))
                pool_free[pi] = sv.at()
                P.add("vector", lambda v: v.reduce_max(
                    out=nmv[:, 0:1], in_=sm[:], axis=AX.X, negate=True))
                P.add("vector", lambda v: v.tensor_single_scalar(
                    out=nmv[:, 1:2], in_=nmv[:, 0:1], scalar=0.125, op=ALU.mult))
                nm_ready = sv.at()
                P.add("scalar", lambda s: s.activation(
                    out=p_bf[:], in_=sm[:], func=AF.Exp, scale=c0125[:],
                    bias=nmv[:, 1:2], accum_out=sums[:, 0:1]),
                    waits=[nm_ready, consts_ready] + dep(p_free))
                sm_free = ss.at()
                exp_ready = ss.at()
                P.add("vector", lambda v: v.reciprocal(out=sums[:, 1:2], in_=sums[:, 0:1]),
                      waits=[exp_ready])
                P.add("vector", lambda v, qt=qt: v.tensor_single_scalar(
                    out=pn_bf[:], in_=p_bf[:], scalar=sums[:, 1:2], op=ALU.mult),
                    waits=dep(pn_free[0]) + dep(pn_free[1]))
                p_free = sv.at()
                pn_ready = sv.at()
                for tt in range(2):
                    P.add("tensor", lambda t, tt=tt: t.transpose(
                        out=ptp[tt][:], in_=pn_bf[:, tt * 128:(tt + 1) * 128], identity=identb[:]),
                        waits=[pn_ready, w_ready] + dep(ptp_free[tt]), inc=pe)
                    pe_t = pe.at()
                    P.add("scalar", lambda s, tt=tt: s.copy(
                        out=pt_bf[:, tt * 128:(tt + 1) * 128], in_=ptp[tt][:]),
                        waits=[pe_t] + dep(pt_free if tt == 0 and qt == 0 and sub == 0 else None))
                    ptp_free[tt] = ss.at()
                pn_free[qt] = pe.at()
                pt_ready = ss.at()
                for m in range(NCH):
                    pi = next_pool()
                    for tt in range(2):
                        P.add("tensor", lambda t, m=m, tt=tt, pi=pi, sub=sub: t.matmul(
                            out=pool[pi][:, 0:128],
                            lhsT=vtok[:, (sub * 2 + tt) * 1024 + m * 128: (sub * 2 + tt) * 1024 + (m + 1) * 128],
                            rhs=pt_bf[:, tt * 128:(tt + 1) * 128],
                            start=(tt == 0), stop=(tt == 1)),
                            waits=[pt_ready, vtok_ready] + dep(pool_free[pi]) + dep(obf_free if (sub == 0 and qt == 0 and m == 0) else None),
                            inc=pe if tt == 1 else None)
                    pe_o = pe.at()
                    P.add("scalar", lambda s, m=m, qt=qt, pi=pi, so=so: s.copy(
                        out=obf[:, m * WP + so + qt * 128: m * WP + so + qt * 128 + 128],
                        in_=pool[pi][:, 0:128]),
                        waits=[pe_o])
                    pool_free[pi] = ss.at()
        pt_free = pe.at()
        o_ready = ss.at()

        for e3 in range(NCH):
            pi = next_pool()
            tpar = e3 % 2
            for c in range(NCH):
                P.add("tensor", lambda t, c=c, e3=e3, pi=pi: t.matmul(
                    out=pool[pi][:],
                    lhsT=wo[:, c * 1024 + e3 * 128: c * 1024 + (e3 + 1) * 128],
                    rhs=obf[:, c * WP:(c + 1) * WP],
                    start=(c == 0), stop=(c == NCH - 1)),
                    waits=[o_ready] + dep(pool_free[pi]),
                    inc=pe if c == NCH - 1 else None)
            pe_wo = pe.at()
            P.add("scalar", lambda s, e3=e3, pi=pi, tpar=tpar: s.activation(
                out=tsb[tpar][:], in_=pool[pi][:],
                func=AF.Identity, bias=bo_c[:, e3:e3 + 1]),
                waits=[pe_wo] + dep(tsb_free[tpar]))
            pool_free[pi] = ss.at()
            ev = ss.at()
            P.add("vector", lambda v, e3=e3, tpar=tpar, wpar=wpar: v.scalar_tensor_tensor(
                out=rsb[:, e3 * WP:(e3 + 1) * WP],
                in0=xw[wpar][:, e3 * WP:(e3 + 1) * WP], scalar=2.0, in1=tsb[tpar][:],
                op0=ALU.mult, op1=ALU.add),
                waits=[ev] + dep(rsb_free if e3 == 0 else None))
            tsb_free[tpar] = sv.at()
        r_ready = sv.at()
        xw_free[wpar] = r_ready
        obf_free = pe.at()

        out_ready = emit_ln(P, pe, consts, lnp, rsb[:], xo_s[:], r_ready,
                            out_free=(st, st.v) if st.v > 0 else None)
        rsb_free = out_ready
        qk_free = pe.at()
        vtok_free = pe.at()
        P.add("gpsimd", lambda g, t0=t0: g.dma_start(
            out=xo_d[:, :, t0:t0 + WP].rearrange("c p t -> p c t"),
            in_=xo_s[:].rearrange("p (c t) -> p c t", c=NCH)),
            waits=[out_ready], inc=st, inc_n=16, is_dma=True)

    es.close()



def emit_stage1(nc, P, D1):
    (x_d, wq_d, wk_d, wv_d, wg_d, wab_d, wo_d, bq_d, bk_d, bg_d, bv_d, bab_d,
     bo_d, cum_d, id_d, madd_d, mstr_d, xo_d) = D1
    es = ExitStack()
    sb = lambda name, shp, dt=F32: es.enter_context(nc.sbuf_tensor('z1_' + name, shp, dt))
    psum = lambda name, shp, dt=F32: es.enter_context(nc.psum_tensor('z1_' + name, shp, dt))

    xb = sb("xb", [128, NCH * T], BF16)
    wq = sb("wq_s", [128, NCH * 128], BF16)
    wk = sb("wk_s", [128, NCH * 128], BF16)
    wv = sb("wv_s", [128, NCH * 128], BF16)
    wg = sb("wg_s", [128, NCH * 128], BF16)
    wab = sb("wab_s", [128, NCH * 2], BF16)
    wo = sb("wo_s", [128, 1024], BF16)
    bqc = sb("bqc", [128, 1]); bkc = sb("bkc", [128, 1]); bgc = sb("bgc", [128, 1])
    bvr = sb("bvr", [1, 128], BF16)
    bab = sb("bab_s", [1, 2])
    boc = sb("boc", [128, NCH])
    cumUT = sb("cum_s", [128, 128])
    identf = sb("idf_s", [128, 128])
    identb = sb("idb_s", [128, 128], BF16)
    maskadd = sb("madd_s", [128, 128])
    maskstr = sb("mstr_s", [128, 128], BF16)
    ones_c = sb("ones_c", [128, 1], BF16)
    onesr_f = sb("onesr_f", [1, 128])
    onesr_b = sb("onesr_b", [1, 128], BF16)
    epsln = sb("epsln", [1, 1])
    epsn2 = sb("epsn2", [1, 1])

    qT = sb("qT", [128, T], BF16)
    kT = sb("kT", [128, T], BF16)
    kr2 = sb("kr2", [128, T], BF16)
    gT = sb("gT", [128, T], BF16)
    vtok = sb("vtok", [128, T], BF16)
    arow = sb("arow", [1, T])
    brow = sb("brow", [1, T])
    lnar = sb("lnar", [1, T])
    rnr = sb("rnr", [1, T])

    Zf = sb("Zf", [128, 128]); ZTf = sb("ZTf", [128, 128])
    Zb = sb("Zb", [128, 128], BF16); ZTb = sb("ZTb", [128, 128], BF16)
    lnac = sb("lnac", [128, 2])
    lgc = sb("lgc", [128, 1])
    lgr = sb("lgr", [1, 128])
    nlgr = sb("nlgr", [1, 128])
    grow = sb("grow", [1, 128])
    gLc = sb("gLc", [128, 1])
    gcol = sb("gcol", [128, 1])
    dcol = sb("dcol", [128, 1])
    gdif = sb("gdif", [128, 128])
    gamT = sb("gamT", [128, 128], BF16)
    Dm = sb("Dm", [128, 128], BF16)
    S0b = sb("S0b", [128, 128], BF16)
    S0Tb = sb("S0Tb", [128, 128], BF16)
    S1b = sb("S1b", [128, 128], BF16)
    S1Tb = sb("S1Tb", [128, 128], BF16)
    Ub = sb("Ub", [128, 128], BF16)
    UTb = sb("UTb", [128, 128], BF16)
    Udb = sb("Udb", [128, 128], BF16)
    Rb = sb("Rb", [128, 128], BF16)
    ktk = sb("ktk", [128, 128], BF16)
    qgb = sb("qgb", [128, 128], BF16)
    ptl = sb("ptl", [128, 128], BF16)
    obf = sb("obf", [128, 128], BF16)
    rsb1 = sb("rsb1", [128, NCH * 128])
    rbf1 = sb("rbf1", [128, NCH * 128], BF16)
    r2b1 = sb("r2b1", [128, NCH * 128], BF16)
    xo1 = sb("xo1", [128, NCH * 128], BF16)
    rows1 = sb("rows1", [1, 6 * 128])

    pool = [psum(f"pool{i}", [128, 512]) for i in range(8)]

    lg = P.nsem("lg")
    st = P.nsem("st")
    pe = P.nsem("pe")
    sv = P.esem["vector"]; ss = P.esem["scalar"]; sg = P.esem["gpsimd"]
    dep = lambda x: [x] if x is not None else []
    pool_free = [None] * 8
    _np = [0]

    def next_pool():
        i = _np[0] % 8
        _np[0] += 1
        return i

    def lp(fn):
        def wrapped(engine):
            with nc.allow_low_precision(reason="bf16 operands for matmul chains"):
                return fn(engine)
        return wrapped

    P.add("gpsimd", lambda g: g.memset(ones_c[:], 1.0))
    P.add("gpsimd", lambda g: g.memset(onesr_f[:], 1.0))
    P.add("gpsimd", lambda g: g.memset(onesr_b[:], 1.0))
    P.add("gpsimd", lambda g: g.memset(epsln[:], 1e-5))
    P.add("gpsimd", lambda g: g.memset(epsn2[:], 1e-24))
    P.add("gpsimd", lambda g: g.memset(Zf[:], 0.0))
    P.add("gpsimd", lambda g: g.memset(ZTf[:], 0.0))
    P.add("gpsimd", lambda g: g.memset(Zb[:], 0.0))
    P.add("gpsimd", lambda g: g.memset(ZTb[:], 0.0))
    consts_mem = sg.at()
    for dst, src in ((wq, wq_d), (wk, wk_d), (wv, wv_d), (wg, wg_d), (wab, wab_d)):
        P.add("gpsimd", lambda g, dst=dst, src=src: g.dma_start(
            out=dst[:].rearrange("p (c k) -> p c k", c=NCH), in_=src.rearrange("c p k -> p c k")),
            inc=lg, inc_n=16, is_dma=True)
    P.add("gpsimd", lambda g: g.dma_start(out=wo[:], in_=wo_d), inc=lg, inc_n=16, is_dma=True)
    for dst, src in ((bqc, bq_d), (bkc, bk_d), (bgc, bg_d), (bvr, bv_d), (bab, bab_d),
                     (boc, bo_d), (cumUT, cum_d), (identf, id_d), (identb, id_d),
                     (maskadd, madd_d), (maskstr, mstr_d)):
        P.add("gpsimd", lambda g, dst=dst, src=src: g.dma_start(out=dst[:], in_=src),
              inc=lg, inc_n=16, is_dma=True)
    for c in range(NCH):
        P.add("gpsimd", lambda g, c=c: g.dma_start(
            out=xb[:, c * T:(c + 1) * T], in_=x_d[c]), inc=lg, inc_n=16, is_dma=True)
    w_ready = lg.at()

    TBP = 512
    qT_ready = None
    for tb in range(T // TBP):
        t0 = tb * TBP
        for which, wt, bcol, func, dsts in (
                ("q", wq, bqc, AF.Identity, (qT,)),
                ("k", wk, bkc, None, (kT, kr2)),
                ("g", wg, bgc, AF.Sigmoid, (gT,))):
            pi = next_pool()
            for c in range(NCH):
                P.add("tensor", lambda t, c=c, pi=pi, wt=wt, t0=t0: t.matmul(
                    out=pool[pi][:],
                    lhsT=wt[:, c * 128:(c + 1) * 128],
                    rhs=xb[:, c * T + t0: c * T + t0 + TBP],
                    start=(c == 0), stop=(c == NCH - 1)),
                    waits=[w_ready] + dep(pool_free[pi]),
                    inc=pe if c == NCH - 1 else None)
            pe_m = pe.at()
            if which == "k":
                P.add("scalar", lambda s, pi=pi, t0=t0: s.activation(
                    out=kT[:, t0:t0 + TBP], in_=pool[pi][:], func=AF.Identity, bias=bkc[:]),
                    waits=[pe_m])
                P.add("scalar", lambda s, pi=pi, t0=t0: s.activation(
                    out=kr2[:, t0:t0 + TBP], in_=pool[pi][:], func=AF.Square, bias=bkc[:]))
            else:
                P.add("scalar", lambda s, pi=pi, t0=t0, dst=dsts[0], func=func, bcol=bcol: s.activation(
                    out=dst[:, t0:t0 + TBP], in_=pool[pi][:], func=func, bias=bcol[:]),
                    waits=[pe_m])
            pool_free[pi] = ss.at()
        for tt in range(4):
            pi = next_pool()
            for c in range(NCH):
                P.add("tensor", lambda t, c=c, pi=pi, t0=t0, tt=tt: t.matmul(
                    out=pool[pi][:, 0:128],
                    lhsT=xb[:, c * T + t0 + tt * 128: c * T + t0 + tt * 128 + 128],
                    rhs=wv[:, c * 128:(c + 1) * 128],
                    start=(c == 0), stop=False),
                    waits=[w_ready] + dep(pool_free[pi]),
                    inc=None)
            P.add("tensor", lambda t, pi=pi: t.matmul(
                out=pool[pi][:, 0:128], lhsT=onesr_b[0:1, :],
                rhs=bvr[0:1, :], start=False, stop=True),
                waits=[consts_mem], inc=pe)
            pe_v = pe.at()
            P.add("scalar", lambda s, pi=pi, t0=t0, tt=tt: s.copy(
                out=vtok[:, t0 + tt * 128: t0 + tt * 128 + 128], in_=pool[pi][:, 0:128]),
                waits=[pe_v])
            pool_free[pi] = ss.at()
        for ab in range(2):
            pi = next_pool()
            for c in range(NCH):
                P.add("tensor", lambda t, c=c, pi=pi, ab=ab, t0=t0: t.matmul(
                    out=pool[pi][0:1, :],
                    lhsT=wab[:, c * 2 + ab: c * 2 + ab + 1],
                    rhs=xb[:, c * T + t0: c * T + t0 + TBP],
                    start=(c == 0), stop=(c == NCH - 1)),
                    waits=[w_ready] + dep(pool_free[pi]),
                    inc=pe if c == NCH - 1 else None)
            pe_ab = pe.at()
            P.add("scalar", lambda s, pi=pi, ab=ab, t0=t0: s.activation(
                out=(arow if ab == 0 else brow)[0:1, t0:t0 + TBP], in_=pool[pi][0:1, :],
                func=AF.Sigmoid, bias=bab[0:1, ab:ab + 1]),
                waits=[pe_ab])
            pool_free[pi] = ss.at()
    p1_done = ss.at()

    for tb in range(T // TBP):
        t0 = tb * TBP
        pi = next_pool()
        for blk in range(1):
            P.add("tensor", lambda t, pi=pi, t0=t0: t.matmul(
                out=pool[pi][0:1, :], lhsT=ones_c[:], rhs=kr2[:, t0:t0 + TBP],
                start=True, stop=True),
                waits=[p1_done, consts_mem] + dep(pool_free[pi]), inc=pe)
        pe_n2 = pe.at()
        P.add("vector", lambda v, pi=pi, t0=t0: v.tensor_single_scalar(
            out=rnr[0:1, t0:t0 + TBP], in_=pool[pi][0:1, :], scalar=1e-24, op=ALU.max),
            waits=[pe_n2])
        pool_free[pi] = sv.at()
        P.add("scalar", lambda s, t0=t0: s.activation(
            out=rnr[0:1, t0:t0 + TBP], in_=rnr[0:1, t0:t0 + TBP], func=AF.Ln),
            waits=[sv])
        P.add("scalar", lambda s, t0=t0: s.activation(
            out=rnr[0:1, t0:t0 + TBP], in_=rnr[0:1, t0:t0 + TBP], func=AF.Exp, scale=-0.5))
        rn_ready = ss.at()
        pi2 = next_pool()
        P.add("tensor", lambda t, pi2=pi2, t0=t0: t.matmul(
            out=pool[pi2][:], lhsT=onesr_f[:], rhs=rnr[0:1, t0:t0 + TBP],
            start=True, stop=True),
            waits=[rn_ready] + dep(pool_free[pi2]), inc=pe)
        pe_bc = pe.at()
        P.add("vector", lambda v, pi2=pi2, t0=t0: v.tensor_tensor(
            out=kT[:, t0:t0 + TBP], in0=kT[:, t0:t0 + TBP], in1=pool[pi2][:], op=ALU.mult),
            waits=[pe_bc])
        pool_free[pi2] = sv.at()
    P.add("scalar", lambda s: s.activation(out=lnar[:], in_=arow[:], func=AF.Ln),
          waits=[p1_done])
    p2_done = (ss.at(), sv.at())

    state_ready = consts_mem
    rsb_free = None
    xo_free = None
    rows_free = None
    kn_ready = p2_done[1]
    for ch in range(32):
        t0 = ch * 128
        sl = lambda tsr, t0=t0: tsr[:, t0:t0 + 128]
        pi = next_pool()
        P.add("tensor", lambda t, pi=pi, t0=t0: t.matmul(
            out=pool[pi][:, 0:1], lhsT=lnar[0:1, t0:t0 + 128], rhs=onesr_f[0:1, 0:1],
            start=True, stop=True),
            waits=[p2_done[0], consts_mem] + dep(pool_free[pi]), inc=pe)
        P.add("tensor", lambda t, pi=pi, t0=t0: t.matmul(
            out=pool[pi][:, 1:2], lhsT=brow[0:1, t0:t0 + 128], rhs=onesr_f[0:1, 0:1],
            start=True, stop=True), inc=pe)
        pe_c = pe.at()
        P.add("vector", lambda v, pi=pi: v.tensor_copy(out=lnac[:], in_=pool[pi][:, 0:2]),
              waits=[pe_c])
        pool_free[pi] = sv.at()
        pi = next_pool()
        P.add("tensor", lambda t, pi=pi: t.matmul(
            out=pool[pi][:, 0:1], lhsT=cumUT[:], rhs=lnac[:, 0:1], start=True, stop=True),
            waits=[(sv, sv.v)] + dep(pool_free[pi]), inc=pe)
        pe_lg = pe.at()
        P.add("vector", lambda v, pi=pi: v.tensor_copy(out=lgc[:], in_=pool[pi][:, 0:1]),
              waits=[pe_lg])
        pool_free[pi] = sv.at()
        pi = next_pool()
        P.add("tensor", lambda t, pi=pi: t.matmul(
            out=pool[pi][0:1, :, ] if False else pool[pi][0:1, 0:128],
            lhsT=lgc[:], rhs=identf[:], start=True, stop=True),
            waits=[(sv, sv.v)] + dep(pool_free[pi]), inc=pe)
        pe_lr = pe.at()
        P.add("vector", lambda v, pi=pi: v.tensor_copy(out=lgr[:], in_=pool[pi][0:1, 0:128]),
              waits=[pe_lr])
        pool_free[pi] = sv.at()
        P.add("scalar", lambda s: s.mul(out=nlgr[:], in_=lgr[:], mul=-1.0), waits=[sv])
        P.add("scalar", lambda s: s.activation(out=grow[:], in_=lgr[:], func=AF.Exp))
        P.add("scalar", lambda s: s.activation(out=gcol[:], in_=lgc[:], func=AF.Exp))
        row_ready = ss.at()
        pi = next_pool()
        P.add("tensor", lambda t, pi=pi: t.matmul(
            out=pool[pi][:, 0:1], lhsT=onesr_f[:], rhs=grow[0:1, 127:128],
            start=True, stop=True),
            waits=[row_ready] + dep(pool_free[pi]), inc=pe)
        pe_gL = pe.at()
        P.add("vector", lambda v, pi=pi: v.tensor_copy(out=gLc[:], in_=pool[pi][:, 0:1]),
              waits=[pe_gL])
        pool_free[pi] = sv.at()
        pi = next_pool()
        P.add("tensor", lambda t, pi=pi: t.matmul(
            out=pool[pi][:, 0:128], lhsT=nlgr[0:1, :], rhs=onesr_f[0:1, :],
            start=True, stop=False),
            waits=[row_ready] + dep(pool_free[pi]), inc=None)
        P.add("tensor", lambda t, pi=pi: t.matmul(
            out=pool[pi][:, 0:128], lhsT=onesr_f[0:1, :], rhs=lgr[0:1, :],
            start=False, stop=True), inc=pe)
        pe_gd = pe.at()
        P.add("vector", lambda v, pi=pi: v.tensor_tensor(
            out=gdif[:], in0=pool[pi][:, 0:128], in1=maskadd[:], op=ALU.add),
            waits=[pe_gd, consts_mem])
        pool_free[pi] = sv.at()
        P.add("scalar", lambda s: s.activation(out=gamT[:], in_=gdif[:], func=AF.Exp),
              waits=[sv])
        gam_ready = ss.at()
        P.add("vector", lambda v: v.tensor_tensor(
            out=dcol[:], in0=gamT[:, 127:128], in1=lnac[:, 1:2], op=ALU.mult),
            waits=[gam_ready])
        pi = next_pool()
        P.add("tensor", lambda t, pi=pi, t0=t0: t.matmul(
            out=pool[pi][:, 0:128], lhsT=kT[:, t0:t0 + 128], rhs=kT[:, t0:t0 + 128],
            start=True, stop=True),
            waits=[kn_ready] + dep(pool_free[pi]), inc=pe)
        pe_D = pe.at()
        P.add("vector", lambda v, pi=pi: v.tensor_tensor(
            out=Dm[:], in0=pool[pi][:, 0:128], in1=maskstr[:], op=ALU.mult),
            waits=[pe_D, consts_mem])
        pool_free[pi] = sv.at()
        P.add("vector", lambda v: v.tensor_single_scalar(
            out=S0Tb[:], in_=gamT[:], scalar=lnac[:, 1:2], op=ALU.mult),
            waits=[gam_ready])
        P.add("vector", lambda v: v.tensor_tensor(
            out=S0Tb[:], in0=S0Tb[:], in1=Dm[:], op=ALU.mult))
        P.add("vector", lp(lambda v: v.tensor_single_scalar(
            out=S0Tb[:], in_=S0Tb[:], scalar=-1.0, op=ALU.mult)))
        s0t_ready = sv.at()
        pi = next_pool()
        P.add("tensor", lambda t, pi=pi: t.transpose(
            out=pool[pi][:, 0:128].bitcast(BF16)[:, 0:128], in_=S0Tb[:], identity=identb[:]),
            waits=[s0t_ready, consts_mem] + dep(pool_free[pi]), inc=pe)
        pe_t0 = pe.at()
        P.add("vector", lambda v, pi=pi: v.tensor_copy(
            out=S0b[:], in_=pool[pi][:, 0:128].bitcast(BF16)[:, 0:128]),
            waits=[pe_t0])
        pool_free[pi] = sv.at()
        pi = next_pool()
        P.add("tensor", lambda t, pi=pi, t0=t0: t.transpose(
            out=pool[pi][:, 0:128].bitcast(BF16)[:, 0:128], in_=kT[:, t0:t0 + 128],
            identity=identb[:]),
            waits=[kn_ready, consts_mem] + dep(pool_free[pi]), inc=pe)
        pe_kt = pe.at()
        P.add("vector", lambda v, pi=pi: v.tensor_copy(
            out=ktk[:], in_=pool[pi][:, 0:128].bitcast(BF16)[:, 0:128]),
            waits=[pe_kt])
        pool_free[pi] = sv.at()
        ktk_ready = sv.at()
        pi = next_pool()
        P.add("tensor", lambda t, pi=pi, t0=t0: t.matmul(
            out=pool[pi][:, 0:128], lhsT=kT[:, t0:t0 + 128], rhs=Zb[:], start=True, stop=True),
            waits=[kn_ready, state_ready] + dep(pool_free[pi]), inc=pe)
        pe_R = pe.at()
        P.add("vector", lambda v, pi=pi, t0=t0: v.scalar_tensor_tensor(
            out=Rb[:], in0=pool[pi][:, 0:128], scalar=gcol[:], op0=ALU.mult,
            in1=vtok[:, t0:t0 + 128], op1=ALU.subtract),
            waits=[pe_R, row_ready])
        pool_free[pi] = sv.at()
        P.add("vector", lp(lambda v: v.tensor_single_scalar(
            out=Rb[:], in_=Rb[:], scalar=-1.0, op=ALU.mult)))
        ub, utmp = Rb, Ub
        sa, sat = S0b, S0Tb
        sb_, sbt = S1b, S1Tb
        for it in range(7):
            pi = next_pool()
            P.add("tensor", lambda t, pi=pi, sat=sat, ub=ub: t.matmul(
                out=pool[pi][:, 0:128], lhsT=sat[:], rhs=ub[:], start=True, stop=True),
                waits=[(sv, sv.v)] + dep(pool_free[pi]), inc=pe)
            pe_u = pe.at()
            P.add("vector", lambda v, pi=pi, ub=ub, utmp=utmp: v.tensor_tensor(
                out=utmp[:], in0=ub[:], in1=pool[pi][:, 0:128], op=ALU.add),
                waits=[pe_u])
            pool_free[pi] = sv.at()
            ub, utmp = utmp, ub
            if it < 6:
                pi1 = next_pool()
                P.add("tensor", lambda t, pi1=pi1, sa=sa, sat=sat: t.matmul(
                    out=pool[pi1][:, 0:128], lhsT=sat[:], rhs=sa[:], start=True, stop=True),
                    waits=dep(pool_free[pi1]), inc=pe)
                pi2 = next_pool()
                P.add("tensor", lambda t, pi2=pi2, sa=sa, sat=sat: t.matmul(
                    out=pool[pi2][:, 0:128], lhsT=sa[:], rhs=sat[:], start=True, stop=True),
                    waits=dep(pool_free[pi2]), inc=pe)
                pe_s = pe.at()
                P.add("vector", lambda v, pi1=pi1, sb_=sb_: v.tensor_copy(
                    out=sb_[:], in_=pool[pi1][:, 0:128]), waits=[pe_s])
                pool_free[pi1] = sv.at()
                P.add("vector", lambda v, pi2=pi2, sbt=sbt: v.tensor_copy(
                    out=sbt[:], in_=pool[pi2][:, 0:128]))
                pool_free[pi2] = sv.at()
                sa, sb_ = sb_, sa
                sat, sbt = sbt, sat
        Ufin = ub
        u_ready = sv.at()
        pi = next_pool()
        P.add("tensor", lambda t, pi=pi, Ufin=Ufin: t.transpose(
            out=pool[pi][:, 0:128].bitcast(BF16)[:, 0:128], in_=Ufin[:], identity=identb[:]),
            waits=[u_ready] + dep(pool_free[pi]), inc=pe)
        pe_ut = pe.at()
        P.add("vector", lambda v, pi=pi: v.tensor_copy(
            out=UTb[:], in_=pool[pi][:, 0:128].bitcast(BF16)[:, 0:128]),
            waits=[pe_ut])
        pool_free[pi] = sv.at()
        pi = next_pool()
        P.add("tensor", lambda t, pi=pi, t0=t0: t.matmul(
            out=pool[pi][:, 0:128], lhsT=UTb[:], rhs=qT[:, t0:t0 + 128], start=True, stop=True),
            waits=[(sv, sv.v), p1_done] + dep(pool_free[pi]), inc=pe)
        pe_g1 = pe.at()
        P.add("vector", lambda v, pi=pi: v.scalar_tensor_tensor(
            out=ptl[:], in0=pool[pi][:, 0:128], scalar=lnac[:, 1:2], op0=ALU.mult,
            in1=gamT[:], op1=ALU.mult),
            waits=[pe_g1])
        pool_free[pi] = sv.at()
        pi = next_pool()
        P.add("tensor", lambda t, pi=pi: t.matmul(
            out=pool[pi][:, 0:128], lhsT=onesr_f[0:1, :], rhs=grow[0:1, :],
            start=True, stop=True),
            waits=[row_ready] + dep(pool_free[pi]), inc=pe)
        pe_gb = pe.at()
        P.add("vector", lambda v, pi=pi, t0=t0: v.tensor_tensor(
            out=qgb[:], in0=qT[:, t0:t0 + 128], in1=pool[pi][:, 0:128], op=ALU.mult),
            waits=[pe_gb])
        pool_free[pi] = sv.at()
        q_ready = sv.at()
        pi = next_pool()
        P.add("tensor", lambda t, pi=pi: t.matmul(
            out=pool[pi][:, 0:128], lhsT=ktk[:], rhs=ptl[:], start=True, stop=False),
            waits=[q_ready, ktk_ready] + dep(pool_free[pi]), inc=None)
        P.add("tensor", lambda t, pi=pi: t.matmul(
            out=pool[pi][:, 0:128], lhsT=ZTb[:], rhs=qgb[:], start=False, stop=True),
            waits=[state_ready], inc=pe)
        pe_o = pe.at()
        P.add("vector", lambda v, pi=pi, t0=t0: v.tensor_tensor(
            out=obf[:], in0=pool[pi][:, 0:128], in1=gT[:, t0:t0 + 128], op=ALU.mult),
            waits=[pe_o])
        pool_free[pi] = sv.at()
        o_ready = sv.at()
        P.add("vector", lambda v, Ufin=Ufin: v.tensor_single_scalar(
            out=Udb[:], in_=Ufin[:], scalar=dcol[:], op=ALU.mult),
            waits=[sv])
        ud_ready = sv.at()
        pi = next_pool()
        P.add("tensor", lambda t, pi=pi: t.matmul(
            out=pool[pi][:, 0:128], lhsT=ktk[:], rhs=Udb[:], start=True, stop=True),
            waits=[ud_ready] + dep(pool_free[pi]), inc=pe)
        pe_z = pe.at()
        P.add("vector", lambda v, pi=pi: v.scalar_tensor_tensor(
            out=Zf[:], in0=Zf[:], scalar=gLc[:], op0=ALU.mult,
            in1=pool[pi][:, 0:128], op1=ALU.add),
            waits=[pe_z])
        pool_free[pi] = sv.at()
        P.add("vector", lambda v: v.tensor_copy(out=Zb[:], in_=Zf[:]))
        pi = next_pool()
        P.add("tensor", lambda t, pi=pi: t.matmul(
            out=pool[pi][:, 0:128], lhsT=Udb[:], rhs=ktk[:], start=True, stop=True),
            waits=[ud_ready] + dep(pool_free[pi]), inc=pe)
        pe_zt = pe.at()
        P.add("vector", lambda v, pi=pi: v.scalar_tensor_tensor(
            out=ZTf[:], in0=ZTf[:], scalar=gLc[:], op0=ALU.mult,
            in1=pool[pi][:, 0:128], op1=ALU.add),
            waits=[pe_zt])
        pool_free[pi] = sv.at()
        P.add("vector", lambda v: v.tensor_copy(out=ZTb[:], in_=ZTf[:]))
        state_ready = sv.at()
        for e in range(NCH):
            pi = next_pool()
            P.add("tensor", lambda t, pi=pi, e=e: t.matmul(
                out=pool[pi][:, 0:128], lhsT=wo[:, e * 128:(e + 1) * 128], rhs=obf[:],
                start=True, stop=True),
                waits=[o_ready] + dep(pool_free[pi]), inc=pe)
            pe_wo = pe.at()
            P.add("vector", lambda v, pi=pi, e=e, t0=t0: v.scalar_tensor_tensor(
                out=rsb1[:, e * 128:(e + 1) * 128],
                in0=pool[pi][:, 0:128], scalar=boc[:, e:e + 1], op0=ALU.add,
                in1=xb[:, e * T + t0: e * T + t0 + 128], op1=ALU.add),
                waits=[pe_wo] + (dep(rsb_free) if e == 0 else []))
            pool_free[pi] = sv.at()
        r_ready = sv.at()
        P.add("vector", lambda v: v.tensor_copy(out=rbf1[:], in_=rsb1[:]),
              waits=[r_ready])
        rb_ready = sv.at()
        P.add("scalar", lambda s: s.activation(out=r2b1[:], in_=rbf1[:], func=AF.Square),
              waits=[rb_ready])
        r2_ready = ss.at()
        piA = next_pool()
        for c in range(NCH):
            P.add("tensor", lambda t, c=c, piA=piA: t.matmul(
                out=pool[piA][0:1, 0:128], lhsT=ones_c[:], rhs=rbf1[:, c * 128:(c + 1) * 128],
                start=(c == 0), stop=(c == NCH - 1)),
                waits=[rb_ready, consts_mem] + dep(pool_free[piA]),
                inc=pe if c == NCH - 1 else None)
        peA = pe.at()
        piB = next_pool()
        for c in range(NCH):
            P.add("tensor", lambda t, c=c, piB=piB: t.matmul(
                out=pool[piB][0:1, 0:128], lhsT=ones_c[:], rhs=r2b1[:, c * 128:(c + 1) * 128],
                start=(c == 0), stop=(c == NCH - 1)),
                waits=[r2_ready] + dep(pool_free[piB]),
                inc=pe if c == NCH - 1 else None)
        peB = pe.at()
        rsb_free = peB
        MU, EX2, VAR, RSTD, NMR, SCR = 0, 1, 2, 3, 4, 5
        rw = lambda i: rows1[0:1, i * 128:(i + 1) * 128]
        P.add("scalar", lambda s, piA=piA: s.mul(out=rw(MU), in_=pool[piA][0:1, 0:128], mul=1.0 / D),
              waits=[peA] + dep(rows_free))
        P.add("scalar", lambda s, piB=piB: s.mul(out=rw(EX2), in_=pool[piB][0:1, 0:128], mul=1.0 / D),
              waits=[peB])
        pool_free[piA] = ss.at()
        pool_free[piB] = ss.at()
        P.add("vector", lambda v: v.tensor_tensor(out=rw(VAR), in0=rw(MU), in1=rw(MU), op=ALU.mult),
              waits=[ss])
        P.add("vector", lambda v: v.tensor_tensor(out=rw(VAR), in0=rw(EX2), in1=rw(VAR), op=ALU.subtract))
        P.add("scalar", lambda s: s.activation(out=rw(SCR), in_=rw(VAR), func=AF.Sqrt, bias=epsln[:]),
              waits=[sv])
        P.add("vector", lambda v: v.reciprocal(out=rw(RSTD), in_=rw(SCR)), waits=[ss])
        P.add("vector", lambda v: v.tensor_tensor(out=rw(NMR), in0=rw(MU), in1=rw(RSTD), op=ALU.mult))
        P.add("scalar", lambda s: s.mul(out=rw(NMR), in_=rw(NMR), mul=-1.0), waits=[sv])
        rows_ready = ss.at()
        pi1 = next_pool()
        P.add("tensor", lambda t, pi1=pi1: t.matmul(
            out=pool[pi1][:, 0:128], lhsT=onesr_f[:], rhs=rw(RSTD), start=True, stop=True),
            waits=[rows_ready] + dep(pool_free[pi1]), inc=pe)
        pi2 = next_pool()
        P.add("tensor", lambda t, pi2=pi2: t.matmul(
            out=pool[pi2][:, 0:128], lhsT=onesr_f[:], rhs=rw(NMR), start=True, stop=True),
            waits=dep(pool_free[pi2]), inc=pe)
        pe_bc2 = pe.at()
        rows_free = pe_bc2
        P.add("vector", lambda v, pi1=pi1: v.tensor_tensor(
            out=xo1[:].rearrange("p (c t) -> p c t", c=NCH),
            in0=rsb1[:].rearrange("p (c t) -> p c t", c=NCH),
            in1=pool[pi1][:, 0:128].unsqueeze(1).broadcast_to((128, NCH, 128)),
            op=ALU.mult),
            waits=[pe_bc2] + dep(xo_free))
        pool_free[pi1] = sv.at()
        P.add("vector", lambda v, pi2=pi2: v.tensor_tensor(
            out=xo1[:].rearrange("p (c t) -> p c t", c=NCH),
            in0=xo1[:].rearrange("p (c t) -> p c t", c=NCH),
            in1=pool[pi2][:, 0:128].unsqueeze(1).broadcast_to((128, NCH, 128)),
            op=ALU.add))
        pool_free[pi2] = sv.at()
        out_ready = sv.at()
        P.add("gpsimd", lambda g, t0=t0: g.dma_start(
            out=xo_d[:, :, t0:t0 + 128].rearrange("c p t -> p c t"),
            in_=xo1[:].rearrange("p (c t) -> p c t", c=NCH)),
            waits=[out_ready, st], inc=st, inc_n=16, is_dma=True)
        xo_free = (st, st.v)

    es.close()


def stage1_inputs(xfm, inputs):
    g = lambda k: np.asarray(inputs[k], np.float32)
    pre = "L0_attn_"
    r128 = lambda w: np.ascontiguousarray(g(pre + w).reshape(NCH, 128, 128))
    wab = np.ascontiguousarray(
        np.concatenate([g(pre + "Wa"), g(pre + "Wb")], axis=1).reshape(NCH, 128, 2))
    i_ = np.arange(128)
    cumUT = (i_[:, None] <= i_[None, :]).astype(np.float32)
    ident = np.eye(128, dtype=np.float32)
    maskadd = np.where(i_[:, None] <= i_[None, :], 0.0, -1e30).astype(np.float32)
    maskstr = (i_[:, None] < i_[None, :]).astype(np.float32)
    return {"x": xfm, "wq": r128("Wq"), "wk": r128("Wk"), "wv": r128("Wv"),
            "wg": r128("Wg"), "wab": wab,
            "wo": np.ascontiguousarray(g(pre + "Wo")),
            "bq_c": g(pre + "bq").reshape(128, 1), "bk_c": g(pre + "bk").reshape(128, 1),
            "bg_c": g(pre + "bg").reshape(128, 1), "bv_r": g(pre + "bv").reshape(1, 128),
            "bab": np.concatenate([g(pre + "ba"), g(pre + "bb")]).reshape(1, 2),
            "bo_c": np.ascontiguousarray(g(pre + "bo").reshape(NCH, 128).T),
            "cumUT": cumUT, "identf": ident, "maskadd": maskadd, "maskstrict": maskstr}



HID = 2730
HIDP = 2816
NFC = HIDP // 128
PADT = T + 1
W4 = 512
NW4 = T // W4


def emit_stage4(nc, P, x_d, xo_d, B):
    dram = lambda name, shp, dt=F32, kind="ExternalInput": nc.dram_tensor(name, shp, dt, kind=kind).ap()
    wqkv_d = B["wqkvB"]
    wo4_d = B["wo4B"]
    w1a_d = B["w1aB"]
    w1g_d = B["w1gB"]
    w2p_d = B["w2pB"]
    pcast = B["pcast"]
    pc_gates = B["pc_gates"]
    b1a_d = dram("b1a_c", [2, 128, NFC])
    b1g_d = dram("b1g_c", [2, 128, NFC])
    b2_d = dram("b2_c", [2, 128, NCH])
    gb_d = dram("gb_r", [2, NCH, 2, 128])
    qs_d = dram("qs_c", [2, 128, 2])
    ks_d = dram("ks_c", [2, 128, 2])
    csq_d, snq_d, csk_d, snk_d = (B["tabB"][i] for i in range(4))
    maskb_d = dram("maskb", [128, 2 * 128])
    id_d = dram("ident", [128, 128])
    p32_d = dram("p32", [128, 128])
    sel8_d = dram("sel8", [128, 4 * 8])
    selq_d = dram("selq", [8, 4 * 128])
    selk_d = dram("selk", [8, 4 * 128])
    e8_d = dram("e8", [1, 64])
    onesr_d = dram("ones512", [1, 512])
    dbounce_d = dram("dbounce", [32, 128], kind="Internal")
    xpost_d = nc.dram_tensor("xpost", [NCH, 128, PADT], BF16, kind="Internal").ap()
    xnew_d = nc.dram_tensor("xnew", [NCH, 128, PADT], BF16, kind="Internal").ap()

    es = ExitStack()
    sbP = lambda name, shp, dt=F32: es.enter_context(nc.sbuf_tensor('z4_' + name, shp, dt))
    psum = lambda name, shp, dt=F32: es.enter_context(nc.psum_tensor('z4_' + name, shp, dt))

    maskb = sbP("maskb_s", [128, 256], BF16)
    identb = sbP("id_s", [128, 128], BF16)
    p32b = sbP("p32_s", [128, 128], BF16)
    sel8 = sbP("sel8_s", [128, 32], BF16)
    selq = sbP("selq_s", [8, 512], BF16)
    selk = sbP("selk_s", [8, 512], BF16)
    e8r = sbP("e8_s", [1, 64], BF16)
    ones_c = sbP("ones_c", [128, 1], BF16)
    identf = sbP("identf", [128, 128])
    onesr_f = sbP("onesr_f", [1, 128])
    epsb = sbP("epsb", [1, 1])
    epsln = sbP("epsln", [1, 1])
    neghf = sbP("neghf", [1, 1])
    qsc = sbP("qsc", [128, 2])
    ksc = sbP("ksc", [128, 2])
    b1ac = sbP("b1ac", [128, NFC])
    b1gc = sbP("b1gc", [128, NFC])
    b2c = sbP("b2c", [128, NCH])


    pool = [psum(f"pool{i}", [128, 512]) for i in range(8)]

    lg = P.nsem("lg")
    ls = P.nsem("ls")
    hl = [P.nsem("hl0"), P.nsem("hl1")]
    stp = P.nsem("stp")
    sto = P.nsem("sto")
    db = P.nsem("db")
    pe = P.nsem("pe")
    sv = P.esem["vector"]; ss = P.esem["scalar"]; sg = P.esem["gpsimd"]

    dep = lambda x: [x] if x is not None else []
    pool_free = [None] * 8
    _np = [0]

    def next_pool():
        i = _np[0] % 8
        _np[0] += 1
        return i

    def lp(fn):
        def wrapped(engine):
            with nc.allow_low_precision(reason="rn/rs feed bf16 matmul operands"):
                return fn(engine)
        return wrapped

    def gdma(dst_ap, src_ap, waits=()):
        P.add("gpsimd", lambda g, d=dst_ap, s=src_ap: g.dma_start(out=d, in_=s),
              waits=waits, inc=lg, inc_n=16, is_dma=True)
        return lg.at()

    def sdma(dst_ap, src_ap, waits=()):
        P.add("sync", lambda e, d=dst_ap, s=src_ap: e.dma_start(out=d, in_=s),
              waits=waits, inc=ls, inc_n=16, is_dma=True)
        return ls.at()

    P.add("gpsimd", lambda g: g.memset(ones_c[:], 1.0))
    P.add("gpsimd", lambda g: g.memset(onesr_f[:], 1.0))
    P.add("gpsimd", lambda g: g.memset(epsb[:], 1e-12))
    P.add("gpsimd", lambda g: g.memset(epsln[:], 1e-5))
    P.add("gpsimd", lambda g: g.memset(neghf[:], -0.5))
    consts_mem = sg.at()
    for dst, src in ((maskb, maskb_d), (identb, id_d), (p32b, p32_d), (sel8, sel8_d),
                     (selq, selq_d), (selk, selk_d), (e8r, e8_d)):
        gdma(dst[:], src)
    P.add("sync", lambda s: s.dma_start(out=identf[:], in_=id_d),
          waits=[ls], inc=ls, inc_n=16, is_dma=True)
    identf_ready = ls.at()
    consts_ready = lg.at()
    init_sync = ls.at()

    xsrc_prev_stores = None

    def layer_phase(l):
        nonlocal xsrc_prev_stores
        esA = ExitStack()
        sbA = lambda name, shp, dt=F32: esA.enter_context(nc.sbuf_tensor('z4_' + name, shp, dt))
        xwin = [sbA(f"xwin{l}_{i}", [128, NCH * 513], BF16) for i in range(2)]
        xatt = sbA(f"xatt{l}", [128, NCH * W4], BF16)
        wqkv = sbA(f"wqkv{l}", [128, NCH * 1536], BF16)
        wo4 = sbA(f"wo4_{l}", [128, 4 * 1024], BF16)
        csq = sbA(f"csq{l}", [128, T], BF16)
        snq = sbA(f"snq{l}", [128, T], BF16)
        csk = sbA(f"csk{l}", [128, T], BF16)
        snk = sbA(f"snk{l}", [128, T], BF16)
        qext = [sbA(f"qext{l}_{i}", [65, 8 * 512], BF16) for i in range(2)]
        kext = [sbA(f"kext{l}_{i}", [65, 8 * 512], BF16) for i in range(3)]
        vext = [sbA(f"vext{l}_{i}", [128, 4 * 520], BF16) for i in range(3)]
        ptbuf = [sbA(f"ptbuf{l}_{i}", [128, 8 * 512], BF16) for i in range(2)]
        qraw = {s: sbA(f"qraw{l}{s}", [128, 2048], BF16) for s in "qk"}
        q2b = {s: sbA(f"q2b{l}{s}", [128, 2048], BF16) for s in "qk"}
        sw_bf = q2b
        t2b = {s: sbA(f"t2b{l}{s}", [128, 2048], BF16) for s in "qk"}
        qtl = qraw
        n2s = sbA(f"n2s{l}", [8, 512])
        rn8 = sbA(f"rn8{l}", [8, 512], BF16)
        nmcols = sbA(f"nmcols{l}", [128, 32])
        nmrows = sbA(f"nmrows{l}", [32, 128])
        sumsall = sbA(f"sumsall{l}", [1, 8 * 512], BF16)
        rs8 = sbA(f"rs8{l}", [8, 512], BF16)
        bcsb = sbA(f"bcsb{l}", [128, 512], BF16)
        oall = sbA(f"oall{l}", [128, 4 * 512], BF16)
        tmp32 = sbA(f"tmp32{l}", [128, 4])

        ph_barrier = [pe, sv, ss, sg, stp, ls]
        barrier_w = [s.at() for s in ph_barrier]

        for i in range(3):
            P.add("gpsimd", lambda g, i=i, kext=kext: g.memset(kext[i][64:65, :], 1.0),
                  waits=barrier_w if i == 0 else [])
            P.add("gpsimd", lambda g, i=i, vext=vext: g.memset(vext[i][:], 0.0))
            P.add("gpsimd", lambda g, i=i, vext=vext: g.memset(
                vext[i][:].rearrange("p (tt h e) -> p tt h e", tt=4, h=8)[:, :, :, 64:65], 1.0))
        P.add("gpsimd", lambda g, ptbuf=ptbuf: g.memset(ptbuf[0][:], 0.0))
        P.add("gpsimd", lambda g, ptbuf=ptbuf: g.memset(ptbuf[1][:], 0.0))
        attn_init = sg.at()

        ls4 = P.nsem("ls")
        first = True
        for c in range(NCH):
            P.add("sync", lambda s, c=c, wqkv=wqkv: s.dma_start(
                out=wqkv[:, c * 1536:(c + 1) * 1536], in_=wqkv_d[l, c]),
                waits=barrier_w + [pc_gates["attn"][l]] + ([ls4] if first else []),
                inc=ls4, inc_n=16, is_dma=True)
            first = False
        for cc in range(4):
            P.add("sync", lambda s, cc=cc, wo4=wo4: s.dma_start(
                out=wo4[:, cc * 1024:(cc + 1) * 1024], in_=wo4_d[l, cc]),
                inc=ls4, inc_n=16, is_dma=True)
        for dst, srcB in ((csq, csq_d), (snq, snq_d), (csk, csk_d), (snk, snk_d)):
            P.add("sync", lambda s, dst=dst, srcB=srcB: s.dma_start(out=dst[:], in_=srcB),
                  inc=ls4, inc_n=16, is_dma=True)
        sync_w = ls4.at()
        sdma(qsc[:], qs_d[l], waits=barrier_w + [ls])
        sdma(ksc[:], ks_d[l], waits=barrier_w)
        tabs_loaded = ls.at()
        xsrc = x_d if l == 0 else xnew_d
        xoff = 0 if l == 0 else 1
        w_ready = sync_w

        P.add("gpsimd", lambda g, csq=csq: g.tensor_single_scalar(
            out=csq[:], in_=csq[:], scalar=qsc[:, 0:1], op=ALU.mult),
            waits=[w_ready, tabs_loaded])
        P.add("gpsimd", lambda g, snq=snq: g.tensor_single_scalar(
            out=snq[:], in_=snq[:], scalar=qsc[:, 1:2], op=ALU.mult))
        P.add("gpsimd", lambda g, csk=csk: g.tensor_single_scalar(
            out=csk[:], in_=csk[:], scalar=ksc[:, 0:1], op=ALU.mult))
        P.add("gpsimd", lambda g, snk=snk: g.tensor_single_scalar(
            out=snk[:], in_=snk[:], scalar=ksc[:, 1:2], op=ALU.mult))
        tabs_ready = sg.at()

        qraw_free = {"q": None, "k": None}
        q2b_free = {"q": None, "k": None}
        sw_free = {"q": None, "k": None}
        t2b_free = {"q": None, "k": None}
        qtl_free = {"q": None, "k": None}
        qext_free = [None, None]
        kext_free = [attn_init, attn_init, attn_init]
        vext_free = [attn_init, attn_init, attn_init]
        ptbuf_free = [attn_init, attn_init]
        oall_free = None
        nmcols_free = None
        sumsall_free = None
        dball = None
        prev_resid = None

        wl = [P.nsem("wl0"), P.nsem("wl1")]
        xwin_free = [None, None]
        x_ready = []
        P.add("gpsimd", lambda g: g.memset(
            xwin[0][:].rearrange("p (c t) -> p c t", c=NCH)[:, :, 0:1], 0.0),
            waits=barrier_w)
        w0z = sg.at()
        P.add("sync", lambda s, xsrc=xsrc, xoff=xoff: s.dma_start(
            out=xwin[0][:].rearrange("p (c t) -> p c t", c=NCH)[:, :, 1:513],
            in_=xsrc[:, :, xoff:xoff + 512].rearrange("c p t -> p c t")),
            waits=barrier_w + [wl[0]] + ([xsrc_prev_stores[0]] if l else []),
            inc=wl[0], inc_n=16, is_dma=True)
        x_ready.append((wl[0], wl[0].v))

        pendA = [None]
        attn_store_evt = []
        for w in range(NW4):
            t0 = w * W4
            cur = w % 2
            kcur = w % 3
            kprv = (w - 1) % 3

            def hs(c, a, n, cur=cur, xwin=xwin):
                o = 0 if c >= 4 else 1
                return xwin[cur][:, c * 513 + o + a: c * 513 + o + a + n]

            for side, base_m in (("q", 0), ("k", 4)):
                qraw_s, q2b_s, sw_s, t2b_s, qtl_s = (
                    qraw[side], q2b[side], sw_bf[side], t2b[side], qtl[side])
                for mm in range(4):
                    m = base_m + mm
                    pi = next_pool()
                    for c in range(NCH):
                        P.add("tensor", lambda t, c=c, m=m, pi=pi, hs=hs: t.matmul(
                            out=pool[pi][:],
                            lhsT=wqkv[:, c * 1536 + m * 128: c * 1536 + (m + 1) * 128],
                            rhs=hs(c, 0, W4),
                            start=(c == 0), stop=(c == NCH - 1)),
                            waits=[w_ready, x_ready[w]] + dep(pool_free[pi]) +
                                  dep(qraw_free[side] if mm == 0 else None) +
                                  ([w0z] if w == 0 else []),
                            inc=pe if c == NCH - 1 else None)
                    pe_m = pe.at()
                    P.add("scalar", lambda s, mm=mm, pi=pi, qraw_s=qraw_s: s.copy(
                        out=qraw_s[:, mm * 512:(mm + 1) * 512], in_=pool[pi][:]),
                        waits=[pe_m])
                    P.add("scalar", lambda s, mm=mm, pi=pi, q2b_s=q2b_s: s.activation(
                        out=q2b_s[:, mm * 512:(mm + 1) * 512], in_=pool[pi][:], func=AF.Square),
                        waits=dep(q2b_free[side] if mm == 0 else None))
                    pool_free[pi] = ss.at()
                raw_ready = ss.at()
                pi = next_pool()
                for ct in range(4):
                    P.add("tensor", lambda t, ct=ct, pi=pi, q2b_s=q2b_s: t.matmul(
                        out=pool[pi][0:8, :],
                        lhsT=sel8[:, ct * 8:(ct + 1) * 8],
                        rhs=q2b_s[:, ct * 512:(ct + 1) * 512],
                        start=(ct == 0), stop=(ct == 3)),
                        waits=[raw_ready, consts_ready] + dep(pool_free[pi]),
                        inc=pe if ct == 3 else None)
                pe_n2 = pe.at()
                q2b_free[side] = pe_n2
                P.add("scalar", lambda s, pi=pi: s.activation(
                    out=n2s[:], in_=pool[pi][0:8, :], func=AF.Sqrt),
                    waits=[pe_n2])
                pool_free[pi] = ss.at()
                P.add("vector", lambda v: v.tensor_single_scalar(
                    out=n2s[:], in_=n2s[:], scalar=1e-12, op=ALU.max), waits=[ss])
                P.add("vector", lp(lambda v: v.reciprocal(out=rn8[:], in_=n2s[:])))
                rn_ready = sv.at()
                cs_t, sn_t = (csq, snq) if side == "q" else (csk, snk)
                for ct in range(4):
                    pi = next_pool()
                    P.add("tensor", lambda t, ct=ct, pi=pi, qraw_s=qraw_s: t.matmul(
                        out=pool[pi][:], lhsT=p32b[:], rhs=qraw_s[:, ct * 512:(ct + 1) * 512],
                        start=True, stop=True),
                        waits=[raw_ready, consts_ready] + dep(pool_free[pi]), inc=pe)
                    pe_sw = pe.at()
                    P.add("scalar", lambda s, ct=ct, pi=pi, sw_s=sw_s: s.copy(
                        out=sw_s[:, ct * 512:(ct + 1) * 512], in_=pool[pi][:]),
                        waits=[pe_sw] + dep(sw_free[side] if ct == 0 else None))
                    pool_free[pi] = ss.at()
                sw_ready = ss.at()
                P.add("vector", lambda v, cs_t=cs_t, t0=t0, t2b_s=t2b_s, qraw_s=qraw_s: v.tensor_tensor(
                    out=t2b_s[:].rearrange("p (ct t) -> p ct t", ct=4),
                    in0=qraw_s[:].rearrange("p (ct t) -> p ct t", ct=4),
                    in1=cs_t[:, t0:t0 + W4].unsqueeze(1).broadcast_to((128, 4, W4)),
                    op=ALU.mult),
                    waits=[raw_ready, tabs_ready] + dep(t2b_free[side]))
                P.add("vector", lambda v, sn_t=sn_t, t0=t0, sw_s=sw_s: v.tensor_tensor(
                    out=sw_s[:].rearrange("p (ct t) -> p ct t", ct=4),
                    in0=sw_s[:].rearrange("p (ct t) -> p ct t", ct=4),
                    in1=sn_t[:, t0:t0 + W4].unsqueeze(1).broadcast_to((128, 4, W4)),
                    op=ALU.mult),
                    waits=[sw_ready])
                P.add("vector", lambda v, qtl_s=qtl_s, t2b_s=t2b_s, sw_s=sw_s: v.tensor_tensor(
                    out=qtl_s[:], in0=t2b_s[:], in1=sw_s[:], op=ALU.add),
                    waits=dep(qtl_free[side]))
                sw_free[side] = sv.at()
                t2b_free[side] = sv.at()
                q2b_free[side] = sv.at()
                rot_ready = sv.at()
                selx = selq if side == "q" else selk
                dst_ext = qext[cur] if side == "q" else kext[kcur]
                dfree = qext_free[cur] if side == "q" else kext_free[kcur]
                for ct in range(4):
                    pi = next_pool()
                    P.add("tensor", lambda t, ct=ct, pi=pi, selx=selx: t.matmul(
                        out=pool[pi][:], lhsT=selx[:, ct * 128:(ct + 1) * 128],
                        rhs=rn8[:], start=True, stop=True),
                        waits=[rn_ready, consts_ready] + dep(pool_free[pi]), inc=pe)
                    pe_bc = pe.at()
                    for half in range(2):
                        h = 2 * ct + half
                        P.add("vector", lambda v, ct=ct, pi=pi, half=half, h=h, dst_ext=dst_ext, qtl_s=qtl_s: v.tensor_tensor(
                            out=dst_ext[0:64, h * 512:(h + 1) * 512],
                            in0=qtl_s[half * 64:(half + 1) * 64, ct * 512:(ct + 1) * 512],
                            in1=pool[pi][half * 64:(half + 1) * 64, :],
                            op=ALU.mult),
                            waits=[pe_bc, rot_ready] + (dep(dfree) if (ct == 0 and half == 0) else []))
                    pool_free[pi] = sv.at()
                qtl_free[side] = sv.at()
                qraw_free[side] = sv.at()
                if side == "q":
                    qext_ready = sv.at()
                else:
                    kext_ready = sv.at()

            for tt in range(4):
                pi = next_pool()
                for c in range(NCH):
                    P.add("tensor", lambda t, c=c, tt=tt, pi=pi, hs=hs: t.matmul(
                        out=pool[pi][:],
                        lhsT=hs(c, tt * 128, 128),
                        rhs=wqkv[:, c * 1536 + 1024: c * 1536 + 1536],
                        start=(c == 0), stop=(c == NCH - 1)),
                        waits=[w_ready, x_ready[w]] + dep(pool_free[pi]) +
                              (dep(vext_free[kcur]) if tt == 0 else []) +
                              ([attn_init] if w < 3 else []) + ([w0z] if w == 0 else []),
                        inc=pe if c == NCH - 1 else None)
                pe_v = pe.at()
                P.add("scalar", lambda s, tt=tt, pi=pi, kcur=kcur, vext=vext: s.copy(
                    out=vext[kcur][:].rearrange("p (t4 h e) -> p t4 h e", t4=4, h=8)[:, tt, :, 0:64],
                    in_=pool[pi][:].rearrange("p (h e) -> p h e", h=8)),
                    waits=[pe_v])
                pool_free[pi] = ss.at()
            vext_ready = ss.at()

            if pendA[0] is not None:
                pendA[0]()

            if w + 1 < NW4:
                pp = (w + 1) % 2
                P.add("sync", lambda s, w=w, pp=pp, xsrc=xsrc, xoff=xoff: s.dma_start(
                    out=xwin[pp][:].rearrange("p (c t) -> p c t", c=NCH),
                    in_=xsrc[:, :, xoff + (w + 1) * W4 - 1: xoff + (w + 2) * W4].rearrange("c p t -> p c t")),
                    waits=dep(xwin_free[pp]) + [wl[pp]] +
                          ([xsrc_prev_stores[w + 1]] if l else []),
                    inc=wl[pp], inc_n=16, is_dma=True)
                x_ready.append((wl[pp], wl[pp].v))

            def attn_tail(w=w, t0=t0, cur=cur, kcur=kcur, kprv=kprv,
                          qext_ready=qext_ready, kext_ready=kext_ready, vext_ready=vext_ready):
                nonlocal oall_free, nmcols_free, sumsall_free
                qb = qext[cur]
                pendO = [None]
                for h in range(8):
                    for qt in range(4):
                        j = h * 4 + qt
                        wB = qt * 128 + 128
                        piB = next_pool()
                        P.add("tensor", lambda t, h=h, qt=qt, piB=piB, wB=wB, qb=qb: t.matmul(
                            out=pool[piB][:, 0:wB],
                            lhsT=qb[0:64, h * 512 + qt * 128: h * 512 + qt * 128 + 128],
                            rhs=kext[kcur][0:64, h * 512: h * 512 + wB],
                            start=True, stop=True),
                            waits=[qext_ready, kext_ready] + dep(pool_free[piB]) +
                                  dep(nmcols_free if j == 0 else None),
                            inc=pe)
                        pe_B = pe.at()
                        if w > 0:
                            wA = 512 - qt * 128
                            piA = next_pool()
                            P.add("tensor", lambda t, h=h, qt=qt, piA=piA, wA=wA, qb=qb: t.matmul(
                                out=pool[piA][:, 0:wA],
                                lhsT=qb[0:64, h * 512 + qt * 128: h * 512 + qt * 128 + 128],
                                rhs=kext[kprv][0:64, h * 512 + qt * 128: h * 512 + 512],
                                start=True, stop=True),
                                waits=dep(pool_free[piA]), inc=pe)
                            pe_A = pe.at()
                            P.add("vector", lambda v, piA=piA, wA=wA: v.reduce_max(
                                out=tmp32[:, 0:1], in_=pool[piA][:, 0:wA], axis=AX.X, negate=True),
                                waits=[pe_A])
                            pool_free[piA] = sv.at()
                            P.add("vector", lambda v, piB=piB, wB=wB: v.reduce_max(
                                out=tmp32[:, 1:2], in_=pool[piB][:, 0:wB], axis=AX.X, negate=True),
                                waits=[pe_B])
                            pool_free[piB] = sv.at()
                            P.add("vector", lambda v, j=j: v.tensor_tensor(
                                out=nmcols[:, j:j + 1], in0=tmp32[:, 0:1], in1=tmp32[:, 1:2],
                                op=ALU.min))
                        else:
                            P.add("vector", lambda v, piB=piB, j=j, wB=wB: v.reduce_max(
                                out=nmcols[:, j:j + 1], in_=pool[piB][:, 0:wB], axis=AX.X, negate=True),
                                waits=[pe_B])
                            pool_free[piB] = sv.at()
                nm_ready = sv.at()
                pi = next_pool()
                P.add("tensor", lambda t, pi=pi: t.transpose(
                    out=pool[pi][0:32, 0:128], in_=nmcols[:, 0:32], identity=identf[:]),
                    waits=[nm_ready, identf_ready] + dep(pool_free[pi]), inc=pe)
                pe_tr = pe.at()
                nmcols_free = pe_tr
                P.add("vector", lambda v, pi=pi: v.tensor_copy(
                    out=nmrows[:], in_=pool[pi][0:32, 0:128]),
                    waits=[pe_tr, db])
                pool_free[pi] = sv.at()
                nmr_ready = sv.at()
                P.add("gpsimd", lambda g: g.dma_start(out=dbounce_d, in_=nmrows[:]),
                      waits=[nmr_ready, db], inc=db, inc_n=16, is_dma=True)
                P.add("gpsimd", lambda g, qb=qb: g.dma_start(
                    out=qb[64:65, :],
                    in_=dbounce_d.rearrange("j p -> (j p)").unsqueeze(0)),
                    waits=[db] + dep(qext_free[cur]), inc=db, inc_n=16, is_dma=True)
                dall = db.at()

                for h in range(8):
                    hpar = h % 2
                    for kc in range(0 if w > 0 else 4, 8):
                        qlo = max(0, kc * 128 - 512)
                        qhi = min(512, kc * 128 + 128)
                        kb = kext[kprv] if kc < 4 else kext[kcur]
                        pi = next_pool()
                        P.add("tensor", lambda t, h=h, kc=kc, pi=pi, kb=kb, qlo=qlo, qhi=qhi, qb=qb: t.matmul(
                            out=pool[pi][:, 0:qhi - qlo],
                            lhsT=kb[0:65, h * 512 + (kc % 4) * 128: h * 512 + (kc % 4) * 128 + 128],
                            rhs=qb[0:65, h * 512 + qlo: h * 512 + qhi],
                            start=True, stop=False),
                            waits=[dall, qext_ready, kext_ready] + dep(pool_free[pi]),
                            inc=None)
                        mcol = (kc * 128 - qlo) if kc < 4 else 0
                        moff = 0 if kc < 4 else 128
                        P.add("tensor", lambda t, pi=pi, mcol=mcol, moff=moff: t.matmul(
                            out=pool[pi][:, mcol:mcol + 128],
                            lhsT=identb[:], rhs=maskb[:, moff:moff + 128],
                            start=False, stop=True),
                            waits=[consts_ready], inc=pe)
                        pe_ks = pe.at()
                        P.add("scalar", lambda s, h=h, kc=kc, pi=pi, hpar=hpar, qlo=qlo, qhi=qhi, ptbuf=ptbuf: s.activation(
                            out=ptbuf[hpar][:, kc * 512 + qlo: kc * 512 + qhi],
                            in_=pool[pi][:, 0:qhi - qlo], func=AF.Exp),
                            waits=[pe_ks] + dep(ptbuf_free[hpar] if kc == (0 if w > 0 else 4) else None))
                        pool_free[pi] = ss.at()
                    pt_ready = ss.at()
                    if pendO[0] is not None:
                        pendO[0]()

                    def o_block(h=h, hpar=hpar, pt_ready=pt_ready):
                        pi = next_pool()
                        kcs = list(range(0 if w > 0 else 4, 8))
                        for ki, kc in enumerate(kcs):
                            vb = vext[kprv] if kc < 4 else vext[kcur]
                            P.add("tensor", lambda t, h=h, kc=kc, pi=pi, vb=vb, ki=ki, hpar=hpar, last=(ki == len(kcs) - 1), ptbuf=ptbuf: t.matmul(
                                out=pool[pi][0:65, :],
                                lhsT=vb[:, (kc % 4) * 520 + h * 65: (kc % 4) * 520 + h * 65 + 65],
                                rhs=ptbuf[hpar][:, kc * 512:(kc + 1) * 512],
                                start=(ki == 0), stop=last),
                                waits=[pt_ready, vext_ready] + dep(pool_free[pi]),
                                inc=pe if ki == len(kcs) - 1 else None)
                        pe_o = pe.at()
                        ptbuf_free[hpar] = pe_o
                        P.add("scalar", lambda s, h=h, pi=pi: s.copy(
                            out=oall[(h % 2) * 64:(h % 2) * 64 + 64, (h // 2) * 512:(h // 2) * 512 + 512],
                            in_=pool[pi][0:64, :]),
                            waits=[pe_o] + dep(oall_free if h == 0 else None))
                        P.add("scalar", lambda s, h=h, pi=pi: s.copy(
                            out=sumsall[0:1, h * 512:(h + 1) * 512], in_=pool[pi][64:65, :]),
                            waits=dep(sumsall_free if h == 0 else None))
                        pool_free[pi] = ss.at()
                    pendO[0] = o_block
                pendO[0]()
                pendO[0] = None
                o_raw_ready = ss.at()

                pi = next_pool()
                for h in range(8):
                    P.add("tensor", lambda t, h=h, pi=pi: t.matmul(
                        out=pool[pi][0:8, :],
                        lhsT=e8r[0:1, h * 8:(h + 1) * 8],
                        rhs=sumsall[0:1, h * 512:(h + 1) * 512],
                        start=(h == 0), stop=(h == 7)),
                        waits=[o_raw_ready, consts_ready] + dep(pool_free[pi]),
                        inc=pe if h == 7 else None)
                pe_su = pe.at()
                sumsall_free = pe_su
                P.add("vector", lp(lambda v, pi=pi: v.reciprocal(out=rs8[:], in_=pool[pi][0:8, :])),
                      waits=[pe_su])
                pool_free[pi] = sv.at()
                rs_ready = sv.at()
                for ct in range(4):
                    pi = next_pool()
                    P.add("tensor", lambda t, ct=ct, pi=pi: t.matmul(
                        out=pool[pi][:], lhsT=selk[:, ct * 128:(ct + 1) * 128],
                        rhs=rs8[:], start=True, stop=True),
                        waits=[rs_ready] + dep(pool_free[pi]), inc=pe)
                    pe_bc = pe.at()
                    P.add("scalar", lambda s, pi=pi: s.copy(out=bcsb[:], in_=pool[pi][:]),
                          waits=[pe_bc, sv])
                    pool_free[pi] = ss.at()
                    P.add("vector", lambda v, ct=ct: v.tensor_tensor(
                        out=oall[:, ct * 512:(ct + 1) * 512],
                        in0=oall[:, ct * 512:(ct + 1) * 512],
                        in1=bcsb[:], op=ALU.mult),
                        waits=[ss])
                o_ready = sv.at()

                for e in range(NCH):
                    pi = next_pool()
                    for cc in range(4):
                        P.add("tensor", lambda t, e=e, cc=cc, pi=pi: t.matmul(
                            out=pool[pi][:],
                            lhsT=wo4[:, cc * 1024 + e * 128: cc * 1024 + (e + 1) * 128],
                            rhs=oall[:, cc * 512:(cc + 1) * 512],
                            start=(cc == 0), stop=(cc == 3)),
                            waits=[o_ready] + dep(pool_free[pi]),
                            inc=pe if cc == 3 else None)
                    pe_wo = pe.at()
                    P.add("vector", lambda v, e=e, pi=pi, cur=cur: v.scalar_tensor_tensor(
                        out=xatt[:, e * W4:(e + 1) * W4],
                        in0=pool[pi][:], scalar=0.0, op0=ALU.bypass,
                        in1=xwin[cur][:, e * 513 + 1: e * 513 + 513],
                        op1=ALU.add),
                        waits=[pe_wo, stp])
                    pool_free[pi] = sv.at()
                resid_done = sv.at()
                oall_free = pe.at()
                qext_free[cur] = pe.at()
                kext_free[kprv] = pe.at() if w > 0 else kext_free[kprv]
                vext_free[kprv] = pe.at() if w > 0 else vext_free[kprv]
                P.add("sync", lambda s, t0=t0: s.dma_start(
                    out=xpost_d[:, :, 1 + t0: 1 + t0 + W4].rearrange("c p t -> p c t"),
                    in_=xatt[:].rearrange("p (c t) -> p c t", c=NCH)),
                    waits=[resid_done, init_sync, stp], inc=stp, inc_n=16, is_dma=True)
                attn_store_evt.append((stp, stp.v))
                xwin_free[cur] = resid_done

            pendA[0] = attn_tail
        pendA[0]()
        attn_stores = stp.at()

        esA.close()

        esF = ExitStack()
        sbF = lambda name, shp, dt=F32: esF.enter_context(nc.sbuf_tensor('z4_' + name, shp, dt))
        w1as = sbF(f"w1a{l}", [128, NCH * HIDP], BF16)
        gbr = sbF(f"gbr{l}", [2, NCH * 128])
        nmr1 = sbF(f"nmr1{l}", [2, 512])
        rows4 = sbF(f"rows4{l}", [1, 3 * 512])
        w1gs = sbF(f"w1g{l}", [128, NCH * HIDP], BF16)
        w2s = sbF(f"w2p{l}", [128, NFC * 1024], BF16)
        hsrc = [sbF(f"hsrc{l}_{i}", [128, NCH * 513], BF16) for i in range(2)]
        hnb = [sbF(f"hnb{l}_{i}", [128, NCH * 512], BF16) for i in range(2)]
        h2all = sbF(f"h2all{l}", [128, NFC * 512], BF16)
        gelb = sbF(f"gelb{l}", [128, 512], BF16)
        abuf = sbF(f"abuf{l}", [128, 512], BF16)
        tmpF = sbF(f"tmpF{l}", [128, 512], BF16)

        fb = [s.at() for s in (pe, sv, ss, sg, stp)]
        ls4 = P.nsem("ls")
        for c in range(NCH):
            P.add("sync", lambda s, c=c, w1as=w1as: s.dma_start(
                out=w1as[:, c * HIDP:(c + 1) * HIDP], in_=w1a_d[l, c]),
                waits=fb + [pc_gates["ffn"][l]] + ([ls4] if c == 0 else []),
                inc=ls4, inc_n=16, is_dma=True)
            P.add("sync", lambda s, c=c, w1gs=w1gs: s.dma_start(
                out=w1gs[:, c * HIDP:(c + 1) * HIDP], in_=w1g_d[l, c]),
                inc=ls4, inc_n=16, is_dma=True)
        for f in range(NFC):
            P.add("sync", lambda s, f=f, w2s=w2s: s.dma_start(
                out=w2s[:, f * 1024:(f + 1) * 1024], in_=w2p_d[l, f]),
                inc=ls4, inc_n=16, is_dma=True)
        sync_wf = ls4.at()
        sdma(b1ac[:], b1a_d[l], waits=fb + [ls])
        sdma(b1gc[:], b1g_d[l], waits=fb)
        sdma(b2c[:], b2_d[l], waits=fb)
        sdma(gbr[0:2, :].rearrange("r (c k) -> r c k", c=NCH),
             gb_d[l].rearrange("c r k -> r c k"), waits=fb)
        wf_ready = sync_wf
        wf_small = ls.at()
        P.add("sync", lambda s, nmr1=nmr1: s.dma_start(out=nmr1[1:2, :], in_=onesr_d),
              waits=fb + [ls], inc=ls, inc_n=16, is_dma=True)
        nmr_ones = ls.at()

        hsrc_free = [None, None]
        h2_free = None
        hnb_free = [None, None]
        r2f_free = None
        statA_free = None
        statB_free = None
        rows_free = None
        pendF = [None]
        ffn_store_evt = []

        hs_ready = []
        P.add("gpsimd", lambda g: g.memset(
            hsrc[0][:].rearrange("p (c t) -> p c t", c=NCH)[:, :, 0:1], 0.0), waits=fb)
        h0z = sg.at()
        P.add("sync", lambda s: s.dma_start(
            out=hsrc[0][:].rearrange("p (c t) -> p c t", c=NCH)[:, :, 1:513],
            in_=xpost_d[:, :, 1:513].rearrange("c p t -> p c t")),
            waits=[attn_store_evt[1], hl[0]], inc=hl[0], inc_n=16, is_dma=True)
        hs_ready.append((hl[0], hl[0].v))
        hs0_extra = h0z

        for tb in range(NW4):
            t0 = tb * W4
            par = tb % 2

            def hsl(c, par=par):
                o = 1 if c < 4 else 0
                return hsrc[par][:, c * 513 + o: c * 513 + o + W4]

            def xol(c, par=par):
                return hsrc[par][:, c * 513 + 1: c * 513 + 513]

            r2v = hnb[par][:].rearrange("p (c t) -> p c t", c=NCH)
            hs_extra = [hs0_extra] if tb == 0 else []
            for grp, (c0, c1, o) in enumerate(((0, 4, 1), (4, 8, 0))):
                P.add("vector", lambda v, c0=c0, c1=c1, o=o, par=par, r2v=r2v: v.tensor_tensor(
                    out=r2v[:, c0:c1, :],
                    in0=hsrc[par][:].rearrange("p (c t) -> p c t", c=NCH)[:, c0:c1, o:o + W4],
                    in1=hsrc[par][:].rearrange("p (c t) -> p c t", c=NCH)[:, c0:c1, o:o + W4],
                    op=ALU.mult),
                    waits=[hs_ready[tb]] + hs_extra + dep(hnb_free[par] if grp == 0 else None))
            r2_ready = sv.at()
            piA = next_pool()
            for c in range(NCH):
                P.add("tensor", lambda t, c=c, piA=piA, hsl=hsl: t.matmul(
                    out=pool[piA][0:1, :], lhsT=ones_c[:], rhs=hsl(c),
                    start=(c == 0), stop=(c == NCH - 1)),
                    waits=[hs_ready[tb], consts_mem] + hs_extra + dep(pool_free[piA]) + dep(statA_free),
                    inc=pe if c == NCH - 1 else None)
            peA = pe.at()
            piB = next_pool()
            for c in range(NCH):
                P.add("tensor", lambda t, c=c, piB=piB, par=par: t.matmul(
                    out=pool[piB][0:1, :],
                    lhsT=ones_c[:], rhs=hnb[par][:, c * W4:(c + 1) * W4],
                    start=(c == 0), stop=(c == NCH - 1)),
                    waits=[r2_ready] + dep(pool_free[piB]) + dep(statB_free),
                    inc=pe if c == NCH - 1 else None)
            peB = pe.at()
            r2f_free = peB
            MU, EX2, VAR = 0, 1, 2
            SCR = EX2
            RSTD = VAR
            rw = lambda i: rows4[0:1, i * W4:(i + 1) * W4]
            P.add("scalar", lambda s, piA=piA: s.mul(out=rw(MU), in_=pool[piA][0:1, :], mul=-1.0 / D),
                  waits=[peA] + dep(rows_free))
            statA_free = ss.at()
            P.add("scalar", lambda s, piB=piB: s.mul(out=rw(EX2), in_=pool[piB][0:1, :], mul=1.0 / D),
                  waits=[peB])
            statB_free = ss.at()
            pool_free[piA] = ss.at()
            pool_free[piB] = ss.at()
            P.add("gpsimd", lambda g: g.tensor_tensor(out=rw(VAR), in0=rw(MU), in1=rw(MU), op=ALU.mult),
                  waits=[ss])
            P.add("gpsimd", lambda g: g.tensor_tensor(out=rw(VAR), in0=rw(EX2), in1=rw(VAR), op=ALU.subtract))
            P.add("scalar", lambda s: s.activation(out=rw(SCR), in_=rw(VAR), func=AF.Ln, bias=epsln[:]),
                  waits=[sg])
            P.add("scalar", lambda s: s.activation(out=rw(RSTD), in_=rw(SCR), func=AF.Exp, scale=neghf[:]))
            P.add("gpsimd", lambda g, nmr1=nmr1: g.tensor_tensor(
                out=nmr1[0:1, :], in0=rw(MU), in1=rw(RSTD), op=ALU.mult),
                waits=[ss, nmr_ones] + dep(rows_free))
            rows_ready = sg.at()
            for c in range(NCH):
                pi1 = next_pool()
                P.add("tensor", lambda t, c=c, pi1=pi1: t.matmul(
                    out=pool[pi1][:], lhsT=gbr[0:1, c * 128:(c + 1) * 128], rhs=rw(RSTD),
                    start=True, stop=True),
                    waits=[rows_ready, wf_ready, wf_small] + dep(pool_free[pi1]), inc=pe)
                pi2 = next_pool()
                P.add("tensor", lambda t, c=c, pi2=pi2: t.matmul(
                    out=pool[pi2][:], lhsT=gbr[0:2, c * 128:(c + 1) * 128], rhs=nmr1[0:2, :],
                    start=True, stop=True),
                    waits=dep(pool_free[pi2]), inc=pe)
                pe_bc = pe.at()
                P.add("vector", lambda v, c=c, pi1=pi1, pe_bc=pe_bc, hsl=hsl: v.tensor_tensor(
                    out=tmpF[:], in0=hsl(c), in1=pool[pi1][:], op=ALU.mult),
                    waits=[pe_bc])
                pool_free[pi1] = sv.at()
                P.add("vector", lambda v, c=c, pi2=pi2, par=par: v.tensor_tensor(
                    out=hnb[par][:, c * W4:(c + 1) * W4], in0=tmpF[:], in1=pool[pi2][:], op=ALU.add),
                    waits=(dep(hnb_free[par]) + dep(r2f_free)) if c == 0 else [])
                pool_free[pi2] = sv.at()
            hn_ready = sv.at()
            rows_free = pe.at()

            if pendF[0] is not None:
                pendF[0]()

            if tb + 1 < NW4:
                pp = (tb + 1) % 2
                P.add("sync", lambda s, tb=tb, pp=pp: s.dma_start(
                    out=hsrc[pp][:].rearrange("p (c t) -> p c t", c=NCH),
                    in_=xpost_d[:, :, (tb + 1) * W4:(tb + 1) * W4 + 513].rearrange("c p t -> p c t")),
                    waits=[attn_store_evt[min(tb + 2, NW4 - 1)]] + dep(hsrc_free[pp]) + [hl[pp]],
                    inc=hl[pp], inc_n=16, is_dma=True)
                hs_ready.append(hl[pp].at())

            def ffn_tail(tb=tb, t0=t0, par=par, xol=xol, hn_ready=hn_ready):
                nonlocal h2_free
                glu_prev = None
                for f in range(NFC):
                    piA2 = next_pool()
                    for c in range(NCH):
                        P.add("tensor", lambda t, c=c, f=f, piA2=piA2, par=par: t.matmul(
                            out=pool[piA2][:],
                            lhsT=w1as[:, c * HIDP + f * 128: c * HIDP + (f + 1) * 128],
                            rhs=hnb[par][:, c * W4:(c + 1) * W4],
                            start=(c == 0), stop=(c == NCH - 1)),
                            waits=[hn_ready, wf_ready] + dep(pool_free[piA2]) +
                                  dep(h2_free if f == 0 else None),
                            inc=pe if c == NCH - 1 else None)
                    peA2 = pe.at()
                    piG = next_pool()
                    for c in range(NCH):
                        P.add("tensor", lambda t, c=c, f=f, piG=piG, par=par: t.matmul(
                            out=pool[piG][:],
                            lhsT=w1gs[:, c * HIDP + f * 128: c * HIDP + (f + 1) * 128],
                            rhs=hnb[par][:, c * W4:(c + 1) * W4],
                            start=(c == 0), stop=(c == NCH - 1)),
                            waits=dep(pool_free[piG]),
                            inc=pe if c == NCH - 1 else None)
                    peG = pe.at()
                    P.add("scalar", lambda s, f=f, piG=piG: s.activation(
                        out=gelb[:], in_=pool[piG][:], func=AF.Gelu, bias=b1gc[:, f:f + 1]),
                        waits=[peG, wf_small] + dep(glu_prev))
                    pool_free[piG] = ss.at()
                    P.add("scalar", lambda s, f=f, piA2=piA2: s.activation(
                        out=abuf[:], in_=pool[piA2][:], func=AF.Identity,
                        bias=b1ac[:, f:f + 1]),
                        waits=[peA2] + dep(glu_prev))
                    pool_free[piA2] = ss.at()
                    P.add("vector", lambda v, f=f: v.tensor_tensor(
                        out=h2all[:, f * W4:(f + 1) * W4],
                        in0=abuf[:], in1=gelb[:], op=ALU.mult),
                        waits=[ss])
                    glu_prev = sv.at()
                h2_ready = sv.at()
                hnb_free[par] = pe.at()
                for e in range(NCH):
                    pi = next_pool()
                    for f in range(NFC):
                        P.add("tensor", lambda t, e=e, f=f, pi=pi: t.matmul(
                            out=pool[pi][:],
                            lhsT=w2s[:, f * 1024 + e * 128: f * 1024 + (e + 1) * 128],
                            rhs=h2all[:, f * W4:(f + 1) * W4],
                            start=(f == 0), stop=(f == NFC - 1)),
                            waits=[h2_ready] + dep(pool_free[pi]),
                            inc=pe if f == NFC - 1 else None)
                    pe_w2 = pe.at()
                    P.add("vector", lambda v, e=e, pi=pi, xol=xol: v.scalar_tensor_tensor(
                        out=xol(e),
                        in0=pool[pi][:], scalar=b2c[:, e:e + 1], op0=ALU.add,
                        in1=xol(e), op1=ALU.add),
                        waits=[pe_w2, wf_small])
                    pool_free[pi] = sv.at()
                xout_ready = sv.at()
                h2_free = pe.at()
                P.add("sync", lambda s, t0=t0, par=par: s.dma_start(
                    out=xnew_d[:, :, 1 + t0: 1 + t0 + W4].rearrange("c p t -> p c t"),
                    in_=hsrc[par][:].rearrange("p (c t) -> p c t", c=NCH)[:, :, 1:513]),
                    waits=[xout_ready, init_sync, stp], inc=stp, inc_n=16, is_dma=True)
                ffn_store_evt.append((stp, stp.v))
                hsrc_free[par] = (stp, stp.v)

            pendF[0] = ffn_tail
        pendF[0]()
        esF.close()
        xsrc_prev_stores = list(ffn_store_evt)

    for l in range(2):
        layer_phase(l)

    esL = ExitStack()
    sbL = lambda name, shp, dt=F32: esL.enter_context(nc.sbuf_tensor('z4_' + name, shp, dt))
    xprf = sbL("xprf", [128, NCH * 512], BF16)
    xnT = sbL("xnT", [128, NCH * 512], BF16)
    rbf4 = sbL("rbf4", [128, NCH * 512], BF16)
    r2bf4 = sbL("r2bf4", [128, NCH * 512], BF16)
    xo_s4 = sbL("xo_s4", [128, NCH * 512])
    rows4 = sbL("rows4L", [1, 6 * 512])
    lb = [s.at() for s in (pe, sv, ss, sg, stp)]
    xp_free = None
    xn_free = None
    rb_free = None
    r2b4_free = None
    stA_free = None
    stB_free = None
    rowsL_free = None
    out_free = None
    for tb in range(NW4):
        t0 = tb * W4
        P.add("sync", lambda s, t0=t0: s.dma_start(
            out=xprf[:].rearrange("p (c t) -> p c t", c=NCH),
            in_=x_d[:, :, t0:t0 + W4].rearrange("c p t -> p c t")),
            waits=lb + dep(xp_free) + [ls], inc=ls, inc_n=16, is_dma=True)
        xpl = ls.at()
        P.add("sync", lambda s, t0=t0: s.dma_start(
            out=xnT[:].rearrange("p (c t) -> p c t", c=NCH),
            in_=xnew_d[:, :, 1 + t0: 1 + t0 + W4].rearrange("c p t -> p c t")),
            waits=[xsrc_prev_stores[tb]] + dep(xn_free), inc=ls, inc_n=16, is_dma=True)
        xnl = ls.at()
        P.add("vector", lambda v: v.tensor_tensor(out=rbf4[:], in0=xprf[:], in1=xnT[:], op=ALU.add),
              waits=[(ls, ls.v)] + dep(rb_free))
        r_ready = sv.at()
        xp_free = r_ready
        xn_free = r_ready
        P.add("scalar", lambda s: s.activation(out=r2bf4[:], in_=rbf4[:], func=AF.Square),
              waits=[r_ready] + dep(r2b4_free))
        r2_ready = ss.at()
        piA = next_pool()
        for c in range(NCH):
            P.add("tensor", lambda t, c=c, piA=piA: t.matmul(
                out=pool[piA][0:1, :], lhsT=ones_c[:], rhs=rbf4[:, c * W4:(c + 1) * W4],
                start=(c == 0), stop=(c == NCH - 1)),
                waits=[r_ready, consts_mem] + dep(pool_free[piA]) + dep(stA_free),
                inc=pe if c == NCH - 1 else None)
        peA = pe.at()
        piB = next_pool()
        for c in range(NCH):
            P.add("tensor", lambda t, c=c, piB=piB: t.matmul(
                out=pool[piB][0:1, :], lhsT=ones_c[:], rhs=r2bf4[:, c * W4:(c + 1) * W4],
                start=(c == 0), stop=(c == NCH - 1)),
                waits=[r2_ready] + dep(pool_free[piB]) + dep(stB_free),
                inc=pe if c == NCH - 1 else None)
        peB = pe.at()
        r2b4_free = peB
        MU, EX2, VAR, RSTD, NMR, SCR = 0, 1, 2, 3, 4, 5
        rw = lambda i: rows4[0:1, i * W4:(i + 1) * W4]
        P.add("scalar", lambda s, piA=piA: s.mul(out=rw(MU), in_=pool[piA][0:1, :], mul=1.0 / D),
              waits=[peA] + dep(rowsL_free))
        stA_free = ss.at()
        P.add("scalar", lambda s, piB=piB: s.mul(out=rw(EX2), in_=pool[piB][0:1, :], mul=1.0 / D),
              waits=[peB])
        stB_free = ss.at()
        pool_free[piA] = ss.at()
        pool_free[piB] = ss.at()
        P.add("vector", lambda v: v.tensor_tensor(out=rw(VAR), in0=rw(MU), in1=rw(MU), op=ALU.mult),
              waits=[ss])
        P.add("vector", lambda v: v.tensor_tensor(out=rw(VAR), in0=rw(EX2), in1=rw(VAR), op=ALU.subtract))
        P.add("scalar", lambda s: s.activation(out=rw(SCR), in_=rw(VAR), func=AF.Sqrt, bias=epsln[:]),
              waits=[sv])
        P.add("vector", lambda v: v.reciprocal(out=rw(RSTD), in_=rw(SCR)), waits=[ss])
        P.add("vector", lambda v: v.tensor_tensor(out=rw(NMR), in0=rw(MU), in1=rw(RSTD), op=ALU.mult))
        P.add("scalar", lambda s: s.mul(out=rw(NMR), in_=rw(NMR), mul=-1.0), waits=[sv])
        rows_ready = ss.at()
        pi1 = next_pool()
        P.add("tensor", lambda t, pi1=pi1: t.matmul(
            out=pool[pi1][:], lhsT=onesr_f[:], rhs=rw(RSTD), start=True, stop=True),
            waits=[rows_ready] + dep(pool_free[pi1]), inc=pe)
        pi2 = next_pool()
        P.add("tensor", lambda t, pi2=pi2: t.matmul(
            out=pool[pi2][:], lhsT=onesr_f[:], rhs=rw(NMR), start=True, stop=True),
            waits=dep(pool_free[pi2]), inc=pe)
        pe_bc = pe.at()
        rowsL_free = pe_bc
        P.add("vector", lambda v, pi1=pi1: v.tensor_tensor(
            out=xo_s4[:].rearrange("p (c t) -> p c t", c=NCH),
            in0=rbf4[:].rearrange("p (c t) -> p c t", c=NCH),
            in1=pool[pi1][:].unsqueeze(1).broadcast_to((128, NCH, W4)),
            op=ALU.mult),
            waits=[pe_bc] + dep(out_free))
        pool_free[pi1] = sv.at()
        P.add("vector", lambda v, pi2=pi2: v.tensor_tensor(
            out=xo_s4[:].rearrange("p (c t) -> p c t", c=NCH),
            in0=xo_s4[:].rearrange("p (c t) -> p c t", c=NCH),
            in1=pool[pi2][:].unsqueeze(1).broadcast_to((128, NCH, W4)),
            op=ALU.add))
        pool_free[pi2] = sv.at()
        rb_free = sv.at()
        out_ready = sv.at()
        P.add("gpsimd", lambda g, t0=t0: g.dma_start(
            out=xo_d[:, :, t0:t0 + W4].rearrange("c p t -> p c t"),
            in_=xo_s4[:].rearrange("p (c t) -> p c t", c=NCH)),
            waits=[out_ready], inc=sto, inc_n=16, is_dma=True)
        out_free = (sto, sto.v)
    esL.close()

    P.add("gpsimd", lambda g: g.nop(), waits=[sto, stp, P.nsem("st")])
    es.close()


def make_rotary_tables_np():
    DH, WIN = 64, 512
    half = np.arange(0, DH, 2, dtype=np.float64)
    inv_freq = 1.0 / 10000.0 ** (half / DH)
    t = np.arange(T, dtype=np.float64)
    freqs = t[:, None] * inv_freq[None, :]
    freqs = np.concatenate([freqs, freqs], -1)
    base = (half + 0.4 * DH) / (1.4 * DH)
    power = (t - T // 2) / (WIN // 2)
    scale = base[None, :] ** power[:, None]
    scale = np.concatenate([scale, scale], -1)
    cos, sin = np.cos(freqs), np.sin(freqs)
    sgn = np.concatenate([-np.ones(32), np.ones(32)])
    cs_q = (cos * scale).astype(np.float32)
    sn_q = (sin * scale * sgn[None, :]).astype(np.float32)
    cs_k = (cos / scale).astype(np.float32)
    sn_k = (sin / scale * sgn[None, :]).astype(np.float32)
    fm = lambda a: np.ascontiguousarray(np.tile(a.T, (2, 1)))
    return fm(cs_q), fm(sn_q), fm(cs_k), fm(sn_k)


def stage4_inputs(xfm, inputs):
    g = lambda k: np.asarray(inputs[k], np.float32)
    wqkv = np.stack([np.ascontiguousarray(g("L1_ffn_wqkv")[l]).reshape(NCH, 128, 1536)
                     for l in range(2)])
    wo4 = np.stack([np.ascontiguousarray(g("L1_ffn_wo")[l]).reshape(4, 128, 1024)
                    for l in range(2)])
    w1 = g("L1_ffn_w1")
    b1 = g("L1_ffn_b1")
    w2 = g("L1_ffn_w2")
    w1a = np.zeros((2, D, HIDP), np.float32); w1a[:, :, :HID] = w1[:, :, :HID]
    w1g = np.zeros((2, D, HIDP), np.float32); w1g[:, :, :HID] = w1[:, :, HID:]
    b1a = np.zeros((2, HIDP), np.float32); b1a[:, :HID] = b1[:, :HID]
    b1g = np.zeros((2, HIDP), np.float32); b1g[:, :HID] = b1[:, HID:]
    w2p = np.zeros((2, HIDP, D), np.float32); w2p[:, :HID] = w2
    w1a = w1a.reshape(2, NCH, 128, HIDP)
    w1g = w1g.reshape(2, NCH, 128, HIDP)
    w2p = np.ascontiguousarray(w2p.reshape(2, NFC, 128, 1024))
    b1a_c = np.ascontiguousarray(b1a.reshape(2, NFC, 128).transpose(0, 2, 1))
    b1g_c = np.ascontiguousarray(b1g.reshape(2, NFC, 128).transpose(0, 2, 1))
    b2_c = np.ascontiguousarray(g("L1_ffn_b2").reshape(2, NCH, 128).transpose(0, 2, 1))
    gb_r = np.ascontiguousarray(np.stack(
        [g("L1_ffn_ln_g").reshape(2, NCH, 128), g("L1_ffn_ln_b").reshape(2, NCH, 128)],
        axis=2))
    qs = g("L1_ffn_q_scale")
    ks = g("L1_ffn_k_scale")
    swp = np.concatenate([np.arange(32, 64), np.arange(0, 32)])
    qs_c = np.stack([np.tile(qs, (1, 2)), np.tile(qs[:, swp], (1, 2))], axis=2)
    ks_c = np.stack([np.tile(ks, (1, 2)), np.tile(ks[:, swp], (1, 2))], axis=2)
    csq, snq, csk, snk = make_rotary_tables_np()
    i = np.arange(128)[None, :]
    j = np.arange(128)[:, None]
    medge_prev = np.where(i <= j, 0.0, -1e31).astype(np.float32)
    medge_cur = np.where(i >= j, 0.0, -1e31).astype(np.float32)
    maskb = np.concatenate([medge_prev, medge_cur], axis=1)
    ident = np.eye(128, dtype=np.float32)
    p32 = np.zeros((128, 128), np.float32)
    for d_ in range(128):
        b_ = (d_ // 64) * 64
        p32[b_ + ((d_ - b_) + 32) % 64, d_] = 1.0
    sel8 = np.zeros((128, 32), np.float32)
    for ct in range(4):
        for half in range(2):
            sel8[half * 64:(half + 1) * 64, ct * 8 + 2 * ct + half] = 1.0
    selq = np.zeros((8, 512), np.float32)
    selk = np.zeros((8, 512), np.float32)
    for ct in range(4):
        for d_ in range(128):
            h = 2 * ct + d_ // 64
            selq[h, ct * 128 + d_] = 8.0
            selk[h, ct * 128 + d_] = 1.0
    e8 = np.zeros((1, 64), np.float32)
    for h in range(8):
        e8[0, h * 8 + h] = 1.0
    ones512 = np.ones((1, 512), np.float32)
    return {"x": xfm, "wqkv": wqkv, "wo4": wo4, "w1a": w1a, "w1g": w1g, "w2p": w2p,
            "b1a_c": b1a_c, "b1g_c": b1g_c, "b2_c": b2_c, "gb_r": gb_r,
            "qs_c": qs_c, "ks_c": ks_c, "csq": csq, "snq": snq, "csk": csk, "snk": snk,
            "maskb": maskb, "ident": ident, "p32": p32, "sel8": sel8, "selq": selq,
            "selk": selk, "e8": e8, "ones512": ones512}


def _featmajor(xb):
    return np.ascontiguousarray(np.asarray(xb, np.float32).T.reshape(NCH, 128, T))


def _unfeat(xt):
    return np.asarray(xt, np.float32).reshape(D, T).T


def stage2_inputs(xfm, inputs):
    wfc = np.ascontiguousarray(np.asarray(inputs["L0_ffn_W_fc"], np.float32).T).reshape(NCH, 128, 4096)
    wproj = np.ascontiguousarray(np.asarray(inputs["L0_ffn_W_proj"], np.float32).T).reshape(32, 128, 1024)
    return {"x": xfm, "wfc": wfc, "wproj": wproj}


def stage3_inputs(xfm, inputs):
    tr = lambda w: np.ascontiguousarray(np.asarray(w, np.float32).T).reshape(NCH, 128, 1024)
    bq = np.asarray(inputs["L1_attn_bq"], np.float32).reshape(NCH, 128).T
    bk = np.asarray(inputs["L1_attn_bk"], np.float32).reshape(NCH, 128).T
    bqk = np.ascontiguousarray(np.concatenate([bq, bk], axis=1))
    bo = np.ascontiguousarray(np.asarray(inputs["L1_attn_bo"], np.float32).reshape(NCH, 128).T)
    bv = np.asarray(inputs["L1_attn_bv"], np.float32).reshape(1, 1024)
    i_loc = np.arange(128)[:, None]
    j = np.arange(256)[None, :]
    mask = np.concatenate(
        [np.where(j <= qt * 128 + i_loc, 0.0, -1e31).astype(np.float32) for qt in (0, 1)],
        axis=1)
    ident = np.eye(128, dtype=np.float32)
    return {"x": xfm, "wq": tr(inputs["L1_attn_Wq"]), "wk": tr(inputs["L1_attn_Wk"]),
            "wv": tr(inputs["L1_attn_Wv"]), "wo": tr(inputs["L1_attn_Wo"]),
            "bqk": bqk, "bo_c": bo, "bv_r": bv, "mask": np.ascontiguousarray(mask),
            "ident": ident}


def build_fused():
    nc = bass.Bass("TRN2", target_bir_lowering=False)
    dram = lambda name, shp, dt=F32, kind="ExternalInput": nc.dram_tensor(name, shp, dt, kind=kind).ap()
    x_d = dram("x", [NCH, 128, T])
    xo_d = dram("xo", [NCH, 128, T], kind="ExternalOutput")
    h12 = nc.dram_tensor("h12", [NCH, 128, T], BF16, kind="Internal").ap()
    h23 = nc.dram_tensor("h23", [NCH, 128, T], BF16, kind="Internal").ap()
    h34 = nc.dram_tensor("h34", [NCH, 128, T], BF16, kind="Internal").ap()

    es = ExitStack()
    P = Prog(nc, es)

    gcb = es.enter_context(nc.sbuf_tensor("gconst", [1, 130], F32))
    g_ones = gcb[0:1, 0:128]
    g_eps5 = gcb[0:1, 128:129]
    g_negh = gcb[0:1, 129:130]
    P.add("gpsimd", lambda g: g.memset(g_ones, 1.0))
    P.add("gpsimd", lambda g: g.memset(g_eps5, 1e-5))
    P.add("gpsimd", lambda g: g.memset(g_negh, -0.5))
    gc = (g_ones, g_eps5, g_negh)

    s1 = (x_d,
          dram("s1_wq", [NCH, 128, 128]), dram("s1_wk", [NCH, 128, 128]),
          dram("s1_wv", [NCH, 128, 128]), dram("s1_wg", [NCH, 128, 128]),
          dram("s1_wab", [NCH, 128, 2]), dram("s1_wo", [128, 1024]),
          dram("s1_bq_c", [128, 1]), dram("s1_bk_c", [128, 1]), dram("s1_bg_c", [128, 1]),
          dram("s1_bv_r", [1, 128]), dram("s1_bab", [1, 2]), dram("s1_bo_c", [128, NCH]),
          dram("s1_cumUT", [128, 128]), dram("s1_identf", [128, 128]),
          dram("s1_maskadd", [128, 128]), dram("s1_maskstrict", [128, 128]),
          h12)
    emit_stage1(nc, P, s1)
    pcast = Sem(nc, es, "pcast")
    wqkvB = nc.dram_tensor("wqkvB", [2, NCH, 128, 1536], BF16, kind="Internal").ap()
    wo4B = nc.dram_tensor("wo4B", [2, 4, 128, 1024], BF16, kind="Internal").ap()
    w1aB = nc.dram_tensor("w1aB", [2, NCH, 128, HIDP], BF16, kind="Internal").ap()
    w1gB = nc.dram_tensor("w1gB", [2, NCH, 128, HIDP], BF16, kind="Internal").ap()
    w2pB = nc.dram_tensor("w2pB", [2, NFC, 128, 1024], BF16, kind="Internal").ap()
    tabB = nc.dram_tensor("tabB", [4, 128, T], BF16, kind="Internal").ap()
    wfcB = nc.dram_tensor("wfcB", [NCH, 128, 4096], BF16, kind="Internal").ap()
    wprojB = nc.dram_tensor("wprojB", [32, 128, 1024], BF16, kind="Internal").ap()
    s3wB = nc.dram_tensor("s3wB", [4, NCH, 128, 1024], BF16, kind="Internal").ap()
    s4f32 = {
        "wqkv": nc.dram_tensor("wqkv", [2, NCH, 128, 1536], F32, kind="ExternalInput").ap(),
        "wo4": nc.dram_tensor("wo4", [2, 4, 128, 1024], F32, kind="ExternalInput").ap(),
        "w1a": nc.dram_tensor("w1a", [2, NCH, 128, HIDP], F32, kind="ExternalInput").ap(),
        "w1g": nc.dram_tensor("w1g", [2, NCH, 128, HIDP], F32, kind="ExternalInput").ap(),
        "w2p": nc.dram_tensor("w2p", [2, NFC, 128, 1024], F32, kind="ExternalInput").ap(),
        "csq": nc.dram_tensor("csq", [128, T], F32, kind="ExternalInput").ap(),
        "snq": nc.dram_tensor("snq", [128, T], F32, kind="ExternalInput").ap(),
        "csk": nc.dram_tensor("csk", [128, T], F32, kind="ExternalInput").ap(),
        "snk": nc.dram_tensor("snk", [128, T], F32, kind="ExternalInput").ap(),
    }

    def emit_precast(s4f32):
        P.add("gpsimd", lambda g: g.dma_start(out=wfcB, in_=s2_wfc_d),
              waits=[pcast], inc=pcast, inc_n=16, is_dma=True)
        P.add("gpsimd", lambda g: g.dma_start(out=wprojB, in_=s2_wproj_d),
              waits=[pcast], inc=pcast, inc_n=16, is_dma=True)
        for i, sw in enumerate(s3w_f32):
            P.add("gpsimd", lambda g, i=i, sw=sw: g.dma_start(out=s3wB[i], in_=sw),
                  waits=[pcast], inc=pcast, inc_n=16, is_dma=True)
        for i, key in enumerate(("csq", "snq", "csk", "snk")):
            P.add("gpsimd", lambda g, d=tabB[i], s=s4f32[key]: g.dma_start(out=d, in_=s),
                  waits=[pcast], inc=pcast, inc_n=16, is_dma=True)
        for l in range(2):
            for dstB, key in ((wqkvB, "wqkv"), (wo4B, "wo4"), (w1aB, "w1a"),
                              (w1gB, "w1g"), (w2pB, "w2p")):
                P.add("gpsimd", lambda g, d=dstB[l], s=s4f32[key][l]: g.dma_start(out=d, in_=s),
                      waits=[pcast], inc=pcast, inc_n=16, is_dma=True)
        return {"attn": [(pcast, 192), (pcast, 272)],
                "ffn": [(pcast, 240), (pcast, 320)]}

    s2_wfc_d = dram("s2_wfc", [NCH, 128, 4096])
    s2_wproj_d = dram("s2_wproj", [32, 128, 1024])
    s3w_f32 = [dram("s3_wq", [NCH, 128, 1024]), dram("s3_wk", [NCH, 128, 1024]),
               dram("s3_wv", [NCH, 128, 1024]), dram("s3_wo", [NCH, 128, 1024])]
    pc_gates = emit_precast(s4f32)

    P.stage_barrier()
    emit_stage2(nc, P, h12, wfcB, wprojB, h23, pc2=(pcast, 32), gc=gc)
    P.stage_barrier()
    emit_stage3(nc, P, h23,
                s3wB[0], s3wB[1], s3wB[2], s3wB[3],
                dram("s3_bqk", [128, 16]), dram("s3_bo_c", [128, 8]),
                dram("s3_bv_r", [1, 1024]), dram("s3_mask", [128, 512]),
                dram("s3_ident", [128, 128]),
                h34, pc3=(pcast, 96))
    P.stage_barrier()
    emit_stage4(nc, P, h34, xo_d,
                dict(wqkvB=wqkvB, wo4B=wo4B, w1aB=w1aB, w1gB=w1gB, w2pB=w2pB,
                     tabB=tabB, pcast=pcast, pc_gates=pc_gates))
    P.finalize()
    es.close()
    return nc


def fused_inputs(xfm, inputs):
    d = {"x": xfm}
    for k, v in stage1_inputs(xfm, inputs).items():
        if k != "x":
            d["s1_" + k] = v
    for k, v in stage2_inputs(xfm, inputs).items():
        if k != "x":
            d["s2_" + k] = v
    for k, v in stage3_inputs(xfm, inputs).items():
        if k != "x":
            d["s3_" + k] = v
    for k, v in stage4_inputs(xfm, inputs).items():
        if k != "x":
            d[k] = v
    return d


def kernel(**inputs):
    x = np.asarray(inputs["L0_attn_x"], np.float32)
    xfms = [np.ascontiguousarray(x[b].T.reshape(NCH, 128, T)) for b in range(8)]
    nc = build_fused()
    base = fused_inputs(xfms[0], inputs)
    res = run_bass_kernel_spmd(nc, [dict(base, x=xfms[b]) for b in range(8)],
                               core_ids=list(range(8)))
    return np.stack([_unfeat(np.asarray(res.results[b]["xo"]))
                     for b in range(8)]).astype(np.float32)


def kernel_multilaunch(**inputs):
    x = np.asarray(inputs["L0_attn_x"], np.float32)
    xfms = [np.ascontiguousarray(x[b].T.reshape(NCH, 128, T)) for b in range(8)]
    cores = list(range(8))

    nc1 = build_stage1()
    base1 = stage1_inputs(xfms[0], inputs)
    r1 = run_bass_kernel_spmd(nc1, [dict(base1, x=xfms[b]) for b in range(8)], core_ids=cores)
    xs = [np.asarray(r1.results[b]["xo"]) for b in range(8)]

    nc2 = build_stage2()
    base2 = stage2_inputs(xs[0], inputs)
    r2 = run_bass_kernel_spmd(nc2, [dict(base2, x=xs[b]) for b in range(8)], core_ids=cores)
    xs = [np.asarray(r2.results[b]["xo"]) for b in range(8)]

    nc3 = build_stage3()
    base3 = stage3_inputs(xs[0], inputs)
    r3 = run_bass_kernel_spmd(nc3, [dict(base3, x=xs[b]) for b in range(8)], core_ids=cores)
    xs = [np.asarray(r3.results[b]["xo"]) for b in range(8)]

    nc4 = build_stage4()
    base4 = stage4_inputs(xs[0], inputs)
    r4 = run_bass_kernel_spmd(nc4, [dict(base4, x=xs[b]) for b in range(8)], core_ids=cores)
    xs = [np.asarray(r4.results[b]["xo"]) for b in range(8)]

    return np.stack([_unfeat(xb) for xb in xs]).astype(np.float32)


# version 25
# speedup vs baseline: 1.0013x; 1.0013x over previous
import numpy as np
from contextlib import ExitStack

import concourse.bass as bass
import concourse.mybir as mybir
from concourse.bass_utils import run_bass_kernel_spmd

F32 = mybir.dt.float32
BF16 = mybir.dt.bfloat16
AF = mybir.ActivationFunctionType
ALU = mybir.AluOpType
AX = mybir.AxisListType

T = 4096
D = 1024
NCH = D // 128



class Sem:
    def __init__(self, nc, es, name):
        self.h = es.enter_context(nc.semaphore(name))
        self.v = 0
        self.name = name

    def at(self):
        return (self, self.v)


class Prog:

    ENGINES = ("sync", "tensor", "vector", "scalar", "gpsimd")

    def __init__(self, nc, es):
        self.nc = nc
        self.es = es
        self.streams = {e: [] for e in self.ENGINES}
        self.named = {}
        self.pending_barrier = {}
        self.esem = {e: self.sem("e_" + e) for e in ("vector", "scalar", "gpsimd")}

    def sem(self, name):
        s = Sem(self.nc, self.es, name)
        self.named[name] = s
        return s

    def nsem(self, name):
        if name not in self.named:
            self.named[name] = Sem(self.nc, self.es, name)
        return self.named[name]

    def stage_barrier(self):
        vals = [(s, s.v) for s in self.named.values() if s.v > 0]
        self.pending_barrier = {e: list(vals) for e in self.ENGINES}

    def add(self, eng, fn, waits=(), inc=None, inc_n=1, is_dma=False):
        wl = []
        pb = self.pending_barrier.pop(eng, None)
        if pb:
            wl.extend(pb)
        for w in waits:
            s, val = (w, w.v) if isinstance(w, Sem) else (w[0], w[1])
            if val > 0:
                wl.append((s, val))
        if not is_dma and eng in self.esem:
            es_ = self.esem[eng]
            if es_.v > 0:
                wl.append((es_, es_.v))
            assert inc is None or inc is es_
            inc = es_
        if inc is not None:
            inc.v += inc_n
        ret = (inc, inc.v) if inc is not None else None

        def emit(engine, fn=fn, wl=wl, inc=inc, inc_n=inc_n):
            for s, val in wl:
                engine.wait_ge(s.h, val)
            instr = fn(engine)
            if inc is not None:
                assert instr is not None, "inc requested but fn returned no instr"
                instr.then_inc(inc.h, inc_n)

        self.streams[eng].append(emit)
        return ret

    def finalize(self):
        nc = self.nc
        with nc.Block() as block:
            for name in self.ENGINES:
                fns = self.streams[name]
                if not fns:
                    continue

                def runner(engine, fns=fns):
                    for f in fns:
                        f(engine)

                getattr(block, name)(runner)



def emit_stage2(nc, P, x_d, wfc_d, wproj_d, xo_d, pc2=None, gc=None):
    es = ExitStack()
    sb = lambda name, shp, dt=F32: es.enter_context(nc.sbuf_tensor('z2_' + name, shp, dt))
    psum = lambda name, shp, dt=F32: es.enter_context(nc.psum_tensor('z2_' + name, shp, dt))

    TB = 512
    NTB = T // TB
    HF = 128

    wfc = sb("wfc_s", [128, NCH * 4096], BF16)
    wproj = sb("wproj_s", [128, 32 * 1024], BF16)
    xbt = [sb(f"xbt{i}", [128, NCH * TB], BF16) for i in range(2)]
    h2all = sb("h2all", [128, 32 * TB], BF16)
    ysb = sb("ysb", [128, NCH * TB])
    spk = sb("spk", [128, NCH * HF], BF16)
    rsb = [sb(f"rsb{i}", [128, NCH * HF], BF16) for i in range(4)]
    r2bf = sb("r2bf", [128, NCH * HF], BF16)
    rows = sb("rows", [1, 3 * HF])
    ones_c = sb("ones_c", [128, 1], BF16)
    xo_s2 = r2bf
    ones_r, epsb, neghalf = gc

    pool = [psum(f"pool{i}", [128, 512]) for i in range(6)]
    mem = psum("memp", [128, NCH * HF])

    xl = [P.nsem("xl0"), P.nsem("xl1")]
    st = P.nsem("stp")
    pe = P.nsem("pe")
    sv = P.esem["vector"]
    ss = P.esem["scalar"]
    sg = P.esem["gpsimd"]

    pool_free = [None] * 6
    _np = [0]

    def next_pool():
        i = _np[0] % 6
        _np[0] += 1
        return i

    P.add("gpsimd", lambda g: g.memset(ones_c[:], 1.0))
    consts_ready = sg.at()
    ls2 = P.nsem("ls")
    first = [True]
    for c in range(NCH):
        P.add("sync", lambda s, c=c: s.dma_start(
            out=wfc[:, c * 4096:(c + 1) * 4096], in_=wfc_d[c]),
            waits=(([ls2] + ([pc2] if pc2 is not None else [])) if first[0] else []),
            inc=ls2, inc_n=16, is_dma=True)
        first[0] = False
    for f in range(32):
        P.add("sync", lambda s, f=f: s.dma_start(
            out=wproj[:, f * 1024:(f + 1) * 1024], in_=wproj_d[f]), inc=ls2, inc_n=16, is_dma=True)
    w_ready = ls2.at()

    def dep(x):
        return [x] if x is not None else []

    S = {
        "xbt_free": [None, None], "h2all_free": None, "ysb_free": None,
        "rsb_free": [None] * 4, "r2bf_free": None, "statA_free": None,
        "statB_free": None, "rows_free": None,
    }

    x_ready = []
    P.add("sync", lambda s: s.dma_start(
        out=xbt[0][:].rearrange("p (c t) -> p c t", c=NCH),
        in_=x_d[:, :, 0:TB].rearrange("c p t -> p c t")),
        waits=[xl[0]], inc=xl[0], inc_n=16, is_dma=True)
    x_ready.append(xl[0].at())

    def emit_tail(gh, r_ready):
        par2 = gh % 4
        t0g = gh * HF
        rbuf = rsb[par2]
        P.add("gpsimd", lambda g, rbuf=rbuf: g.tensor_tensor(
            out=r2bf[:], in0=rbuf[:], in1=rbuf[:], op=ALU.mult),
            waits=[r_ready, (st, st.v)] + dep(S["r2bf_free"]))
        r2_ready = sg.at()
        piA = next_pool()
        for c in range(NCH):
            P.add("tensor", lambda t, c=c, piA=piA, rbuf=rbuf: t.matmul(
                out=pool[piA][0:1, 0:HF], lhsT=ones_c[:], rhs=rbuf[:, c * HF:(c + 1) * HF],
                start=(c == 0), stop=(c == NCH - 1)),
                waits=[r_ready, consts_ready] + dep(pool_free[piA]) + dep(S["statA_free"]),
                inc=pe if c == NCH - 1 else None)
        peA = pe.at()
        piB = next_pool()
        for c in range(NCH):
            P.add("tensor", lambda t, c=c, piB=piB: t.matmul(
                out=pool[piB][0:1, 0:HF], lhsT=ones_c[:], rhs=r2bf[:, c * HF:(c + 1) * HF],
                start=(c == 0), stop=(c == NCH - 1)),
                waits=[r2_ready] + dep(pool_free[piB]) + dep(S["statB_free"]),
                inc=pe if c == NCH - 1 else None)
        peB = pe.at()
        S["r2bf_free"] = peB
        MU, EX2, W = 0, 1, 2
        rw = lambda i: rows[0:1, i * HF:(i + 1) * HF]
        P.add("scalar", lambda s, piA=piA: s.mul(
            out=rw(MU), in_=pool[piA][0:1, 0:HF], mul=-1.0 / D),
            waits=[peA] + dep(S["rows_free"]))
        S["statA_free"] = ss.at()
        P.add("scalar", lambda s, piB=piB: s.mul(
            out=rw(EX2), in_=pool[piB][0:1, 0:HF], mul=1.0 / D),
            waits=[peB])
        S["statB_free"] = ss.at()
        pool_free[piA] = S["statA_free"]
        pool_free[piB] = S["statB_free"]
        P.add("gpsimd", lambda g: g.tensor_tensor(out=rw(W), in0=rw(MU), in1=rw(MU), op=ALU.mult),
              waits=[ss])
        P.add("gpsimd", lambda g: g.tensor_tensor(out=rw(W), in0=rw(EX2), in1=rw(W), op=ALU.subtract))
        P.add("scalar", lambda s: s.activation(out=rw(W), in_=rw(W), func=AF.Ln, bias=epsb),
              waits=[sg])
        P.add("scalar", lambda s: s.activation(out=rw(W), in_=rw(W), func=AF.Exp, scale=neghalf))
        rows_ready = ss.at()
        pi1 = next_pool()
        P.add("tensor", lambda t, pi1=pi1: t.matmul(
            out=pool[pi1][:, 0:HF], lhsT=ones_r, rhs=rw(W), start=True, stop=True),
            waits=[rows_ready] + dep(pool_free[pi1]), inc=pe)
        pi2 = next_pool()
        P.add("tensor", lambda t, pi2=pi2: t.matmul(
            out=pool[pi2][:, 0:HF], lhsT=ones_r, rhs=rw(MU), start=True, stop=True),
            waits=dep(pool_free[pi2]), inc=pe)
        pe_bc = pe.at()
        S["rows_free"] = pe_bc
        P.add("vector", lambda v, pi2=pi2, rbuf=rbuf: v.tensor_tensor(
            out=xo_s2[:].rearrange("p (c t) -> p c t", c=NCH),
            in0=rbuf[:].rearrange("p (c t) -> p c t", c=NCH),
            in1=pool[pi2][:, 0:HF].unsqueeze(1).broadcast_to((128, NCH, HF)),
            op=ALU.add),
            waits=[pe_bc, (st, st.v)])
        pool_free[pi2] = sv.at()
        P.add("vector", lambda v, pi1=pi1: v.tensor_tensor(
            out=xo_s2[:].rearrange("p (c t) -> p c t", c=NCH),
            in0=xo_s2[:].rearrange("p (c t) -> p c t", c=NCH),
            in1=pool[pi1][:, 0:HF].unsqueeze(1).broadcast_to((128, NCH, HF)),
            op=ALU.mult))
        pool_free[pi1] = sv.at()
        out_ready = sv.at()
        S["rsb_free"][par2] = out_ready
        P.add("sync", lambda s, t0g=t0g: s.dma_start(
            out=xo_d[:, :, t0g:t0g + HF].rearrange("c p t -> p c t"),
            in_=xo_s2[:].rearrange("p (c t) -> p c t", c=NCH)),
            waits=[out_ready, st], inc=st, inc_n=16, is_dma=True)

    pend = []
    for tb in range(NTB):
        xpar = tb % 2
        if tb + 1 < NTB:
            pp = (tb + 1) % 2
            P.add("sync", lambda s, tb=tb, pp=pp: s.dma_start(
                out=xbt[pp][:].rearrange("p (c t) -> p c t", c=NCH),
                in_=x_d[:, :, (tb + 1) * TB:(tb + 2) * TB].rearrange("c p t -> p c t")),
                waits=dep(S["xbt_free"][pp]) + [xl[pp]], inc=xl[pp], inc_n=16, is_dma=True)
            x_ready.append(xl[pp].at())

        sq_ready = [None] * 32
        for fc in range(32):
            pi = next_pool()
            for c in range(NCH):
                P.add("tensor", lambda t, c=c, fc=fc, pi=pi, xpar=xpar: t.matmul(
                    out=pool[pi][:],
                    lhsT=wfc[:, c * 4096 + fc * 128: c * 4096 + (fc + 1) * 128],
                    rhs=xbt[xpar][:, c * TB:(c + 1) * TB],
                    start=(c == 0), stop=(c == NCH - 1)),
                    waits=[w_ready, x_ready[tb]] + dep(pool_free[pi]),
                    inc=pe if c == NCH - 1 else None)
            pe_h1 = pe.at()
            P.add("scalar", lambda s, pi=pi, fc=fc: s.activation(
                out=h2all[:, fc * TB:(fc + 1) * TB], in_=pool[pi][:], func=AF.Relu),
                waits=[pe_h1] + dep(S["h2all_free"] if fc == 0 else None))
            P.add("vector", lambda v, pi=pi, fc=fc: v.tensor_tensor(
                out=h2all[:, fc * TB:(fc + 1) * TB],
                in0=h2all[:, fc * TB:(fc + 1) * TB],
                in1=pool[pi][:], op=ALU.mult),
                waits=[ss])
            pool_free[pi] = sv.at()
            sq_ready[fc] = sv.at()

        for d2 in range(NCH):
            pi = next_pool()
            for fc in range(32):
                P.add("tensor", lambda t, d2=d2, fc=fc, pi=pi: t.matmul(
                    out=pool[pi][:],
                    lhsT=wproj[:, fc * 1024 + d2 * 128: fc * 1024 + (d2 + 1) * 128],
                    rhs=h2all[:, fc * TB:(fc + 1) * TB],
                    start=(fc == 0), stop=(fc == 31)),
                    waits=[sq_ready[fc]] + dep(pool_free[pi]),
                    inc=pe if fc == 31 else None)
            pe_y = pe.at()
            P.add("scalar", lambda s, d2=d2, pi=pi: s.copy(
                out=ysb[:, d2 * TB:(d2 + 1) * TB], in_=pool[pi][:]),
                waits=[pe_y] + (dep(S["ysb_free"]) if d2 == 0 else []))
            pool_free[pi] = ss.at()
        S["h2all_free"] = pe.at()
        y_ready = ss.at()

        for h in range(4):
            gh = tb * 4 + h
            rpar = gh % 4
            yv = ysb[:].rearrange("p (c t) -> p c t", c=NCH)[:, :, h * HF:(h + 1) * HF]
            xv = xbt[xpar][:].rearrange("p (c t) -> p c t", c=NCH)[:, :, h * HF:(h + 1) * HF]
            mv = mem[:].rearrange("p (c t) -> p c t", c=NCH)
            sp = spk[:].rearrange("p (c t) -> p c t", c=NCH)
            rv = rsb[rpar][:].rearrange("p (c t) -> p c t", c=NCH)
            P.add("scalar", lambda s, yv=yv, mv=mv: s.copy(out=mv, in_=yv),
                  waits=[y_ready, sv])
            P.add("vector", lambda v, yv=yv, sp=sp: v.tensor_single_scalar(
                out=sp, in_=yv, scalar=1.0, op=ALU.is_gt),
                waits=[y_ready])
            P.add("scalar", lambda s, sp=sp, rv=rv: s.copy(out=rv, in_=sp),
                  waits=[sv] + dep(S["rsb_free"][rpar]))
            for step in range(4):
                P.add("vector", lambda v, yv=yv, mv=mv: v.scalar_tensor_tensor(
                    out=mv, in0=mv, scalar=0.9, op0=ALU.mult, in1=yv, op1=ALU.add),
                    waits=[ss] if step == 0 else [])
                P.add("vector", lambda v, mv=mv, sp=sp: v.tensor_tensor(
                    out=mv, in0=mv, in1=sp, op=ALU.subtract))
                P.add("vector", lambda v, mv=mv, sp=sp: v.tensor_single_scalar(
                    out=sp, in_=mv, scalar=1.0, op=ALU.is_gt))
                P.add("vector", lambda v, sp=sp, rv=rv: v.tensor_tensor(
                    out=rv, in0=rv, in1=sp, op=ALU.add))
            P.add("vector", lambda v, rv=rv, xv=xv: v.scalar_tensor_tensor(
                out=rv, in0=rv, scalar=0.2, op0=ALU.mult, in1=xv, op1=ALU.add))
            r_ready = sv.at()
            if h == 3:
                S["xbt_free"][xpar] = r_ready
                S["ysb_free"] = r_ready

            if len(pend) == 3:
                pend.pop(0)()
            pend.append(lambda gh=gh, r=r_ready: emit_tail(gh, r))

    for f in pend:
        f()
    es.close()




class LNPack:

    def __init__(self, nc, es, TB, tag=""):
        sb = lambda name, shp, dt=F32: es.enter_context(nc.sbuf_tensor(name, shp, dt))
        self.TB = TB
        self.rbf = sb(f"ln_rbf{tag}", [128, NCH * TB], BF16)
        self.r2bf = sb(f"ln_r2bf{tag}", [128, NCH * TB], BF16)
        self.rows = sb(f"ln_rows{tag}", [1, 6 * TB])
        self.statp = es.enter_context(nc.psum_tensor(f"ln_statp{tag}", [1, 2 * TB], F32))
        self.bcp = es.enter_context(nc.psum_tensor(f"ln_bcp{tag}", [128, 2 * TB], F32))
        self.statp_free = None
        self.bcp_free = None
        self.rbuf_free = None


def emit_ln(P, pe, consts, lnp, rsb, xo_s, r_ready, out_free=None):
    sv, ss = P.esem["vector"], P.esem["scalar"]
    TB = lnp.TB
    dep = lambda x: [x] if x is not None else []
    ones_c, ones_r, epsb, consts_ready = consts

    P.add("vector", lambda v: v.tensor_copy(out=lnp.rbf[:], in_=rsb),
          waits=[r_ready])
    rbf_ready = sv.at()
    P.add("scalar", lambda s: s.activation(out=lnp.r2bf[:], in_=rsb, func=AF.Square),
          waits=[r_ready])
    r2_ready = ss.at()
    for c in range(NCH):
        P.add("tensor", lambda t, c=c: t.matmul(
            out=lnp.statp[0:1, 0:TB],
            lhsT=ones_c[:], rhs=lnp.rbf[:, c * TB:(c + 1) * TB],
            start=(c == 0), stop=(c == NCH - 1)),
            waits=[rbf_ready, consts_ready] + dep(lnp.statp_free),
            inc=pe if c == NCH - 1 else None)
    for c in range(NCH):
        P.add("tensor", lambda t, c=c: t.matmul(
            out=lnp.statp[0:1, TB:2 * TB],
            lhsT=ones_c[:], rhs=lnp.r2bf[:, c * TB:(c + 1) * TB],
            start=(c == 0), stop=(c == NCH - 1)),
            waits=[r2_ready], inc=pe if c == NCH - 1 else None)
    pe_stat = pe.at()
    lnp.rbuf_free = pe_stat
    MU, EX2, VAR, RSTD, NMR, SCR = 0, 1, 2, 3, 4, 5
    rw = lambda i: lnp.rows[0:1, i * TB:(i + 1) * TB]
    P.add("scalar", lambda s: s.mul(out=rw(MU), in_=lnp.statp[0:1, 0:TB], mul=1.0 / D),
          waits=[pe_stat])
    P.add("scalar", lambda s: s.mul(out=rw(EX2), in_=lnp.statp[0:1, TB:2 * TB], mul=1.0 / D),
          waits=[pe_stat])
    lnp.statp_free = ss.at()
    P.add("vector", lambda v: v.tensor_tensor(
        out=rw(VAR), in0=rw(MU), in1=rw(MU), op=ALU.mult), waits=[ss])
    P.add("vector", lambda v: v.tensor_tensor(
        out=rw(VAR), in0=rw(EX2), in1=rw(VAR), op=ALU.subtract))
    P.add("scalar", lambda s: s.activation(
        out=rw(SCR), in_=rw(VAR), func=AF.Sqrt, bias=epsb[:]), waits=[sv])
    P.add("vector", lambda v: v.reciprocal(out=rw(RSTD), in_=rw(SCR)), waits=[ss])
    P.add("vector", lambda v: v.tensor_tensor(
        out=rw(NMR), in0=rw(MU), in1=rw(RSTD), op=ALU.mult))
    P.add("scalar", lambda s: s.mul(out=rw(NMR), in_=rw(NMR), mul=-1.0), waits=[sv])
    rows_ready = ss.at()
    P.add("tensor", lambda t: t.matmul(
        out=lnp.bcp[:, 0:TB], lhsT=ones_r[:], rhs=rw(RSTD), start=True, stop=True),
        waits=[rows_ready] + dep(lnp.bcp_free), inc=pe)
    P.add("tensor", lambda t: t.matmul(
        out=lnp.bcp[:, TB:2 * TB], lhsT=ones_r[:], rhs=rw(NMR), start=True, stop=True),
        inc=pe)
    pe_bc = pe.at()
    P.add("vector", lambda v: v.tensor_tensor(
        out=xo_s.rearrange("p (c t) -> p c t", c=NCH),
        in0=rsb.rearrange("p (c t) -> p c t", c=NCH),
        in1=lnp.bcp[:, 0:TB].unsqueeze(1).broadcast_to((128, NCH, TB)),
        op=ALU.mult),
        waits=[pe_bc, sv] + dep(out_free))
    P.add("vector", lambda v: v.tensor_tensor(
        out=xo_s.rearrange("p (c t) -> p c t", c=NCH),
        in0=xo_s.rearrange("p (c t) -> p c t", c=NCH),
        in1=lnp.bcp[:, TB:2 * TB].unsqueeze(1).broadcast_to((128, NCH, TB)),
        op=ALU.add))
    out_ready = sv.at()
    lnp.bcp_free = out_ready
    return out_ready



def emit_stage3(nc, P, x_d, wq_d, wk_d, wv_d, wo_d, bqk_d, bo_d, bv_d, mask_d, id_d, xo_d, pc3=None):
    W = 256
    NW = T // W
    es = ExitStack()
    sb = lambda name, shp, dt=F32: es.enter_context(nc.sbuf_tensor('z3_' + name, shp, dt))
    psum = lambda name, shp, dt=F32: es.enter_context(nc.psum_tensor('z3_' + name, shp, dt))

    wq = sb("wq_s", [128, NCH * 1024], BF16)
    wk = sb("wk_s", [128, NCH * 1024], BF16)
    wv = sb("wv_s", [128, NCH * 1024], BF16)
    wo = sb("wo_s", [128, NCH * 1024], BF16)
    bqk = sb("bqk_s", [128, 16])
    bo_c = sb("bo_s", [128, 8])
    bv_r = sb("bv_s", [1, 1024], BF16)
    maskc = sb("mask_s", [128, 512])
    identb = sb("id_s", [128, 128], BF16)
    ones_c = sb("ones_c", [128, 1], BF16)
    ones_r = sb("ones_r", [1, 128])
    onesr_b = sb("onesr_b", [1, 128], BF16)
    epsb = sb("epsb", [1, 1])
    c0125 = sb("c0125", [128, 1])

    WP = 2 * W
    xw = [sb(f"xw{i}", [128, NCH * WP], BF16) for i in range(2)]
    q_bf = sb("q_bf", [128, NCH * WP], BF16)
    k_bf = sb("k_bf", [128, NCH * WP], BF16)
    vtok = sb("vtok", [128, 4 * 1024], BF16)
    obf = sb("obf", [128, NCH * WP], BF16)
    sm = sb("sm", [128, W])
    p_bf = sb("p_bf", [128, W], BF16)
    pn_bf = sb("pn_bf", [128, W], BF16)
    pt_bf = sb("pt_bf", [128, 2 * 128], BF16)
    nmv = sb("nmv", [128, 2])
    sums = sb("sums", [128, 2])
    tsb = [sb(f"tsb{i}", [128, WP]) for i in range(2)]
    rsb = sb("rsb", [128, NCH * WP])
    xo_s = sb("xo_s", [128, NCH * WP])

    pool = [psum(f"pool{i}", [128, 512]) for i in range(2)]
    ptp = [psum(f"ptp{i}", [128, 128], BF16) for i in range(2)]
    lnp = LNPack(nc, es, WP)

    lg = P.nsem("lg")
    xl = [P.nsem("xl0"), P.nsem("xl1")]
    st = P.nsem("st")
    pe = P.nsem("pe")
    sv = P.esem["vector"]
    ss = P.esem["scalar"]
    sg = P.esem["gpsimd"]

    P.add("gpsimd", lambda g: g.memset(ones_c[:], 1.0))
    P.add("gpsimd", lambda g: g.memset(ones_r[:], 1.0))
    P.add("gpsimd", lambda g: g.memset(onesr_b[:], 1.0))
    P.add("gpsimd", lambda g: g.memset(epsb[:], 1e-5))
    P.add("gpsimd", lambda g: g.memset(c0125[:], 0.125))
    consts_ready = sg.at()
    consts = (ones_c, ones_r, epsb, consts_ready)
    ls3 = P.nsem("ls")
    first3 = [True]
    for c in range(NCH):
        for dst, srcw in ((wq, wq_d), (wk, wk_d), (wv, wv_d), (wo, wo_d)):
            P.add("sync", lambda s, c=c, dst=dst, srcw=srcw: s.dma_start(
                out=dst[:, c * 1024:(c + 1) * 1024], in_=srcw[c]),
                waits=([ls3, pc3] if first3[0] else []), inc=ls3, inc_n=16, is_dma=True)
            first3[0] = False
    P.add("gpsimd", lambda g: g.dma_start(out=bqk[:], in_=bqk_d), inc=lg, inc_n=16, is_dma=True)
    P.add("gpsimd", lambda g: g.dma_start(out=bo_c[:], in_=bo_d), inc=lg, inc_n=16, is_dma=True)
    P.add("gpsimd", lambda g: g.dma_start(out=bv_r[:], in_=bv_d), inc=lg, inc_n=16, is_dma=True)
    P.add("gpsimd", lambda g: g.dma_start(out=maskc[:], in_=mask_d), inc=lg, inc_n=16, is_dma=True)
    P.add("gpsimd", lambda g: g.dma_start(out=identb[:], in_=id_d), inc=lg, inc_n=16, is_dma=True)
    w_ready = lg.at()
    w_big = ls3.at()

    dep = lambda x: [x] if x is not None else []
    pool_free = [None, None]
    ptp_free = [None, None]
    qk_free = None
    vtok_free = None
    obf_free = None
    xw_free = [None, None]
    sm_free = None
    p_free = None
    pn_free = [None, None]
    pt_free = None
    tsb_free = [None, None]
    rsb_free = None
    out_free = None

    x_ready = []
    P.add("sync", lambda s: s.dma_start(
        out=xw[0][:].rearrange("p (c t) -> p c t", c=NCH),
        in_=x_d[:, :, 0:WP].rearrange("c p t -> p c t")),
        waits=[xl[0]], inc=xl[0], inc_n=16, is_dma=True)
    x_ready.append(xl[0].at())

    npool = 0

    def next_pool():
        nonlocal npool
        i = npool % 2
        npool += 1
        return i

    NWP = NW // 2
    for wp in range(NWP):
        t0 = wp * WP
        wpar = wp % 2
        if wp + 1 < NWP:
            pp = (wp + 1) % 2
            P.add("sync", lambda s, wp=wp, pp=pp: s.dma_start(
                out=xw[pp][:].rearrange("p (c t) -> p c t", c=NCH),
                in_=x_d[:, :, (wp + 1) * WP:(wp + 2) * WP].rearrange("c p t -> p c t")),
                waits=dep(xw_free[pp]) + [xl[pp]], inc=xl[pp], inc_n=16, is_dma=True)
            x_ready.append(xl[pp].at())

        for which, wt, dst, boff in (("q", wq, q_bf, 0), ("k", wk, k_bf, 8)):
            for m in range(NCH):
                pi = next_pool()
                for c in range(NCH):
                    P.add("tensor", lambda t, c=c, m=m, pi=pi, wt=wt, wpar=wpar: t.matmul(
                        out=pool[pi][:],
                        lhsT=wt[:, c * 1024 + m * 128: c * 1024 + (m + 1) * 128],
                        rhs=xw[wpar][:, c * WP:(c + 1) * WP],
                        start=(c == 0), stop=(c == NCH - 1)),
                        waits=[w_ready, w_big, x_ready[wp]] + dep(pool_free[pi]) + dep(qk_free if (which == "q" and m == 0) else None),
                        inc=pe if c == NCH - 1 else None)
                pe_m = pe.at()
                P.add("scalar", lambda s, m=m, pi=pi, dst=dst, boff=boff: s.activation(
                    out=dst[:, m * WP:(m + 1) * WP], in_=pool[pi][:],
                    func=AF.Identity, bias=bqk[:, boff + m:boff + m + 1]),
                    waits=[pe_m])
                pool_free[pi] = ss.at()
        qk_ready = ss.at()

        for tt in range(4):
            for half in range(2):
                pi = next_pool()
                for c in range(NCH):
                    P.add("tensor", lambda t, c=c, tt=tt, half=half, pi=pi, wpar=wpar: t.matmul(
                        out=pool[pi][:],
                        lhsT=xw[wpar][:, c * WP + tt * 128: c * WP + tt * 128 + 128],
                        rhs=wv[:, c * 1024 + half * 512: c * 1024 + (half + 1) * 512],
                        start=(c == 0), stop=False),
                        waits=[w_ready, w_big, x_ready[wp]] + dep(pool_free[pi]) + dep(vtok_free if (tt == 0 and half == 0) else None),
                        inc=None)
                P.add("tensor", lambda t, tt=tt, half=half, pi=pi: t.matmul(
                    out=pool[pi][:],
                    lhsT=onesr_b[:],
                    rhs=bv_r[0:1, half * 512:(half + 1) * 512],
                    start=False, stop=True),
                    waits=[consts_ready], inc=pe)
                pe_v = pe.at()
                P.add("vector", lambda v, tt=tt, half=half, pi=pi: v.tensor_copy(
                    out=vtok[:, tt * 1024 + half * 512: tt * 1024 + (half + 1) * 512],
                    in_=pool[pi][:]),
                    waits=[pe_v])
                pool_free[pi] = sv.at()
        vtok_ready = sv.at()

        for sub in range(2):
            so = sub * W
            for qt in range(2):
                pi = next_pool()
                for c in range(NCH):
                    P.add("tensor", lambda t, c=c, qt=qt, pi=pi, so=so: t.matmul(
                        out=pool[pi][:, 0:W],
                        lhsT=q_bf[:, c * WP + so + qt * 128: c * WP + so + qt * 128 + 128],
                        rhs=k_bf[:, c * WP + so: c * WP + so + W],
                        start=(c == 0), stop=(c == NCH - 1)),
                        waits=[qk_ready] + dep(pool_free[pi]),
                        inc=pe if c == NCH - 1 else None)
                pe_sim = pe.at()
                P.add("vector", lambda v, qt=qt, pi=pi: v.tensor_tensor(
                    out=sm[:], in0=pool[pi][:, 0:W], in1=maskc[:, qt * W:(qt + 1) * W],
                    op=ALU.add),
                    waits=[pe_sim, w_ready] + dep(sm_free))
                pool_free[pi] = sv.at()
                P.add("vector", lambda v: v.reduce_max(
                    out=nmv[:, 0:1], in_=sm[:], axis=AX.X, negate=True))
                P.add("vector", lambda v: v.tensor_single_scalar(
                    out=nmv[:, 1:2], in_=nmv[:, 0:1], scalar=0.125, op=ALU.mult))
                nm_ready = sv.at()
                P.add("scalar", lambda s: s.activation(
                    out=p_bf[:], in_=sm[:], func=AF.Exp, scale=c0125[:],
                    bias=nmv[:, 1:2], accum_out=sums[:, 0:1]),
                    waits=[nm_ready, consts_ready] + dep(p_free))
                sm_free = ss.at()
                exp_ready = ss.at()
                P.add("vector", lambda v: v.reciprocal(out=sums[:, 1:2], in_=sums[:, 0:1]),
                      waits=[exp_ready])
                P.add("vector", lambda v, qt=qt: v.tensor_single_scalar(
                    out=pn_bf[:], in_=p_bf[:], scalar=sums[:, 1:2], op=ALU.mult),
                    waits=dep(pn_free[0]) + dep(pn_free[1]))
                p_free = sv.at()
                pn_ready = sv.at()
                for tt in range(2):
                    P.add("tensor", lambda t, tt=tt: t.transpose(
                        out=ptp[tt][:], in_=pn_bf[:, tt * 128:(tt + 1) * 128], identity=identb[:]),
                        waits=[pn_ready, w_ready] + dep(ptp_free[tt]), inc=pe)
                    pe_t = pe.at()
                    P.add("scalar", lambda s, tt=tt: s.copy(
                        out=pt_bf[:, tt * 128:(tt + 1) * 128], in_=ptp[tt][:]),
                        waits=[pe_t] + dep(pt_free if tt == 0 and qt == 0 and sub == 0 else None))
                    ptp_free[tt] = ss.at()
                pn_free[qt] = pe.at()
                pt_ready = ss.at()
                for m in range(NCH):
                    pi = next_pool()
                    for tt in range(2):
                        P.add("tensor", lambda t, m=m, tt=tt, pi=pi, sub=sub: t.matmul(
                            out=pool[pi][:, 0:128],
                            lhsT=vtok[:, (sub * 2 + tt) * 1024 + m * 128: (sub * 2 + tt) * 1024 + (m + 1) * 128],
                            rhs=pt_bf[:, tt * 128:(tt + 1) * 128],
                            start=(tt == 0), stop=(tt == 1)),
                            waits=[pt_ready, vtok_ready] + dep(pool_free[pi]) + dep(obf_free if (sub == 0 and qt == 0 and m == 0) else None),
                            inc=pe if tt == 1 else None)
                    pe_o = pe.at()
                    P.add("scalar", lambda s, m=m, qt=qt, pi=pi, so=so: s.copy(
                        out=obf[:, m * WP + so + qt * 128: m * WP + so + qt * 128 + 128],
                        in_=pool[pi][:, 0:128]),
                        waits=[pe_o])
                    pool_free[pi] = ss.at()
        pt_free = pe.at()
        o_ready = ss.at()

        for e3 in range(NCH):
            pi = next_pool()
            tpar = e3 % 2
            for c in range(NCH):
                P.add("tensor", lambda t, c=c, e3=e3, pi=pi: t.matmul(
                    out=pool[pi][:],
                    lhsT=wo[:, c * 1024 + e3 * 128: c * 1024 + (e3 + 1) * 128],
                    rhs=obf[:, c * WP:(c + 1) * WP],
                    start=(c == 0), stop=(c == NCH - 1)),
                    waits=[o_ready] + dep(pool_free[pi]),
                    inc=pe if c == NCH - 1 else None)
            pe_wo = pe.at()
            P.add("scalar", lambda s, e3=e3, pi=pi, tpar=tpar: s.activation(
                out=tsb[tpar][:], in_=pool[pi][:],
                func=AF.Identity, bias=bo_c[:, e3:e3 + 1]),
                waits=[pe_wo] + dep(tsb_free[tpar]))
            pool_free[pi] = ss.at()
            ev = ss.at()
            P.add("vector", lambda v, e3=e3, tpar=tpar, wpar=wpar: v.scalar_tensor_tensor(
                out=rsb[:, e3 * WP:(e3 + 1) * WP],
                in0=xw[wpar][:, e3 * WP:(e3 + 1) * WP], scalar=2.0, in1=tsb[tpar][:],
                op0=ALU.mult, op1=ALU.add),
                waits=[ev] + dep(rsb_free if e3 == 0 else None))
            tsb_free[tpar] = sv.at()
        r_ready = sv.at()
        xw_free[wpar] = r_ready
        obf_free = pe.at()

        out_ready = emit_ln(P, pe, consts, lnp, rsb[:], xo_s[:], r_ready,
                            out_free=(st, st.v) if st.v > 0 else None)
        rsb_free = out_ready
        qk_free = pe.at()
        vtok_free = pe.at()
        P.add("gpsimd", lambda g, t0=t0: g.dma_start(
            out=xo_d[:, :, t0:t0 + WP].rearrange("c p t -> p c t"),
            in_=xo_s[:].rearrange("p (c t) -> p c t", c=NCH)),
            waits=[out_ready], inc=st, inc_n=16, is_dma=True)

    es.close()



def emit_stage1(nc, P, D1):
    (x_d, wq_d, wk_d, wv_d, wg_d, wab_d, wo_d, bq_d, bk_d, bg_d, bv_d, bab_d,
     bo_d, cum_d, id_d, madd_d, mstr_d, xo_d) = D1
    es = ExitStack()
    sb = lambda name, shp, dt=F32: es.enter_context(nc.sbuf_tensor('z1_' + name, shp, dt))
    psum = lambda name, shp, dt=F32: es.enter_context(nc.psum_tensor('z1_' + name, shp, dt))

    xb = sb("xb", [128, NCH * T], BF16)
    wq = sb("wq_s", [128, NCH * 128], BF16)
    wk = sb("wk_s", [128, NCH * 128], BF16)
    wv = sb("wv_s", [128, NCH * 128], BF16)
    wg = sb("wg_s", [128, NCH * 128], BF16)
    wab = sb("wab_s", [128, NCH * 2], BF16)
    wo = sb("wo_s", [128, 1024], BF16)
    bqc = sb("bqc", [128, 1]); bkc = sb("bkc", [128, 1]); bgc = sb("bgc", [128, 1])
    bvr = sb("bvr", [1, 128], BF16)
    bab = sb("bab_s", [1, 2])
    boc = sb("boc", [128, NCH])
    cumUT = sb("cum_s", [128, 128])
    identf = sb("idf_s", [128, 128])
    identb = sb("idb_s", [128, 128], BF16)
    maskadd = sb("madd_s", [128, 128])
    maskstr = sb("mstr_s", [128, 128], BF16)
    ones_c = sb("ones_c", [128, 1], BF16)
    onesr_f = sb("onesr_f", [1, 128])
    onesr_b = sb("onesr_b", [1, 128], BF16)
    epsln = sb("epsln", [1, 1])
    epsn2 = sb("epsn2", [1, 1])

    qT = sb("qT", [128, T], BF16)
    kT = sb("kT", [128, T], BF16)
    kr2 = sb("kr2", [128, T], BF16)
    gT = sb("gT", [128, T], BF16)
    vtok = sb("vtok", [128, T], BF16)
    arow = sb("arow", [1, T])
    brow = sb("brow", [1, T])
    lnar = sb("lnar", [1, T])
    rnr = sb("rnr", [1, T])

    Zf = sb("Zf", [128, 128]); ZTf = sb("ZTf", [128, 128])
    Zb = sb("Zb", [128, 128], BF16); ZTb = sb("ZTb", [128, 128], BF16)
    lnac = sb("lnac", [128, 2])
    lgc = sb("lgc", [128, 1])
    lgr = sb("lgr", [1, 128])
    nlgr = sb("nlgr", [1, 128])
    grow = sb("grow", [1, 128])
    gLc = sb("gLc", [128, 1])
    gcol = sb("gcol", [128, 1])
    dcol = sb("dcol", [128, 1])
    gdif = sb("gdif", [128, 128])
    gamT = sb("gamT", [128, 128], BF16)
    Dm = sb("Dm", [128, 128], BF16)
    S0b = sb("S0b", [128, 128], BF16)
    S0Tb = sb("S0Tb", [128, 128], BF16)
    S1b = sb("S1b", [128, 128], BF16)
    S1Tb = sb("S1Tb", [128, 128], BF16)
    Ub = sb("Ub", [128, 128], BF16)
    UTb = sb("UTb", [128, 128], BF16)
    Udb = sb("Udb", [128, 128], BF16)
    Rb = sb("Rb", [128, 128], BF16)
    ktk = sb("ktk", [128, 128], BF16)
    qgb = sb("qgb", [128, 128], BF16)
    ptl = sb("ptl", [128, 128], BF16)
    obf = sb("obf", [128, 128], BF16)
    rsb1 = sb("rsb1", [128, NCH * 128])
    rbf1 = sb("rbf1", [128, NCH * 128], BF16)
    r2b1 = sb("r2b1", [128, NCH * 128], BF16)
    xo1 = sb("xo1", [128, NCH * 128], BF16)
    rows1 = sb("rows1", [1, 6 * 128])

    pool = [psum(f"pool{i}", [128, 512]) for i in range(8)]

    lg = P.nsem("lg")
    st = P.nsem("st")
    pe = P.nsem("pe")
    sv = P.esem["vector"]; ss = P.esem["scalar"]; sg = P.esem["gpsimd"]
    dep = lambda x: [x] if x is not None else []
    pool_free = [None] * 8
    _np = [0]

    def next_pool():
        i = _np[0] % 8
        _np[0] += 1
        return i

    def lp(fn):
        def wrapped(engine):
            with nc.allow_low_precision(reason="bf16 operands for matmul chains"):
                return fn(engine)
        return wrapped

    P.add("gpsimd", lambda g: g.memset(ones_c[:], 1.0))
    P.add("gpsimd", lambda g: g.memset(onesr_f[:], 1.0))
    P.add("gpsimd", lambda g: g.memset(onesr_b[:], 1.0))
    P.add("gpsimd", lambda g: g.memset(epsln[:], 1e-5))
    P.add("gpsimd", lambda g: g.memset(epsn2[:], 1e-24))
    P.add("gpsimd", lambda g: g.memset(Zf[:], 0.0))
    P.add("gpsimd", lambda g: g.memset(ZTf[:], 0.0))
    P.add("gpsimd", lambda g: g.memset(Zb[:], 0.0))
    P.add("gpsimd", lambda g: g.memset(ZTb[:], 0.0))
    consts_mem = sg.at()
    for dst, src in ((wq, wq_d), (wk, wk_d), (wv, wv_d), (wg, wg_d), (wab, wab_d)):
        P.add("gpsimd", lambda g, dst=dst, src=src: g.dma_start(
            out=dst[:].rearrange("p (c k) -> p c k", c=NCH), in_=src.rearrange("c p k -> p c k")),
            inc=lg, inc_n=16, is_dma=True)
    P.add("gpsimd", lambda g: g.dma_start(out=wo[:], in_=wo_d), inc=lg, inc_n=16, is_dma=True)
    for dst, src in ((bqc, bq_d), (bkc, bk_d), (bgc, bg_d), (bvr, bv_d), (bab, bab_d),
                     (boc, bo_d), (cumUT, cum_d), (identf, id_d), (identb, id_d),
                     (maskadd, madd_d), (maskstr, mstr_d)):
        P.add("gpsimd", lambda g, dst=dst, src=src: g.dma_start(out=dst[:], in_=src),
              inc=lg, inc_n=16, is_dma=True)
    for c in range(NCH):
        P.add("gpsimd", lambda g, c=c: g.dma_start(
            out=xb[:, c * T:(c + 1) * T], in_=x_d[c]), inc=lg, inc_n=16, is_dma=True)
    w_ready = lg.at()

    TBP = 512
    qT_ready = None
    for tb in range(T // TBP):
        t0 = tb * TBP
        for which, wt, bcol, func, dsts in (
                ("q", wq, bqc, AF.Identity, (qT,)),
                ("k", wk, bkc, None, (kT, kr2)),
                ("g", wg, bgc, AF.Sigmoid, (gT,))):
            pi = next_pool()
            for c in range(NCH):
                P.add("tensor", lambda t, c=c, pi=pi, wt=wt, t0=t0: t.matmul(
                    out=pool[pi][:],
                    lhsT=wt[:, c * 128:(c + 1) * 128],
                    rhs=xb[:, c * T + t0: c * T + t0 + TBP],
                    start=(c == 0), stop=(c == NCH - 1)),
                    waits=[w_ready] + dep(pool_free[pi]),
                    inc=pe if c == NCH - 1 else None)
            pe_m = pe.at()
            if which == "k":
                P.add("scalar", lambda s, pi=pi, t0=t0: s.activation(
                    out=kT[:, t0:t0 + TBP], in_=pool[pi][:], func=AF.Identity, bias=bkc[:]),
                    waits=[pe_m])
                P.add("scalar", lambda s, pi=pi, t0=t0: s.activation(
                    out=kr2[:, t0:t0 + TBP], in_=pool[pi][:], func=AF.Square, bias=bkc[:]))
            else:
                P.add("scalar", lambda s, pi=pi, t0=t0, dst=dsts[0], func=func, bcol=bcol: s.activation(
                    out=dst[:, t0:t0 + TBP], in_=pool[pi][:], func=func, bias=bcol[:]),
                    waits=[pe_m])
            pool_free[pi] = ss.at()
        for tt in range(4):
            pi = next_pool()
            for c in range(NCH):
                P.add("tensor", lambda t, c=c, pi=pi, t0=t0, tt=tt: t.matmul(
                    out=pool[pi][:, 0:128],
                    lhsT=xb[:, c * T + t0 + tt * 128: c * T + t0 + tt * 128 + 128],
                    rhs=wv[:, c * 128:(c + 1) * 128],
                    start=(c == 0), stop=False),
                    waits=[w_ready] + dep(pool_free[pi]),
                    inc=None)
            P.add("tensor", lambda t, pi=pi: t.matmul(
                out=pool[pi][:, 0:128], lhsT=onesr_b[0:1, :],
                rhs=bvr[0:1, :], start=False, stop=True),
                waits=[consts_mem], inc=pe)
            pe_v = pe.at()
            P.add("scalar", lambda s, pi=pi, t0=t0, tt=tt: s.copy(
                out=vtok[:, t0 + tt * 128: t0 + tt * 128 + 128], in_=pool[pi][:, 0:128]),
                waits=[pe_v])
            pool_free[pi] = ss.at()
        for ab in range(2):
            pi = next_pool()
            for c in range(NCH):
                P.add("tensor", lambda t, c=c, pi=pi, ab=ab, t0=t0: t.matmul(
                    out=pool[pi][0:1, :],
                    lhsT=wab[:, c * 2 + ab: c * 2 + ab + 1],
                    rhs=xb[:, c * T + t0: c * T + t0 + TBP],
                    start=(c == 0), stop=(c == NCH - 1)),
                    waits=[w_ready] + dep(pool_free[pi]),
                    inc=pe if c == NCH - 1 else None)
            pe_ab = pe.at()
            P.add("scalar", lambda s, pi=pi, ab=ab, t0=t0: s.activation(
                out=(arow if ab == 0 else brow)[0:1, t0:t0 + TBP], in_=pool[pi][0:1, :],
                func=AF.Sigmoid, bias=bab[0:1, ab:ab + 1]),
                waits=[pe_ab])
            pool_free[pi] = ss.at()
    p1_done = ss.at()

    for tb in range(T // TBP):
        t0 = tb * TBP
        pi = next_pool()
        for blk in range(1):
            P.add("tensor", lambda t, pi=pi, t0=t0: t.matmul(
                out=pool[pi][0:1, :], lhsT=ones_c[:], rhs=kr2[:, t0:t0 + TBP],
                start=True, stop=True),
                waits=[p1_done, consts_mem] + dep(pool_free[pi]), inc=pe)
        pe_n2 = pe.at()
        P.add("vector", lambda v, pi=pi, t0=t0: v.tensor_single_scalar(
            out=rnr[0:1, t0:t0 + TBP], in_=pool[pi][0:1, :], scalar=1e-24, op=ALU.max),
            waits=[pe_n2])
        pool_free[pi] = sv.at()
        P.add("scalar", lambda s, t0=t0: s.activation(
            out=rnr[0:1, t0:t0 + TBP], in_=rnr[0:1, t0:t0 + TBP], func=AF.Ln),
            waits=[sv])
        P.add("scalar", lambda s, t0=t0: s.activation(
            out=rnr[0:1, t0:t0 + TBP], in_=rnr[0:1, t0:t0 + TBP], func=AF.Exp, scale=-0.5))
        rn_ready = ss.at()
        pi2 = next_pool()
        P.add("tensor", lambda t, pi2=pi2, t0=t0: t.matmul(
            out=pool[pi2][:], lhsT=onesr_f[:], rhs=rnr[0:1, t0:t0 + TBP],
            start=True, stop=True),
            waits=[rn_ready] + dep(pool_free[pi2]), inc=pe)
        pe_bc = pe.at()
        P.add("vector", lambda v, pi2=pi2, t0=t0: v.tensor_tensor(
            out=kT[:, t0:t0 + TBP], in0=kT[:, t0:t0 + TBP], in1=pool[pi2][:], op=ALU.mult),
            waits=[pe_bc])
        pool_free[pi2] = sv.at()
    P.add("scalar", lambda s: s.activation(out=lnar[:], in_=arow[:], func=AF.Ln),
          waits=[p1_done])
    p2_done = (ss.at(), sv.at())

    state_ready = consts_mem
    rsb_free = None
    xo_free = None
    rows_free = None
    kn_ready = p2_done[1]
    for ch in range(32):
        t0 = ch * 128
        sl = lambda tsr, t0=t0: tsr[:, t0:t0 + 128]
        pi = next_pool()
        P.add("tensor", lambda t, pi=pi, t0=t0: t.matmul(
            out=pool[pi][:, 0:1], lhsT=lnar[0:1, t0:t0 + 128], rhs=onesr_f[0:1, 0:1],
            start=True, stop=True),
            waits=[p2_done[0], consts_mem] + dep(pool_free[pi]), inc=pe)
        P.add("tensor", lambda t, pi=pi, t0=t0: t.matmul(
            out=pool[pi][:, 1:2], lhsT=brow[0:1, t0:t0 + 128], rhs=onesr_f[0:1, 0:1],
            start=True, stop=True), inc=pe)
        pe_c = pe.at()
        P.add("vector", lambda v, pi=pi: v.tensor_copy(out=lnac[:], in_=pool[pi][:, 0:2]),
              waits=[pe_c])
        pool_free[pi] = sv.at()
        pi = next_pool()
        P.add("tensor", lambda t, pi=pi: t.matmul(
            out=pool[pi][:, 0:1], lhsT=cumUT[:], rhs=lnac[:, 0:1], start=True, stop=True),
            waits=[(sv, sv.v)] + dep(pool_free[pi]), inc=pe)
        pe_lg = pe.at()
        P.add("vector", lambda v, pi=pi: v.tensor_copy(out=lgc[:], in_=pool[pi][:, 0:1]),
              waits=[pe_lg])
        pool_free[pi] = sv.at()
        pi = next_pool()
        P.add("tensor", lambda t, pi=pi: t.matmul(
            out=pool[pi][0:1, :, ] if False else pool[pi][0:1, 0:128],
            lhsT=lgc[:], rhs=identf[:], start=True, stop=True),
            waits=[(sv, sv.v)] + dep(pool_free[pi]), inc=pe)
        pe_lr = pe.at()
        P.add("vector", lambda v, pi=pi: v.tensor_copy(out=lgr[:], in_=pool[pi][0:1, 0:128]),
              waits=[pe_lr])
        pool_free[pi] = sv.at()
        P.add("scalar", lambda s: s.mul(out=nlgr[:], in_=lgr[:], mul=-1.0), waits=[sv])
        P.add("scalar", lambda s: s.activation(out=grow[:], in_=lgr[:], func=AF.Exp))
        P.add("scalar", lambda s: s.activation(out=gcol[:], in_=lgc[:], func=AF.Exp))
        row_ready = ss.at()
        pi = next_pool()
        P.add("tensor", lambda t, pi=pi: t.matmul(
            out=pool[pi][:, 0:1], lhsT=onesr_f[:], rhs=grow[0:1, 127:128],
            start=True, stop=True),
            waits=[row_ready] + dep(pool_free[pi]), inc=pe)
        pe_gL = pe.at()
        P.add("vector", lambda v, pi=pi: v.tensor_copy(out=gLc[:], in_=pool[pi][:, 0:1]),
              waits=[pe_gL])
        pool_free[pi] = sv.at()
        pi = next_pool()
        P.add("tensor", lambda t, pi=pi: t.matmul(
            out=pool[pi][:, 0:128], lhsT=nlgr[0:1, :], rhs=onesr_f[0:1, :],
            start=True, stop=False),
            waits=[row_ready] + dep(pool_free[pi]), inc=None)
        P.add("tensor", lambda t, pi=pi: t.matmul(
            out=pool[pi][:, 0:128], lhsT=onesr_f[0:1, :], rhs=lgr[0:1, :],
            start=False, stop=True), inc=pe)
        pe_gd = pe.at()
        P.add("vector", lambda v, pi=pi: v.tensor_tensor(
            out=gdif[:], in0=pool[pi][:, 0:128], in1=maskadd[:], op=ALU.add),
            waits=[pe_gd, consts_mem])
        pool_free[pi] = sv.at()
        P.add("scalar", lambda s: s.activation(out=gamT[:], in_=gdif[:], func=AF.Exp),
              waits=[sv])
        gam_ready = ss.at()
        P.add("vector", lambda v: v.tensor_tensor(
            out=dcol[:], in0=gamT[:, 127:128], in1=lnac[:, 1:2], op=ALU.mult),
            waits=[gam_ready])
        pi = next_pool()
        P.add("tensor", lambda t, pi=pi, t0=t0: t.matmul(
            out=pool[pi][:, 0:128], lhsT=kT[:, t0:t0 + 128], rhs=kT[:, t0:t0 + 128],
            start=True, stop=True),
            waits=[kn_ready] + dep(pool_free[pi]), inc=pe)
        pe_D = pe.at()
        P.add("vector", lambda v, pi=pi: v.tensor_tensor(
            out=Dm[:], in0=pool[pi][:, 0:128], in1=maskstr[:], op=ALU.mult),
            waits=[pe_D, consts_mem])
        pool_free[pi] = sv.at()
        P.add("vector", lambda v: v.tensor_single_scalar(
            out=S0Tb[:], in_=gamT[:], scalar=lnac[:, 1:2], op=ALU.mult),
            waits=[gam_ready])
        P.add("vector", lambda v: v.tensor_tensor(
            out=S0Tb[:], in0=S0Tb[:], in1=Dm[:], op=ALU.mult))
        P.add("vector", lp(lambda v: v.tensor_single_scalar(
            out=S0Tb[:], in_=S0Tb[:], scalar=-1.0, op=ALU.mult)))
        s0t_ready = sv.at()
        pi = next_pool()
        P.add("tensor", lambda t, pi=pi: t.transpose(
            out=pool[pi][:, 0:128].bitcast(BF16)[:, 0:128], in_=S0Tb[:], identity=identb[:]),
            waits=[s0t_ready, consts_mem] + dep(pool_free[pi]), inc=pe)
        pe_t0 = pe.at()
        P.add("vector", lambda v, pi=pi: v.tensor_copy(
            out=S0b[:], in_=pool[pi][:, 0:128].bitcast(BF16)[:, 0:128]),
            waits=[pe_t0])
        pool_free[pi] = sv.at()
        pi = next_pool()
        P.add("tensor", lambda t, pi=pi, t0=t0: t.transpose(
            out=pool[pi][:, 0:128].bitcast(BF16)[:, 0:128], in_=kT[:, t0:t0 + 128],
            identity=identb[:]),
            waits=[kn_ready, consts_mem] + dep(pool_free[pi]), inc=pe)
        pe_kt = pe.at()
        P.add("vector", lambda v, pi=pi: v.tensor_copy(
            out=ktk[:], in_=pool[pi][:, 0:128].bitcast(BF16)[:, 0:128]),
            waits=[pe_kt])
        pool_free[pi] = sv.at()
        ktk_ready = sv.at()
        pi = next_pool()
        P.add("tensor", lambda t, pi=pi, t0=t0: t.matmul(
            out=pool[pi][:, 0:128], lhsT=kT[:, t0:t0 + 128], rhs=Zb[:], start=True, stop=True),
            waits=[kn_ready, state_ready] + dep(pool_free[pi]), inc=pe)
        pe_R = pe.at()
        P.add("vector", lambda v, pi=pi, t0=t0: v.scalar_tensor_tensor(
            out=Rb[:], in0=pool[pi][:, 0:128], scalar=gcol[:], op0=ALU.mult,
            in1=vtok[:, t0:t0 + 128], op1=ALU.subtract),
            waits=[pe_R, row_ready])
        pool_free[pi] = sv.at()
        P.add("vector", lp(lambda v: v.tensor_single_scalar(
            out=Rb[:], in_=Rb[:], scalar=-1.0, op=ALU.mult)))
        ub, utmp = Rb, Ub
        sa, sat = S0b, S0Tb
        sb_, sbt = S1b, S1Tb
        for it in range(7):
            pi = next_pool()
            P.add("tensor", lambda t, pi=pi, sat=sat, ub=ub: t.matmul(
                out=pool[pi][:, 0:128], lhsT=sat[:], rhs=ub[:], start=True, stop=True),
                waits=[(sv, sv.v)] + dep(pool_free[pi]), inc=pe)
            pe_u = pe.at()
            P.add("vector", lambda v, pi=pi, ub=ub, utmp=utmp: v.tensor_tensor(
                out=utmp[:], in0=ub[:], in1=pool[pi][:, 0:128], op=ALU.add),
                waits=[pe_u])
            pool_free[pi] = sv.at()
            ub, utmp = utmp, ub
            if it < 6:
                pi1 = next_pool()
                P.add("tensor", lambda t, pi1=pi1, sa=sa, sat=sat: t.matmul(
                    out=pool[pi1][:, 0:128], lhsT=sat[:], rhs=sa[:], start=True, stop=True),
                    waits=dep(pool_free[pi1]), inc=pe)
                pi2 = next_pool()
                P.add("tensor", lambda t, pi2=pi2, sa=sa, sat=sat: t.matmul(
                    out=pool[pi2][:, 0:128], lhsT=sa[:], rhs=sat[:], start=True, stop=True),
                    waits=dep(pool_free[pi2]), inc=pe)
                pe_s = pe.at()
                P.add("vector", lambda v, pi1=pi1, sb_=sb_: v.tensor_copy(
                    out=sb_[:], in_=pool[pi1][:, 0:128]), waits=[pe_s])
                pool_free[pi1] = sv.at()
                P.add("vector", lambda v, pi2=pi2, sbt=sbt: v.tensor_copy(
                    out=sbt[:], in_=pool[pi2][:, 0:128]))
                pool_free[pi2] = sv.at()
                sa, sb_ = sb_, sa
                sat, sbt = sbt, sat
        Ufin = ub
        u_ready = sv.at()
        pi = next_pool()
        P.add("tensor", lambda t, pi=pi, Ufin=Ufin: t.transpose(
            out=pool[pi][:, 0:128].bitcast(BF16)[:, 0:128], in_=Ufin[:], identity=identb[:]),
            waits=[u_ready] + dep(pool_free[pi]), inc=pe)
        pe_ut = pe.at()
        P.add("vector", lambda v, pi=pi: v.tensor_copy(
            out=UTb[:], in_=pool[pi][:, 0:128].bitcast(BF16)[:, 0:128]),
            waits=[pe_ut])
        pool_free[pi] = sv.at()
        pi = next_pool()
        P.add("tensor", lambda t, pi=pi, t0=t0: t.matmul(
            out=pool[pi][:, 0:128], lhsT=UTb[:], rhs=qT[:, t0:t0 + 128], start=True, stop=True),
            waits=[(sv, sv.v), p1_done] + dep(pool_free[pi]), inc=pe)
        pe_g1 = pe.at()
        P.add("vector", lambda v, pi=pi: v.scalar_tensor_tensor(
            out=ptl[:], in0=pool[pi][:, 0:128], scalar=lnac[:, 1:2], op0=ALU.mult,
            in1=gamT[:], op1=ALU.mult),
            waits=[pe_g1])
        pool_free[pi] = sv.at()
        pi = next_pool()
        P.add("tensor", lambda t, pi=pi: t.matmul(
            out=pool[pi][:, 0:128], lhsT=onesr_f[0:1, :], rhs=grow[0:1, :],
            start=True, stop=True),
            waits=[row_ready] + dep(pool_free[pi]), inc=pe)
        pe_gb = pe.at()
        P.add("vector", lambda v, pi=pi, t0=t0: v.tensor_tensor(
            out=qgb[:], in0=qT[:, t0:t0 + 128], in1=pool[pi][:, 0:128], op=ALU.mult),
            waits=[pe_gb])
        pool_free[pi] = sv.at()
        q_ready = sv.at()
        pi = next_pool()
        P.add("tensor", lambda t, pi=pi: t.matmul(
            out=pool[pi][:, 0:128], lhsT=ktk[:], rhs=ptl[:], start=True, stop=False),
            waits=[q_ready, ktk_ready] + dep(pool_free[pi]), inc=None)
        P.add("tensor", lambda t, pi=pi: t.matmul(
            out=pool[pi][:, 0:128], lhsT=ZTb[:], rhs=qgb[:], start=False, stop=True),
            waits=[state_ready], inc=pe)
        pe_o = pe.at()
        P.add("vector", lambda v, pi=pi, t0=t0: v.tensor_tensor(
            out=obf[:], in0=pool[pi][:, 0:128], in1=gT[:, t0:t0 + 128], op=ALU.mult),
            waits=[pe_o])
        pool_free[pi] = sv.at()
        o_ready = sv.at()
        P.add("vector", lambda v, Ufin=Ufin: v.tensor_single_scalar(
            out=Udb[:], in_=Ufin[:], scalar=dcol[:], op=ALU.mult),
            waits=[sv])
        ud_ready = sv.at()
        pi = next_pool()
        P.add("tensor", lambda t, pi=pi: t.matmul(
            out=pool[pi][:, 0:128], lhsT=ktk[:], rhs=Udb[:], start=True, stop=True),
            waits=[ud_ready] + dep(pool_free[pi]), inc=pe)
        pe_z = pe.at()
        P.add("vector", lambda v, pi=pi: v.scalar_tensor_tensor(
            out=Zf[:], in0=Zf[:], scalar=gLc[:], op0=ALU.mult,
            in1=pool[pi][:, 0:128], op1=ALU.add),
            waits=[pe_z])
        pool_free[pi] = sv.at()
        P.add("vector", lambda v: v.tensor_copy(out=Zb[:], in_=Zf[:]))
        pi = next_pool()
        P.add("tensor", lambda t, pi=pi: t.matmul(
            out=pool[pi][:, 0:128], lhsT=Udb[:], rhs=ktk[:], start=True, stop=True),
            waits=[ud_ready] + dep(pool_free[pi]), inc=pe)
        pe_zt = pe.at()
        P.add("vector", lambda v, pi=pi: v.scalar_tensor_tensor(
            out=ZTf[:], in0=ZTf[:], scalar=gLc[:], op0=ALU.mult,
            in1=pool[pi][:, 0:128], op1=ALU.add),
            waits=[pe_zt])
        pool_free[pi] = sv.at()
        P.add("vector", lambda v: v.tensor_copy(out=ZTb[:], in_=ZTf[:]))
        state_ready = sv.at()
        for e in range(NCH):
            pi = next_pool()
            P.add("tensor", lambda t, pi=pi, e=e: t.matmul(
                out=pool[pi][:, 0:128], lhsT=wo[:, e * 128:(e + 1) * 128], rhs=obf[:],
                start=True, stop=True),
                waits=[o_ready] + dep(pool_free[pi]), inc=pe)
            pe_wo = pe.at()
            P.add("vector", lambda v, pi=pi, e=e, t0=t0: v.scalar_tensor_tensor(
                out=rsb1[:, e * 128:(e + 1) * 128],
                in0=pool[pi][:, 0:128], scalar=boc[:, e:e + 1], op0=ALU.add,
                in1=xb[:, e * T + t0: e * T + t0 + 128], op1=ALU.add),
                waits=[pe_wo] + (dep(rsb_free) if e == 0 else []))
            pool_free[pi] = sv.at()
        r_ready = sv.at()
        P.add("vector", lambda v: v.tensor_copy(out=rbf1[:], in_=rsb1[:]),
              waits=[r_ready])
        rb_ready = sv.at()
        P.add("scalar", lambda s: s.activation(out=r2b1[:], in_=rbf1[:], func=AF.Square),
              waits=[rb_ready])
        r2_ready = ss.at()
        piA = next_pool()
        for c in range(NCH):
            P.add("tensor", lambda t, c=c, piA=piA: t.matmul(
                out=pool[piA][0:1, 0:128], lhsT=ones_c[:], rhs=rbf1[:, c * 128:(c + 1) * 128],
                start=(c == 0), stop=(c == NCH - 1)),
                waits=[rb_ready, consts_mem] + dep(pool_free[piA]),
                inc=pe if c == NCH - 1 else None)
        peA = pe.at()
        piB = next_pool()
        for c in range(NCH):
            P.add("tensor", lambda t, c=c, piB=piB: t.matmul(
                out=pool[piB][0:1, 0:128], lhsT=ones_c[:], rhs=r2b1[:, c * 128:(c + 1) * 128],
                start=(c == 0), stop=(c == NCH - 1)),
                waits=[r2_ready] + dep(pool_free[piB]),
                inc=pe if c == NCH - 1 else None)
        peB = pe.at()
        rsb_free = peB
        MU, EX2, VAR, RSTD, NMR, SCR = 0, 1, 2, 3, 4, 5
        rw = lambda i: rows1[0:1, i * 128:(i + 1) * 128]
        P.add("scalar", lambda s, piA=piA: s.mul(out=rw(MU), in_=pool[piA][0:1, 0:128], mul=1.0 / D),
              waits=[peA] + dep(rows_free))
        P.add("scalar", lambda s, piB=piB: s.mul(out=rw(EX2), in_=pool[piB][0:1, 0:128], mul=1.0 / D),
              waits=[peB])
        pool_free[piA] = ss.at()
        pool_free[piB] = ss.at()
        P.add("vector", lambda v: v.tensor_tensor(out=rw(VAR), in0=rw(MU), in1=rw(MU), op=ALU.mult),
              waits=[ss])
        P.add("vector", lambda v: v.tensor_tensor(out=rw(VAR), in0=rw(EX2), in1=rw(VAR), op=ALU.subtract))
        P.add("scalar", lambda s: s.activation(out=rw(SCR), in_=rw(VAR), func=AF.Sqrt, bias=epsln[:]),
              waits=[sv])
        P.add("vector", lambda v: v.reciprocal(out=rw(RSTD), in_=rw(SCR)), waits=[ss])
        P.add("vector", lambda v: v.tensor_tensor(out=rw(NMR), in0=rw(MU), in1=rw(RSTD), op=ALU.mult))
        P.add("scalar", lambda s: s.mul(out=rw(NMR), in_=rw(NMR), mul=-1.0), waits=[sv])
        rows_ready = ss.at()
        pi1 = next_pool()
        P.add("tensor", lambda t, pi1=pi1: t.matmul(
            out=pool[pi1][:, 0:128], lhsT=onesr_f[:], rhs=rw(RSTD), start=True, stop=True),
            waits=[rows_ready] + dep(pool_free[pi1]), inc=pe)
        pi2 = next_pool()
        P.add("tensor", lambda t, pi2=pi2: t.matmul(
            out=pool[pi2][:, 0:128], lhsT=onesr_f[:], rhs=rw(NMR), start=True, stop=True),
            waits=dep(pool_free[pi2]), inc=pe)
        pe_bc2 = pe.at()
        rows_free = pe_bc2
        P.add("vector", lambda v, pi1=pi1: v.tensor_tensor(
            out=xo1[:].rearrange("p (c t) -> p c t", c=NCH),
            in0=rsb1[:].rearrange("p (c t) -> p c t", c=NCH),
            in1=pool[pi1][:, 0:128].unsqueeze(1).broadcast_to((128, NCH, 128)),
            op=ALU.mult),
            waits=[pe_bc2] + dep(xo_free))
        pool_free[pi1] = sv.at()
        P.add("vector", lambda v, pi2=pi2: v.tensor_tensor(
            out=xo1[:].rearrange("p (c t) -> p c t", c=NCH),
            in0=xo1[:].rearrange("p (c t) -> p c t", c=NCH),
            in1=pool[pi2][:, 0:128].unsqueeze(1).broadcast_to((128, NCH, 128)),
            op=ALU.add))
        pool_free[pi2] = sv.at()
        out_ready = sv.at()
        P.add("gpsimd", lambda g, t0=t0: g.dma_start(
            out=xo_d[:, :, t0:t0 + 128].rearrange("c p t -> p c t"),
            in_=xo1[:].rearrange("p (c t) -> p c t", c=NCH)),
            waits=[out_ready, st], inc=st, inc_n=16, is_dma=True)
        xo_free = (st, st.v)

    es.close()


def stage1_inputs(xfm, inputs):
    g = lambda k: np.asarray(inputs[k], np.float32)
    pre = "L0_attn_"
    r128 = lambda w: np.ascontiguousarray(g(pre + w).reshape(NCH, 128, 128))
    wab = np.ascontiguousarray(
        np.concatenate([g(pre + "Wa"), g(pre + "Wb")], axis=1).reshape(NCH, 128, 2))
    i_ = np.arange(128)
    cumUT = (i_[:, None] <= i_[None, :]).astype(np.float32)
    ident = np.eye(128, dtype=np.float32)
    maskadd = np.where(i_[:, None] <= i_[None, :], 0.0, -1e30).astype(np.float32)
    maskstr = (i_[:, None] < i_[None, :]).astype(np.float32)
    return {"x": xfm, "wq": r128("Wq"), "wk": r128("Wk"), "wv": r128("Wv"),
            "wg": r128("Wg"), "wab": wab,
            "wo": np.ascontiguousarray(g(pre + "Wo")),
            "bq_c": g(pre + "bq").reshape(128, 1), "bk_c": g(pre + "bk").reshape(128, 1),
            "bg_c": g(pre + "bg").reshape(128, 1), "bv_r": g(pre + "bv").reshape(1, 128),
            "bab": np.concatenate([g(pre + "ba"), g(pre + "bb")]).reshape(1, 2),
            "bo_c": np.ascontiguousarray(g(pre + "bo").reshape(NCH, 128).T),
            "cumUT": cumUT, "identf": ident, "maskadd": maskadd, "maskstrict": maskstr}



HID = 2730
HIDP = 2816
NFC = HIDP // 128
PADT = T + 1
W4 = 512
NW4 = T // W4


def emit_stage4(nc, P, x_d, xo_d, B):
    dram = lambda name, shp, dt=F32, kind="ExternalInput": nc.dram_tensor(name, shp, dt, kind=kind).ap()
    wqkv_d = B["wqkvB"]
    wo4_d = B["wo4B"]
    w1a_d = B["w1aB"]
    w1g_d = B["w1gB"]
    w2p_d = B["w2pB"]
    pcast = B["pcast"]
    pc_gates = B["pc_gates"]
    b1a_d = dram("b1a_c", [2, 128, NFC])
    b1g_d = dram("b1g_c", [2, 128, NFC])
    b2_d = dram("b2_c", [2, 128, NCH])
    gb_d = dram("gb_r", [2, NCH, 2, 128])
    qs_d = dram("qs_c", [2, 128, 2])
    ks_d = dram("ks_c", [2, 128, 2])
    csq_d, snq_d, csk_d, snk_d = (B["tabB"][i] for i in range(4))
    maskb_d = dram("maskb", [128, 2 * 128])
    id_d = dram("ident", [128, 128])
    p32_d = dram("p32", [128, 128])
    sel8_d = dram("sel8", [128, 4 * 8])
    selq_d = dram("selq", [8, 4 * 128])
    selk_d = dram("selk", [8, 4 * 128])
    e8_d = dram("e8", [1, 64])
    onesr_d = dram("ones512", [1, 512])
    dbounce_d = dram("dbounce", [32, 128], kind="Internal")
    xpost_d = nc.dram_tensor("xpost", [NCH, 128, PADT], BF16, kind="Internal").ap()
    xnew_d = nc.dram_tensor("xnew", [NCH, 128, PADT], BF16, kind="Internal").ap()

    es = ExitStack()
    sbP = lambda name, shp, dt=F32: es.enter_context(nc.sbuf_tensor('z4_' + name, shp, dt))
    psum = lambda name, shp, dt=F32: es.enter_context(nc.psum_tensor('z4_' + name, shp, dt))

    maskb = sbP("maskb_s", [128, 256], BF16)
    identb = sbP("id_s", [128, 128], BF16)
    p32b = sbP("p32_s", [128, 128], BF16)
    sel8 = sbP("sel8_s", [128, 32], BF16)
    selq = sbP("selq_s", [8, 512], BF16)
    selk = sbP("selk_s", [8, 512], BF16)
    e8r = sbP("e8_s", [1, 64], BF16)
    ones_c = sbP("ones_c", [128, 1], BF16)
    identf = sbP("identf", [128, 128])
    onesr_f = sbP("onesr_f", [1, 128])
    epsb = sbP("epsb", [1, 1])
    epsln = sbP("epsln", [1, 1])
    neghf = sbP("neghf", [1, 1])
    qsc = sbP("qsc", [128, 2])
    ksc = sbP("ksc", [128, 2])
    b1ac = sbP("b1ac", [128, NFC])
    b1gc = sbP("b1gc", [128, NFC])
    b2c = sbP("b2c", [128, NCH])


    pool = [psum(f"pool{i}", [128, 512]) for i in range(8)]

    lg = P.nsem("lg")
    ls = P.nsem("ls")
    hl = [P.nsem("hl0"), P.nsem("hl1")]
    stp = P.nsem("stp")
    sto = P.nsem("sto")
    db = P.nsem("db")
    pe = P.nsem("pe")
    sv = P.esem["vector"]; ss = P.esem["scalar"]; sg = P.esem["gpsimd"]

    dep = lambda x: [x] if x is not None else []
    pool_free = [None] * 8
    _np = [0]

    def next_pool():
        i = _np[0] % 8
        _np[0] += 1
        return i

    def lp(fn):
        def wrapped(engine):
            with nc.allow_low_precision(reason="rn/rs feed bf16 matmul operands"):
                return fn(engine)
        return wrapped

    def gdma(dst_ap, src_ap, waits=()):
        P.add("gpsimd", lambda g, d=dst_ap, s=src_ap: g.dma_start(out=d, in_=s),
              waits=waits, inc=lg, inc_n=16, is_dma=True)
        return lg.at()

    def sdma(dst_ap, src_ap, waits=()):
        P.add("sync", lambda e, d=dst_ap, s=src_ap: e.dma_start(out=d, in_=s),
              waits=waits, inc=ls, inc_n=16, is_dma=True)
        return ls.at()

    P.add("gpsimd", lambda g: g.memset(ones_c[:], 1.0))
    P.add("gpsimd", lambda g: g.memset(onesr_f[:], 1.0))
    P.add("gpsimd", lambda g: g.memset(epsb[:], 1e-12))
    P.add("gpsimd", lambda g: g.memset(epsln[:], 1e-5))
    P.add("gpsimd", lambda g: g.memset(neghf[:], -0.5))
    consts_mem = sg.at()
    for dst, src in ((maskb, maskb_d), (identb, id_d), (p32b, p32_d), (sel8, sel8_d),
                     (selq, selq_d), (selk, selk_d), (e8r, e8_d)):
        gdma(dst[:], src)
    P.add("sync", lambda s: s.dma_start(out=identf[:], in_=id_d),
          waits=[ls], inc=ls, inc_n=16, is_dma=True)
    identf_ready = ls.at()
    consts_ready = lg.at()
    init_sync = ls.at()

    xsrc_prev_stores = None

    def layer_phase(l):
        nonlocal xsrc_prev_stores
        esA = ExitStack()
        sbA = lambda name, shp, dt=F32: esA.enter_context(nc.sbuf_tensor('z4_' + name, shp, dt))
        xwin = [sbA(f"xwin{l}_{i}", [128, NCH * 513], BF16) for i in range(2)]
        xatt = sbA(f"xatt{l}", [128, NCH * W4], BF16)
        wqkv = sbA(f"wqkv{l}", [128, NCH * 1536], BF16)
        wo4 = sbA(f"wo4_{l}", [128, 4 * 1024], BF16)
        csq = sbA(f"csq{l}", [128, T], BF16)
        snq = sbA(f"snq{l}", [128, T], BF16)
        csk = sbA(f"csk{l}", [128, T], BF16)
        snk = sbA(f"snk{l}", [128, T], BF16)
        qext = [sbA(f"qext{l}_{i}", [65, 8 * 512], BF16) for i in range(2)]
        kext = [sbA(f"kext{l}_{i}", [65, 8 * 512], BF16) for i in range(3)]
        vext = [sbA(f"vext{l}_{i}", [128, 4 * 520], BF16) for i in range(3)]
        ptbuf = [sbA(f"ptbuf{l}_{i}", [128, 8 * 512], BF16) for i in range(2)]
        qraw = {s: sbA(f"qraw{l}{s}", [128, 2048], BF16) for s in "qk"}
        q2b = {s: sbA(f"q2b{l}{s}", [128, 2048], BF16) for s in "qk"}
        sw_bf = q2b
        t2b = {s: sbA(f"t2b{l}{s}", [128, 2048], BF16) for s in "qk"}
        qtl = qraw
        n2s = sbA(f"n2s{l}", [8, 512])
        rn8 = sbA(f"rn8{l}", [8, 512], BF16)
        nmcols = sbA(f"nmcols{l}", [128, 32])
        nmrows = sbA(f"nmrows{l}", [32, 128])
        sumsall = sbA(f"sumsall{l}", [1, 8 * 512], BF16)
        rs8 = sbA(f"rs8{l}", [8, 512], BF16)
        bcsb = sbA(f"bcsb{l}", [128, 512], BF16)
        oall = sbA(f"oall{l}", [128, 4 * 512], BF16)
        tmp32 = sbA(f"tmp32{l}", [128, 4])

        ph_barrier = [pe, sv, ss, sg, stp, ls]
        barrier_w = [s.at() for s in ph_barrier]

        for i in range(3):
            P.add("gpsimd", lambda g, i=i, kext=kext: g.memset(kext[i][64:65, :], 1.0),
                  waits=barrier_w if i == 0 else [])
            P.add("gpsimd", lambda g, i=i, vext=vext: g.memset(vext[i][:], 0.0))
            P.add("gpsimd", lambda g, i=i, vext=vext: g.memset(
                vext[i][:].rearrange("p (tt h e) -> p tt h e", tt=4, h=8)[:, :, :, 64:65], 1.0))
        P.add("gpsimd", lambda g, ptbuf=ptbuf: g.memset(ptbuf[0][:], 0.0))
        P.add("gpsimd", lambda g, ptbuf=ptbuf: g.memset(ptbuf[1][:], 0.0))
        attn_init = sg.at()

        ls4 = P.nsem("ls")
        first = True
        for c in range(NCH):
            P.add("sync", lambda s, c=c, wqkv=wqkv: s.dma_start(
                out=wqkv[:, c * 1536:(c + 1) * 1536], in_=wqkv_d[l, c]),
                waits=barrier_w + [pc_gates["attn"][l]] + ([ls4] if first else []),
                inc=ls4, inc_n=16, is_dma=True)
            first = False
        for cc in range(4):
            P.add("sync", lambda s, cc=cc, wo4=wo4: s.dma_start(
                out=wo4[:, cc * 1024:(cc + 1) * 1024], in_=wo4_d[l, cc]),
                inc=ls4, inc_n=16, is_dma=True)
        for dst, srcB in ((csq, csq_d), (snq, snq_d), (csk, csk_d), (snk, snk_d)):
            P.add("sync", lambda s, dst=dst, srcB=srcB: s.dma_start(out=dst[:], in_=srcB),
                  inc=ls4, inc_n=16, is_dma=True)
        sync_w = ls4.at()
        sdma(qsc[:], qs_d[l], waits=barrier_w + [ls])
        sdma(ksc[:], ks_d[l], waits=barrier_w)
        tabs_loaded = ls.at()
        xsrc = x_d if l == 0 else xnew_d
        xoff = 0 if l == 0 else 1
        w_ready = sync_w

        P.add("gpsimd", lambda g, csq=csq: g.tensor_single_scalar(
            out=csq[:], in_=csq[:], scalar=qsc[:, 0:1], op=ALU.mult),
            waits=[w_ready, tabs_loaded])
        P.add("gpsimd", lambda g, snq=snq: g.tensor_single_scalar(
            out=snq[:], in_=snq[:], scalar=qsc[:, 1:2], op=ALU.mult))
        P.add("gpsimd", lambda g, csk=csk: g.tensor_single_scalar(
            out=csk[:], in_=csk[:], scalar=ksc[:, 0:1], op=ALU.mult))
        P.add("gpsimd", lambda g, snk=snk: g.tensor_single_scalar(
            out=snk[:], in_=snk[:], scalar=ksc[:, 1:2], op=ALU.mult))
        tabs_ready = sg.at()

        qraw_free = {"q": None, "k": None}
        q2b_free = {"q": None, "k": None}
        sw_free = {"q": None, "k": None}
        t2b_free = {"q": None, "k": None}
        qtl_free = {"q": None, "k": None}
        qext_free = [None, None]
        kext_free = [attn_init, attn_init, attn_init]
        vext_free = [attn_init, attn_init, attn_init]
        ptbuf_free = [attn_init, attn_init]
        oall_free = None
        nmcols_free = None
        sumsall_free = None
        dball = None
        prev_resid = None

        wl = [P.nsem("wl0"), P.nsem("wl1")]
        xwin_free = [None, None]
        x_ready = []
        P.add("gpsimd", lambda g: g.memset(
            xwin[0][:].rearrange("p (c t) -> p c t", c=NCH)[:, :, 0:1], 0.0),
            waits=barrier_w)
        w0z = sg.at()
        P.add("sync", lambda s, xsrc=xsrc, xoff=xoff: s.dma_start(
            out=xwin[0][:].rearrange("p (c t) -> p c t", c=NCH)[:, :, 1:513],
            in_=xsrc[:, :, xoff:xoff + 512].rearrange("c p t -> p c t")),
            waits=barrier_w + [wl[0]] + ([xsrc_prev_stores[0]] if l else []),
            inc=wl[0], inc_n=16, is_dma=True)
        x_ready.append((wl[0], wl[0].v))

        pendA = [None]
        attn_store_evt = []
        for w in range(NW4):
            t0 = w * W4
            cur = w % 2
            kcur = w % 3
            kprv = (w - 1) % 3

            def hs(c, a, n, cur=cur, xwin=xwin):
                o = 0 if c >= 4 else 1
                return xwin[cur][:, c * 513 + o + a: c * 513 + o + a + n]

            for side, base_m in (("q", 0), ("k", 4)):
                qraw_s, q2b_s, sw_s, t2b_s, qtl_s = (
                    qraw[side], q2b[side], sw_bf[side], t2b[side], qtl[side])
                for mm in range(4):
                    m = base_m + mm
                    pi = next_pool()
                    for c in range(NCH):
                        P.add("tensor", lambda t, c=c, m=m, pi=pi, hs=hs: t.matmul(
                            out=pool[pi][:],
                            lhsT=wqkv[:, c * 1536 + m * 128: c * 1536 + (m + 1) * 128],
                            rhs=hs(c, 0, W4),
                            start=(c == 0), stop=(c == NCH - 1)),
                            waits=[w_ready, x_ready[w]] + dep(pool_free[pi]) +
                                  dep(qraw_free[side] if mm == 0 else None) +
                                  ([w0z] if w == 0 else []),
                            inc=pe if c == NCH - 1 else None)
                    pe_m = pe.at()
                    P.add("scalar", lambda s, mm=mm, pi=pi, qraw_s=qraw_s: s.copy(
                        out=qraw_s[:, mm * 512:(mm + 1) * 512], in_=pool[pi][:]),
                        waits=[pe_m])
                    P.add("scalar", lambda s, mm=mm, pi=pi, q2b_s=q2b_s: s.activation(
                        out=q2b_s[:, mm * 512:(mm + 1) * 512], in_=pool[pi][:], func=AF.Square),
                        waits=dep(q2b_free[side] if mm == 0 else None))
                    pool_free[pi] = ss.at()
                raw_ready = ss.at()
                pi = next_pool()
                for ct in range(4):
                    P.add("tensor", lambda t, ct=ct, pi=pi, q2b_s=q2b_s: t.matmul(
                        out=pool[pi][0:8, :],
                        lhsT=sel8[:, ct * 8:(ct + 1) * 8],
                        rhs=q2b_s[:, ct * 512:(ct + 1) * 512],
                        start=(ct == 0), stop=(ct == 3)),
                        waits=[raw_ready, consts_ready] + dep(pool_free[pi]),
                        inc=pe if ct == 3 else None)
                pe_n2 = pe.at()
                q2b_free[side] = pe_n2
                P.add("scalar", lambda s, pi=pi: s.activation(
                    out=n2s[:], in_=pool[pi][0:8, :], func=AF.Sqrt),
                    waits=[pe_n2])
                pool_free[pi] = ss.at()
                P.add("vector", lambda v: v.tensor_single_scalar(
                    out=n2s[:], in_=n2s[:], scalar=1e-12, op=ALU.max), waits=[ss])
                P.add("vector", lp(lambda v: v.reciprocal(out=rn8[:], in_=n2s[:])))
                rn_ready = sv.at()
                cs_t, sn_t = (csq, snq) if side == "q" else (csk, snk)
                for ct in range(4):
                    pi = next_pool()
                    P.add("tensor", lambda t, ct=ct, pi=pi, qraw_s=qraw_s: t.matmul(
                        out=pool[pi][:], lhsT=p32b[:], rhs=qraw_s[:, ct * 512:(ct + 1) * 512],
                        start=True, stop=True),
                        waits=[raw_ready, consts_ready] + dep(pool_free[pi]), inc=pe)
                    pe_sw = pe.at()
                    P.add("scalar", lambda s, ct=ct, pi=pi, sw_s=sw_s: s.copy(
                        out=sw_s[:, ct * 512:(ct + 1) * 512], in_=pool[pi][:]),
                        waits=[pe_sw] + dep(sw_free[side] if ct == 0 else None))
                    pool_free[pi] = ss.at()
                sw_ready = ss.at()
                P.add("vector", lambda v, cs_t=cs_t, t0=t0, t2b_s=t2b_s, qraw_s=qraw_s: v.tensor_tensor(
                    out=t2b_s[:].rearrange("p (ct t) -> p ct t", ct=4),
                    in0=qraw_s[:].rearrange("p (ct t) -> p ct t", ct=4),
                    in1=cs_t[:, t0:t0 + W4].unsqueeze(1).broadcast_to((128, 4, W4)),
                    op=ALU.mult),
                    waits=[raw_ready, tabs_ready] + dep(t2b_free[side]))
                P.add("vector", lambda v, sn_t=sn_t, t0=t0, sw_s=sw_s: v.tensor_tensor(
                    out=sw_s[:].rearrange("p (ct t) -> p ct t", ct=4),
                    in0=sw_s[:].rearrange("p (ct t) -> p ct t", ct=4),
                    in1=sn_t[:, t0:t0 + W4].unsqueeze(1).broadcast_to((128, 4, W4)),
                    op=ALU.mult),
                    waits=[sw_ready])
                P.add("vector", lambda v, qtl_s=qtl_s, t2b_s=t2b_s, sw_s=sw_s: v.tensor_tensor(
                    out=qtl_s[:], in0=t2b_s[:], in1=sw_s[:], op=ALU.add),
                    waits=dep(qtl_free[side]))
                sw_free[side] = sv.at()
                t2b_free[side] = sv.at()
                q2b_free[side] = sv.at()
                rot_ready = sv.at()
                selx = selq if side == "q" else selk
                dst_ext = qext[cur] if side == "q" else kext[kcur]
                dfree = qext_free[cur] if side == "q" else kext_free[kcur]
                for ct in range(4):
                    pi = next_pool()
                    P.add("tensor", lambda t, ct=ct, pi=pi, selx=selx: t.matmul(
                        out=pool[pi][:], lhsT=selx[:, ct * 128:(ct + 1) * 128],
                        rhs=rn8[:], start=True, stop=True),
                        waits=[rn_ready, consts_ready] + dep(pool_free[pi]), inc=pe)
                    pe_bc = pe.at()
                    for half in range(2):
                        h = 2 * ct + half
                        P.add("vector", lambda v, ct=ct, pi=pi, half=half, h=h, dst_ext=dst_ext, qtl_s=qtl_s: v.tensor_tensor(
                            out=dst_ext[0:64, h * 512:(h + 1) * 512],
                            in0=qtl_s[half * 64:(half + 1) * 64, ct * 512:(ct + 1) * 512],
                            in1=pool[pi][half * 64:(half + 1) * 64, :],
                            op=ALU.mult),
                            waits=[pe_bc, rot_ready] + (dep(dfree) if (ct == 0 and half == 0) else []))
                    pool_free[pi] = sv.at()
                qtl_free[side] = sv.at()
                qraw_free[side] = sv.at()
                if side == "q":
                    qext_ready = sv.at()
                else:
                    kext_ready = sv.at()

            for tt in range(4):
                pi = next_pool()
                for c in range(NCH):
                    P.add("tensor", lambda t, c=c, tt=tt, pi=pi, hs=hs: t.matmul(
                        out=pool[pi][:],
                        lhsT=hs(c, tt * 128, 128),
                        rhs=wqkv[:, c * 1536 + 1024: c * 1536 + 1536],
                        start=(c == 0), stop=(c == NCH - 1)),
                        waits=[w_ready, x_ready[w]] + dep(pool_free[pi]) +
                              (dep(vext_free[kcur]) if tt == 0 else []) +
                              ([attn_init] if w < 3 else []) + ([w0z] if w == 0 else []),
                        inc=pe if c == NCH - 1 else None)
                pe_v = pe.at()
                P.add("scalar", lambda s, tt=tt, pi=pi, kcur=kcur, vext=vext: s.copy(
                    out=vext[kcur][:].rearrange("p (t4 h e) -> p t4 h e", t4=4, h=8)[:, tt, :, 0:64],
                    in_=pool[pi][:].rearrange("p (h e) -> p h e", h=8)),
                    waits=[pe_v])
                pool_free[pi] = ss.at()
            vext_ready = ss.at()

            if pendA[0] is not None:
                pendA[0]()

            if w + 1 < NW4:
                pp = (w + 1) % 2
                P.add("sync", lambda s, w=w, pp=pp, xsrc=xsrc, xoff=xoff: s.dma_start(
                    out=xwin[pp][:].rearrange("p (c t) -> p c t", c=NCH),
                    in_=xsrc[:, :, xoff + (w + 1) * W4 - 1: xoff + (w + 2) * W4].rearrange("c p t -> p c t")),
                    waits=dep(xwin_free[pp]) + [wl[pp]] +
                          ([xsrc_prev_stores[w + 1]] if l else []),
                    inc=wl[pp], inc_n=16, is_dma=True)
                x_ready.append((wl[pp], wl[pp].v))

            def attn_tail(w=w, t0=t0, cur=cur, kcur=kcur, kprv=kprv,
                          qext_ready=qext_ready, kext_ready=kext_ready, vext_ready=vext_ready):
                nonlocal oall_free, nmcols_free, sumsall_free
                qb = qext[cur]
                pendO = [None]
                for h in range(8):
                    for qt in range(4):
                        j = h * 4 + qt
                        wB = qt * 128 + 128
                        piB = next_pool()
                        P.add("tensor", lambda t, h=h, qt=qt, piB=piB, wB=wB, qb=qb: t.matmul(
                            out=pool[piB][:, 0:wB],
                            lhsT=qb[0:64, h * 512 + qt * 128: h * 512 + qt * 128 + 128],
                            rhs=kext[kcur][0:64, h * 512: h * 512 + wB],
                            start=True, stop=True),
                            waits=[qext_ready, kext_ready] + dep(pool_free[piB]) +
                                  dep(nmcols_free if j == 0 else None),
                            inc=pe)
                        pe_B = pe.at()
                        if w > 0:
                            wA = 512 - qt * 128
                            piA = next_pool()
                            P.add("tensor", lambda t, h=h, qt=qt, piA=piA, wA=wA, qb=qb: t.matmul(
                                out=pool[piA][:, 0:wA],
                                lhsT=qb[0:64, h * 512 + qt * 128: h * 512 + qt * 128 + 128],
                                rhs=kext[kprv][0:64, h * 512 + qt * 128: h * 512 + 512],
                                start=True, stop=True),
                                waits=dep(pool_free[piA]), inc=pe)
                            pe_A = pe.at()
                            P.add("vector", lambda v, piA=piA, wA=wA: v.reduce_max(
                                out=tmp32[:, 0:1], in_=pool[piA][:, 0:wA], axis=AX.X, negate=True),
                                waits=[pe_A])
                            pool_free[piA] = sv.at()
                            P.add("vector", lambda v, piB=piB, wB=wB: v.reduce_max(
                                out=tmp32[:, 1:2], in_=pool[piB][:, 0:wB], axis=AX.X, negate=True),
                                waits=[pe_B])
                            pool_free[piB] = sv.at()
                            P.add("vector", lambda v, j=j: v.tensor_tensor(
                                out=nmcols[:, j:j + 1], in0=tmp32[:, 0:1], in1=tmp32[:, 1:2],
                                op=ALU.min))
                        else:
                            P.add("vector", lambda v, piB=piB, j=j, wB=wB: v.reduce_max(
                                out=nmcols[:, j:j + 1], in_=pool[piB][:, 0:wB], axis=AX.X, negate=True),
                                waits=[pe_B])
                            pool_free[piB] = sv.at()
                nm_ready = sv.at()
                pi = next_pool()
                P.add("tensor", lambda t, pi=pi: t.transpose(
                    out=pool[pi][0:32, 0:128], in_=nmcols[:, 0:32], identity=identf[:]),
                    waits=[nm_ready, identf_ready] + dep(pool_free[pi]), inc=pe)
                pe_tr = pe.at()
                nmcols_free = pe_tr
                P.add("vector", lambda v, pi=pi: v.tensor_copy(
                    out=nmrows[:], in_=pool[pi][0:32, 0:128]),
                    waits=[pe_tr, db])
                pool_free[pi] = sv.at()
                nmr_ready = sv.at()
                P.add("gpsimd", lambda g: g.dma_start(out=dbounce_d, in_=nmrows[:]),
                      waits=[nmr_ready, db], inc=db, inc_n=16, is_dma=True)
                P.add("gpsimd", lambda g, qb=qb: g.dma_start(
                    out=qb[64:65, :],
                    in_=dbounce_d.rearrange("j p -> (j p)").unsqueeze(0)),
                    waits=[db] + dep(qext_free[cur]), inc=db, inc_n=16, is_dma=True)
                dall = db.at()

                for h in range(8):
                    hpar = h % 2
                    for kc in range(0 if w > 0 else 4, 8):
                        qlo = max(0, kc * 128 - 512)
                        qhi = min(512, kc * 128 + 128)
                        kb = kext[kprv] if kc < 4 else kext[kcur]
                        pi = next_pool()
                        P.add("tensor", lambda t, h=h, kc=kc, pi=pi, kb=kb, qlo=qlo, qhi=qhi, qb=qb: t.matmul(
                            out=pool[pi][:, 0:qhi - qlo],
                            lhsT=kb[0:65, h * 512 + (kc % 4) * 128: h * 512 + (kc % 4) * 128 + 128],
                            rhs=qb[0:65, h * 512 + qlo: h * 512 + qhi],
                            start=True, stop=False),
                            waits=[dall, qext_ready, kext_ready] + dep(pool_free[pi]),
                            inc=None)
                        mcol = (kc * 128 - qlo) if kc < 4 else 0
                        moff = 0 if kc < 4 else 128
                        P.add("tensor", lambda t, pi=pi, mcol=mcol, moff=moff: t.matmul(
                            out=pool[pi][:, mcol:mcol + 128],
                            lhsT=identb[:], rhs=maskb[:, moff:moff + 128],
                            start=False, stop=True),
                            waits=[consts_ready], inc=pe)
                        pe_ks = pe.at()
                        P.add("scalar", lambda s, h=h, kc=kc, pi=pi, hpar=hpar, qlo=qlo, qhi=qhi, ptbuf=ptbuf: s.activation(
                            out=ptbuf[hpar][:, kc * 512 + qlo: kc * 512 + qhi],
                            in_=pool[pi][:, 0:qhi - qlo], func=AF.Exp),
                            waits=[pe_ks] + dep(ptbuf_free[hpar] if kc == (0 if w > 0 else 4) else None))
                        pool_free[pi] = ss.at()
                    pt_ready = ss.at()
                    if pendO[0] is not None:
                        pendO[0]()

                    def o_block(h=h, hpar=hpar, pt_ready=pt_ready):
                        pi = next_pool()
                        kcs = list(range(0 if w > 0 else 4, 8))
                        for ki, kc in enumerate(kcs):
                            vb = vext[kprv] if kc < 4 else vext[kcur]
                            P.add("tensor", lambda t, h=h, kc=kc, pi=pi, vb=vb, ki=ki, hpar=hpar, last=(ki == len(kcs) - 1), ptbuf=ptbuf: t.matmul(
                                out=pool[pi][0:65, :],
                                lhsT=vb[:, (kc % 4) * 520 + h * 65: (kc % 4) * 520 + h * 65 + 65],
                                rhs=ptbuf[hpar][:, kc * 512:(kc + 1) * 512],
                                start=(ki == 0), stop=last),
                                waits=[pt_ready, vext_ready] + dep(pool_free[pi]),
                                inc=pe if ki == len(kcs) - 1 else None)
                        pe_o = pe.at()
                        ptbuf_free[hpar] = pe_o
                        P.add("scalar", lambda s, h=h, pi=pi: s.copy(
                            out=oall[(h % 2) * 64:(h % 2) * 64 + 64, (h // 2) * 512:(h // 2) * 512 + 512],
                            in_=pool[pi][0:64, :]),
                            waits=[pe_o] + dep(oall_free if h == 0 else None))
                        P.add("scalar", lambda s, h=h, pi=pi: s.copy(
                            out=sumsall[0:1, h * 512:(h + 1) * 512], in_=pool[pi][64:65, :]),
                            waits=dep(sumsall_free if h == 0 else None))
                        pool_free[pi] = ss.at()
                    pendO[0] = o_block
                pendO[0]()
                pendO[0] = None
                o_raw_ready = ss.at()

                pi = next_pool()
                for h in range(8):
                    P.add("tensor", lambda t, h=h, pi=pi: t.matmul(
                        out=pool[pi][0:8, :],
                        lhsT=e8r[0:1, h * 8:(h + 1) * 8],
                        rhs=sumsall[0:1, h * 512:(h + 1) * 512],
                        start=(h == 0), stop=(h == 7)),
                        waits=[o_raw_ready, consts_ready] + dep(pool_free[pi]),
                        inc=pe if h == 7 else None)
                pe_su = pe.at()
                sumsall_free = pe_su
                P.add("vector", lp(lambda v, pi=pi: v.reciprocal(out=rs8[:], in_=pool[pi][0:8, :])),
                      waits=[pe_su])
                pool_free[pi] = sv.at()
                rs_ready = sv.at()
                for ct in range(4):
                    pi = next_pool()
                    P.add("tensor", lambda t, ct=ct, pi=pi: t.matmul(
                        out=pool[pi][:], lhsT=selk[:, ct * 128:(ct + 1) * 128],
                        rhs=rs8[:], start=True, stop=True),
                        waits=[rs_ready] + dep(pool_free[pi]), inc=pe)
                    pe_bc = pe.at()
                    P.add("scalar", lambda s, pi=pi: s.copy(out=bcsb[:], in_=pool[pi][:]),
                          waits=[pe_bc, sv])
                    pool_free[pi] = ss.at()
                    P.add("vector", lambda v, ct=ct: v.tensor_tensor(
                        out=oall[:, ct * 512:(ct + 1) * 512],
                        in0=oall[:, ct * 512:(ct + 1) * 512],
                        in1=bcsb[:], op=ALU.mult),
                        waits=[ss])
                o_ready = sv.at()

                for e in range(NCH):
                    pi = next_pool()
                    for cc in range(4):
                        P.add("tensor", lambda t, e=e, cc=cc, pi=pi: t.matmul(
                            out=pool[pi][:],
                            lhsT=wo4[:, cc * 1024 + e * 128: cc * 1024 + (e + 1) * 128],
                            rhs=oall[:, cc * 512:(cc + 1) * 512],
                            start=(cc == 0), stop=(cc == 3)),
                            waits=[o_ready] + dep(pool_free[pi]),
                            inc=pe if cc == 3 else None)
                    pe_wo = pe.at()
                    P.add("vector", lambda v, e=e, pi=pi, cur=cur: v.scalar_tensor_tensor(
                        out=xatt[:, e * W4:(e + 1) * W4],
                        in0=pool[pi][:], scalar=0.0, op0=ALU.bypass,
                        in1=xwin[cur][:, e * 513 + 1: e * 513 + 513],
                        op1=ALU.add),
                        waits=[pe_wo, stp])
                    pool_free[pi] = sv.at()
                resid_done = sv.at()
                oall_free = pe.at()
                qext_free[cur] = pe.at()
                kext_free[kprv] = pe.at() if w > 0 else kext_free[kprv]
                vext_free[kprv] = pe.at() if w > 0 else vext_free[kprv]
                P.add("sync", lambda s, t0=t0: s.dma_start(
                    out=xpost_d[:, :, 1 + t0: 1 + t0 + W4].rearrange("c p t -> p c t"),
                    in_=xatt[:].rearrange("p (c t) -> p c t", c=NCH)),
                    waits=[resid_done, init_sync, stp], inc=stp, inc_n=16, is_dma=True)
                attn_store_evt.append((stp, stp.v))
                xwin_free[cur] = resid_done

            pendA[0] = attn_tail
        pendA[0]()
        attn_stores = stp.at()

        esA.close()

        esF = ExitStack()
        sbF = lambda name, shp, dt=F32: esF.enter_context(nc.sbuf_tensor('z4_' + name, shp, dt))
        w1as = sbF(f"w1a{l}", [128, NCH * HIDP], BF16)
        gbr = sbF(f"gbr{l}", [2, NCH * 128])
        nmr1 = sbF(f"nmr1{l}", [2, 512])
        rows4 = sbF(f"rows4{l}", [1, 4 * 512])
        w1gs = sbF(f"w1g{l}", [128, NCH * HIDP], BF16)
        w2s = sbF(f"w2p{l}", [128, NFC * 1024], BF16)
        hsrc = [sbF(f"hsrc{l}_{i}", [128, NCH * 513], BF16) for i in range(2)]
        hnb = [sbF(f"hnb{l}_{i}", [128, NCH * 512], BF16) for i in range(2)]
        h2all = sbF(f"h2all{l}", [128, NFC * 512], BF16)
        gelb = sbF(f"gelb{l}", [128, 512], BF16)
        tmpF = sbF(f"tmpF{l}", [128, 512], BF16)

        fb = [s.at() for s in (pe, sv, ss, sg, stp)]
        ls4 = P.nsem("ls")
        for c in range(NCH):
            P.add("sync", lambda s, c=c, w1as=w1as: s.dma_start(
                out=w1as[:, c * HIDP:(c + 1) * HIDP], in_=w1a_d[l, c]),
                waits=fb + [pc_gates["ffn"][l]] + ([ls4] if c == 0 else []),
                inc=ls4, inc_n=16, is_dma=True)
            P.add("sync", lambda s, c=c, w1gs=w1gs: s.dma_start(
                out=w1gs[:, c * HIDP:(c + 1) * HIDP], in_=w1g_d[l, c]),
                inc=ls4, inc_n=16, is_dma=True)
        for f in range(NFC):
            P.add("sync", lambda s, f=f, w2s=w2s: s.dma_start(
                out=w2s[:, f * 1024:(f + 1) * 1024], in_=w2p_d[l, f]),
                inc=ls4, inc_n=16, is_dma=True)
        sync_wf = ls4.at()
        sdma(b1ac[:], b1a_d[l], waits=fb + [ls])
        sdma(b1gc[:], b1g_d[l], waits=fb)
        sdma(b2c[:], b2_d[l], waits=fb)
        sdma(gbr[0:2, :].rearrange("r (c k) -> r c k", c=NCH),
             gb_d[l].rearrange("c r k -> r c k"), waits=fb)
        wf_ready = sync_wf
        wf_small = ls.at()
        P.add("sync", lambda s, nmr1=nmr1: s.dma_start(out=nmr1[1:2, :], in_=onesr_d),
              waits=fb + [ls], inc=ls, inc_n=16, is_dma=True)
        nmr_ones = ls.at()

        hsrc_free = [None, None]
        h2_free = None
        hnb_free = [None, None]
        r2f_free = None
        statA_free = None
        statB_free = None
        rows_free = None
        pendF = [None]
        ffn_store_evt = []

        hs_ready = []
        P.add("gpsimd", lambda g: g.memset(
            hsrc[0][:].rearrange("p (c t) -> p c t", c=NCH)[:, :, 0:1], 0.0), waits=fb)
        h0z = sg.at()
        P.add("sync", lambda s: s.dma_start(
            out=hsrc[0][:].rearrange("p (c t) -> p c t", c=NCH)[:, :, 1:513],
            in_=xpost_d[:, :, 1:513].rearrange("c p t -> p c t")),
            waits=[attn_store_evt[1], hl[0]], inc=hl[0], inc_n=16, is_dma=True)
        hs_ready.append((hl[0], hl[0].v))
        hs0_extra = h0z

        for tb in range(NW4):
            t0 = tb * W4
            par = tb % 2

            def hsl(c, par=par):
                o = 1 if c < 4 else 0
                return hsrc[par][:, c * 513 + o: c * 513 + o + W4]

            def xol(c, par=par):
                return hsrc[par][:, c * 513 + 1: c * 513 + 513]

            r2v = hnb[par][:].rearrange("p (c t) -> p c t", c=NCH)
            hs_extra = [hs0_extra] if tb == 0 else []
            for grp, (c0, c1, o) in enumerate(((0, 4, 1), (4, 8, 0))):
                P.add("vector", lambda v, c0=c0, c1=c1, o=o, par=par, r2v=r2v: v.tensor_tensor(
                    out=r2v[:, c0:c1, :],
                    in0=hsrc[par][:].rearrange("p (c t) -> p c t", c=NCH)[:, c0:c1, o:o + W4],
                    in1=hsrc[par][:].rearrange("p (c t) -> p c t", c=NCH)[:, c0:c1, o:o + W4],
                    op=ALU.mult),
                    waits=[hs_ready[tb]] + hs_extra + dep(hnb_free[par] if grp == 0 else None))
            r2_ready = sv.at()
            piA = next_pool()
            for c in range(NCH):
                P.add("tensor", lambda t, c=c, piA=piA, hsl=hsl: t.matmul(
                    out=pool[piA][0:1, :], lhsT=ones_c[:], rhs=hsl(c),
                    start=(c == 0), stop=(c == NCH - 1)),
                    waits=[hs_ready[tb], consts_mem] + hs_extra + dep(pool_free[piA]) + dep(statA_free),
                    inc=pe if c == NCH - 1 else None)
            peA = pe.at()
            piB = next_pool()
            for c in range(NCH):
                P.add("tensor", lambda t, c=c, piB=piB, par=par: t.matmul(
                    out=pool[piB][0:1, :],
                    lhsT=ones_c[:], rhs=hnb[par][:, c * W4:(c + 1) * W4],
                    start=(c == 0), stop=(c == NCH - 1)),
                    waits=[r2_ready] + dep(pool_free[piB]) + dep(statB_free),
                    inc=pe if c == NCH - 1 else None)
            peB = pe.at()
            r2f_free = peB
            MU, EX2, VAR, RSTD = 0, 1, 2, 3
            SCR = EX2
            rw = lambda i: rows4[0:1, i * W4:(i + 1) * W4]
            P.add("scalar", lambda s, piA=piA: s.mul(out=rw(MU), in_=pool[piA][0:1, :], mul=-1.0 / D),
                  waits=[peA] + dep(rows_free))
            statA_free = ss.at()
            P.add("scalar", lambda s, piB=piB: s.mul(out=rw(EX2), in_=pool[piB][0:1, :], mul=1.0 / D),
                  waits=[peB])
            statB_free = ss.at()
            pool_free[piA] = ss.at()
            pool_free[piB] = ss.at()
            P.add("gpsimd", lambda g: g.tensor_tensor(out=rw(VAR), in0=rw(MU), in1=rw(MU), op=ALU.mult),
                  waits=[ss])
            P.add("gpsimd", lambda g: g.tensor_tensor(out=rw(VAR), in0=rw(EX2), in1=rw(VAR), op=ALU.subtract))
            P.add("scalar", lambda s: s.activation(out=rw(SCR), in_=rw(VAR), func=AF.Ln, bias=epsln[:]),
                  waits=[sg])
            P.add("scalar", lambda s: s.activation(out=rw(RSTD), in_=rw(SCR), func=AF.Exp, scale=neghf[:]))
            P.add("gpsimd", lambda g, nmr1=nmr1: g.tensor_tensor(
                out=nmr1[0:1, :], in0=rw(MU), in1=rw(RSTD), op=ALU.mult),
                waits=[ss, nmr_ones] + dep(rows_free))
            rows_ready = sg.at()
            for c in range(NCH):
                pi1 = next_pool()
                P.add("tensor", lambda t, c=c, pi1=pi1: t.matmul(
                    out=pool[pi1][:], lhsT=gbr[0:1, c * 128:(c + 1) * 128], rhs=rw(RSTD),
                    start=True, stop=True),
                    waits=[rows_ready, wf_ready, wf_small] + dep(pool_free[pi1]), inc=pe)
                pi2 = next_pool()
                P.add("tensor", lambda t, c=c, pi2=pi2: t.matmul(
                    out=pool[pi2][:], lhsT=gbr[0:2, c * 128:(c + 1) * 128], rhs=nmr1[0:2, :],
                    start=True, stop=True),
                    waits=dep(pool_free[pi2]), inc=pe)
                pe_bc = pe.at()
                P.add("vector", lambda v, c=c, pi1=pi1, pe_bc=pe_bc, hsl=hsl: v.tensor_tensor(
                    out=tmpF[:], in0=hsl(c), in1=pool[pi1][:], op=ALU.mult),
                    waits=[pe_bc])
                pool_free[pi1] = sv.at()
                P.add("vector", lambda v, c=c, pi2=pi2, par=par: v.tensor_tensor(
                    out=hnb[par][:, c * W4:(c + 1) * W4], in0=tmpF[:], in1=pool[pi2][:], op=ALU.add),
                    waits=(dep(hnb_free[par]) + dep(r2f_free)) if c == 0 else [])
                pool_free[pi2] = sv.at()
            hn_ready = sv.at()
            rows_free = pe.at()

            if pendF[0] is not None:
                pendF[0]()

            if tb + 1 < NW4:
                pp = (tb + 1) % 2
                P.add("sync", lambda s, tb=tb, pp=pp: s.dma_start(
                    out=hsrc[pp][:].rearrange("p (c t) -> p c t", c=NCH),
                    in_=xpost_d[:, :, (tb + 1) * W4:(tb + 1) * W4 + 513].rearrange("c p t -> p c t")),
                    waits=[attn_store_evt[min(tb + 2, NW4 - 1)]] + dep(hsrc_free[pp]) + [hl[pp]],
                    inc=hl[pp], inc_n=16, is_dma=True)
                hs_ready.append(hl[pp].at())

            def ffn_tail(tb=tb, t0=t0, par=par, xol=xol, hn_ready=hn_ready):
                nonlocal h2_free
                glu_prev = None
                for f in range(NFC):
                    piA2 = next_pool()
                    for c in range(NCH):
                        P.add("tensor", lambda t, c=c, f=f, piA2=piA2, par=par: t.matmul(
                            out=pool[piA2][:],
                            lhsT=w1as[:, c * HIDP + f * 128: c * HIDP + (f + 1) * 128],
                            rhs=hnb[par][:, c * W4:(c + 1) * W4],
                            start=(c == 0), stop=(c == NCH - 1)),
                            waits=[hn_ready, wf_ready] + dep(pool_free[piA2]) +
                                  dep(h2_free if f == 0 else None),
                            inc=pe if c == NCH - 1 else None)
                    peA2 = pe.at()
                    piG = next_pool()
                    for c in range(NCH):
                        P.add("tensor", lambda t, c=c, f=f, piG=piG, par=par: t.matmul(
                            out=pool[piG][:],
                            lhsT=w1gs[:, c * HIDP + f * 128: c * HIDP + (f + 1) * 128],
                            rhs=hnb[par][:, c * W4:(c + 1) * W4],
                            start=(c == 0), stop=(c == NCH - 1)),
                            waits=dep(pool_free[piG]),
                            inc=pe if c == NCH - 1 else None)
                    peG = pe.at()
                    P.add("scalar", lambda s, f=f, piG=piG: s.activation(
                        out=gelb[:], in_=pool[piG][:], func=AF.Gelu, bias=b1gc[:, f:f + 1]),
                        waits=[peG, wf_small] + dep(glu_prev))
                    pool_free[piG] = ss.at()
                    P.add("vector", lambda v, f=f, piA2=piA2: v.scalar_tensor_tensor(
                        out=h2all[:, f * W4:(f + 1) * W4],
                        in0=pool[piA2][:], scalar=b1ac[:, f:f + 1], op0=ALU.add,
                        in1=gelb[:], op1=ALU.mult),
                        waits=[ss, peA2])
                    pool_free[piA2] = sv.at()
                    glu_prev = sv.at()
                h2_ready = sv.at()
                hnb_free[par] = pe.at()
                for e in range(NCH):
                    pi = next_pool()
                    for f in range(NFC):
                        P.add("tensor", lambda t, e=e, f=f, pi=pi: t.matmul(
                            out=pool[pi][:],
                            lhsT=w2s[:, f * 1024 + e * 128: f * 1024 + (e + 1) * 128],
                            rhs=h2all[:, f * W4:(f + 1) * W4],
                            start=(f == 0), stop=(f == NFC - 1)),
                            waits=[h2_ready] + dep(pool_free[pi]),
                            inc=pe if f == NFC - 1 else None)
                    pe_w2 = pe.at()
                    P.add("vector", lambda v, e=e, pi=pi, xol=xol: v.scalar_tensor_tensor(
                        out=xol(e),
                        in0=pool[pi][:], scalar=b2c[:, e:e + 1], op0=ALU.add,
                        in1=xol(e), op1=ALU.add),
                        waits=[pe_w2, wf_small])
                    pool_free[pi] = sv.at()
                xout_ready = sv.at()
                h2_free = pe.at()
                P.add("sync", lambda s, t0=t0, par=par: s.dma_start(
                    out=xnew_d[:, :, 1 + t0: 1 + t0 + W4].rearrange("c p t -> p c t"),
                    in_=hsrc[par][:].rearrange("p (c t) -> p c t", c=NCH)[:, :, 1:513]),
                    waits=[xout_ready, init_sync, stp], inc=stp, inc_n=16, is_dma=True)
                ffn_store_evt.append((stp, stp.v))
                hsrc_free[par] = (stp, stp.v)

            pendF[0] = ffn_tail
        pendF[0]()
        esF.close()
        xsrc_prev_stores = list(ffn_store_evt)

    for l in range(2):
        layer_phase(l)

    esL = ExitStack()
    sbL = lambda name, shp, dt=F32: esL.enter_context(nc.sbuf_tensor('z4_' + name, shp, dt))
    xprf = [sbL(f"xprf{i}", [128, NCH * 512], BF16) for i in range(2)]
    xnT = [sbL(f"xnT{i}", [128, NCH * 512], BF16) for i in range(2)]
    rbf4 = sbL("rbf4", [128, NCH * 512], BF16)
    r2bf4 = sbL("r2bf4", [128, NCH * 512], BF16)
    xo_s4 = sbL("xo_s4", [128, NCH * 512])
    rows4 = sbL("rows4L", [1, 6 * 512])
    lb = [s.at() for s in (pe, sv, ss, sg, stp)]
    xp_free = [None, None]
    xn_free = [None, None]
    rb_free = None
    r2b4_free = None
    stA_free = None
    stB_free = None
    rowsL_free = None
    out_free = None
    def ln_load(tb, w0=()):
        par = tb % 2
        t0 = tb * W4
        P.add("sync", lambda s, t0=t0, par=par: s.dma_start(
            out=xprf[par][:].rearrange("p (c t) -> p c t", c=NCH),
            in_=x_d[:, :, t0:t0 + W4].rearrange("c p t -> p c t")),
            waits=list(w0) + dep(xp_free[par]) + [ls], inc=ls, inc_n=16, is_dma=True)
        P.add("sync", lambda s, t0=t0, par=par: s.dma_start(
            out=xnT[par][:].rearrange("p (c t) -> p c t", c=NCH),
            in_=xnew_d[:, :, 1 + t0: 1 + t0 + W4].rearrange("c p t -> p c t")),
            waits=[xsrc_prev_stores[tb]] + dep(xn_free[par]), inc=ls, inc_n=16, is_dma=True)
        return ls.at()

    x_rdy = [ln_load(0, w0=lb)]
    for tb in range(NW4):
        t0 = tb * W4
        par = tb % 2
        if tb + 1 < NW4:
            x_rdy.append(ln_load(tb + 1))
        P.add("vector", lambda v, par=par: v.tensor_tensor(
            out=rbf4[:], in0=xprf[par][:], in1=xnT[par][:], op=ALU.add),
            waits=[x_rdy[tb]] + dep(rb_free))
        r_ready = sv.at()
        xp_free[par] = r_ready
        xn_free[par] = r_ready
        P.add("scalar", lambda s: s.activation(out=r2bf4[:], in_=rbf4[:], func=AF.Square),
              waits=[r_ready] + dep(r2b4_free))
        r2_ready = ss.at()
        piA = next_pool()
        for c in range(NCH):
            P.add("tensor", lambda t, c=c, piA=piA: t.matmul(
                out=pool[piA][0:1, :], lhsT=ones_c[:], rhs=rbf4[:, c * W4:(c + 1) * W4],
                start=(c == 0), stop=(c == NCH - 1)),
                waits=[r_ready, consts_mem] + dep(pool_free[piA]) + dep(stA_free),
                inc=pe if c == NCH - 1 else None)
        peA = pe.at()
        piB = next_pool()
        for c in range(NCH):
            P.add("tensor", lambda t, c=c, piB=piB: t.matmul(
                out=pool[piB][0:1, :], lhsT=ones_c[:], rhs=r2bf4[:, c * W4:(c + 1) * W4],
                start=(c == 0), stop=(c == NCH - 1)),
                waits=[r2_ready] + dep(pool_free[piB]) + dep(stB_free),
                inc=pe if c == NCH - 1 else None)
        peB = pe.at()
        r2b4_free = peB
        MU, EX2, VAR, RSTD, NMR, SCR = 0, 1, 2, 3, 4, 5
        rw = lambda i: rows4[0:1, i * W4:(i + 1) * W4]
        P.add("scalar", lambda s, piA=piA: s.mul(out=rw(MU), in_=pool[piA][0:1, :], mul=1.0 / D),
              waits=[peA] + dep(rowsL_free))
        stA_free = ss.at()
        P.add("scalar", lambda s, piB=piB: s.mul(out=rw(EX2), in_=pool[piB][0:1, :], mul=1.0 / D),
              waits=[peB])
        stB_free = ss.at()
        pool_free[piA] = ss.at()
        pool_free[piB] = ss.at()
        P.add("vector", lambda v: v.tensor_tensor(out=rw(VAR), in0=rw(MU), in1=rw(MU), op=ALU.mult),
              waits=[ss])
        P.add("vector", lambda v: v.tensor_tensor(out=rw(VAR), in0=rw(EX2), in1=rw(VAR), op=ALU.subtract))
        P.add("scalar", lambda s: s.activation(out=rw(SCR), in_=rw(VAR), func=AF.Sqrt, bias=epsln[:]),
              waits=[sv])
        P.add("vector", lambda v: v.reciprocal(out=rw(RSTD), in_=rw(SCR)), waits=[ss])
        P.add("vector", lambda v: v.tensor_tensor(out=rw(NMR), in0=rw(MU), in1=rw(RSTD), op=ALU.mult))
        P.add("scalar", lambda s: s.mul(out=rw(NMR), in_=rw(NMR), mul=-1.0), waits=[sv])
        rows_ready = ss.at()
        pi1 = next_pool()
        P.add("tensor", lambda t, pi1=pi1: t.matmul(
            out=pool[pi1][:], lhsT=onesr_f[:], rhs=rw(RSTD), start=True, stop=True),
            waits=[rows_ready] + dep(pool_free[pi1]), inc=pe)
        pi2 = next_pool()
        P.add("tensor", lambda t, pi2=pi2: t.matmul(
            out=pool[pi2][:], lhsT=onesr_f[:], rhs=rw(NMR), start=True, stop=True),
            waits=dep(pool_free[pi2]), inc=pe)
        pe_bc = pe.at()
        rowsL_free = pe_bc
        P.add("vector", lambda v, pi1=pi1: v.tensor_tensor(
            out=xo_s4[:].rearrange("p (c t) -> p c t", c=NCH),
            in0=rbf4[:].rearrange("p (c t) -> p c t", c=NCH),
            in1=pool[pi1][:].unsqueeze(1).broadcast_to((128, NCH, W4)),
            op=ALU.mult),
            waits=[pe_bc] + dep(out_free))
        pool_free[pi1] = sv.at()
        P.add("vector", lambda v, pi2=pi2: v.tensor_tensor(
            out=xo_s4[:].rearrange("p (c t) -> p c t", c=NCH),
            in0=xo_s4[:].rearrange("p (c t) -> p c t", c=NCH),
            in1=pool[pi2][:].unsqueeze(1).broadcast_to((128, NCH, W4)),
            op=ALU.add))
        pool_free[pi2] = sv.at()
        rb_free = sv.at()
        out_ready = sv.at()
        P.add("gpsimd", lambda g, t0=t0: g.dma_start(
            out=xo_d[:, :, t0:t0 + W4].rearrange("c p t -> p c t"),
            in_=xo_s4[:].rearrange("p (c t) -> p c t", c=NCH)),
            waits=[out_ready], inc=sto, inc_n=16, is_dma=True)
        out_free = (sto, sto.v)
    esL.close()

    P.add("gpsimd", lambda g: g.nop(), waits=[sto, stp, P.nsem("st")])
    es.close()


def make_rotary_tables_np():
    DH, WIN = 64, 512
    half = np.arange(0, DH, 2, dtype=np.float64)
    inv_freq = 1.0 / 10000.0 ** (half / DH)
    t = np.arange(T, dtype=np.float64)
    freqs = t[:, None] * inv_freq[None, :]
    freqs = np.concatenate([freqs, freqs], -1)
    base = (half + 0.4 * DH) / (1.4 * DH)
    power = (t - T // 2) / (WIN // 2)
    scale = base[None, :] ** power[:, None]
    scale = np.concatenate([scale, scale], -1)
    cos, sin = np.cos(freqs), np.sin(freqs)
    sgn = np.concatenate([-np.ones(32), np.ones(32)])
    cs_q = (cos * scale).astype(np.float32)
    sn_q = (sin * scale * sgn[None, :]).astype(np.float32)
    cs_k = (cos / scale).astype(np.float32)
    sn_k = (sin / scale * sgn[None, :]).astype(np.float32)
    fm = lambda a: np.ascontiguousarray(np.tile(a.T, (2, 1)))
    return fm(cs_q), fm(sn_q), fm(cs_k), fm(sn_k)


def stage4_inputs(xfm, inputs):
    g = lambda k: np.asarray(inputs[k], np.float32)
    wqkv = np.stack([np.ascontiguousarray(g("L1_ffn_wqkv")[l]).reshape(NCH, 128, 1536)
                     for l in range(2)])
    wo4 = np.stack([np.ascontiguousarray(g("L1_ffn_wo")[l]).reshape(4, 128, 1024)
                    for l in range(2)])
    w1 = g("L1_ffn_w1")
    b1 = g("L1_ffn_b1")
    w2 = g("L1_ffn_w2")
    w1a = np.zeros((2, D, HIDP), np.float32); w1a[:, :, :HID] = w1[:, :, :HID]
    w1g = np.zeros((2, D, HIDP), np.float32); w1g[:, :, :HID] = w1[:, :, HID:]
    b1a = np.zeros((2, HIDP), np.float32); b1a[:, :HID] = b1[:, :HID]
    b1g = np.zeros((2, HIDP), np.float32); b1g[:, :HID] = b1[:, HID:]
    w2p = np.zeros((2, HIDP, D), np.float32); w2p[:, :HID] = w2
    w1a = w1a.reshape(2, NCH, 128, HIDP)
    w1g = w1g.reshape(2, NCH, 128, HIDP)
    w2p = np.ascontiguousarray(w2p.reshape(2, NFC, 128, 1024))
    b1a_c = np.ascontiguousarray(b1a.reshape(2, NFC, 128).transpose(0, 2, 1))
    b1g_c = np.ascontiguousarray(b1g.reshape(2, NFC, 128).transpose(0, 2, 1))
    b2_c = np.ascontiguousarray(g("L1_ffn_b2").reshape(2, NCH, 128).transpose(0, 2, 1))
    gb_r = np.ascontiguousarray(np.stack(
        [g("L1_ffn_ln_g").reshape(2, NCH, 128), g("L1_ffn_ln_b").reshape(2, NCH, 128)],
        axis=2))
    qs = g("L1_ffn_q_scale")
    ks = g("L1_ffn_k_scale")
    swp = np.concatenate([np.arange(32, 64), np.arange(0, 32)])
    qs_c = np.stack([np.tile(qs, (1, 2)), np.tile(qs[:, swp], (1, 2))], axis=2)
    ks_c = np.stack([np.tile(ks, (1, 2)), np.tile(ks[:, swp], (1, 2))], axis=2)
    csq, snq, csk, snk = make_rotary_tables_np()
    i = np.arange(128)[None, :]
    j = np.arange(128)[:, None]
    medge_prev = np.where(i <= j, 0.0, -1e31).astype(np.float32)
    medge_cur = np.where(i >= j, 0.0, -1e31).astype(np.float32)
    maskb = np.concatenate([medge_prev, medge_cur], axis=1)
    ident = np.eye(128, dtype=np.float32)
    p32 = np.zeros((128, 128), np.float32)
    for d_ in range(128):
        b_ = (d_ // 64) * 64
        p32[b_ + ((d_ - b_) + 32) % 64, d_] = 1.0
    sel8 = np.zeros((128, 32), np.float32)
    for ct in range(4):
        for half in range(2):
            sel8[half * 64:(half + 1) * 64, ct * 8 + 2 * ct + half] = 1.0
    selq = np.zeros((8, 512), np.float32)
    selk = np.zeros((8, 512), np.float32)
    for ct in range(4):
        for d_ in range(128):
            h = 2 * ct + d_ // 64
            selq[h, ct * 128 + d_] = 8.0
            selk[h, ct * 128 + d_] = 1.0
    e8 = np.zeros((1, 64), np.float32)
    for h in range(8):
        e8[0, h * 8 + h] = 1.0
    ones512 = np.ones((1, 512), np.float32)
    return {"x": xfm, "wqkv": wqkv, "wo4": wo4, "w1a": w1a, "w1g": w1g, "w2p": w2p,
            "b1a_c": b1a_c, "b1g_c": b1g_c, "b2_c": b2_c, "gb_r": gb_r,
            "qs_c": qs_c, "ks_c": ks_c, "csq": csq, "snq": snq, "csk": csk, "snk": snk,
            "maskb": maskb, "ident": ident, "p32": p32, "sel8": sel8, "selq": selq,
            "selk": selk, "e8": e8, "ones512": ones512}


def _featmajor(xb):
    return np.ascontiguousarray(np.asarray(xb, np.float32).T.reshape(NCH, 128, T))


def _unfeat(xt):
    return np.asarray(xt, np.float32).reshape(D, T).T


def stage2_inputs(xfm, inputs):
    wfc = np.ascontiguousarray(np.asarray(inputs["L0_ffn_W_fc"], np.float32).T).reshape(NCH, 128, 4096)
    wproj = np.ascontiguousarray(np.asarray(inputs["L0_ffn_W_proj"], np.float32).T).reshape(32, 128, 1024)
    return {"x": xfm, "wfc": wfc, "wproj": wproj}


def stage3_inputs(xfm, inputs):
    tr = lambda w: np.ascontiguousarray(np.asarray(w, np.float32).T).reshape(NCH, 128, 1024)
    bq = np.asarray(inputs["L1_attn_bq"], np.float32).reshape(NCH, 128).T
    bk = np.asarray(inputs["L1_attn_bk"], np.float32).reshape(NCH, 128).T
    bqk = np.ascontiguousarray(np.concatenate([bq, bk], axis=1))
    bo = np.ascontiguousarray(np.asarray(inputs["L1_attn_bo"], np.float32).reshape(NCH, 128).T)
    bv = np.asarray(inputs["L1_attn_bv"], np.float32).reshape(1, 1024)
    i_loc = np.arange(128)[:, None]
    j = np.arange(256)[None, :]
    mask = np.concatenate(
        [np.where(j <= qt * 128 + i_loc, 0.0, -1e31).astype(np.float32) for qt in (0, 1)],
        axis=1)
    ident = np.eye(128, dtype=np.float32)
    return {"x": xfm, "wq": tr(inputs["L1_attn_Wq"]), "wk": tr(inputs["L1_attn_Wk"]),
            "wv": tr(inputs["L1_attn_Wv"]), "wo": tr(inputs["L1_attn_Wo"]),
            "bqk": bqk, "bo_c": bo, "bv_r": bv, "mask": np.ascontiguousarray(mask),
            "ident": ident}


def build_fused():
    nc = bass.Bass("TRN2", target_bir_lowering=False)
    dram = lambda name, shp, dt=F32, kind="ExternalInput": nc.dram_tensor(name, shp, dt, kind=kind).ap()
    x_d = dram("x", [NCH, 128, T])
    xo_d = dram("xo", [NCH, 128, T], kind="ExternalOutput")
    h12 = nc.dram_tensor("h12", [NCH, 128, T], BF16, kind="Internal").ap()
    h23 = nc.dram_tensor("h23", [NCH, 128, T], BF16, kind="Internal").ap()
    h34 = nc.dram_tensor("h34", [NCH, 128, T], BF16, kind="Internal").ap()

    es = ExitStack()
    P = Prog(nc, es)

    gcb = es.enter_context(nc.sbuf_tensor("gconst", [1, 130], F32))
    g_ones = gcb[0:1, 0:128]
    g_eps5 = gcb[0:1, 128:129]
    g_negh = gcb[0:1, 129:130]
    P.add("gpsimd", lambda g: g.memset(g_ones, 1.0))
    P.add("gpsimd", lambda g: g.memset(g_eps5, 1e-5))
    P.add("gpsimd", lambda g: g.memset(g_negh, -0.5))
    gc = (g_ones, g_eps5, g_negh)

    s1 = (x_d,
          dram("s1_wq", [NCH, 128, 128]), dram("s1_wk", [NCH, 128, 128]),
          dram("s1_wv", [NCH, 128, 128]), dram("s1_wg", [NCH, 128, 128]),
          dram("s1_wab", [NCH, 128, 2]), dram("s1_wo", [128, 1024]),
          dram("s1_bq_c", [128, 1]), dram("s1_bk_c", [128, 1]), dram("s1_bg_c", [128, 1]),
          dram("s1_bv_r", [1, 128]), dram("s1_bab", [1, 2]), dram("s1_bo_c", [128, NCH]),
          dram("s1_cumUT", [128, 128]), dram("s1_identf", [128, 128]),
          dram("s1_maskadd", [128, 128]), dram("s1_maskstrict", [128, 128]),
          h12)
    emit_stage1(nc, P, s1)
    pcast = Sem(nc, es, "pcast")
    wqkvB = nc.dram_tensor("wqkvB", [2, NCH, 128, 1536], BF16, kind="Internal").ap()
    wo4B = nc.dram_tensor("wo4B", [2, 4, 128, 1024], BF16, kind="Internal").ap()
    w1aB = nc.dram_tensor("w1aB", [2, NCH, 128, HIDP], BF16, kind="Internal").ap()
    w1gB = nc.dram_tensor("w1gB", [2, NCH, 128, HIDP], BF16, kind="Internal").ap()
    w2pB = nc.dram_tensor("w2pB", [2, NFC, 128, 1024], BF16, kind="Internal").ap()
    tabB = nc.dram_tensor("tabB", [4, 128, T], BF16, kind="Internal").ap()
    wfcB = nc.dram_tensor("wfcB", [NCH, 128, 4096], BF16, kind="Internal").ap()
    wprojB = nc.dram_tensor("wprojB", [32, 128, 1024], BF16, kind="Internal").ap()
    s3wB = nc.dram_tensor("s3wB", [4, NCH, 128, 1024], BF16, kind="Internal").ap()
    s4f32 = {
        "wqkv": nc.dram_tensor("wqkv", [2, NCH, 128, 1536], F32, kind="ExternalInput").ap(),
        "wo4": nc.dram_tensor("wo4", [2, 4, 128, 1024], F32, kind="ExternalInput").ap(),
        "w1a": nc.dram_tensor("w1a", [2, NCH, 128, HIDP], F32, kind="ExternalInput").ap(),
        "w1g": nc.dram_tensor("w1g", [2, NCH, 128, HIDP], F32, kind="ExternalInput").ap(),
        "w2p": nc.dram_tensor("w2p", [2, NFC, 128, 1024], F32, kind="ExternalInput").ap(),
        "csq": nc.dram_tensor("csq", [128, T], F32, kind="ExternalInput").ap(),
        "snq": nc.dram_tensor("snq", [128, T], F32, kind="ExternalInput").ap(),
        "csk": nc.dram_tensor("csk", [128, T], F32, kind="ExternalInput").ap(),
        "snk": nc.dram_tensor("snk", [128, T], F32, kind="ExternalInput").ap(),
    }

    def emit_precast(s4f32):
        P.add("gpsimd", lambda g: g.dma_start(out=wfcB, in_=s2_wfc_d),
              waits=[pcast], inc=pcast, inc_n=16, is_dma=True)
        P.add("gpsimd", lambda g: g.dma_start(out=wprojB, in_=s2_wproj_d),
              waits=[pcast], inc=pcast, inc_n=16, is_dma=True)
        for i, sw in enumerate(s3w_f32):
            P.add("gpsimd", lambda g, i=i, sw=sw: g.dma_start(out=s3wB[i], in_=sw),
                  waits=[pcast], inc=pcast, inc_n=16, is_dma=True)
        for i, key in enumerate(("csq", "snq", "csk", "snk")):
            P.add("gpsimd", lambda g, d=tabB[i], s=s4f32[key]: g.dma_start(out=d, in_=s),
                  waits=[pcast], inc=pcast, inc_n=16, is_dma=True)
        for l in range(2):
            for dstB, key in ((wqkvB, "wqkv"), (wo4B, "wo4"), (w1aB, "w1a"),
                              (w1gB, "w1g"), (w2pB, "w2p")):
                P.add("gpsimd", lambda g, d=dstB[l], s=s4f32[key][l]: g.dma_start(out=d, in_=s),
                      waits=[pcast], inc=pcast, inc_n=16, is_dma=True)
        return {"attn": [(pcast, 192), (pcast, 272)],
                "ffn": [(pcast, 240), (pcast, 320)]}

    s2_wfc_d = dram("s2_wfc", [NCH, 128, 4096])
    s2_wproj_d = dram("s2_wproj", [32, 128, 1024])
    s3w_f32 = [dram("s3_wq", [NCH, 128, 1024]), dram("s3_wk", [NCH, 128, 1024]),
               dram("s3_wv", [NCH, 128, 1024]), dram("s3_wo", [NCH, 128, 1024])]
    pc_gates = emit_precast(s4f32)

    P.stage_barrier()
    emit_stage2(nc, P, h12, wfcB, wprojB, h23, pc2=(pcast, 32), gc=gc)
    P.stage_barrier()
    emit_stage3(nc, P, h23,
                s3wB[0], s3wB[1], s3wB[2], s3wB[3],
                dram("s3_bqk", [128, 16]), dram("s3_bo_c", [128, 8]),
                dram("s3_bv_r", [1, 1024]), dram("s3_mask", [128, 512]),
                dram("s3_ident", [128, 128]),
                h34, pc3=(pcast, 96))
    P.stage_barrier()
    emit_stage4(nc, P, h34, xo_d,
                dict(wqkvB=wqkvB, wo4B=wo4B, w1aB=w1aB, w1gB=w1gB, w2pB=w2pB,
                     tabB=tabB, pcast=pcast, pc_gates=pc_gates))
    P.finalize()
    es.close()
    return nc


def fused_inputs(xfm, inputs):
    d = {"x": xfm}
    for k, v in stage1_inputs(xfm, inputs).items():
        if k != "x":
            d["s1_" + k] = v
    for k, v in stage2_inputs(xfm, inputs).items():
        if k != "x":
            d["s2_" + k] = v
    for k, v in stage3_inputs(xfm, inputs).items():
        if k != "x":
            d["s3_" + k] = v
    for k, v in stage4_inputs(xfm, inputs).items():
        if k != "x":
            d[k] = v
    return d


def kernel(**inputs):
    x = np.asarray(inputs["L0_attn_x"], np.float32)
    xfms = [np.ascontiguousarray(x[b].T.reshape(NCH, 128, T)) for b in range(8)]
    nc = build_fused()
    base = fused_inputs(xfms[0], inputs)
    res = run_bass_kernel_spmd(nc, [dict(base, x=xfms[b]) for b in range(8)],
                               core_ids=list(range(8)))
    return np.stack([_unfeat(np.asarray(res.results[b]["xo"]))
                     for b in range(8)]).astype(np.float32)


def kernel_multilaunch(**inputs):
    x = np.asarray(inputs["L0_attn_x"], np.float32)
    xfms = [np.ascontiguousarray(x[b].T.reshape(NCH, 128, T)) for b in range(8)]
    cores = list(range(8))

    nc1 = build_stage1()
    base1 = stage1_inputs(xfms[0], inputs)
    r1 = run_bass_kernel_spmd(nc1, [dict(base1, x=xfms[b]) for b in range(8)], core_ids=cores)
    xs = [np.asarray(r1.results[b]["xo"]) for b in range(8)]

    nc2 = build_stage2()
    base2 = stage2_inputs(xs[0], inputs)
    r2 = run_bass_kernel_spmd(nc2, [dict(base2, x=xs[b]) for b in range(8)], core_ids=cores)
    xs = [np.asarray(r2.results[b]["xo"]) for b in range(8)]

    nc3 = build_stage3()
    base3 = stage3_inputs(xs[0], inputs)
    r3 = run_bass_kernel_spmd(nc3, [dict(base3, x=xs[b]) for b in range(8)], core_ids=cores)
    xs = [np.asarray(r3.results[b]["xo"]) for b in range(8)]

    nc4 = build_stage4()
    base4 = stage4_inputs(xs[0], inputs)
    r4 = run_bass_kernel_spmd(nc4, [dict(base4, x=xs[b]) for b in range(8)], core_ids=cores)
    xs = [np.asarray(r4.results[b]["xo"]) for b in range(8)]

    return np.stack([_unfeat(xb) for xb in xs]).astype(np.float32)


# version 26 (final)
# speedup vs baseline: 1.0101x; 1.0088x over previous
import numpy as np
from contextlib import ExitStack

import concourse.bass as bass
import concourse.mybir as mybir
from concourse.bass_utils import run_bass_kernel_spmd

F32 = mybir.dt.float32
BF16 = mybir.dt.bfloat16
AF = mybir.ActivationFunctionType
ALU = mybir.AluOpType
AX = mybir.AxisListType

T = 4096
D = 1024
NCH = D // 128



class Sem:
    def __init__(self, nc, es, name):
        self.h = es.enter_context(nc.semaphore(name))
        self.v = 0
        self.name = name

    def at(self):
        return (self, self.v)


class Prog:

    ENGINES = ("sync", "tensor", "vector", "scalar", "gpsimd")

    def __init__(self, nc, es):
        self.nc = nc
        self.es = es
        self.streams = {e: [] for e in self.ENGINES}
        self.named = {}
        self.pending_barrier = {}
        self.esem = {e: self.sem("e_" + e) for e in ("vector", "scalar", "gpsimd")}

    def sem(self, name):
        s = Sem(self.nc, self.es, name)
        self.named[name] = s
        return s

    def nsem(self, name):
        if name not in self.named:
            self.named[name] = Sem(self.nc, self.es, name)
        return self.named[name]

    def stage_barrier(self):
        vals = [(s, s.v) for s in self.named.values() if s.v > 0]
        self.pending_barrier = {e: list(vals) for e in self.ENGINES}

    def add(self, eng, fn, waits=(), inc=None, inc_n=1, is_dma=False):
        wl = []
        pb = self.pending_barrier.pop(eng, None)
        if pb:
            wl.extend(pb)
        for w in waits:
            s, val = (w, w.v) if isinstance(w, Sem) else (w[0], w[1])
            if val > 0:
                wl.append((s, val))
        if not is_dma and eng in self.esem:
            es_ = self.esem[eng]
            if es_.v > 0:
                wl.append((es_, es_.v))
            assert inc is None or inc is es_
            inc = es_
        if inc is not None:
            inc.v += inc_n
        ret = (inc, inc.v) if inc is not None else None

        def emit(engine, fn=fn, wl=wl, inc=inc, inc_n=inc_n):
            for s, val in wl:
                engine.wait_ge(s.h, val)
            instr = fn(engine)
            if inc is not None:
                assert instr is not None, "inc requested but fn returned no instr"
                instr.then_inc(inc.h, inc_n)

        self.streams[eng].append(emit)
        return ret

    def finalize(self):
        nc = self.nc
        with nc.Block() as block:
            for name in self.ENGINES:
                fns = self.streams[name]
                if not fns:
                    continue

                def runner(engine, fns=fns):
                    for f in fns:
                        f(engine)

                getattr(block, name)(runner)



def emit_stage2(nc, P, x_d, wfc_d, wproj_d, xo_d, pc2=None, gc=None):
    es = ExitStack()
    sb = lambda name, shp, dt=F32: es.enter_context(nc.sbuf_tensor('z2_' + name, shp, dt))
    psum = lambda name, shp, dt=F32: es.enter_context(nc.psum_tensor('z2_' + name, shp, dt))

    TB = 512
    NTB = T // TB
    HF = 128

    wfc = sb("wfc_s", [128, NCH * 4096], BF16)
    wproj = sb("wproj_s", [128, 32 * 1024], BF16)
    xbt = [sb(f"xbt{i}", [128, NCH * TB], BF16) for i in range(2)]
    h2all = sb("h2all", [128, 32 * TB], BF16)
    ysb = sb("ysb", [128, NCH * TB])
    spk = sb("spk", [128, NCH * HF], BF16)
    rsb = [sb(f"rsb{i}", [128, NCH * HF], BF16) for i in range(4)]
    r2bf = sb("r2bf", [128, NCH * HF], BF16)
    rows = sb("rows", [1, 3 * HF])
    ones_c = sb("ones_c", [128, 1], BF16)
    xo_s2 = r2bf
    ones_r, epsb, neghalf = gc

    pool = [psum(f"pool{i}", [128, 512]) for i in range(6)]
    mem = psum("memp", [128, NCH * HF])

    xl = [P.nsem("xl0"), P.nsem("xl1")]
    st = P.nsem("stp")
    pe = P.nsem("pe")
    sv = P.esem["vector"]
    ss = P.esem["scalar"]
    sg = P.esem["gpsimd"]

    pool_free = [None] * 6
    _np = [0]

    def next_pool():
        i = _np[0] % 6
        _np[0] += 1
        return i

    P.add("gpsimd", lambda g: g.memset(ones_c[:], 1.0))
    consts_ready = sg.at()
    ls2 = P.nsem("ls")
    first = [True]
    for c in range(NCH):
        P.add("sync", lambda s, c=c: s.dma_start(
            out=wfc[:, c * 4096:(c + 1) * 4096], in_=wfc_d[c]),
            waits=(([ls2] + ([pc2] if pc2 is not None else [])) if first[0] else []),
            inc=ls2, inc_n=16, is_dma=True)
        first[0] = False
    for f in range(32):
        P.add("sync", lambda s, f=f: s.dma_start(
            out=wproj[:, f * 1024:(f + 1) * 1024], in_=wproj_d[f]), inc=ls2, inc_n=16, is_dma=True)
    w_ready = ls2.at()

    def dep(x):
        return [x] if x is not None else []

    S = {
        "xbt_free": [None, None], "h2all_free": None, "ysb_free": None,
        "rsb_free": [None] * 4, "r2bf_free": None, "statA_free": None,
        "statB_free": None, "rows_free": None,
    }

    x_ready = []
    P.add("sync", lambda s: s.dma_start(
        out=xbt[0][:].rearrange("p (c t) -> p c t", c=NCH),
        in_=x_d[:, :, 0:TB].rearrange("c p t -> p c t")),
        waits=[xl[0]], inc=xl[0], inc_n=16, is_dma=True)
    x_ready.append(xl[0].at())

    def emit_tail(gh, r_ready):
        par2 = gh % 4
        t0g = gh * HF
        rbuf = rsb[par2]
        P.add("gpsimd", lambda g, rbuf=rbuf: g.tensor_tensor(
            out=r2bf[:], in0=rbuf[:], in1=rbuf[:], op=ALU.mult),
            waits=[r_ready, (st, st.v)] + dep(S["r2bf_free"]))
        r2_ready = sg.at()
        piA = next_pool()
        for c in range(NCH):
            P.add("tensor", lambda t, c=c, piA=piA, rbuf=rbuf: t.matmul(
                out=pool[piA][0:1, 0:HF], lhsT=ones_c[:], rhs=rbuf[:, c * HF:(c + 1) * HF],
                start=(c == 0), stop=(c == NCH - 1)),
                waits=[r_ready, consts_ready] + dep(pool_free[piA]) + dep(S["statA_free"]),
                inc=pe if c == NCH - 1 else None)
        peA = pe.at()
        piB = next_pool()
        for c in range(NCH):
            P.add("tensor", lambda t, c=c, piB=piB: t.matmul(
                out=pool[piB][0:1, 0:HF], lhsT=ones_c[:], rhs=r2bf[:, c * HF:(c + 1) * HF],
                start=(c == 0), stop=(c == NCH - 1)),
                waits=[r2_ready] + dep(pool_free[piB]) + dep(S["statB_free"]),
                inc=pe if c == NCH - 1 else None)
        peB = pe.at()
        S["r2bf_free"] = peB
        MU, EX2, W = 0, 1, 2
        rw = lambda i: rows[0:1, i * HF:(i + 1) * HF]
        P.add("scalar", lambda s, piA=piA: s.mul(
            out=rw(MU), in_=pool[piA][0:1, 0:HF], mul=-1.0 / D),
            waits=[peA] + dep(S["rows_free"]))
        S["statA_free"] = ss.at()
        P.add("scalar", lambda s, piB=piB: s.mul(
            out=rw(EX2), in_=pool[piB][0:1, 0:HF], mul=1.0 / D),
            waits=[peB])
        S["statB_free"] = ss.at()
        pool_free[piA] = S["statA_free"]
        pool_free[piB] = S["statB_free"]
        P.add("gpsimd", lambda g: g.tensor_tensor(out=rw(W), in0=rw(MU), in1=rw(MU), op=ALU.mult),
              waits=[ss])
        P.add("gpsimd", lambda g: g.tensor_tensor(out=rw(W), in0=rw(EX2), in1=rw(W), op=ALU.subtract))
        P.add("scalar", lambda s: s.activation(out=rw(W), in_=rw(W), func=AF.Ln, bias=epsb),
              waits=[sg])
        P.add("scalar", lambda s: s.activation(out=rw(W), in_=rw(W), func=AF.Exp, scale=neghalf))
        rows_ready = ss.at()
        pi1 = next_pool()
        P.add("tensor", lambda t, pi1=pi1: t.matmul(
            out=pool[pi1][:, 0:HF], lhsT=ones_r, rhs=rw(W), start=True, stop=True),
            waits=[rows_ready] + dep(pool_free[pi1]), inc=pe)
        pi2 = next_pool()
        P.add("tensor", lambda t, pi2=pi2: t.matmul(
            out=pool[pi2][:, 0:HF], lhsT=ones_r, rhs=rw(MU), start=True, stop=True),
            waits=dep(pool_free[pi2]), inc=pe)
        pe_bc = pe.at()
        S["rows_free"] = pe_bc
        P.add("vector", lambda v, pi2=pi2, rbuf=rbuf: v.tensor_tensor(
            out=xo_s2[:].rearrange("p (c t) -> p c t", c=NCH),
            in0=rbuf[:].rearrange("p (c t) -> p c t", c=NCH),
            in1=pool[pi2][:, 0:HF].unsqueeze(1).broadcast_to((128, NCH, HF)),
            op=ALU.add),
            waits=[pe_bc, (st, st.v)])
        pool_free[pi2] = sv.at()
        P.add("vector", lambda v, pi1=pi1: v.tensor_tensor(
            out=xo_s2[:].rearrange("p (c t) -> p c t", c=NCH),
            in0=xo_s2[:].rearrange("p (c t) -> p c t", c=NCH),
            in1=pool[pi1][:, 0:HF].unsqueeze(1).broadcast_to((128, NCH, HF)),
            op=ALU.mult))
        pool_free[pi1] = sv.at()
        out_ready = sv.at()
        S["rsb_free"][par2] = out_ready
        P.add("sync", lambda s, t0g=t0g: s.dma_start(
            out=xo_d[:, :, t0g:t0g + HF].rearrange("c p t -> p c t"),
            in_=xo_s2[:].rearrange("p (c t) -> p c t", c=NCH)),
            waits=[out_ready, st], inc=st, inc_n=16, is_dma=True)

    pend = []
    for tb in range(NTB):
        xpar = tb % 2
        if tb + 1 < NTB:
            pp = (tb + 1) % 2
            P.add("sync", lambda s, tb=tb, pp=pp: s.dma_start(
                out=xbt[pp][:].rearrange("p (c t) -> p c t", c=NCH),
                in_=x_d[:, :, (tb + 1) * TB:(tb + 2) * TB].rearrange("c p t -> p c t")),
                waits=dep(S["xbt_free"][pp]) + [xl[pp]], inc=xl[pp], inc_n=16, is_dma=True)
            x_ready.append(xl[pp].at())

        sq_ready = [None] * 32
        for fc in range(32):
            pi = next_pool()
            for c in range(NCH):
                P.add("tensor", lambda t, c=c, fc=fc, pi=pi, xpar=xpar: t.matmul(
                    out=pool[pi][:],
                    lhsT=wfc[:, c * 4096 + fc * 128: c * 4096 + (fc + 1) * 128],
                    rhs=xbt[xpar][:, c * TB:(c + 1) * TB],
                    start=(c == 0), stop=(c == NCH - 1)),
                    waits=[w_ready, x_ready[tb]] + dep(pool_free[pi]),
                    inc=pe if c == NCH - 1 else None)
            pe_h1 = pe.at()
            P.add("scalar", lambda s, pi=pi, fc=fc: s.activation(
                out=h2all[:, fc * TB:(fc + 1) * TB], in_=pool[pi][:], func=AF.Relu),
                waits=[pe_h1] + dep(S["h2all_free"] if fc == 0 else None))
            P.add("vector", lambda v, pi=pi, fc=fc: v.tensor_tensor(
                out=h2all[:, fc * TB:(fc + 1) * TB],
                in0=h2all[:, fc * TB:(fc + 1) * TB],
                in1=pool[pi][:], op=ALU.mult),
                waits=[ss])
            pool_free[pi] = sv.at()
            sq_ready[fc] = sv.at()

        for d2 in range(NCH):
            pi = next_pool()
            for fc in range(32):
                P.add("tensor", lambda t, d2=d2, fc=fc, pi=pi: t.matmul(
                    out=pool[pi][:],
                    lhsT=wproj[:, fc * 1024 + d2 * 128: fc * 1024 + (d2 + 1) * 128],
                    rhs=h2all[:, fc * TB:(fc + 1) * TB],
                    start=(fc == 0), stop=(fc == 31)),
                    waits=[sq_ready[fc]] + dep(pool_free[pi]),
                    inc=pe if fc == 31 else None)
            pe_y = pe.at()
            P.add("scalar", lambda s, d2=d2, pi=pi: s.copy(
                out=ysb[:, d2 * TB:(d2 + 1) * TB], in_=pool[pi][:]),
                waits=[pe_y] + (dep(S["ysb_free"]) if d2 == 0 else []))
            pool_free[pi] = ss.at()
        S["h2all_free"] = pe.at()
        y_ready = ss.at()

        for h in range(4):
            gh = tb * 4 + h
            rpar = gh % 4
            yv = ysb[:].rearrange("p (c t) -> p c t", c=NCH)[:, :, h * HF:(h + 1) * HF]
            xv = xbt[xpar][:].rearrange("p (c t) -> p c t", c=NCH)[:, :, h * HF:(h + 1) * HF]
            mv = mem[:].rearrange("p (c t) -> p c t", c=NCH)
            sp = spk[:].rearrange("p (c t) -> p c t", c=NCH)
            rv = rsb[rpar][:].rearrange("p (c t) -> p c t", c=NCH)
            P.add("scalar", lambda s, yv=yv, mv=mv: s.copy(out=mv, in_=yv),
                  waits=[y_ready, sv])
            P.add("vector", lambda v, yv=yv, sp=sp: v.tensor_single_scalar(
                out=sp, in_=yv, scalar=1.0, op=ALU.is_gt),
                waits=[y_ready])
            P.add("scalar", lambda s, sp=sp, rv=rv: s.copy(out=rv, in_=sp),
                  waits=[sv] + dep(S["rsb_free"][rpar]))
            for step in range(4):
                P.add("vector", lambda v, yv=yv, mv=mv: v.scalar_tensor_tensor(
                    out=mv, in0=mv, scalar=0.9, op0=ALU.mult, in1=yv, op1=ALU.add),
                    waits=[ss] if step == 0 else [])
                P.add("vector", lambda v, mv=mv, sp=sp: v.tensor_tensor(
                    out=mv, in0=mv, in1=sp, op=ALU.subtract))
                P.add("vector", lambda v, mv=mv, sp=sp: v.tensor_single_scalar(
                    out=sp, in_=mv, scalar=1.0, op=ALU.is_gt))
                P.add("vector", lambda v, sp=sp, rv=rv: v.tensor_tensor(
                    out=rv, in0=rv, in1=sp, op=ALU.add))
            P.add("vector", lambda v, rv=rv, xv=xv: v.scalar_tensor_tensor(
                out=rv, in0=rv, scalar=0.2, op0=ALU.mult, in1=xv, op1=ALU.add))
            r_ready = sv.at()
            if h == 3:
                S["xbt_free"][xpar] = r_ready
                S["ysb_free"] = r_ready

            if len(pend) == 3:
                pend.pop(0)()
            pend.append(lambda gh=gh, r=r_ready: emit_tail(gh, r))

    for f in pend:
        f()
    es.close()




class LNPack:

    def __init__(self, nc, es, TB, tag=""):
        sb = lambda name, shp, dt=F32: es.enter_context(nc.sbuf_tensor(name, shp, dt))
        self.TB = TB
        self.rbf = sb(f"ln_rbf{tag}", [128, NCH * TB], BF16)
        self.r2bf = sb(f"ln_r2bf{tag}", [128, NCH * TB], BF16)
        self.rows = sb(f"ln_rows{tag}", [1, 6 * TB])
        self.statp = es.enter_context(nc.psum_tensor(f"ln_statp{tag}", [1, 2 * TB], F32))
        self.bcp = es.enter_context(nc.psum_tensor(f"ln_bcp{tag}", [128, 2 * TB], F32))
        self.statp_free = None
        self.bcp_free = None
        self.rbuf_free = None


def emit_ln(P, pe, consts, lnp, rsb, xo_s, r_ready, out_free=None):
    sv, ss = P.esem["vector"], P.esem["scalar"]
    TB = lnp.TB
    dep = lambda x: [x] if x is not None else []
    ones_c, ones_r, epsb, consts_ready = consts

    P.add("vector", lambda v: v.tensor_copy(out=lnp.rbf[:], in_=rsb),
          waits=[r_ready])
    rbf_ready = sv.at()
    P.add("scalar", lambda s: s.activation(out=lnp.r2bf[:], in_=rsb, func=AF.Square),
          waits=[r_ready])
    r2_ready = ss.at()
    for c in range(NCH):
        P.add("tensor", lambda t, c=c: t.matmul(
            out=lnp.statp[0:1, 0:TB],
            lhsT=ones_c[:], rhs=lnp.rbf[:, c * TB:(c + 1) * TB],
            start=(c == 0), stop=(c == NCH - 1)),
            waits=[rbf_ready, consts_ready] + dep(lnp.statp_free),
            inc=pe if c == NCH - 1 else None)
    for c in range(NCH):
        P.add("tensor", lambda t, c=c: t.matmul(
            out=lnp.statp[0:1, TB:2 * TB],
            lhsT=ones_c[:], rhs=lnp.r2bf[:, c * TB:(c + 1) * TB],
            start=(c == 0), stop=(c == NCH - 1)),
            waits=[r2_ready], inc=pe if c == NCH - 1 else None)
    pe_stat = pe.at()
    lnp.rbuf_free = pe_stat
    MU, EX2, VAR, RSTD, NMR, SCR = 0, 1, 2, 3, 4, 5
    rw = lambda i: lnp.rows[0:1, i * TB:(i + 1) * TB]
    P.add("scalar", lambda s: s.mul(out=rw(MU), in_=lnp.statp[0:1, 0:TB], mul=1.0 / D),
          waits=[pe_stat])
    P.add("scalar", lambda s: s.mul(out=rw(EX2), in_=lnp.statp[0:1, TB:2 * TB], mul=1.0 / D),
          waits=[pe_stat])
    lnp.statp_free = ss.at()
    P.add("vector", lambda v: v.tensor_tensor(
        out=rw(VAR), in0=rw(MU), in1=rw(MU), op=ALU.mult), waits=[ss])
    P.add("vector", lambda v: v.tensor_tensor(
        out=rw(VAR), in0=rw(EX2), in1=rw(VAR), op=ALU.subtract))
    P.add("scalar", lambda s: s.activation(
        out=rw(SCR), in_=rw(VAR), func=AF.Sqrt, bias=epsb[:]), waits=[sv])
    P.add("vector", lambda v: v.reciprocal(out=rw(RSTD), in_=rw(SCR)), waits=[ss])
    P.add("vector", lambda v: v.tensor_tensor(
        out=rw(NMR), in0=rw(MU), in1=rw(RSTD), op=ALU.mult))
    P.add("scalar", lambda s: s.mul(out=rw(NMR), in_=rw(NMR), mul=-1.0), waits=[sv])
    rows_ready = ss.at()
    P.add("tensor", lambda t: t.matmul(
        out=lnp.bcp[:, 0:TB], lhsT=ones_r[:], rhs=rw(RSTD), start=True, stop=True),
        waits=[rows_ready] + dep(lnp.bcp_free), inc=pe)
    P.add("tensor", lambda t: t.matmul(
        out=lnp.bcp[:, TB:2 * TB], lhsT=ones_r[:], rhs=rw(NMR), start=True, stop=True),
        inc=pe)
    pe_bc = pe.at()
    P.add("vector", lambda v: v.tensor_tensor(
        out=xo_s.rearrange("p (c t) -> p c t", c=NCH),
        in0=rsb.rearrange("p (c t) -> p c t", c=NCH),
        in1=lnp.bcp[:, 0:TB].unsqueeze(1).broadcast_to((128, NCH, TB)),
        op=ALU.mult),
        waits=[pe_bc, sv] + dep(out_free))
    P.add("vector", lambda v: v.tensor_tensor(
        out=xo_s.rearrange("p (c t) -> p c t", c=NCH),
        in0=xo_s.rearrange("p (c t) -> p c t", c=NCH),
        in1=lnp.bcp[:, TB:2 * TB].unsqueeze(1).broadcast_to((128, NCH, TB)),
        op=ALU.add))
    out_ready = sv.at()
    lnp.bcp_free = out_ready
    return out_ready



def emit_stage3(nc, P, x_d, wq_d, wk_d, wv_d, wo_d, bqk_d, bo_d, bv_d, mask_d, id_d, xo_d, pc3=None):
    W = 256
    NW = T // W
    es = ExitStack()
    sb = lambda name, shp, dt=F32: es.enter_context(nc.sbuf_tensor('z3_' + name, shp, dt))
    psum = lambda name, shp, dt=F32: es.enter_context(nc.psum_tensor('z3_' + name, shp, dt))

    wq = sb("wq_s", [128, NCH * 1024], BF16)
    wk = sb("wk_s", [128, NCH * 1024], BF16)
    wv = sb("wv_s", [128, NCH * 1024], BF16)
    wo = sb("wo_s", [128, NCH * 1024], BF16)
    bqk = sb("bqk_s", [128, 16])
    bo_c = sb("bo_s", [128, 8])
    bv_r = sb("bv_s", [1, 1024], BF16)
    maskc = sb("mask_s", [128, 512])
    identb = sb("id_s", [128, 128], BF16)
    ones_c = sb("ones_c", [128, 1], BF16)
    ones_r = sb("ones_r", [1, 128])
    onesr_b = sb("onesr_b", [1, 128], BF16)
    epsb = sb("epsb", [1, 1])
    c0125 = sb("c0125", [128, 1])

    WP = 2 * W
    xw = [sb(f"xw{i}", [128, NCH * WP], BF16) for i in range(2)]
    q_bf = sb("q_bf", [128, NCH * WP], BF16)
    k_bf = sb("k_bf", [128, NCH * WP], BF16)
    vtok = sb("vtok", [128, 4 * 1024], BF16)
    obf = sb("obf", [128, NCH * WP], BF16)
    sm = sb("sm", [128, W])
    p_bf = sb("p_bf", [128, W], BF16)
    pn_bf = sb("pn_bf", [128, W], BF16)
    pt_bf = sb("pt_bf", [128, 2 * 128], BF16)
    nmv = sb("nmv", [128, 2])
    sums = sb("sums", [128, 2])
    tsb = [sb(f"tsb{i}", [128, WP]) for i in range(2)]
    rsb = sb("rsb", [128, NCH * WP])
    xo_s = sb("xo_s", [128, NCH * WP])

    pool = [psum(f"pool{i}", [128, 512]) for i in range(2)]
    ptp = [psum(f"ptp{i}", [128, 128], BF16) for i in range(2)]
    lnp = LNPack(nc, es, WP)

    lg = P.nsem("lg")
    xl = [P.nsem("xl0"), P.nsem("xl1")]
    st = P.nsem("st")
    pe = P.nsem("pe")
    sv = P.esem["vector"]
    ss = P.esem["scalar"]
    sg = P.esem["gpsimd"]

    P.add("gpsimd", lambda g: g.memset(ones_c[:], 1.0))
    P.add("gpsimd", lambda g: g.memset(ones_r[:], 1.0))
    P.add("gpsimd", lambda g: g.memset(onesr_b[:], 1.0))
    P.add("gpsimd", lambda g: g.memset(epsb[:], 1e-5))
    P.add("gpsimd", lambda g: g.memset(c0125[:], 0.125))
    consts_ready = sg.at()
    consts = (ones_c, ones_r, epsb, consts_ready)
    ls3 = P.nsem("ls")
    first3 = [True]
    for c in range(NCH):
        for dst, srcw in ((wq, wq_d), (wk, wk_d), (wv, wv_d), (wo, wo_d)):
            P.add("sync", lambda s, c=c, dst=dst, srcw=srcw: s.dma_start(
                out=dst[:, c * 1024:(c + 1) * 1024], in_=srcw[c]),
                waits=([ls3, pc3] if first3[0] else []), inc=ls3, inc_n=16, is_dma=True)
            first3[0] = False
    P.add("gpsimd", lambda g: g.dma_start(out=bqk[:], in_=bqk_d), inc=lg, inc_n=16, is_dma=True)
    P.add("gpsimd", lambda g: g.dma_start(out=bo_c[:], in_=bo_d), inc=lg, inc_n=16, is_dma=True)
    P.add("gpsimd", lambda g: g.dma_start(out=bv_r[:], in_=bv_d), inc=lg, inc_n=16, is_dma=True)
    P.add("gpsimd", lambda g: g.dma_start(out=maskc[:], in_=mask_d), inc=lg, inc_n=16, is_dma=True)
    P.add("gpsimd", lambda g: g.dma_start(out=identb[:], in_=id_d), inc=lg, inc_n=16, is_dma=True)
    w_ready = lg.at()
    w_big = ls3.at()

    dep = lambda x: [x] if x is not None else []
    pool_free = [None, None]
    ptp_free = [None, None]
    qk_free = None
    vtok_free = None
    obf_free = None
    xw_free = [None, None]
    sm_free = None
    p_free = None
    pn_free = [None, None]
    pt_free = None
    tsb_free = [None, None]
    rsb_free = None
    out_free = None

    x_ready = []
    P.add("sync", lambda s: s.dma_start(
        out=xw[0][:].rearrange("p (c t) -> p c t", c=NCH),
        in_=x_d[:, :, 0:WP].rearrange("c p t -> p c t")),
        waits=[xl[0]], inc=xl[0], inc_n=16, is_dma=True)
    x_ready.append(xl[0].at())

    npool = 0

    def next_pool():
        nonlocal npool
        i = npool % 2
        npool += 1
        return i

    NWP = NW // 2
    for wp in range(NWP):
        t0 = wp * WP
        wpar = wp % 2
        if wp + 1 < NWP:
            pp = (wp + 1) % 2
            P.add("sync", lambda s, wp=wp, pp=pp: s.dma_start(
                out=xw[pp][:].rearrange("p (c t) -> p c t", c=NCH),
                in_=x_d[:, :, (wp + 1) * WP:(wp + 2) * WP].rearrange("c p t -> p c t")),
                waits=dep(xw_free[pp]) + [xl[pp]], inc=xl[pp], inc_n=16, is_dma=True)
            x_ready.append(xl[pp].at())

        for which, wt, dst, boff in (("q", wq, q_bf, 0), ("k", wk, k_bf, 8)):
            for m in range(NCH):
                pi = next_pool()
                for c in range(NCH):
                    P.add("tensor", lambda t, c=c, m=m, pi=pi, wt=wt, wpar=wpar: t.matmul(
                        out=pool[pi][:],
                        lhsT=wt[:, c * 1024 + m * 128: c * 1024 + (m + 1) * 128],
                        rhs=xw[wpar][:, c * WP:(c + 1) * WP],
                        start=(c == 0), stop=(c == NCH - 1)),
                        waits=[w_ready, w_big, x_ready[wp]] + dep(pool_free[pi]) + dep(qk_free if (which == "q" and m == 0) else None),
                        inc=pe if c == NCH - 1 else None)
                pe_m = pe.at()
                P.add("scalar", lambda s, m=m, pi=pi, dst=dst, boff=boff: s.activation(
                    out=dst[:, m * WP:(m + 1) * WP], in_=pool[pi][:],
                    func=AF.Identity, bias=bqk[:, boff + m:boff + m + 1]),
                    waits=[pe_m])
                pool_free[pi] = ss.at()
        qk_ready = ss.at()

        for tt in range(4):
            for half in range(2):
                pi = next_pool()
                for c in range(NCH):
                    P.add("tensor", lambda t, c=c, tt=tt, half=half, pi=pi, wpar=wpar: t.matmul(
                        out=pool[pi][:],
                        lhsT=xw[wpar][:, c * WP + tt * 128: c * WP + tt * 128 + 128],
                        rhs=wv[:, c * 1024 + half * 512: c * 1024 + (half + 1) * 512],
                        start=(c == 0), stop=False),
                        waits=[w_ready, w_big, x_ready[wp]] + dep(pool_free[pi]) + dep(vtok_free if (tt == 0 and half == 0) else None),
                        inc=None)
                P.add("tensor", lambda t, tt=tt, half=half, pi=pi: t.matmul(
                    out=pool[pi][:],
                    lhsT=onesr_b[:],
                    rhs=bv_r[0:1, half * 512:(half + 1) * 512],
                    start=False, stop=True),
                    waits=[consts_ready], inc=pe)
                pe_v = pe.at()
                P.add("vector", lambda v, tt=tt, half=half, pi=pi: v.tensor_copy(
                    out=vtok[:, tt * 1024 + half * 512: tt * 1024 + (half + 1) * 512],
                    in_=pool[pi][:]),
                    waits=[pe_v])
                pool_free[pi] = sv.at()
        vtok_ready = sv.at()

        for sub in range(2):
            so = sub * W
            for qt in range(2):
                pi = next_pool()
                for c in range(NCH):
                    P.add("tensor", lambda t, c=c, qt=qt, pi=pi, so=so: t.matmul(
                        out=pool[pi][:, 0:W],
                        lhsT=q_bf[:, c * WP + so + qt * 128: c * WP + so + qt * 128 + 128],
                        rhs=k_bf[:, c * WP + so: c * WP + so + W],
                        start=(c == 0), stop=(c == NCH - 1)),
                        waits=[qk_ready] + dep(pool_free[pi]),
                        inc=pe if c == NCH - 1 else None)
                pe_sim = pe.at()
                P.add("vector", lambda v, qt=qt, pi=pi: v.tensor_tensor(
                    out=sm[:], in0=pool[pi][:, 0:W], in1=maskc[:, qt * W:(qt + 1) * W],
                    op=ALU.add),
                    waits=[pe_sim, w_ready] + dep(sm_free))
                pool_free[pi] = sv.at()
                P.add("vector", lambda v: v.reduce_max(
                    out=nmv[:, 0:1], in_=sm[:], axis=AX.X, negate=True))
                P.add("vector", lambda v: v.tensor_single_scalar(
                    out=nmv[:, 1:2], in_=nmv[:, 0:1], scalar=0.125, op=ALU.mult))
                nm_ready = sv.at()
                P.add("scalar", lambda s: s.activation(
                    out=p_bf[:], in_=sm[:], func=AF.Exp, scale=c0125[:],
                    bias=nmv[:, 1:2], accum_out=sums[:, 0:1]),
                    waits=[nm_ready, consts_ready] + dep(p_free))
                sm_free = ss.at()
                exp_ready = ss.at()
                P.add("vector", lambda v: v.reciprocal(out=sums[:, 1:2], in_=sums[:, 0:1]),
                      waits=[exp_ready])
                P.add("vector", lambda v, qt=qt: v.tensor_single_scalar(
                    out=pn_bf[:], in_=p_bf[:], scalar=sums[:, 1:2], op=ALU.mult),
                    waits=dep(pn_free[0]) + dep(pn_free[1]))
                p_free = sv.at()
                pn_ready = sv.at()
                for tt in range(2):
                    P.add("tensor", lambda t, tt=tt: t.transpose(
                        out=ptp[tt][:], in_=pn_bf[:, tt * 128:(tt + 1) * 128], identity=identb[:]),
                        waits=[pn_ready, w_ready] + dep(ptp_free[tt]), inc=pe)
                    pe_t = pe.at()
                    P.add("scalar", lambda s, tt=tt: s.copy(
                        out=pt_bf[:, tt * 128:(tt + 1) * 128], in_=ptp[tt][:]),
                        waits=[pe_t] + dep(pt_free if tt == 0 and qt == 0 and sub == 0 else None))
                    ptp_free[tt] = ss.at()
                pn_free[qt] = pe.at()
                pt_ready = ss.at()
                for m in range(NCH):
                    pi = next_pool()
                    for tt in range(2):
                        P.add("tensor", lambda t, m=m, tt=tt, pi=pi, sub=sub: t.matmul(
                            out=pool[pi][:, 0:128],
                            lhsT=vtok[:, (sub * 2 + tt) * 1024 + m * 128: (sub * 2 + tt) * 1024 + (m + 1) * 128],
                            rhs=pt_bf[:, tt * 128:(tt + 1) * 128],
                            start=(tt == 0), stop=(tt == 1)),
                            waits=[pt_ready, vtok_ready] + dep(pool_free[pi]) + dep(obf_free if (sub == 0 and qt == 0 and m == 0) else None),
                            inc=pe if tt == 1 else None)
                    pe_o = pe.at()
                    P.add("scalar", lambda s, m=m, qt=qt, pi=pi, so=so: s.copy(
                        out=obf[:, m * WP + so + qt * 128: m * WP + so + qt * 128 + 128],
                        in_=pool[pi][:, 0:128]),
                        waits=[pe_o])
                    pool_free[pi] = ss.at()
        pt_free = pe.at()
        o_ready = ss.at()

        for e3 in range(NCH):
            pi = next_pool()
            tpar = e3 % 2
            for c in range(NCH):
                P.add("tensor", lambda t, c=c, e3=e3, pi=pi: t.matmul(
                    out=pool[pi][:],
                    lhsT=wo[:, c * 1024 + e3 * 128: c * 1024 + (e3 + 1) * 128],
                    rhs=obf[:, c * WP:(c + 1) * WP],
                    start=(c == 0), stop=(c == NCH - 1)),
                    waits=[o_ready] + dep(pool_free[pi]),
                    inc=pe if c == NCH - 1 else None)
            pe_wo = pe.at()
            P.add("scalar", lambda s, e3=e3, pi=pi, tpar=tpar: s.activation(
                out=tsb[tpar][:], in_=pool[pi][:],
                func=AF.Identity, bias=bo_c[:, e3:e3 + 1]),
                waits=[pe_wo] + dep(tsb_free[tpar]))
            pool_free[pi] = ss.at()
            ev = ss.at()
            P.add("vector", lambda v, e3=e3, tpar=tpar, wpar=wpar: v.scalar_tensor_tensor(
                out=rsb[:, e3 * WP:(e3 + 1) * WP],
                in0=xw[wpar][:, e3 * WP:(e3 + 1) * WP], scalar=2.0, in1=tsb[tpar][:],
                op0=ALU.mult, op1=ALU.add),
                waits=[ev] + dep(rsb_free if e3 == 0 else None))
            tsb_free[tpar] = sv.at()
        r_ready = sv.at()
        xw_free[wpar] = r_ready
        obf_free = pe.at()

        out_ready = emit_ln(P, pe, consts, lnp, rsb[:], xo_s[:], r_ready,
                            out_free=(st, st.v) if st.v > 0 else None)
        rsb_free = out_ready
        qk_free = pe.at()
        vtok_free = pe.at()
        P.add("gpsimd", lambda g, t0=t0: g.dma_start(
            out=xo_d[:, :, t0:t0 + WP].rearrange("c p t -> p c t"),
            in_=xo_s[:].rearrange("p (c t) -> p c t", c=NCH)),
            waits=[out_ready], inc=st, inc_n=16, is_dma=True)

    es.close()



def emit_stage1(nc, P, D1):
    (x_d, wq_d, wk_d, wv_d, wg_d, wab_d, wo_d, bq_d, bk_d, bg_d, bv_d, bab_d,
     bo_d, cum_d, id_d, madd_d, mstr_d, xo_d) = D1
    es = ExitStack()
    sb = lambda name, shp, dt=F32: es.enter_context(nc.sbuf_tensor('z1_' + name, shp, dt))
    psum = lambda name, shp, dt=F32: es.enter_context(nc.psum_tensor('z1_' + name, shp, dt))

    xb = sb("xb", [128, NCH * T], BF16)
    wq = sb("wq_s", [128, NCH * 128], BF16)
    wk = sb("wk_s", [128, NCH * 128], BF16)
    wv = sb("wv_s", [128, NCH * 128], BF16)
    wg = sb("wg_s", [128, NCH * 128], BF16)
    wab = sb("wab_s", [128, NCH * 2], BF16)
    wo = sb("wo_s", [128, 1024], BF16)
    bqc = sb("bqc", [128, 1]); bkc = sb("bkc", [128, 1]); bgc = sb("bgc", [128, 1])
    bvr = sb("bvr", [1, 128], BF16)
    bab = sb("bab_s", [1, 2])
    boc = sb("boc", [128, NCH])
    cumUT = sb("cum_s", [128, 128])
    identf = sb("idf_s", [128, 128])
    identb = sb("idb_s", [128, 128], BF16)
    maskadd = sb("madd_s", [128, 128])
    maskstr = sb("mstr_s", [128, 128], BF16)
    ones_c = sb("ones_c", [128, 1], BF16)
    onesr_f = sb("onesr_f", [1, 128])
    onesr_b = sb("onesr_b", [1, 128], BF16)
    epsln = sb("epsln", [1, 1])
    epsn2 = sb("epsn2", [1, 1])

    qT = sb("qT", [128, T], BF16)
    kT = sb("kT", [128, T], BF16)
    kr2 = sb("kr2", [128, T], BF16)
    gT = sb("gT", [128, T], BF16)
    vtok = sb("vtok", [128, T], BF16)
    arow = sb("arow", [1, T])
    brow = sb("brow", [1, T])
    lnar = sb("lnar", [1, T])
    rnr = sb("rnr", [1, T])

    Zf = sb("Zf", [128, 128]); ZTf = sb("ZTf", [128, 128])
    Zb = sb("Zb", [128, 128], BF16); ZTb = sb("ZTb", [128, 128], BF16)
    lnac = sb("lnac", [128, 2])
    lgc = sb("lgc", [128, 1])
    lgr = sb("lgr", [1, 128])
    nlgr = sb("nlgr", [1, 128])
    grow = sb("grow", [1, 128])
    gLc = sb("gLc", [128, 1])
    gcol = sb("gcol", [128, 1])
    dcol = sb("dcol", [128, 1])
    gdif = sb("gdif", [128, 128])
    gamT = sb("gamT", [128, 128], BF16)
    Dm = sb("Dm", [128, 128], BF16)
    S0b = sb("S0b", [128, 128], BF16)
    S0Tb = sb("S0Tb", [128, 128], BF16)
    S1b = sb("S1b", [128, 128], BF16)
    S1Tb = sb("S1Tb", [128, 128], BF16)
    Ub = sb("Ub", [128, 128], BF16)
    UTb = sb("UTb", [128, 128], BF16)
    Udb = sb("Udb", [128, 128], BF16)
    Rb = sb("Rb", [128, 128], BF16)
    ktk = sb("ktk", [128, 128], BF16)
    qgb = sb("qgb", [128, 128], BF16)
    ptl = sb("ptl", [128, 128], BF16)
    obf = sb("obf", [128, 128], BF16)
    rsb1 = sb("rsb1", [128, NCH * 128])
    rbf1 = sb("rbf1", [128, NCH * 128], BF16)
    r2b1 = sb("r2b1", [128, NCH * 128], BF16)
    xo1 = sb("xo1", [128, NCH * 128], BF16)
    rows1 = sb("rows1", [1, 6 * 128])

    pool = [psum(f"pool{i}", [128, 512]) for i in range(8)]

    lg = P.nsem("lg")
    st = P.nsem("st")
    pe = P.nsem("pe")
    sv = P.esem["vector"]; ss = P.esem["scalar"]; sg = P.esem["gpsimd"]
    dep = lambda x: [x] if x is not None else []
    pool_free = [None] * 8
    _np = [0]

    def next_pool():
        i = _np[0] % 8
        _np[0] += 1
        return i

    def lp(fn):
        def wrapped(engine):
            with nc.allow_low_precision(reason="bf16 operands for matmul chains"):
                return fn(engine)
        return wrapped

    P.add("gpsimd", lambda g: g.memset(ones_c[:], 1.0))
    P.add("gpsimd", lambda g: g.memset(onesr_f[:], 1.0))
    P.add("gpsimd", lambda g: g.memset(onesr_b[:], 1.0))
    P.add("gpsimd", lambda g: g.memset(epsln[:], 1e-5))
    P.add("gpsimd", lambda g: g.memset(epsn2[:], 1e-24))
    P.add("gpsimd", lambda g: g.memset(Zf[:], 0.0))
    P.add("gpsimd", lambda g: g.memset(ZTf[:], 0.0))
    P.add("gpsimd", lambda g: g.memset(Zb[:], 0.0))
    P.add("gpsimd", lambda g: g.memset(ZTb[:], 0.0))
    consts_mem = sg.at()
    for dst, src in ((wq, wq_d), (wk, wk_d), (wv, wv_d), (wg, wg_d), (wab, wab_d)):
        P.add("gpsimd", lambda g, dst=dst, src=src: g.dma_start(
            out=dst[:].rearrange("p (c k) -> p c k", c=NCH), in_=src.rearrange("c p k -> p c k")),
            inc=lg, inc_n=16, is_dma=True)
    P.add("gpsimd", lambda g: g.dma_start(out=wo[:], in_=wo_d), inc=lg, inc_n=16, is_dma=True)
    for dst, src in ((bqc, bq_d), (bkc, bk_d), (bgc, bg_d), (bvr, bv_d), (bab, bab_d),
                     (boc, bo_d), (cumUT, cum_d), (identf, id_d), (identb, id_d),
                     (maskadd, madd_d), (maskstr, mstr_d)):
        P.add("gpsimd", lambda g, dst=dst, src=src: g.dma_start(out=dst[:], in_=src),
              inc=lg, inc_n=16, is_dma=True)
    for c in range(NCH):
        P.add("gpsimd", lambda g, c=c: g.dma_start(
            out=xb[:, c * T:(c + 1) * T], in_=x_d[c]), inc=lg, inc_n=16, is_dma=True)
    w_ready = lg.at()

    TBP = 512
    qT_ready = None
    for tb in range(T // TBP):
        t0 = tb * TBP
        for which, wt, bcol, func, dsts in (
                ("q", wq, bqc, AF.Identity, (qT,)),
                ("k", wk, bkc, None, (kT, kr2)),
                ("g", wg, bgc, AF.Sigmoid, (gT,))):
            pi = next_pool()
            for c in range(NCH):
                P.add("tensor", lambda t, c=c, pi=pi, wt=wt, t0=t0: t.matmul(
                    out=pool[pi][:],
                    lhsT=wt[:, c * 128:(c + 1) * 128],
                    rhs=xb[:, c * T + t0: c * T + t0 + TBP],
                    start=(c == 0), stop=(c == NCH - 1)),
                    waits=[w_ready] + dep(pool_free[pi]),
                    inc=pe if c == NCH - 1 else None)
            pe_m = pe.at()
            if which == "k":
                P.add("scalar", lambda s, pi=pi, t0=t0: s.activation(
                    out=kT[:, t0:t0 + TBP], in_=pool[pi][:], func=AF.Identity, bias=bkc[:]),
                    waits=[pe_m])
                P.add("scalar", lambda s, pi=pi, t0=t0: s.activation(
                    out=kr2[:, t0:t0 + TBP], in_=pool[pi][:], func=AF.Square, bias=bkc[:]))
            else:
                P.add("scalar", lambda s, pi=pi, t0=t0, dst=dsts[0], func=func, bcol=bcol: s.activation(
                    out=dst[:, t0:t0 + TBP], in_=pool[pi][:], func=func, bias=bcol[:]),
                    waits=[pe_m])
            pool_free[pi] = ss.at()
        for tt in range(4):
            pi = next_pool()
            for c in range(NCH):
                P.add("tensor", lambda t, c=c, pi=pi, t0=t0, tt=tt: t.matmul(
                    out=pool[pi][:, 0:128],
                    lhsT=xb[:, c * T + t0 + tt * 128: c * T + t0 + tt * 128 + 128],
                    rhs=wv[:, c * 128:(c + 1) * 128],
                    start=(c == 0), stop=False),
                    waits=[w_ready] + dep(pool_free[pi]),
                    inc=None)
            P.add("tensor", lambda t, pi=pi: t.matmul(
                out=pool[pi][:, 0:128], lhsT=onesr_b[0:1, :],
                rhs=bvr[0:1, :], start=False, stop=True),
                waits=[consts_mem], inc=pe)
            pe_v = pe.at()
            P.add("scalar", lambda s, pi=pi, t0=t0, tt=tt: s.copy(
                out=vtok[:, t0 + tt * 128: t0 + tt * 128 + 128], in_=pool[pi][:, 0:128]),
                waits=[pe_v])
            pool_free[pi] = ss.at()
        for ab in range(2):
            pi = next_pool()
            for c in range(NCH):
                P.add("tensor", lambda t, c=c, pi=pi, ab=ab, t0=t0: t.matmul(
                    out=pool[pi][0:1, :],
                    lhsT=wab[:, c * 2 + ab: c * 2 + ab + 1],
                    rhs=xb[:, c * T + t0: c * T + t0 + TBP],
                    start=(c == 0), stop=(c == NCH - 1)),
                    waits=[w_ready] + dep(pool_free[pi]),
                    inc=pe if c == NCH - 1 else None)
            pe_ab = pe.at()
            P.add("scalar", lambda s, pi=pi, ab=ab, t0=t0: s.activation(
                out=(arow if ab == 0 else brow)[0:1, t0:t0 + TBP], in_=pool[pi][0:1, :],
                func=AF.Sigmoid, bias=bab[0:1, ab:ab + 1]),
                waits=[pe_ab])
            pool_free[pi] = ss.at()
    p1_done = ss.at()

    for tb in range(T // TBP):
        t0 = tb * TBP
        pi = next_pool()
        for blk in range(1):
            P.add("tensor", lambda t, pi=pi, t0=t0: t.matmul(
                out=pool[pi][0:1, :], lhsT=ones_c[:], rhs=kr2[:, t0:t0 + TBP],
                start=True, stop=True),
                waits=[p1_done, consts_mem] + dep(pool_free[pi]), inc=pe)
        pe_n2 = pe.at()
        P.add("vector", lambda v, pi=pi, t0=t0: v.tensor_single_scalar(
            out=rnr[0:1, t0:t0 + TBP], in_=pool[pi][0:1, :], scalar=1e-24, op=ALU.max),
            waits=[pe_n2])
        pool_free[pi] = sv.at()
        P.add("scalar", lambda s, t0=t0: s.activation(
            out=rnr[0:1, t0:t0 + TBP], in_=rnr[0:1, t0:t0 + TBP], func=AF.Ln),
            waits=[sv])
        P.add("scalar", lambda s, t0=t0: s.activation(
            out=rnr[0:1, t0:t0 + TBP], in_=rnr[0:1, t0:t0 + TBP], func=AF.Exp, scale=-0.5))
        rn_ready = ss.at()
        pi2 = next_pool()
        P.add("tensor", lambda t, pi2=pi2, t0=t0: t.matmul(
            out=pool[pi2][:], lhsT=onesr_f[:], rhs=rnr[0:1, t0:t0 + TBP],
            start=True, stop=True),
            waits=[rn_ready] + dep(pool_free[pi2]), inc=pe)
        pe_bc = pe.at()
        P.add("vector", lambda v, pi2=pi2, t0=t0: v.tensor_tensor(
            out=kT[:, t0:t0 + TBP], in0=kT[:, t0:t0 + TBP], in1=pool[pi2][:], op=ALU.mult),
            waits=[pe_bc])
        pool_free[pi2] = sv.at()
    P.add("scalar", lambda s: s.activation(out=lnar[:], in_=arow[:], func=AF.Ln),
          waits=[p1_done])
    p2_done = (ss.at(), sv.at())

    state_ready = consts_mem
    rsb_free = None
    xo_free = None
    rows_free = None
    kn_ready = p2_done[1]
    for ch in range(32):
        t0 = ch * 128
        sl = lambda tsr, t0=t0: tsr[:, t0:t0 + 128]
        pi = next_pool()
        P.add("tensor", lambda t, pi=pi, t0=t0: t.matmul(
            out=pool[pi][:, 0:1], lhsT=lnar[0:1, t0:t0 + 128], rhs=onesr_f[0:1, 0:1],
            start=True, stop=True),
            waits=[p2_done[0], consts_mem] + dep(pool_free[pi]), inc=pe)
        P.add("tensor", lambda t, pi=pi, t0=t0: t.matmul(
            out=pool[pi][:, 1:2], lhsT=brow[0:1, t0:t0 + 128], rhs=onesr_f[0:1, 0:1],
            start=True, stop=True), inc=pe)
        pe_c = pe.at()
        P.add("vector", lambda v, pi=pi: v.tensor_copy(out=lnac[:], in_=pool[pi][:, 0:2]),
              waits=[pe_c])
        pool_free[pi] = sv.at()
        pi = next_pool()
        P.add("tensor", lambda t, pi=pi: t.matmul(
            out=pool[pi][:, 0:1], lhsT=cumUT[:], rhs=lnac[:, 0:1], start=True, stop=True),
            waits=[(sv, sv.v)] + dep(pool_free[pi]), inc=pe)
        pe_lg = pe.at()
        P.add("vector", lambda v, pi=pi: v.tensor_copy(out=lgc[:], in_=pool[pi][:, 0:1]),
              waits=[pe_lg])
        pool_free[pi] = sv.at()
        pi = next_pool()
        P.add("tensor", lambda t, pi=pi: t.matmul(
            out=pool[pi][0:1, :, ] if False else pool[pi][0:1, 0:128],
            lhsT=lgc[:], rhs=identf[:], start=True, stop=True),
            waits=[(sv, sv.v)] + dep(pool_free[pi]), inc=pe)
        pe_lr = pe.at()
        P.add("vector", lambda v, pi=pi: v.tensor_copy(out=lgr[:], in_=pool[pi][0:1, 0:128]),
              waits=[pe_lr])
        pool_free[pi] = sv.at()
        P.add("scalar", lambda s: s.mul(out=nlgr[:], in_=lgr[:], mul=-1.0), waits=[sv])
        P.add("scalar", lambda s: s.activation(out=grow[:], in_=lgr[:], func=AF.Exp))
        P.add("scalar", lambda s: s.activation(out=gcol[:], in_=lgc[:], func=AF.Exp))
        row_ready = ss.at()
        pi = next_pool()
        P.add("tensor", lambda t, pi=pi: t.matmul(
            out=pool[pi][:, 0:1], lhsT=onesr_f[:], rhs=grow[0:1, 127:128],
            start=True, stop=True),
            waits=[row_ready] + dep(pool_free[pi]), inc=pe)
        pe_gL = pe.at()
        P.add("vector", lambda v, pi=pi: v.tensor_copy(out=gLc[:], in_=pool[pi][:, 0:1]),
              waits=[pe_gL])
        pool_free[pi] = sv.at()
        pi = next_pool()
        P.add("tensor", lambda t, pi=pi: t.matmul(
            out=pool[pi][:, 0:128], lhsT=nlgr[0:1, :], rhs=onesr_f[0:1, :],
            start=True, stop=False),
            waits=[row_ready] + dep(pool_free[pi]), inc=None)
        P.add("tensor", lambda t, pi=pi: t.matmul(
            out=pool[pi][:, 0:128], lhsT=onesr_f[0:1, :], rhs=lgr[0:1, :],
            start=False, stop=True), inc=pe)
        pe_gd = pe.at()
        P.add("vector", lambda v, pi=pi: v.tensor_tensor(
            out=gdif[:], in0=pool[pi][:, 0:128], in1=maskadd[:], op=ALU.add),
            waits=[pe_gd, consts_mem])
        pool_free[pi] = sv.at()
        P.add("scalar", lambda s: s.activation(out=gamT[:], in_=gdif[:], func=AF.Exp),
              waits=[sv])
        gam_ready = ss.at()
        P.add("vector", lambda v: v.tensor_tensor(
            out=dcol[:], in0=gamT[:, 127:128], in1=lnac[:, 1:2], op=ALU.mult),
            waits=[gam_ready])
        pi = next_pool()
        P.add("tensor", lambda t, pi=pi, t0=t0: t.matmul(
            out=pool[pi][:, 0:128], lhsT=kT[:, t0:t0 + 128], rhs=kT[:, t0:t0 + 128],
            start=True, stop=True),
            waits=[kn_ready] + dep(pool_free[pi]), inc=pe)
        pe_D = pe.at()
        P.add("vector", lambda v, pi=pi: v.tensor_tensor(
            out=Dm[:], in0=pool[pi][:, 0:128], in1=maskstr[:], op=ALU.mult),
            waits=[pe_D, consts_mem])
        pool_free[pi] = sv.at()
        P.add("vector", lambda v: v.tensor_single_scalar(
            out=S0Tb[:], in_=gamT[:], scalar=lnac[:, 1:2], op=ALU.mult),
            waits=[gam_ready])
        P.add("vector", lambda v: v.tensor_tensor(
            out=S0Tb[:], in0=S0Tb[:], in1=Dm[:], op=ALU.mult))
        P.add("vector", lp(lambda v: v.tensor_single_scalar(
            out=S0Tb[:], in_=S0Tb[:], scalar=-1.0, op=ALU.mult)))
        s0t_ready = sv.at()
        pi = next_pool()
        P.add("tensor", lambda t, pi=pi: t.transpose(
            out=pool[pi][:, 0:128].bitcast(BF16)[:, 0:128], in_=S0Tb[:], identity=identb[:]),
            waits=[s0t_ready, consts_mem] + dep(pool_free[pi]), inc=pe)
        pe_t0 = pe.at()
        P.add("vector", lambda v, pi=pi: v.tensor_copy(
            out=S0b[:], in_=pool[pi][:, 0:128].bitcast(BF16)[:, 0:128]),
            waits=[pe_t0])
        pool_free[pi] = sv.at()
        pi = next_pool()
        P.add("tensor", lambda t, pi=pi, t0=t0: t.transpose(
            out=pool[pi][:, 0:128].bitcast(BF16)[:, 0:128], in_=kT[:, t0:t0 + 128],
            identity=identb[:]),
            waits=[kn_ready, consts_mem] + dep(pool_free[pi]), inc=pe)
        pe_kt = pe.at()
        P.add("vector", lambda v, pi=pi: v.tensor_copy(
            out=ktk[:], in_=pool[pi][:, 0:128].bitcast(BF16)[:, 0:128]),
            waits=[pe_kt])
        pool_free[pi] = sv.at()
        ktk_ready = sv.at()
        pi = next_pool()
        P.add("tensor", lambda t, pi=pi, t0=t0: t.matmul(
            out=pool[pi][:, 0:128], lhsT=kT[:, t0:t0 + 128], rhs=Zb[:], start=True, stop=True),
            waits=[kn_ready, state_ready] + dep(pool_free[pi]), inc=pe)
        pe_R = pe.at()
        P.add("vector", lambda v, pi=pi, t0=t0: v.scalar_tensor_tensor(
            out=Rb[:], in0=pool[pi][:, 0:128], scalar=gcol[:], op0=ALU.mult,
            in1=vtok[:, t0:t0 + 128], op1=ALU.subtract),
            waits=[pe_R, row_ready])
        pool_free[pi] = sv.at()
        P.add("vector", lp(lambda v: v.tensor_single_scalar(
            out=Rb[:], in_=Rb[:], scalar=-1.0, op=ALU.mult)))
        ub, utmp = Rb, Ub
        sa, sat = S0b, S0Tb
        sb_, sbt = S1b, S1Tb
        for it in range(7):
            pi = next_pool()
            P.add("tensor", lambda t, pi=pi, sat=sat, ub=ub: t.matmul(
                out=pool[pi][:, 0:128], lhsT=sat[:], rhs=ub[:], start=True, stop=True),
                waits=[(sv, sv.v)] + dep(pool_free[pi]), inc=pe)
            pe_u = pe.at()
            P.add("vector", lambda v, pi=pi, ub=ub, utmp=utmp: v.tensor_tensor(
                out=utmp[:], in0=ub[:], in1=pool[pi][:, 0:128], op=ALU.add),
                waits=[pe_u])
            pool_free[pi] = sv.at()
            ub, utmp = utmp, ub
            if it < 6:
                pi1 = next_pool()
                P.add("tensor", lambda t, pi1=pi1, sa=sa, sat=sat: t.matmul(
                    out=pool[pi1][:, 0:128], lhsT=sat[:], rhs=sa[:], start=True, stop=True),
                    waits=dep(pool_free[pi1]), inc=pe)
                pi2 = next_pool()
                P.add("tensor", lambda t, pi2=pi2, sa=sa, sat=sat: t.matmul(
                    out=pool[pi2][:, 0:128], lhsT=sa[:], rhs=sat[:], start=True, stop=True),
                    waits=dep(pool_free[pi2]), inc=pe)
                pe_s = pe.at()
                P.add("vector", lambda v, pi1=pi1, sb_=sb_: v.tensor_copy(
                    out=sb_[:], in_=pool[pi1][:, 0:128]), waits=[pe_s])
                pool_free[pi1] = sv.at()
                P.add("vector", lambda v, pi2=pi2, sbt=sbt: v.tensor_copy(
                    out=sbt[:], in_=pool[pi2][:, 0:128]))
                pool_free[pi2] = sv.at()
                sa, sb_ = sb_, sa
                sat, sbt = sbt, sat
        Ufin = ub
        u_ready = sv.at()
        pi = next_pool()
        P.add("tensor", lambda t, pi=pi, Ufin=Ufin: t.transpose(
            out=pool[pi][:, 0:128].bitcast(BF16)[:, 0:128], in_=Ufin[:], identity=identb[:]),
            waits=[u_ready] + dep(pool_free[pi]), inc=pe)
        pe_ut = pe.at()
        P.add("vector", lambda v, pi=pi: v.tensor_copy(
            out=UTb[:], in_=pool[pi][:, 0:128].bitcast(BF16)[:, 0:128]),
            waits=[pe_ut])
        pool_free[pi] = sv.at()
        pi = next_pool()
        P.add("tensor", lambda t, pi=pi, t0=t0: t.matmul(
            out=pool[pi][:, 0:128], lhsT=UTb[:], rhs=qT[:, t0:t0 + 128], start=True, stop=True),
            waits=[(sv, sv.v), p1_done] + dep(pool_free[pi]), inc=pe)
        pe_g1 = pe.at()
        P.add("vector", lambda v, pi=pi: v.scalar_tensor_tensor(
            out=ptl[:], in0=pool[pi][:, 0:128], scalar=lnac[:, 1:2], op0=ALU.mult,
            in1=gamT[:], op1=ALU.mult),
            waits=[pe_g1])
        pool_free[pi] = sv.at()
        pi = next_pool()
        P.add("tensor", lambda t, pi=pi: t.matmul(
            out=pool[pi][:, 0:128], lhsT=onesr_f[0:1, :], rhs=grow[0:1, :],
            start=True, stop=True),
            waits=[row_ready] + dep(pool_free[pi]), inc=pe)
        pe_gb = pe.at()
        P.add("vector", lambda v, pi=pi, t0=t0: v.tensor_tensor(
            out=qgb[:], in0=qT[:, t0:t0 + 128], in1=pool[pi][:, 0:128], op=ALU.mult),
            waits=[pe_gb])
        pool_free[pi] = sv.at()
        q_ready = sv.at()
        pi = next_pool()
        P.add("tensor", lambda t, pi=pi: t.matmul(
            out=pool[pi][:, 0:128], lhsT=ktk[:], rhs=ptl[:], start=True, stop=False),
            waits=[q_ready, ktk_ready] + dep(pool_free[pi]), inc=None)
        P.add("tensor", lambda t, pi=pi: t.matmul(
            out=pool[pi][:, 0:128], lhsT=ZTb[:], rhs=qgb[:], start=False, stop=True),
            waits=[state_ready], inc=pe)
        pe_o = pe.at()
        P.add("vector", lambda v, pi=pi, t0=t0: v.tensor_tensor(
            out=obf[:], in0=pool[pi][:, 0:128], in1=gT[:, t0:t0 + 128], op=ALU.mult),
            waits=[pe_o])
        pool_free[pi] = sv.at()
        o_ready = sv.at()
        P.add("vector", lambda v, Ufin=Ufin: v.tensor_single_scalar(
            out=Udb[:], in_=Ufin[:], scalar=dcol[:], op=ALU.mult),
            waits=[sv])
        ud_ready = sv.at()
        pi = next_pool()
        P.add("tensor", lambda t, pi=pi: t.matmul(
            out=pool[pi][:, 0:128], lhsT=ktk[:], rhs=Udb[:], start=True, stop=True),
            waits=[ud_ready] + dep(pool_free[pi]), inc=pe)
        pe_z = pe.at()
        P.add("vector", lambda v, pi=pi: v.scalar_tensor_tensor(
            out=Zf[:], in0=Zf[:], scalar=gLc[:], op0=ALU.mult,
            in1=pool[pi][:, 0:128], op1=ALU.add),
            waits=[pe_z])
        pool_free[pi] = sv.at()
        P.add("vector", lambda v: v.tensor_copy(out=Zb[:], in_=Zf[:]))
        pi = next_pool()
        P.add("tensor", lambda t, pi=pi: t.matmul(
            out=pool[pi][:, 0:128], lhsT=Udb[:], rhs=ktk[:], start=True, stop=True),
            waits=[ud_ready] + dep(pool_free[pi]), inc=pe)
        pe_zt = pe.at()
        P.add("vector", lambda v, pi=pi: v.scalar_tensor_tensor(
            out=ZTf[:], in0=ZTf[:], scalar=gLc[:], op0=ALU.mult,
            in1=pool[pi][:, 0:128], op1=ALU.add),
            waits=[pe_zt])
        pool_free[pi] = sv.at()
        P.add("vector", lambda v: v.tensor_copy(out=ZTb[:], in_=ZTf[:]))
        state_ready = sv.at()
        for e in range(NCH):
            pi = next_pool()
            P.add("tensor", lambda t, pi=pi, e=e: t.matmul(
                out=pool[pi][:, 0:128], lhsT=wo[:, e * 128:(e + 1) * 128], rhs=obf[:],
                start=True, stop=True),
                waits=[o_ready] + dep(pool_free[pi]), inc=pe)
            pe_wo = pe.at()
            P.add("vector", lambda v, pi=pi, e=e, t0=t0: v.scalar_tensor_tensor(
                out=rsb1[:, e * 128:(e + 1) * 128],
                in0=pool[pi][:, 0:128], scalar=boc[:, e:e + 1], op0=ALU.add,
                in1=xb[:, e * T + t0: e * T + t0 + 128], op1=ALU.add),
                waits=[pe_wo] + (dep(rsb_free) if e == 0 else []))
            pool_free[pi] = sv.at()
        r_ready = sv.at()
        P.add("vector", lambda v: v.tensor_copy(out=rbf1[:], in_=rsb1[:]),
              waits=[r_ready])
        rb_ready = sv.at()
        P.add("scalar", lambda s: s.activation(out=r2b1[:], in_=rbf1[:], func=AF.Square),
              waits=[rb_ready])
        r2_ready = ss.at()
        piA = next_pool()
        for c in range(NCH):
            P.add("tensor", lambda t, c=c, piA=piA: t.matmul(
                out=pool[piA][0:1, 0:128], lhsT=ones_c[:], rhs=rbf1[:, c * 128:(c + 1) * 128],
                start=(c == 0), stop=(c == NCH - 1)),
                waits=[rb_ready, consts_mem] + dep(pool_free[piA]),
                inc=pe if c == NCH - 1 else None)
        peA = pe.at()
        piB = next_pool()
        for c in range(NCH):
            P.add("tensor", lambda t, c=c, piB=piB: t.matmul(
                out=pool[piB][0:1, 0:128], lhsT=ones_c[:], rhs=r2b1[:, c * 128:(c + 1) * 128],
                start=(c == 0), stop=(c == NCH - 1)),
                waits=[r2_ready] + dep(pool_free[piB]),
                inc=pe if c == NCH - 1 else None)
        peB = pe.at()
        rsb_free = peB
        MU, EX2, VAR, RSTD, NMR, SCR = 0, 1, 2, 3, 4, 5
        rw = lambda i: rows1[0:1, i * 128:(i + 1) * 128]
        P.add("scalar", lambda s, piA=piA: s.mul(out=rw(MU), in_=pool[piA][0:1, 0:128], mul=1.0 / D),
              waits=[peA] + dep(rows_free))
        P.add("scalar", lambda s, piB=piB: s.mul(out=rw(EX2), in_=pool[piB][0:1, 0:128], mul=1.0 / D),
              waits=[peB])
        pool_free[piA] = ss.at()
        pool_free[piB] = ss.at()
        P.add("vector", lambda v: v.tensor_tensor(out=rw(VAR), in0=rw(MU), in1=rw(MU), op=ALU.mult),
              waits=[ss])
        P.add("vector", lambda v: v.tensor_tensor(out=rw(VAR), in0=rw(EX2), in1=rw(VAR), op=ALU.subtract))
        P.add("scalar", lambda s: s.activation(out=rw(SCR), in_=rw(VAR), func=AF.Sqrt, bias=epsln[:]),
              waits=[sv])
        P.add("vector", lambda v: v.reciprocal(out=rw(RSTD), in_=rw(SCR)), waits=[ss])
        P.add("vector", lambda v: v.tensor_tensor(out=rw(NMR), in0=rw(MU), in1=rw(RSTD), op=ALU.mult))
        P.add("scalar", lambda s: s.mul(out=rw(NMR), in_=rw(NMR), mul=-1.0), waits=[sv])
        rows_ready = ss.at()
        pi1 = next_pool()
        P.add("tensor", lambda t, pi1=pi1: t.matmul(
            out=pool[pi1][:, 0:128], lhsT=onesr_f[:], rhs=rw(RSTD), start=True, stop=True),
            waits=[rows_ready] + dep(pool_free[pi1]), inc=pe)
        pi2 = next_pool()
        P.add("tensor", lambda t, pi2=pi2: t.matmul(
            out=pool[pi2][:, 0:128], lhsT=onesr_f[:], rhs=rw(NMR), start=True, stop=True),
            waits=dep(pool_free[pi2]), inc=pe)
        pe_bc2 = pe.at()
        rows_free = pe_bc2
        P.add("vector", lambda v, pi1=pi1: v.tensor_tensor(
            out=xo1[:].rearrange("p (c t) -> p c t", c=NCH),
            in0=rsb1[:].rearrange("p (c t) -> p c t", c=NCH),
            in1=pool[pi1][:, 0:128].unsqueeze(1).broadcast_to((128, NCH, 128)),
            op=ALU.mult),
            waits=[pe_bc2] + dep(xo_free))
        pool_free[pi1] = sv.at()
        P.add("vector", lambda v, pi2=pi2: v.tensor_tensor(
            out=xo1[:].rearrange("p (c t) -> p c t", c=NCH),
            in0=xo1[:].rearrange("p (c t) -> p c t", c=NCH),
            in1=pool[pi2][:, 0:128].unsqueeze(1).broadcast_to((128, NCH, 128)),
            op=ALU.add))
        pool_free[pi2] = sv.at()
        out_ready = sv.at()
        P.add("gpsimd", lambda g, t0=t0: g.dma_start(
            out=xo_d[:, :, t0:t0 + 128].rearrange("c p t -> p c t"),
            in_=xo1[:].rearrange("p (c t) -> p c t", c=NCH)),
            waits=[out_ready, st], inc=st, inc_n=16, is_dma=True)
        xo_free = (st, st.v)

    es.close()


def stage1_inputs(xfm, inputs):
    g = lambda k: np.asarray(inputs[k], np.float32)
    pre = "L0_attn_"
    r128 = lambda w: np.ascontiguousarray(g(pre + w).reshape(NCH, 128, 128))
    wab = np.ascontiguousarray(
        np.concatenate([g(pre + "Wa"), g(pre + "Wb")], axis=1).reshape(NCH, 128, 2))
    i_ = np.arange(128)
    cumUT = (i_[:, None] <= i_[None, :]).astype(np.float32)
    ident = np.eye(128, dtype=np.float32)
    maskadd = np.where(i_[:, None] <= i_[None, :], 0.0, -1e30).astype(np.float32)
    maskstr = (i_[:, None] < i_[None, :]).astype(np.float32)
    return {"x": xfm, "wq": r128("Wq"), "wk": r128("Wk"), "wv": r128("Wv"),
            "wg": r128("Wg"), "wab": wab,
            "wo": np.ascontiguousarray(g(pre + "Wo")),
            "bq_c": g(pre + "bq").reshape(128, 1), "bk_c": g(pre + "bk").reshape(128, 1),
            "bg_c": g(pre + "bg").reshape(128, 1), "bv_r": g(pre + "bv").reshape(1, 128),
            "bab": np.concatenate([g(pre + "ba"), g(pre + "bb")]).reshape(1, 2),
            "bo_c": np.ascontiguousarray(g(pre + "bo").reshape(NCH, 128).T),
            "cumUT": cumUT, "identf": ident, "maskadd": maskadd, "maskstrict": maskstr}



HID = 2730
HIDP = 2816
NFC = HIDP // 128
PADT = T + 1
W4 = 512
NW4 = T // W4


def emit_stage4(nc, P, x_d, xo_d, B):
    dram = lambda name, shp, dt=F32, kind="ExternalInput": nc.dram_tensor(name, shp, dt, kind=kind).ap()
    wqkv_d = B["wqkvB"]
    wo4_d = B["wo4B"]
    w1a_d = B["w1aB"]
    w1g_d = B["w1gB"]
    w2p_d = B["w2pB"]
    pcast = B["pcast"]
    pc_gates = B["pc_gates"]
    b1a_d = dram("b1a_c", [2, 128, NFC])
    b1g_d = dram("b1g_c", [2, 128, NFC])
    b2_d = dram("b2_c", [2, 128, NCH])
    gb_d = dram("gb_r", [2, NCH, 2, 128])
    qs_d = dram("qs_c", [2, 128, 2])
    ks_d = dram("ks_c", [2, 128, 2])
    csq_d, snq_d, csk_d, snk_d = (B["tabB"][i] for i in range(4))
    maskb_d = dram("maskb", [128, 2 * 128])
    id_d = dram("ident", [128, 128])
    p32_d = dram("p32", [128, 128])
    sel8_d = dram("sel8", [128, 4 * 8])
    selq_d = dram("selq", [8, 4 * 128])
    selk_d = dram("selk", [8, 4 * 128])
    e8_d = dram("e8", [1, 64])
    onesr_d = dram("ones512", [1, 512])
    dbounce_d = dram("dbounce", [32, 128], kind="Internal")
    xpost_d = nc.dram_tensor("xpost", [NCH, 128, PADT], BF16, kind="Internal").ap()
    xnew_d = nc.dram_tensor("xnew", [NCH, 128, PADT], BF16, kind="Internal").ap()

    es = ExitStack()
    sbP = lambda name, shp, dt=F32: es.enter_context(nc.sbuf_tensor('z4_' + name, shp, dt))
    psum = lambda name, shp, dt=F32: es.enter_context(nc.psum_tensor('z4_' + name, shp, dt))

    maskb = sbP("maskb_s", [128, 256], BF16)
    identb = sbP("id_s", [128, 128], BF16)
    p32b = sbP("p32_s", [128, 128], BF16)
    sel8 = sbP("sel8_s", [128, 32], BF16)
    selq = sbP("selq_s", [8, 512], BF16)
    selk = sbP("selk_s", [8, 512], BF16)
    e8r = sbP("e8_s", [1, 64], BF16)
    ones_c = sbP("ones_c", [128, 1], BF16)
    identf = sbP("identf", [128, 128])
    onesr_f = sbP("onesr_f", [1, 128])
    epsb = sbP("epsb", [1, 1])
    epsln = sbP("epsln", [1, 1])
    neghf = sbP("neghf", [1, 1])
    qsc = sbP("qsc", [128, 2])
    ksc = sbP("ksc", [128, 2])
    b1ac = sbP("b1ac", [128, NFC])
    b1gc = sbP("b1gc", [128, NFC])
    b2c = sbP("b2c", [128, NCH])


    pool = [psum(f"pool{i}", [128, 512]) for i in range(8)]

    lg = P.nsem("lg")
    ls = P.nsem("ls")
    hl = [P.nsem("hl0"), P.nsem("hl1")]
    stp = P.nsem("stp")
    sto = P.nsem("sto")
    db = P.nsem("db")
    pe = P.nsem("pe")
    sv = P.esem["vector"]; ss = P.esem["scalar"]; sg = P.esem["gpsimd"]

    dep = lambda x: [x] if x is not None else []
    pool_free = [None] * 8
    _np = [0]

    def next_pool():
        i = _np[0] % 8
        _np[0] += 1
        return i

    def lp(fn):
        def wrapped(engine):
            with nc.allow_low_precision(reason="rn/rs feed bf16 matmul operands"):
                return fn(engine)
        return wrapped

    def gdma(dst_ap, src_ap, waits=()):
        P.add("gpsimd", lambda g, d=dst_ap, s=src_ap: g.dma_start(out=d, in_=s),
              waits=waits, inc=lg, inc_n=16, is_dma=True)
        return lg.at()

    def sdma(dst_ap, src_ap, waits=()):
        P.add("sync", lambda e, d=dst_ap, s=src_ap: e.dma_start(out=d, in_=s),
              waits=waits, inc=ls, inc_n=16, is_dma=True)
        return ls.at()

    P.add("gpsimd", lambda g: g.memset(ones_c[:], 1.0))
    P.add("gpsimd", lambda g: g.memset(onesr_f[:], 1.0))
    P.add("gpsimd", lambda g: g.memset(epsb[:], 1e-12))
    P.add("gpsimd", lambda g: g.memset(epsln[:], 1e-5))
    P.add("gpsimd", lambda g: g.memset(neghf[:], -0.5))
    consts_mem = sg.at()
    for dst, src in ((maskb, maskb_d), (identb, id_d), (p32b, p32_d), (sel8, sel8_d),
                     (selq, selq_d), (selk, selk_d), (e8r, e8_d)):
        gdma(dst[:], src)
    P.add("sync", lambda s: s.dma_start(out=identf[:], in_=id_d),
          waits=[ls], inc=ls, inc_n=16, is_dma=True)
    identf_ready = ls.at()
    consts_ready = lg.at()
    init_sync = ls.at()

    xsrc_prev_stores = None

    def layer_phase(l):
        nonlocal xsrc_prev_stores
        esA = ExitStack()
        sbA = lambda name, shp, dt=F32: esA.enter_context(nc.sbuf_tensor('z4_' + name, shp, dt))
        xwin = [sbA(f"xwin{l}_{i}", [128, NCH * 513], BF16) for i in range(2)]
        xatt = sbA(f"xatt{l}", [128, NCH * W4], BF16)
        wqkv = sbA(f"wqkv{l}", [128, NCH * 1536], BF16)
        wo4 = sbA(f"wo4_{l}", [128, 4 * 1024], BF16)
        csq = sbA(f"csq{l}", [128, T], BF16)
        snq = sbA(f"snq{l}", [128, T], BF16)
        csk = sbA(f"csk{l}", [128, T], BF16)
        snk = sbA(f"snk{l}", [128, T], BF16)
        qext = [sbA(f"qext{l}_{i}", [65, 8 * 512], BF16) for i in range(2)]
        kext = [sbA(f"kext{l}_{i}", [65, 8 * 512], BF16) for i in range(3)]
        vext = [sbA(f"vext{l}_{i}", [128, 4 * 520], BF16) for i in range(3)]
        ptbuf = [sbA(f"ptbuf{l}_{i}", [128, 8 * 512], BF16) for i in range(2)]
        qraw = {s: sbA(f"qraw{l}{s}", [128, 2048], BF16) for s in "qk"}
        q2b = {s: sbA(f"q2b{l}{s}", [128, 2048], BF16) for s in "qk"}
        sw_bf = q2b
        t2b = {s: sbA(f"t2b{l}{s}", [128, 2048], BF16) for s in "qk"}
        qtl = qraw
        n2s = sbA(f"n2s{l}", [8, 512])
        rn8 = sbA(f"rn8{l}", [8, 512], BF16)
        nmcols = sbA(f"nmcols{l}", [128, 32])
        nmrows = sbA(f"nmrows{l}", [32, 128])
        sumsall = sbA(f"sumsall{l}", [1, 8 * 512], BF16)
        rs8 = sbA(f"rs8{l}", [8, 512], BF16)
        bcsb = sbA(f"bcsb{l}", [128, 512], BF16)
        oall = sbA(f"oall{l}", [128, 4 * 512], BF16)
        tmp32 = sbA(f"tmp32{l}", [128, 4])

        ph_barrier = [pe, sv, ss, sg, stp, ls]
        barrier_w = [s.at() for s in ph_barrier]

        for i in range(3):
            P.add("gpsimd", lambda g, i=i, kext=kext: g.memset(kext[i][64:65, :], 1.0),
                  waits=barrier_w if i == 0 else [])
            P.add("gpsimd", lambda g, i=i, vext=vext: g.memset(vext[i][:], 0.0))
            P.add("gpsimd", lambda g, i=i, vext=vext: g.memset(
                vext[i][:].rearrange("p (tt h e) -> p tt h e", tt=4, h=8)[:, :, :, 64:65], 1.0))
        P.add("gpsimd", lambda g, ptbuf=ptbuf: g.memset(ptbuf[0][:], 0.0))
        P.add("gpsimd", lambda g, ptbuf=ptbuf: g.memset(ptbuf[1][:], 0.0))
        attn_init = sg.at()

        ls4 = P.nsem("ls")
        first = True
        for c in range(NCH):
            P.add("sync", lambda s, c=c, wqkv=wqkv: s.dma_start(
                out=wqkv[:, c * 1536:(c + 1) * 1536], in_=wqkv_d[l, c]),
                waits=barrier_w + [pc_gates["attn"][l]] + ([ls4] if first else []),
                inc=ls4, inc_n=16, is_dma=True)
            first = False
        for cc in range(4):
            P.add("sync", lambda s, cc=cc, wo4=wo4: s.dma_start(
                out=wo4[:, cc * 1024:(cc + 1) * 1024], in_=wo4_d[l, cc]),
                inc=ls4, inc_n=16, is_dma=True)
        for dst, srcB in ((csq, csq_d), (snq, snq_d), (csk, csk_d), (snk, snk_d)):
            P.add("sync", lambda s, dst=dst, srcB=srcB: s.dma_start(out=dst[:], in_=srcB),
                  inc=ls4, inc_n=16, is_dma=True)
        sync_w = ls4.at()
        sdma(qsc[:], qs_d[l], waits=barrier_w + [ls])
        sdma(ksc[:], ks_d[l], waits=barrier_w)
        tabs_loaded = ls.at()
        xsrc = x_d if l == 0 else xnew_d
        xoff = 0 if l == 0 else 1
        w_ready = sync_w

        P.add("gpsimd", lambda g, csq=csq: g.tensor_single_scalar(
            out=csq[:], in_=csq[:], scalar=qsc[:, 0:1], op=ALU.mult),
            waits=[w_ready, tabs_loaded])
        P.add("gpsimd", lambda g, snq=snq: g.tensor_single_scalar(
            out=snq[:], in_=snq[:], scalar=qsc[:, 1:2], op=ALU.mult))
        P.add("gpsimd", lambda g, csk=csk: g.tensor_single_scalar(
            out=csk[:], in_=csk[:], scalar=ksc[:, 0:1], op=ALU.mult))
        P.add("gpsimd", lambda g, snk=snk: g.tensor_single_scalar(
            out=snk[:], in_=snk[:], scalar=ksc[:, 1:2], op=ALU.mult))
        tabs_ready = sg.at()

        qraw_free = {"q": None, "k": None}
        q2b_free = {"q": None, "k": None}
        sw_free = {"q": None, "k": None}
        t2b_free = {"q": None, "k": None}
        qtl_free = {"q": None, "k": None}
        qext_free = [None, None]
        kext_free = [attn_init, attn_init, attn_init]
        vext_free = [attn_init, attn_init, attn_init]
        ptbuf_free = [attn_init, attn_init]
        oall_free = None
        nmcols_free = None
        sumsall_free = None
        dball = None
        prev_resid = None

        wl = [P.nsem("wl0"), P.nsem("wl1")]
        xwin_free = [None, None]
        x_ready = []
        P.add("gpsimd", lambda g: g.memset(
            xwin[0][:].rearrange("p (c t) -> p c t", c=NCH)[:, :, 0:1], 0.0),
            waits=barrier_w)
        w0z = sg.at()
        P.add("sync", lambda s, xsrc=xsrc, xoff=xoff: s.dma_start(
            out=xwin[0][:].rearrange("p (c t) -> p c t", c=NCH)[:, :, 1:513],
            in_=xsrc[:, :, xoff:xoff + 512].rearrange("c p t -> p c t")),
            waits=barrier_w + [wl[0]] + ([xsrc_prev_stores[0]] if l else []),
            inc=wl[0], inc_n=16, is_dma=True)
        x_ready.append((wl[0], wl[0].v))

        pendA = [None]
        attn_store_evt = []
        for w in range(NW4):
            t0 = w * W4
            cur = w % 2
            kcur = w % 3
            kprv = (w - 1) % 3

            def hs(c, a, n, cur=cur, xwin=xwin):
                o = 0 if c >= 4 else 1
                return xwin[cur][:, c * 513 + o + a: c * 513 + o + a + n]

            for side, base_m in (("q", 0), ("k", 4)):
                qraw_s, q2b_s, sw_s, t2b_s, qtl_s = (
                    qraw[side], q2b[side], sw_bf[side], t2b[side], qtl[side])
                for mm in range(4):
                    m = base_m + mm
                    pi = next_pool()
                    for c in range(NCH):
                        P.add("tensor", lambda t, c=c, m=m, pi=pi, hs=hs: t.matmul(
                            out=pool[pi][:],
                            lhsT=wqkv[:, c * 1536 + m * 128: c * 1536 + (m + 1) * 128],
                            rhs=hs(c, 0, W4),
                            start=(c == 0), stop=(c == NCH - 1)),
                            waits=[w_ready, x_ready[w]] + dep(pool_free[pi]) +
                                  dep(qraw_free[side] if mm == 0 else None) +
                                  ([w0z] if w == 0 else []),
                            inc=pe if c == NCH - 1 else None)
                    pe_m = pe.at()
                    P.add("scalar", lambda s, mm=mm, pi=pi, qraw_s=qraw_s: s.copy(
                        out=qraw_s[:, mm * 512:(mm + 1) * 512], in_=pool[pi][:]),
                        waits=[pe_m])
                    P.add("scalar", lambda s, mm=mm, pi=pi, q2b_s=q2b_s: s.activation(
                        out=q2b_s[:, mm * 512:(mm + 1) * 512], in_=pool[pi][:], func=AF.Square),
                        waits=dep(q2b_free[side] if mm == 0 else None))
                    pool_free[pi] = ss.at()
                raw_ready = ss.at()
                pi = next_pool()
                for ct in range(4):
                    P.add("tensor", lambda t, ct=ct, pi=pi, q2b_s=q2b_s: t.matmul(
                        out=pool[pi][0:8, :],
                        lhsT=sel8[:, ct * 8:(ct + 1) * 8],
                        rhs=q2b_s[:, ct * 512:(ct + 1) * 512],
                        start=(ct == 0), stop=(ct == 3)),
                        waits=[raw_ready, consts_ready] + dep(pool_free[pi]),
                        inc=pe if ct == 3 else None)
                pe_n2 = pe.at()
                q2b_free[side] = pe_n2
                P.add("scalar", lambda s, pi=pi: s.activation(
                    out=n2s[:], in_=pool[pi][0:8, :], func=AF.Sqrt),
                    waits=[pe_n2])
                pool_free[pi] = ss.at()
                P.add("vector", lambda v: v.tensor_single_scalar(
                    out=n2s[:], in_=n2s[:], scalar=1e-12, op=ALU.max), waits=[ss])
                P.add("vector", lp(lambda v: v.reciprocal(out=rn8[:], in_=n2s[:])))
                rn_ready = sv.at()
                cs_t, sn_t = (csq, snq) if side == "q" else (csk, snk)
                for ct in range(4):
                    pi = next_pool()
                    P.add("tensor", lambda t, ct=ct, pi=pi, qraw_s=qraw_s: t.matmul(
                        out=pool[pi][:], lhsT=p32b[:], rhs=qraw_s[:, ct * 512:(ct + 1) * 512],
                        start=True, stop=True),
                        waits=[raw_ready, consts_ready] + dep(pool_free[pi]), inc=pe)
                    pe_sw = pe.at()
                    P.add("scalar", lambda s, ct=ct, pi=pi, sw_s=sw_s: s.copy(
                        out=sw_s[:, ct * 512:(ct + 1) * 512], in_=pool[pi][:]),
                        waits=[pe_sw] + dep(sw_free[side] if ct == 0 else None))
                    pool_free[pi] = ss.at()
                sw_ready = ss.at()
                P.add("vector", lambda v, cs_t=cs_t, t0=t0, t2b_s=t2b_s, qraw_s=qraw_s: v.tensor_tensor(
                    out=t2b_s[:].rearrange("p (ct t) -> p ct t", ct=4),
                    in0=qraw_s[:].rearrange("p (ct t) -> p ct t", ct=4),
                    in1=cs_t[:, t0:t0 + W4].unsqueeze(1).broadcast_to((128, 4, W4)),
                    op=ALU.mult),
                    waits=[raw_ready, tabs_ready] + dep(t2b_free[side]))
                P.add("vector", lambda v, sn_t=sn_t, t0=t0, sw_s=sw_s: v.tensor_tensor(
                    out=sw_s[:].rearrange("p (ct t) -> p ct t", ct=4),
                    in0=sw_s[:].rearrange("p (ct t) -> p ct t", ct=4),
                    in1=sn_t[:, t0:t0 + W4].unsqueeze(1).broadcast_to((128, 4, W4)),
                    op=ALU.mult),
                    waits=[sw_ready])
                P.add("vector", lambda v, qtl_s=qtl_s, t2b_s=t2b_s, sw_s=sw_s: v.tensor_tensor(
                    out=qtl_s[:], in0=t2b_s[:], in1=sw_s[:], op=ALU.add),
                    waits=dep(qtl_free[side]))
                sw_free[side] = sv.at()
                t2b_free[side] = sv.at()
                q2b_free[side] = sv.at()
                rot_ready = sv.at()
                selx = selq if side == "q" else selk
                dst_ext = qext[cur] if side == "q" else kext[kcur]
                dfree = qext_free[cur] if side == "q" else kext_free[kcur]
                for ct in range(4):
                    pi = next_pool()
                    P.add("tensor", lambda t, ct=ct, pi=pi, selx=selx: t.matmul(
                        out=pool[pi][:], lhsT=selx[:, ct * 128:(ct + 1) * 128],
                        rhs=rn8[:], start=True, stop=True),
                        waits=[rn_ready, consts_ready] + dep(pool_free[pi]), inc=pe)
                    pe_bc = pe.at()
                    for half in range(2):
                        h = 2 * ct + half
                        P.add("vector", lambda v, ct=ct, pi=pi, half=half, h=h, dst_ext=dst_ext, qtl_s=qtl_s: v.tensor_tensor(
                            out=dst_ext[0:64, h * 512:(h + 1) * 512],
                            in0=qtl_s[half * 64:(half + 1) * 64, ct * 512:(ct + 1) * 512],
                            in1=pool[pi][half * 64:(half + 1) * 64, :],
                            op=ALU.mult),
                            waits=[pe_bc, rot_ready] + (dep(dfree) if (ct == 0 and half == 0) else []))
                    pool_free[pi] = sv.at()
                qtl_free[side] = sv.at()
                qraw_free[side] = sv.at()
                if side == "q":
                    qext_ready = sv.at()
                else:
                    kext_ready = sv.at()

            for tt in range(4):
                pi = next_pool()
                for c in range(NCH):
                    P.add("tensor", lambda t, c=c, tt=tt, pi=pi, hs=hs: t.matmul(
                        out=pool[pi][:],
                        lhsT=hs(c, tt * 128, 128),
                        rhs=wqkv[:, c * 1536 + 1024: c * 1536 + 1536],
                        start=(c == 0), stop=(c == NCH - 1)),
                        waits=[w_ready, x_ready[w]] + dep(pool_free[pi]) +
                              (dep(vext_free[kcur]) if tt == 0 else []) +
                              ([attn_init] if w < 3 else []) + ([w0z] if w == 0 else []),
                        inc=pe if c == NCH - 1 else None)
                pe_v = pe.at()
                P.add("scalar", lambda s, tt=tt, pi=pi, kcur=kcur, vext=vext: s.copy(
                    out=vext[kcur][:].rearrange("p (t4 h e) -> p t4 h e", t4=4, h=8)[:, tt, :, 0:64],
                    in_=pool[pi][:].rearrange("p (h e) -> p h e", h=8)),
                    waits=[pe_v])
                pool_free[pi] = ss.at()
            vext_ready = ss.at()

            if pendA[0] is not None:
                pendA[0]()

            if w + 1 < NW4:
                pp = (w + 1) % 2
                P.add("sync", lambda s, w=w, pp=pp, xsrc=xsrc, xoff=xoff: s.dma_start(
                    out=xwin[pp][:].rearrange("p (c t) -> p c t", c=NCH),
                    in_=xsrc[:, :, xoff + (w + 1) * W4 - 1: xoff + (w + 2) * W4].rearrange("c p t -> p c t")),
                    waits=dep(xwin_free[pp]) + [wl[pp]] +
                          ([xsrc_prev_stores[w + 1]] if l else []),
                    inc=wl[pp], inc_n=16, is_dma=True)
                x_ready.append((wl[pp], wl[pp].v))

            def attn_tail(w=w, t0=t0, cur=cur, kcur=kcur, kprv=kprv,
                          qext_ready=qext_ready, kext_ready=kext_ready, vext_ready=vext_ready):
                nonlocal oall_free, nmcols_free, sumsall_free
                qb = qext[cur]
                pendO = [None]
                for h in range(8):
                    for qt in range(4):
                        j = h * 4 + qt
                        wB = qt * 128 + 128
                        piB = next_pool()
                        P.add("tensor", lambda t, h=h, qt=qt, piB=piB, wB=wB, qb=qb: t.matmul(
                            out=pool[piB][:, 0:wB],
                            lhsT=qb[0:64, h * 512 + qt * 128: h * 512 + qt * 128 + 128],
                            rhs=kext[kcur][0:64, h * 512: h * 512 + wB],
                            start=True, stop=True),
                            waits=[qext_ready, kext_ready] + dep(pool_free[piB]) +
                                  dep(nmcols_free if j == 0 else None),
                            inc=pe)
                        pe_B = pe.at()
                        if w > 0:
                            wA = 512 - qt * 128
                            piA = next_pool()
                            P.add("tensor", lambda t, h=h, qt=qt, piA=piA, wA=wA, qb=qb: t.matmul(
                                out=pool[piA][:, 0:wA],
                                lhsT=qb[0:64, h * 512 + qt * 128: h * 512 + qt * 128 + 128],
                                rhs=kext[kprv][0:64, h * 512 + qt * 128: h * 512 + 512],
                                start=True, stop=True),
                                waits=dep(pool_free[piA]), inc=pe)
                            pe_A = pe.at()
                            P.add("vector", lambda v, piA=piA, wA=wA: v.reduce_max(
                                out=tmp32[:, 0:1], in_=pool[piA][:, 0:wA], axis=AX.X, negate=True),
                                waits=[pe_A])
                            pool_free[piA] = sv.at()
                            P.add("vector", lambda v, piB=piB, wB=wB: v.reduce_max(
                                out=tmp32[:, 1:2], in_=pool[piB][:, 0:wB], axis=AX.X, negate=True),
                                waits=[pe_B])
                            pool_free[piB] = sv.at()
                            P.add("vector", lambda v, j=j: v.tensor_tensor(
                                out=nmcols[:, j:j + 1], in0=tmp32[:, 0:1], in1=tmp32[:, 1:2],
                                op=ALU.min))
                        else:
                            P.add("vector", lambda v, piB=piB, j=j, wB=wB: v.reduce_max(
                                out=nmcols[:, j:j + 1], in_=pool[piB][:, 0:wB], axis=AX.X, negate=True),
                                waits=[pe_B])
                            pool_free[piB] = sv.at()
                nm_ready = sv.at()
                pi = next_pool()
                P.add("tensor", lambda t, pi=pi: t.transpose(
                    out=pool[pi][0:32, 0:128], in_=nmcols[:, 0:32], identity=identf[:]),
                    waits=[nm_ready, identf_ready] + dep(pool_free[pi]), inc=pe)
                pe_tr = pe.at()
                nmcols_free = pe_tr
                P.add("vector", lambda v, pi=pi: v.tensor_copy(
                    out=nmrows[:], in_=pool[pi][0:32, 0:128]),
                    waits=[pe_tr, db])
                pool_free[pi] = sv.at()
                nmr_ready = sv.at()
                P.add("gpsimd", lambda g: g.dma_start(out=dbounce_d, in_=nmrows[:]),
                      waits=[nmr_ready, db], inc=db, inc_n=16, is_dma=True)
                P.add("gpsimd", lambda g, qb=qb: g.dma_start(
                    out=qb[64:65, :],
                    in_=dbounce_d.rearrange("j p -> (j p)").unsqueeze(0)),
                    waits=[db] + dep(qext_free[cur]), inc=db, inc_n=16, is_dma=True)
                dall = db.at()

                for h in range(8):
                    hpar = h % 2
                    for kc in range(0 if w > 0 else 4, 8):
                        qlo = max(0, kc * 128 - 512)
                        qhi = min(512, kc * 128 + 128)
                        kb = kext[kprv] if kc < 4 else kext[kcur]
                        pi = next_pool()
                        P.add("tensor", lambda t, h=h, kc=kc, pi=pi, kb=kb, qlo=qlo, qhi=qhi, qb=qb: t.matmul(
                            out=pool[pi][:, 0:qhi - qlo],
                            lhsT=kb[0:65, h * 512 + (kc % 4) * 128: h * 512 + (kc % 4) * 128 + 128],
                            rhs=qb[0:65, h * 512 + qlo: h * 512 + qhi],
                            start=True, stop=True),
                            waits=[dall, qext_ready, kext_ready] + dep(pool_free[pi]),
                            inc=pe)
                        pe_ks = pe.at()
                        mcol = (kc * 128 - qlo) if kc < 4 else 0
                        moff = 0 if kc < 4 else 128
                        P.add("vector", lambda v, pi=pi, mcol=mcol, moff=moff: v.tensor_tensor(
                            out=pool[pi][:, mcol:mcol + 128],
                            in0=pool[pi][:, mcol:mcol + 128],
                            in1=maskb[:, moff:moff + 128], op=ALU.add),
                            waits=[pe_ks, consts_ready])
                        mask_done = sv.at()
                        P.add("scalar", lambda s, h=h, kc=kc, pi=pi, hpar=hpar, qlo=qlo, qhi=qhi, ptbuf=ptbuf: s.activation(
                            out=ptbuf[hpar][:, kc * 512 + qlo: kc * 512 + qhi],
                            in_=pool[pi][:, 0:qhi - qlo], func=AF.Exp),
                            waits=[mask_done] + dep(ptbuf_free[hpar] if kc == (0 if w > 0 else 4) else None))
                        pool_free[pi] = ss.at()
                    pt_ready = ss.at()
                    if pendO[0] is not None:
                        pendO[0]()

                    def o_block(h=h, hpar=hpar, pt_ready=pt_ready):
                        pi = next_pool()
                        kcs = list(range(0 if w > 0 else 4, 8))
                        for ki, kc in enumerate(kcs):
                            vb = vext[kprv] if kc < 4 else vext[kcur]
                            P.add("tensor", lambda t, h=h, kc=kc, pi=pi, vb=vb, ki=ki, hpar=hpar, last=(ki == len(kcs) - 1), ptbuf=ptbuf: t.matmul(
                                out=pool[pi][0:65, :],
                                lhsT=vb[:, (kc % 4) * 520 + h * 65: (kc % 4) * 520 + h * 65 + 65],
                                rhs=ptbuf[hpar][:, kc * 512:(kc + 1) * 512],
                                start=(ki == 0), stop=last),
                                waits=[pt_ready, vext_ready] + dep(pool_free[pi]),
                                inc=pe if ki == len(kcs) - 1 else None)
                        pe_o = pe.at()
                        ptbuf_free[hpar] = pe_o
                        P.add("scalar", lambda s, h=h, pi=pi: s.copy(
                            out=oall[(h % 2) * 64:(h % 2) * 64 + 64, (h // 2) * 512:(h // 2) * 512 + 512],
                            in_=pool[pi][0:64, :]),
                            waits=[pe_o] + dep(oall_free if h == 0 else None))
                        P.add("scalar", lambda s, h=h, pi=pi: s.copy(
                            out=sumsall[0:1, h * 512:(h + 1) * 512], in_=pool[pi][64:65, :]),
                            waits=dep(sumsall_free if h == 0 else None))
                        pool_free[pi] = ss.at()
                    pendO[0] = o_block
                pendO[0]()
                pendO[0] = None
                o_raw_ready = ss.at()

                pi = next_pool()
                for h in range(8):
                    P.add("tensor", lambda t, h=h, pi=pi: t.matmul(
                        out=pool[pi][0:8, :],
                        lhsT=e8r[0:1, h * 8:(h + 1) * 8],
                        rhs=sumsall[0:1, h * 512:(h + 1) * 512],
                        start=(h == 0), stop=(h == 7)),
                        waits=[o_raw_ready, consts_ready] + dep(pool_free[pi]),
                        inc=pe if h == 7 else None)
                pe_su = pe.at()
                sumsall_free = pe_su
                P.add("vector", lp(lambda v, pi=pi: v.reciprocal(out=rs8[:], in_=pool[pi][0:8, :])),
                      waits=[pe_su])
                pool_free[pi] = sv.at()
                rs_ready = sv.at()
                for ct in range(4):
                    pi = next_pool()
                    P.add("tensor", lambda t, ct=ct, pi=pi: t.matmul(
                        out=pool[pi][:], lhsT=selk[:, ct * 128:(ct + 1) * 128],
                        rhs=rs8[:], start=True, stop=True),
                        waits=[rs_ready] + dep(pool_free[pi]), inc=pe)
                    pe_bc = pe.at()
                    P.add("scalar", lambda s, pi=pi: s.copy(out=bcsb[:], in_=pool[pi][:]),
                          waits=[pe_bc, sv])
                    pool_free[pi] = ss.at()
                    P.add("vector", lambda v, ct=ct: v.tensor_tensor(
                        out=oall[:, ct * 512:(ct + 1) * 512],
                        in0=oall[:, ct * 512:(ct + 1) * 512],
                        in1=bcsb[:], op=ALU.mult),
                        waits=[ss])
                o_ready = sv.at()

                for e in range(NCH):
                    pi = next_pool()
                    for cc in range(4):
                        P.add("tensor", lambda t, e=e, cc=cc, pi=pi: t.matmul(
                            out=pool[pi][:],
                            lhsT=wo4[:, cc * 1024 + e * 128: cc * 1024 + (e + 1) * 128],
                            rhs=oall[:, cc * 512:(cc + 1) * 512],
                            start=(cc == 0), stop=(cc == 3)),
                            waits=[o_ready] + dep(pool_free[pi]),
                            inc=pe if cc == 3 else None)
                    pe_wo = pe.at()
                    P.add("vector", lambda v, e=e, pi=pi, cur=cur: v.scalar_tensor_tensor(
                        out=xatt[:, e * W4:(e + 1) * W4],
                        in0=pool[pi][:], scalar=0.0, op0=ALU.bypass,
                        in1=xwin[cur][:, e * 513 + 1: e * 513 + 513],
                        op1=ALU.add),
                        waits=[pe_wo, stp])
                    pool_free[pi] = sv.at()
                resid_done = sv.at()
                oall_free = pe.at()
                qext_free[cur] = pe.at()
                kext_free[kprv] = pe.at() if w > 0 else kext_free[kprv]
                vext_free[kprv] = pe.at() if w > 0 else vext_free[kprv]
                P.add("sync", lambda s, t0=t0: s.dma_start(
                    out=xpost_d[:, :, 1 + t0: 1 + t0 + W4].rearrange("c p t -> p c t"),
                    in_=xatt[:].rearrange("p (c t) -> p c t", c=NCH)),
                    waits=[resid_done, init_sync, stp], inc=stp, inc_n=16, is_dma=True)
                attn_store_evt.append((stp, stp.v))
                xwin_free[cur] = resid_done

            pendA[0] = attn_tail
        pendA[0]()
        attn_stores = stp.at()

        esA.close()

        esF = ExitStack()
        sbF = lambda name, shp, dt=F32: esF.enter_context(nc.sbuf_tensor('z4_' + name, shp, dt))
        w1as = sbF(f"w1a{l}", [128, NCH * HIDP], BF16)
        gbr = sbF(f"gbr{l}", [2, NCH * 128])
        nmr1 = sbF(f"nmr1{l}", [2, 512])
        rows4 = sbF(f"rows4{l}", [1, 4 * 512])
        w1gs = sbF(f"w1g{l}", [128, NCH * HIDP], BF16)
        w2s = sbF(f"w2p{l}", [128, NFC * 1024], BF16)
        hsrc = [sbF(f"hsrc{l}_{i}", [128, NCH * 513], BF16) for i in range(2)]
        hnb = [sbF(f"hnb{l}_{i}", [128, NCH * 512], BF16) for i in range(2)]
        h2all = sbF(f"h2all{l}", [128, NFC * 512], BF16)
        gelb = sbF(f"gelb{l}", [128, 512], BF16)
        tmpF = sbF(f"tmpF{l}", [128, 512], BF16)

        fb = [s.at() for s in (pe, sv, ss, sg, stp)]
        ls4 = P.nsem("ls")
        for c in range(NCH):
            P.add("sync", lambda s, c=c, w1as=w1as: s.dma_start(
                out=w1as[:, c * HIDP:(c + 1) * HIDP], in_=w1a_d[l, c]),
                waits=fb + [pc_gates["ffn"][l]] + ([ls4] if c == 0 else []),
                inc=ls4, inc_n=16, is_dma=True)
            P.add("sync", lambda s, c=c, w1gs=w1gs: s.dma_start(
                out=w1gs[:, c * HIDP:(c + 1) * HIDP], in_=w1g_d[l, c]),
                inc=ls4, inc_n=16, is_dma=True)
        for f in range(NFC):
            P.add("sync", lambda s, f=f, w2s=w2s: s.dma_start(
                out=w2s[:, f * 1024:(f + 1) * 1024], in_=w2p_d[l, f]),
                inc=ls4, inc_n=16, is_dma=True)
        sync_wf = ls4.at()
        sdma(b1ac[:], b1a_d[l], waits=fb + [ls])
        sdma(b1gc[:], b1g_d[l], waits=fb)
        sdma(b2c[:], b2_d[l], waits=fb)
        sdma(gbr[0:2, :].rearrange("r (c k) -> r c k", c=NCH),
             gb_d[l].rearrange("c r k -> r c k"), waits=fb)
        wf_ready = sync_wf
        wf_small = ls.at()
        P.add("sync", lambda s, nmr1=nmr1: s.dma_start(out=nmr1[1:2, :], in_=onesr_d),
              waits=fb + [ls], inc=ls, inc_n=16, is_dma=True)
        nmr_ones = ls.at()

        hsrc_free = [None, None]
        h2_free = None
        hnb_free = [None, None]
        r2f_free = None
        statA_free = None
        statB_free = None
        rows_free = None
        pendF = [None]
        ffn_store_evt = []

        hs_ready = []
        P.add("gpsimd", lambda g: g.memset(
            hsrc[0][:].rearrange("p (c t) -> p c t", c=NCH)[:, :, 0:1], 0.0), waits=fb)
        h0z = sg.at()
        P.add("sync", lambda s: s.dma_start(
            out=hsrc[0][:].rearrange("p (c t) -> p c t", c=NCH)[:, :, 1:513],
            in_=xpost_d[:, :, 1:513].rearrange("c p t -> p c t")),
            waits=[attn_store_evt[1], hl[0]], inc=hl[0], inc_n=16, is_dma=True)
        hs_ready.append((hl[0], hl[0].v))
        hs0_extra = h0z

        for tb in range(NW4):
            t0 = tb * W4
            par = tb % 2

            def hsl(c, par=par):
                o = 1 if c < 4 else 0
                return hsrc[par][:, c * 513 + o: c * 513 + o + W4]

            def xol(c, par=par):
                return hsrc[par][:, c * 513 + 1: c * 513 + 513]

            r2v = hnb[par][:].rearrange("p (c t) -> p c t", c=NCH)
            hs_extra = [hs0_extra] if tb == 0 else []
            for grp, (c0, c1, o) in enumerate(((0, 4, 1), (4, 8, 0))):
                P.add("vector", lambda v, c0=c0, c1=c1, o=o, par=par, r2v=r2v: v.tensor_tensor(
                    out=r2v[:, c0:c1, :],
                    in0=hsrc[par][:].rearrange("p (c t) -> p c t", c=NCH)[:, c0:c1, o:o + W4],
                    in1=hsrc[par][:].rearrange("p (c t) -> p c t", c=NCH)[:, c0:c1, o:o + W4],
                    op=ALU.mult),
                    waits=[hs_ready[tb]] + hs_extra + dep(hnb_free[par] if grp == 0 else None))
            r2_ready = sv.at()
            piA = next_pool()
            for c in range(NCH):
                P.add("tensor", lambda t, c=c, piA=piA, hsl=hsl: t.matmul(
                    out=pool[piA][0:1, :], lhsT=ones_c[:], rhs=hsl(c),
                    start=(c == 0), stop=(c == NCH - 1)),
                    waits=[hs_ready[tb], consts_mem] + hs_extra + dep(pool_free[piA]) + dep(statA_free),
                    inc=pe if c == NCH - 1 else None)
            peA = pe.at()
            piB = next_pool()
            for c in range(NCH):
                P.add("tensor", lambda t, c=c, piB=piB, par=par: t.matmul(
                    out=pool[piB][0:1, :],
                    lhsT=ones_c[:], rhs=hnb[par][:, c * W4:(c + 1) * W4],
                    start=(c == 0), stop=(c == NCH - 1)),
                    waits=[r2_ready] + dep(pool_free[piB]) + dep(statB_free),
                    inc=pe if c == NCH - 1 else None)
            peB = pe.at()
            r2f_free = peB
            MU, EX2, VAR, RSTD = 0, 1, 2, 3
            SCR = EX2
            rw = lambda i: rows4[0:1, i * W4:(i + 1) * W4]
            P.add("scalar", lambda s, piA=piA: s.mul(out=rw(MU), in_=pool[piA][0:1, :], mul=-1.0 / D),
                  waits=[peA] + dep(rows_free))
            statA_free = ss.at()
            P.add("scalar", lambda s, piB=piB: s.mul(out=rw(EX2), in_=pool[piB][0:1, :], mul=1.0 / D),
                  waits=[peB])
            statB_free = ss.at()
            pool_free[piA] = ss.at()
            pool_free[piB] = ss.at()
            P.add("gpsimd", lambda g: g.tensor_tensor(out=rw(VAR), in0=rw(MU), in1=rw(MU), op=ALU.mult),
                  waits=[ss])
            P.add("gpsimd", lambda g: g.tensor_tensor(out=rw(VAR), in0=rw(EX2), in1=rw(VAR), op=ALU.subtract))
            P.add("scalar", lambda s: s.activation(out=rw(SCR), in_=rw(VAR), func=AF.Ln, bias=epsln[:]),
                  waits=[sg])
            P.add("scalar", lambda s: s.activation(out=rw(RSTD), in_=rw(SCR), func=AF.Exp, scale=neghf[:]))
            P.add("gpsimd", lambda g, nmr1=nmr1: g.tensor_tensor(
                out=nmr1[0:1, :], in0=rw(MU), in1=rw(RSTD), op=ALU.mult),
                waits=[ss, nmr_ones] + dep(rows_free))
            rows_ready = sg.at()
            for c in range(NCH):
                pi1 = next_pool()
                P.add("tensor", lambda t, c=c, pi1=pi1: t.matmul(
                    out=pool[pi1][:], lhsT=gbr[0:1, c * 128:(c + 1) * 128], rhs=rw(RSTD),
                    start=True, stop=True),
                    waits=[rows_ready, wf_ready, wf_small] + dep(pool_free[pi1]), inc=pe)
                pi2 = next_pool()
                P.add("tensor", lambda t, c=c, pi2=pi2: t.matmul(
                    out=pool[pi2][:], lhsT=gbr[0:2, c * 128:(c + 1) * 128], rhs=nmr1[0:2, :],
                    start=True, stop=True),
                    waits=dep(pool_free[pi2]), inc=pe)
                pe_bc = pe.at()
                P.add("vector", lambda v, c=c, pi1=pi1, pe_bc=pe_bc, hsl=hsl: v.tensor_tensor(
                    out=tmpF[:], in0=hsl(c), in1=pool[pi1][:], op=ALU.mult),
                    waits=[pe_bc])
                pool_free[pi1] = sv.at()
                P.add("vector", lambda v, c=c, pi2=pi2, par=par: v.tensor_tensor(
                    out=hnb[par][:, c * W4:(c + 1) * W4], in0=tmpF[:], in1=pool[pi2][:], op=ALU.add),
                    waits=(dep(hnb_free[par]) + dep(r2f_free)) if c == 0 else [])
                pool_free[pi2] = sv.at()
            hn_ready = sv.at()
            rows_free = pe.at()

            if pendF[0] is not None:
                pendF[0]()

            if tb + 1 < NW4:
                pp = (tb + 1) % 2
                P.add("sync", lambda s, tb=tb, pp=pp: s.dma_start(
                    out=hsrc[pp][:].rearrange("p (c t) -> p c t", c=NCH),
                    in_=xpost_d[:, :, (tb + 1) * W4:(tb + 1) * W4 + 513].rearrange("c p t -> p c t")),
                    waits=[attn_store_evt[min(tb + 2, NW4 - 1)]] + dep(hsrc_free[pp]) + [hl[pp]],
                    inc=hl[pp], inc_n=16, is_dma=True)
                hs_ready.append(hl[pp].at())

            def ffn_tail(tb=tb, t0=t0, par=par, xol=xol, hn_ready=hn_ready):
                nonlocal h2_free
                glu_prev = None
                for f in range(NFC):
                    piA2 = next_pool()
                    for c in range(NCH):
                        P.add("tensor", lambda t, c=c, f=f, piA2=piA2, par=par: t.matmul(
                            out=pool[piA2][:],
                            lhsT=w1as[:, c * HIDP + f * 128: c * HIDP + (f + 1) * 128],
                            rhs=hnb[par][:, c * W4:(c + 1) * W4],
                            start=(c == 0), stop=(c == NCH - 1)),
                            waits=[hn_ready, wf_ready] + dep(pool_free[piA2]) +
                                  dep(h2_free if f == 0 else None),
                            inc=pe if c == NCH - 1 else None)
                    peA2 = pe.at()
                    piG = next_pool()
                    for c in range(NCH):
                        P.add("tensor", lambda t, c=c, f=f, piG=piG, par=par: t.matmul(
                            out=pool[piG][:],
                            lhsT=w1gs[:, c * HIDP + f * 128: c * HIDP + (f + 1) * 128],
                            rhs=hnb[par][:, c * W4:(c + 1) * W4],
                            start=(c == 0), stop=(c == NCH - 1)),
                            waits=dep(pool_free[piG]),
                            inc=pe if c == NCH - 1 else None)
                    peG = pe.at()
                    P.add("scalar", lambda s, f=f, piG=piG: s.activation(
                        out=gelb[:], in_=pool[piG][:], func=AF.Gelu, bias=b1gc[:, f:f + 1]),
                        waits=[peG, wf_small] + dep(glu_prev))
                    pool_free[piG] = ss.at()
                    P.add("vector", lambda v, f=f, piA2=piA2: v.scalar_tensor_tensor(
                        out=h2all[:, f * W4:(f + 1) * W4],
                        in0=pool[piA2][:], scalar=b1ac[:, f:f + 1], op0=ALU.add,
                        in1=gelb[:], op1=ALU.mult),
                        waits=[ss, peA2])
                    pool_free[piA2] = sv.at()
                    glu_prev = sv.at()
                h2_ready = sv.at()
                hnb_free[par] = pe.at()
                for e in range(NCH):
                    pi = next_pool()
                    for f in range(NFC):
                        P.add("tensor", lambda t, e=e, f=f, pi=pi: t.matmul(
                            out=pool[pi][:],
                            lhsT=w2s[:, f * 1024 + e * 128: f * 1024 + (e + 1) * 128],
                            rhs=h2all[:, f * W4:(f + 1) * W4],
                            start=(f == 0), stop=(f == NFC - 1)),
                            waits=[h2_ready] + dep(pool_free[pi]),
                            inc=pe if f == NFC - 1 else None)
                    pe_w2 = pe.at()
                    P.add("vector", lambda v, e=e, pi=pi, xol=xol: v.scalar_tensor_tensor(
                        out=xol(e),
                        in0=pool[pi][:], scalar=b2c[:, e:e + 1], op0=ALU.add,
                        in1=xol(e), op1=ALU.add),
                        waits=[pe_w2, wf_small])
                    pool_free[pi] = sv.at()
                xout_ready = sv.at()
                h2_free = pe.at()
                P.add("sync", lambda s, t0=t0, par=par: s.dma_start(
                    out=xnew_d[:, :, 1 + t0: 1 + t0 + W4].rearrange("c p t -> p c t"),
                    in_=hsrc[par][:].rearrange("p (c t) -> p c t", c=NCH)[:, :, 1:513]),
                    waits=[xout_ready, init_sync, stp], inc=stp, inc_n=16, is_dma=True)
                ffn_store_evt.append((stp, stp.v))
                hsrc_free[par] = (stp, stp.v)

            pendF[0] = ffn_tail
        pendF[0]()
        esF.close()
        xsrc_prev_stores = list(ffn_store_evt)

    for l in range(2):
        layer_phase(l)

    esL = ExitStack()
    sbL = lambda name, shp, dt=F32: esL.enter_context(nc.sbuf_tensor('z4_' + name, shp, dt))
    xprf = sbL("xprf", [128, NCH * 512], BF16)
    xnT = sbL("xnT", [128, NCH * 512], BF16)
    rbf4 = sbL("rbf4", [128, NCH * 512], BF16)
    r2bf4 = sbL("r2bf4", [128, NCH * 512], BF16)
    xo_s4 = sbL("xo_s4", [128, NCH * 512])
    rows4 = sbL("rows4L", [1, 6 * 512])
    lb = [s.at() for s in (pe, sv, ss, sg, stp)]
    xp_free = None
    xn_free = None
    rb_free = None
    r2b4_free = None
    stA_free = None
    stB_free = None
    rowsL_free = None
    out_free = None
    for tb in range(NW4):
        t0 = tb * W4
        P.add("sync", lambda s, t0=t0: s.dma_start(
            out=xprf[:].rearrange("p (c t) -> p c t", c=NCH),
            in_=x_d[:, :, t0:t0 + W4].rearrange("c p t -> p c t")),
            waits=lb + dep(xp_free) + [ls], inc=ls, inc_n=16, is_dma=True)
        xpl = ls.at()
        P.add("sync", lambda s, t0=t0: s.dma_start(
            out=xnT[:].rearrange("p (c t) -> p c t", c=NCH),
            in_=xnew_d[:, :, 1 + t0: 1 + t0 + W4].rearrange("c p t -> p c t")),
            waits=[xsrc_prev_stores[tb]] + dep(xn_free), inc=ls, inc_n=16, is_dma=True)
        xnl = ls.at()
        P.add("vector", lambda v: v.tensor_tensor(out=rbf4[:], in0=xprf[:], in1=xnT[:], op=ALU.add),
              waits=[(ls, ls.v)] + dep(rb_free))
        r_ready = sv.at()
        xp_free = r_ready
        xn_free = r_ready
        P.add("scalar", lambda s: s.activation(out=r2bf4[:], in_=rbf4[:], func=AF.Square),
              waits=[r_ready] + dep(r2b4_free))
        r2_ready = ss.at()
        piA = next_pool()
        for c in range(NCH):
            P.add("tensor", lambda t, c=c, piA=piA: t.matmul(
                out=pool[piA][0:1, :], lhsT=ones_c[:], rhs=rbf4[:, c * W4:(c + 1) * W4],
                start=(c == 0), stop=(c == NCH - 1)),
                waits=[r_ready, consts_mem] + dep(pool_free[piA]) + dep(stA_free),
                inc=pe if c == NCH - 1 else None)
        peA = pe.at()
        piB = next_pool()
        for c in range(NCH):
            P.add("tensor", lambda t, c=c, piB=piB: t.matmul(
                out=pool[piB][0:1, :], lhsT=ones_c[:], rhs=r2bf4[:, c * W4:(c + 1) * W4],
                start=(c == 0), stop=(c == NCH - 1)),
                waits=[r2_ready] + dep(pool_free[piB]) + dep(stB_free),
                inc=pe if c == NCH - 1 else None)
        peB = pe.at()
        r2b4_free = peB
        MU, EX2, VAR, RSTD, NMR, SCR = 0, 1, 2, 3, 4, 5
        rw = lambda i: rows4[0:1, i * W4:(i + 1) * W4]
        P.add("scalar", lambda s, piA=piA: s.mul(out=rw(MU), in_=pool[piA][0:1, :], mul=1.0 / D),
              waits=[peA] + dep(rowsL_free))
        stA_free = ss.at()
        P.add("scalar", lambda s, piB=piB: s.mul(out=rw(EX2), in_=pool[piB][0:1, :], mul=1.0 / D),
              waits=[peB])
        stB_free = ss.at()
        pool_free[piA] = ss.at()
        pool_free[piB] = ss.at()
        P.add("vector", lambda v: v.tensor_tensor(out=rw(VAR), in0=rw(MU), in1=rw(MU), op=ALU.mult),
              waits=[ss])
        P.add("vector", lambda v: v.tensor_tensor(out=rw(VAR), in0=rw(EX2), in1=rw(VAR), op=ALU.subtract))
        P.add("scalar", lambda s: s.activation(out=rw(SCR), in_=rw(VAR), func=AF.Sqrt, bias=epsln[:]),
              waits=[sv])
        P.add("vector", lambda v: v.reciprocal(out=rw(RSTD), in_=rw(SCR)), waits=[ss])
        P.add("vector", lambda v: v.tensor_tensor(out=rw(NMR), in0=rw(MU), in1=rw(RSTD), op=ALU.mult))
        P.add("scalar", lambda s: s.mul(out=rw(NMR), in_=rw(NMR), mul=-1.0), waits=[sv])
        rows_ready = ss.at()
        pi1 = next_pool()
        P.add("tensor", lambda t, pi1=pi1: t.matmul(
            out=pool[pi1][:], lhsT=onesr_f[:], rhs=rw(RSTD), start=True, stop=True),
            waits=[rows_ready] + dep(pool_free[pi1]), inc=pe)
        pi2 = next_pool()
        P.add("tensor", lambda t, pi2=pi2: t.matmul(
            out=pool[pi2][:], lhsT=onesr_f[:], rhs=rw(NMR), start=True, stop=True),
            waits=dep(pool_free[pi2]), inc=pe)
        pe_bc = pe.at()
        rowsL_free = pe_bc
        P.add("vector", lambda v, pi1=pi1: v.tensor_tensor(
            out=xo_s4[:].rearrange("p (c t) -> p c t", c=NCH),
            in0=rbf4[:].rearrange("p (c t) -> p c t", c=NCH),
            in1=pool[pi1][:].unsqueeze(1).broadcast_to((128, NCH, W4)),
            op=ALU.mult),
            waits=[pe_bc] + dep(out_free))
        pool_free[pi1] = sv.at()
        P.add("vector", lambda v, pi2=pi2: v.tensor_tensor(
            out=xo_s4[:].rearrange("p (c t) -> p c t", c=NCH),
            in0=xo_s4[:].rearrange("p (c t) -> p c t", c=NCH),
            in1=pool[pi2][:].unsqueeze(1).broadcast_to((128, NCH, W4)),
            op=ALU.add))
        pool_free[pi2] = sv.at()
        rb_free = sv.at()
        out_ready = sv.at()
        P.add("gpsimd", lambda g, t0=t0: g.dma_start(
            out=xo_d[:, :, t0:t0 + W4].rearrange("c p t -> p c t"),
            in_=xo_s4[:].rearrange("p (c t) -> p c t", c=NCH)),
            waits=[out_ready], inc=sto, inc_n=16, is_dma=True)
        out_free = (sto, sto.v)
    esL.close()

    P.add("gpsimd", lambda g: g.nop(), waits=[sto, stp, P.nsem("st")])
    es.close()


def make_rotary_tables_np():
    DH, WIN = 64, 512
    half = np.arange(0, DH, 2, dtype=np.float64)
    inv_freq = 1.0 / 10000.0 ** (half / DH)
    t = np.arange(T, dtype=np.float64)
    freqs = t[:, None] * inv_freq[None, :]
    freqs = np.concatenate([freqs, freqs], -1)
    base = (half + 0.4 * DH) / (1.4 * DH)
    power = (t - T // 2) / (WIN // 2)
    scale = base[None, :] ** power[:, None]
    scale = np.concatenate([scale, scale], -1)
    cos, sin = np.cos(freqs), np.sin(freqs)
    sgn = np.concatenate([-np.ones(32), np.ones(32)])
    cs_q = (cos * scale).astype(np.float32)
    sn_q = (sin * scale * sgn[None, :]).astype(np.float32)
    cs_k = (cos / scale).astype(np.float32)
    sn_k = (sin / scale * sgn[None, :]).astype(np.float32)
    fm = lambda a: np.ascontiguousarray(np.tile(a.T, (2, 1)))
    return fm(cs_q), fm(sn_q), fm(cs_k), fm(sn_k)


def stage4_inputs(xfm, inputs):
    g = lambda k: np.asarray(inputs[k], np.float32)
    wqkv = np.stack([np.ascontiguousarray(g("L1_ffn_wqkv")[l]).reshape(NCH, 128, 1536)
                     for l in range(2)])
    wo4 = np.stack([np.ascontiguousarray(g("L1_ffn_wo")[l]).reshape(4, 128, 1024)
                    for l in range(2)])
    w1 = g("L1_ffn_w1")
    b1 = g("L1_ffn_b1")
    w2 = g("L1_ffn_w2")
    w1a = np.zeros((2, D, HIDP), np.float32); w1a[:, :, :HID] = w1[:, :, :HID]
    w1g = np.zeros((2, D, HIDP), np.float32); w1g[:, :, :HID] = w1[:, :, HID:]
    b1a = np.zeros((2, HIDP), np.float32); b1a[:, :HID] = b1[:, :HID]
    b1g = np.zeros((2, HIDP), np.float32); b1g[:, :HID] = b1[:, HID:]
    w2p = np.zeros((2, HIDP, D), np.float32); w2p[:, :HID] = w2
    w1a = w1a.reshape(2, NCH, 128, HIDP)
    w1g = w1g.reshape(2, NCH, 128, HIDP)
    w2p = np.ascontiguousarray(w2p.reshape(2, NFC, 128, 1024))
    b1a_c = np.ascontiguousarray(b1a.reshape(2, NFC, 128).transpose(0, 2, 1))
    b1g_c = np.ascontiguousarray(b1g.reshape(2, NFC, 128).transpose(0, 2, 1))
    b2_c = np.ascontiguousarray(g("L1_ffn_b2").reshape(2, NCH, 128).transpose(0, 2, 1))
    gb_r = np.ascontiguousarray(np.stack(
        [g("L1_ffn_ln_g").reshape(2, NCH, 128), g("L1_ffn_ln_b").reshape(2, NCH, 128)],
        axis=2))
    qs = g("L1_ffn_q_scale")
    ks = g("L1_ffn_k_scale")
    swp = np.concatenate([np.arange(32, 64), np.arange(0, 32)])
    qs_c = np.stack([np.tile(qs, (1, 2)), np.tile(qs[:, swp], (1, 2))], axis=2)
    ks_c = np.stack([np.tile(ks, (1, 2)), np.tile(ks[:, swp], (1, 2))], axis=2)
    csq, snq, csk, snk = make_rotary_tables_np()
    i = np.arange(128)[None, :]
    j = np.arange(128)[:, None]
    medge_prev = np.where(i <= j, 0.0, -1e31).astype(np.float32)
    medge_cur = np.where(i >= j, 0.0, -1e31).astype(np.float32)
    maskb = np.concatenate([medge_prev, medge_cur], axis=1)
    ident = np.eye(128, dtype=np.float32)
    p32 = np.zeros((128, 128), np.float32)
    for d_ in range(128):
        b_ = (d_ // 64) * 64
        p32[b_ + ((d_ - b_) + 32) % 64, d_] = 1.0
    sel8 = np.zeros((128, 32), np.float32)
    for ct in range(4):
        for half in range(2):
            sel8[half * 64:(half + 1) * 64, ct * 8 + 2 * ct + half] = 1.0
    selq = np.zeros((8, 512), np.float32)
    selk = np.zeros((8, 512), np.float32)
    for ct in range(4):
        for d_ in range(128):
            h = 2 * ct + d_ // 64
            selq[h, ct * 128 + d_] = 8.0
            selk[h, ct * 128 + d_] = 1.0
    e8 = np.zeros((1, 64), np.float32)
    for h in range(8):
        e8[0, h * 8 + h] = 1.0
    ones512 = np.ones((1, 512), np.float32)
    return {"x": xfm, "wqkv": wqkv, "wo4": wo4, "w1a": w1a, "w1g": w1g, "w2p": w2p,
            "b1a_c": b1a_c, "b1g_c": b1g_c, "b2_c": b2_c, "gb_r": gb_r,
            "qs_c": qs_c, "ks_c": ks_c, "csq": csq, "snq": snq, "csk": csk, "snk": snk,
            "maskb": maskb, "ident": ident, "p32": p32, "sel8": sel8, "selq": selq,
            "selk": selk, "e8": e8, "ones512": ones512}


def _featmajor(xb):
    return np.ascontiguousarray(np.asarray(xb, np.float32).T.reshape(NCH, 128, T))


def _unfeat(xt):
    return np.asarray(xt, np.float32).reshape(D, T).T


def stage2_inputs(xfm, inputs):
    wfc = np.ascontiguousarray(np.asarray(inputs["L0_ffn_W_fc"], np.float32).T).reshape(NCH, 128, 4096)
    wproj = np.ascontiguousarray(np.asarray(inputs["L0_ffn_W_proj"], np.float32).T).reshape(32, 128, 1024)
    return {"x": xfm, "wfc": wfc, "wproj": wproj}


def stage3_inputs(xfm, inputs):
    tr = lambda w: np.ascontiguousarray(np.asarray(w, np.float32).T).reshape(NCH, 128, 1024)
    bq = np.asarray(inputs["L1_attn_bq"], np.float32).reshape(NCH, 128).T
    bk = np.asarray(inputs["L1_attn_bk"], np.float32).reshape(NCH, 128).T
    bqk = np.ascontiguousarray(np.concatenate([bq, bk], axis=1))
    bo = np.ascontiguousarray(np.asarray(inputs["L1_attn_bo"], np.float32).reshape(NCH, 128).T)
    bv = np.asarray(inputs["L1_attn_bv"], np.float32).reshape(1, 1024)
    i_loc = np.arange(128)[:, None]
    j = np.arange(256)[None, :]
    mask = np.concatenate(
        [np.where(j <= qt * 128 + i_loc, 0.0, -1e31).astype(np.float32) for qt in (0, 1)],
        axis=1)
    ident = np.eye(128, dtype=np.float32)
    return {"x": xfm, "wq": tr(inputs["L1_attn_Wq"]), "wk": tr(inputs["L1_attn_Wk"]),
            "wv": tr(inputs["L1_attn_Wv"]), "wo": tr(inputs["L1_attn_Wo"]),
            "bqk": bqk, "bo_c": bo, "bv_r": bv, "mask": np.ascontiguousarray(mask),
            "ident": ident}


def build_fused():
    nc = bass.Bass("TRN2", target_bir_lowering=False)
    dram = lambda name, shp, dt=F32, kind="ExternalInput": nc.dram_tensor(name, shp, dt, kind=kind).ap()
    x_d = dram("x", [NCH, 128, T])
    xo_d = dram("xo", [NCH, 128, T], kind="ExternalOutput")
    h12 = nc.dram_tensor("h12", [NCH, 128, T], BF16, kind="Internal").ap()
    h23 = nc.dram_tensor("h23", [NCH, 128, T], BF16, kind="Internal").ap()
    h34 = nc.dram_tensor("h34", [NCH, 128, T], BF16, kind="Internal").ap()

    es = ExitStack()
    P = Prog(nc, es)

    gcb = es.enter_context(nc.sbuf_tensor("gconst", [1, 130], F32))
    g_ones = gcb[0:1, 0:128]
    g_eps5 = gcb[0:1, 128:129]
    g_negh = gcb[0:1, 129:130]
    P.add("gpsimd", lambda g: g.memset(g_ones, 1.0))
    P.add("gpsimd", lambda g: g.memset(g_eps5, 1e-5))
    P.add("gpsimd", lambda g: g.memset(g_negh, -0.5))
    gc = (g_ones, g_eps5, g_negh)

    s1 = (x_d,
          dram("s1_wq", [NCH, 128, 128]), dram("s1_wk", [NCH, 128, 128]),
          dram("s1_wv", [NCH, 128, 128]), dram("s1_wg", [NCH, 128, 128]),
          dram("s1_wab", [NCH, 128, 2]), dram("s1_wo", [128, 1024]),
          dram("s1_bq_c", [128, 1]), dram("s1_bk_c", [128, 1]), dram("s1_bg_c", [128, 1]),
          dram("s1_bv_r", [1, 128]), dram("s1_bab", [1, 2]), dram("s1_bo_c", [128, NCH]),
          dram("s1_cumUT", [128, 128]), dram("s1_identf", [128, 128]),
          dram("s1_maskadd", [128, 128]), dram("s1_maskstrict", [128, 128]),
          h12)
    emit_stage1(nc, P, s1)
    pcast = Sem(nc, es, "pcast")
    wqkvB = nc.dram_tensor("wqkvB", [2, NCH, 128, 1536], BF16, kind="Internal").ap()
    wo4B = nc.dram_tensor("wo4B", [2, 4, 128, 1024], BF16, kind="Internal").ap()
    w1aB = nc.dram_tensor("w1aB", [2, NCH, 128, HIDP], BF16, kind="Internal").ap()
    w1gB = nc.dram_tensor("w1gB", [2, NCH, 128, HIDP], BF16, kind="Internal").ap()
    w2pB = nc.dram_tensor("w2pB", [2, NFC, 128, 1024], BF16, kind="Internal").ap()
    tabB = nc.dram_tensor("tabB", [4, 128, T], BF16, kind="Internal").ap()
    wfcB = nc.dram_tensor("wfcB", [NCH, 128, 4096], BF16, kind="Internal").ap()
    wprojB = nc.dram_tensor("wprojB", [32, 128, 1024], BF16, kind="Internal").ap()
    s3wB = nc.dram_tensor("s3wB", [4, NCH, 128, 1024], BF16, kind="Internal").ap()
    s4f32 = {
        "wqkv": nc.dram_tensor("wqkv", [2, NCH, 128, 1536], F32, kind="ExternalInput").ap(),
        "wo4": nc.dram_tensor("wo4", [2, 4, 128, 1024], F32, kind="ExternalInput").ap(),
        "w1a": nc.dram_tensor("w1a", [2, NCH, 128, HIDP], F32, kind="ExternalInput").ap(),
        "w1g": nc.dram_tensor("w1g", [2, NCH, 128, HIDP], F32, kind="ExternalInput").ap(),
        "w2p": nc.dram_tensor("w2p", [2, NFC, 128, 1024], F32, kind="ExternalInput").ap(),
        "csq": nc.dram_tensor("csq", [128, T], F32, kind="ExternalInput").ap(),
        "snq": nc.dram_tensor("snq", [128, T], F32, kind="ExternalInput").ap(),
        "csk": nc.dram_tensor("csk", [128, T], F32, kind="ExternalInput").ap(),
        "snk": nc.dram_tensor("snk", [128, T], F32, kind="ExternalInput").ap(),
    }

    def emit_precast(s4f32):
        P.add("gpsimd", lambda g: g.dma_start(out=wfcB, in_=s2_wfc_d),
              waits=[pcast], inc=pcast, inc_n=16, is_dma=True)
        P.add("gpsimd", lambda g: g.dma_start(out=wprojB, in_=s2_wproj_d),
              waits=[pcast], inc=pcast, inc_n=16, is_dma=True)
        for i, sw in enumerate(s3w_f32):
            P.add("gpsimd", lambda g, i=i, sw=sw: g.dma_start(out=s3wB[i], in_=sw),
                  waits=[pcast], inc=pcast, inc_n=16, is_dma=True)
        for i, key in enumerate(("csq", "snq", "csk", "snk")):
            P.add("gpsimd", lambda g, d=tabB[i], s=s4f32[key]: g.dma_start(out=d, in_=s),
                  waits=[pcast], inc=pcast, inc_n=16, is_dma=True)
        for l in range(2):
            for dstB, key in ((wqkvB, "wqkv"), (wo4B, "wo4"), (w1aB, "w1a"),
                              (w1gB, "w1g"), (w2pB, "w2p")):
                P.add("gpsimd", lambda g, d=dstB[l], s=s4f32[key][l]: g.dma_start(out=d, in_=s),
                      waits=[pcast], inc=pcast, inc_n=16, is_dma=True)
        return {"attn": [(pcast, 192), (pcast, 272)],
                "ffn": [(pcast, 240), (pcast, 320)]}

    s2_wfc_d = dram("s2_wfc", [NCH, 128, 4096])
    s2_wproj_d = dram("s2_wproj", [32, 128, 1024])
    s3w_f32 = [dram("s3_wq", [NCH, 128, 1024]), dram("s3_wk", [NCH, 128, 1024]),
               dram("s3_wv", [NCH, 128, 1024]), dram("s3_wo", [NCH, 128, 1024])]
    pc_gates = emit_precast(s4f32)

    P.stage_barrier()
    emit_stage2(nc, P, h12, wfcB, wprojB, h23, pc2=(pcast, 32), gc=gc)
    P.stage_barrier()
    emit_stage3(nc, P, h23,
                s3wB[0], s3wB[1], s3wB[2], s3wB[3],
                dram("s3_bqk", [128, 16]), dram("s3_bo_c", [128, 8]),
                dram("s3_bv_r", [1, 1024]), dram("s3_mask", [128, 512]),
                dram("s3_ident", [128, 128]),
                h34, pc3=(pcast, 96))
    P.stage_barrier()
    emit_stage4(nc, P, h34, xo_d,
                dict(wqkvB=wqkvB, wo4B=wo4B, w1aB=w1aB, w1gB=w1gB, w2pB=w2pB,
                     tabB=tabB, pcast=pcast, pc_gates=pc_gates))
    P.finalize()
    es.close()
    return nc


def fused_inputs(xfm, inputs):
    d = {"x": xfm}
    for k, v in stage1_inputs(xfm, inputs).items():
        if k != "x":
            d["s1_" + k] = v
    for k, v in stage2_inputs(xfm, inputs).items():
        if k != "x":
            d["s2_" + k] = v
    for k, v in stage3_inputs(xfm, inputs).items():
        if k != "x":
            d["s3_" + k] = v
    for k, v in stage4_inputs(xfm, inputs).items():
        if k != "x":
            d[k] = v
    return d


def kernel(**inputs):
    x = np.asarray(inputs["L0_attn_x"], np.float32)
    xfms = [np.ascontiguousarray(x[b].T.reshape(NCH, 128, T)) for b in range(8)]
    nc = build_fused()
    base = fused_inputs(xfms[0], inputs)
    res = run_bass_kernel_spmd(nc, [dict(base, x=xfms[b]) for b in range(8)],
                               core_ids=list(range(8)))
    return np.stack([_unfeat(np.asarray(res.results[b]["xo"]))
                     for b in range(8)]).astype(np.float32)


def kernel_multilaunch(**inputs):
    x = np.asarray(inputs["L0_attn_x"], np.float32)
    xfms = [np.ascontiguousarray(x[b].T.reshape(NCH, 128, T)) for b in range(8)]
    cores = list(range(8))

    nc1 = build_stage1()
    base1 = stage1_inputs(xfms[0], inputs)
    r1 = run_bass_kernel_spmd(nc1, [dict(base1, x=xfms[b]) for b in range(8)], core_ids=cores)
    xs = [np.asarray(r1.results[b]["xo"]) for b in range(8)]

    nc2 = build_stage2()
    base2 = stage2_inputs(xs[0], inputs)
    r2 = run_bass_kernel_spmd(nc2, [dict(base2, x=xs[b]) for b in range(8)], core_ids=cores)
    xs = [np.asarray(r2.results[b]["xo"]) for b in range(8)]

    nc3 = build_stage3()
    base3 = stage3_inputs(xs[0], inputs)
    r3 = run_bass_kernel_spmd(nc3, [dict(base3, x=xs[b]) for b in range(8)], core_ids=cores)
    xs = [np.asarray(r3.results[b]["xo"]) for b in range(8)]

    nc4 = build_stage4()
    base4 = stage4_inputs(xs[0], inputs)
    r4 = run_bass_kernel_spmd(nc4, [dict(base4, x=xs[b]) for b in range(8)], core_ids=cores)
    xs = [np.asarray(r4.results[b]["xo"]) for b in range(8)]

    return np.stack([_unfeat(xb) for xb in xs]).astype(np.float32)
